# version 02
# baseline (speedup 1.0000x reference)
import contextlib
import numpy as np
import ml_dtypes
import concourse.bass as bass
import concourse.mybir as mybir
from concourse.bass_utils import run_bass_kernel_spmd

F32 = mybir.dt.float32
BF16 = mybir.dt.bfloat16
U32 = mybir.dt.uint32
AF = mybir.ActivationFunctionType
ALU = mybir.AluOpType
AX = mybir.AxisListType

B, N, D = 8, 2048, 1024
NH, DH = 16, 64
NE, CAP = 8, 2048
NT = N // 128
GT = B * NT
SLOTS = 384
GM = 96
SM = 128

ENGINES = ("tensor", "vector", "scalar", "gpsimd", "sync")
ESEM = {e: "sem_" + e for e in ENGINES}
_DT_SIZE = {"float32": 4, "float32r": 4, "bfloat16": 2, "float16": 2,
            "uint32": 4, "int32": 4, "uint16": 2, "int16": 2, "uint8": 1}


def _dt_size(dt):
    n = getattr(dt, "name", None) or str(dt).split(".")[-1]
    return _DT_SIZE.get(n, 4)


def _as_ap(x):
    return x.ap() if callable(getattr(x, "ap", None)) else x


def _ranges(x, out=None):
    if out is None:
        out = []
    if x is None or isinstance(x, (int, float, str)):
        return out
    if isinstance(x, (list, tuple)):
        for e in x:
            _ranges(e, out)
        return out
    ap = _as_ap(x)
    try:
        space = str(ap.space)
    except Exception:
        space = "?"
    name = ap.name
    if "SB" not in space.upper():
        out.append((name, 0, 1 << 20, 0, 1 << 60))
        return out
    try:
        pat = ap.ap
        offset = ap.offset
        esz = _dt_size(ap.dtype)
        pstride, pcount = pat[0]
        plo = (offset // pstride) if pstride else 0
        phi = plo + pcount
        row_off = offset - plo * pstride
        span = 1
        for step, count in pat[1:]:
            span += step * (count - 1)
        out.append((name, plo, phi, row_off * esz, (row_off + span) * esz))
    except Exception:
        out.append((name, 0, 1 << 20, 0, 1 << 60))
    return out


def _is_sbuf(x):
    try:
        return "SB" in str(_as_ap(x).space).upper()
    except Exception:
        return False


class Op:
    __slots__ = ("engine", "fn", "idx", "deps", "needs_inc", "fixed_inc",
                 "sem_val", "tag")

    def __init__(self, engine, fn, idx, tag=""):
        self.engine = engine
        self.fn = fn
        self.idx = idx
        self.deps = []
        self.needs_inc = False
        self.fixed_inc = None
        self.sem_val = None
        self.tag = tag


SERIALIZE = False


class TK:
    def __init__(self, nc, dma_queues=("gpsimd", "sync", "scalar")):
        self.nc = nc
        self.all_ops = []
        self.ctx = contextlib.ExitStack()
        self.streams = {e: [] for e in ENGINES}
        self.sem = {}
        self.cnt = {}
        self.access = {}
        self.n_edges = 0
        for e in ENGINES:
            self.semaphore(ESEM[e])
        self.dma_ring = {}
        self.dma_ring_pos = {}
        for q in dma_queues:
            depth = 6 if q != "scalar" else 2
            ring = []
            for i in range(depth):
                s = f"dma_{q}_{i}"
                self.semaphore(s)
                ring.append(s)
            self.dma_ring[q] = ring
            self.dma_ring_pos[q] = 0
        self.pending_T = []
        self.pending_P = []

    def semaphore(self, name):
        h = self.ctx.enter_context(self.nc.semaphore(name))
        self.sem[name] = h
        self.cnt[name] = 0
        return h

    def sbuf(self, name, shape, dtype):
        return self.ctx.enter_context(self.nc.sbuf_tensor(name, shape, dtype))

    def psum(self, name, shape, dtype):
        return self.ctx.enter_context(self.nc.psum_tensor(name, shape, dtype))

    def _add_access(self, op, kind, rects):
        for rect in rects:
            name, plo, phi, blo, bhi = rect
            hist = self.access.setdefault(name, [])
            for (p, k, r) in hist:
                if p is op:
                    continue
                if kind == "R" and k == "R":
                    continue
                if not (plo < r[2] and r[1] < phi and blo < r[4] and r[3] < bhi):
                    continue
                if p.engine == op.engine == "tensor":
                    continue
                op.deps.append(p)
                self.n_edges += 1
            hist.append((op, kind, rect))

    def op(self, eng, fn, reads=(), writes=(), tag=""):
        o = Op(eng, fn, len(self.streams[eng]), tag)
        self._add_access(o, "R", _ranges(list(reads)))
        self._add_access(o, "W", _ranges(list(writes)))
        if SERIALIZE and self.all_ops:
            o.deps.append(self.all_ops[-1])
        self.all_ops.append(o)
        self.streams[eng].append(o)
        return o

    def _dma_slot(self, queue, n_inc):
        ring = self.dma_ring[queue]
        i = self.dma_ring_pos[queue]
        self.dma_ring_pos[queue] = (i + 1) % len(ring)
        sname = ring[i]
        pre = self.cnt[sname]
        self.cnt[sname] += n_inc
        return sname, pre, self.cnt[sname]

    def dma(self, queue, out, in_, tag="", transpose=False):
        sname, pre, post = self._dma_slot(queue, 16)
        o = Op(queue, None, len(self.streams[queue]), tag)
        o.fixed_inc = (sname, 16, post)
        h = self.sem[sname]

        def fn(e, out=out, in_=in_, h=h, pre=pre, transpose=transpose):
            if pre:
                e.wait_ge(h, pre)
            if transpose:
                return e.dma_start_transpose(out, in_).then_inc(h, 16)
            return e.dma_start(out=out, in_=in_).then_inc(h, 16)
        o.fn = fn
        self._add_access(o, "R", _ranges(in_))
        self._add_access(o, "W", _ranges(out))
        if SERIALIZE and self.all_ops:
            o.deps.append(self.all_ops[-1])
        self.all_ops.append(o)
        if _is_sbuf(in_) and _is_sbuf(out):
            if transpose:
                o.deps.extend(self.pending_P)
                self.n_edges += len(self.pending_P)
                self.pending_P = []
                self.pending_T.append(o)
            else:
                o.deps.extend(self.pending_T)
                self.n_edges += len(self.pending_T)
                self.pending_T = []
                self.pending_P.append(o)
        self.streams[queue].append(o)
        return o

    def build(self, final_waits=(), final_engine="sync"):
        nc = self.nc
        for e in ENGINES:
            for o in self.streams[e]:
                for p in o.deps:
                    if p.fixed_inc is None:
                        p.needs_inc = True
        for e in ENGINES:
            for o in self.streams[e]:
                if o.needs_inc and o.fixed_inc is None:
                    self.cnt[ESEM[e]] += 1
                    o.sem_val = self.cnt[ESEM[e]]

        def token(p):
            if p.fixed_inc is not None:
                return (p.fixed_inc[0], p.fixed_inc[2])
            return (ESEM[p.engine], p.sem_val)

        if final_waits:
            fo = Op(final_engine, None, len(self.streams[final_engine]), "final")
            toks = [token(p) for p in final_waits]

            def ffn(e, toks=toks):
                best = {}
                for s, v in toks:
                    best[s] = max(best.get(s, 0), v)
                last = None
                for s, v in sorted(best.items()):
                    last = e.wait_ge(self.sem[s], v)
                return last
            fo.fn = ffn
            self.streams[final_engine].append(fo)

        emit_plan = {}
        n_waits = 0
        for e in ENGINES:
            water = {}
            plan = []
            for o in self.streams[e]:
                best = {}
                for p in o.deps:
                    s, v = token(p)
                    assert v is not None, f"unresolved producer {p.tag} <- {o.tag}"
                    if best.get(s, -1) < v:
                        best[s] = v
                waits = []
                for s, v in sorted(best.items()):
                    if water.get(s, 0) < v:
                        water[s] = v
                        waits.append((self.sem[s], v))
                n_waits += len(waits)
                plan.append((waits, o))
            emit_plan[e] = plan
        self.n_waits = n_waits

        with nc.Block() as block:
            for e in ENGINES:
                plan = emit_plan[e]
                if not plan:
                    continue

                def body(eng, plan=plan, e=e):
                    sem_h = self.sem[ESEM[e]]
                    for waits, o in plan:
                        for (h, v) in waits:
                            eng.wait_ge(h, v)
                        ins = o.fn(eng)
                        if o.needs_inc and o.fixed_inc is None:
                            assert ins is not None, f"op {o.tag} returned None"
                            ins.then_inc(sem_h, 1)
                getattr(block, e)(body)
        return nc


def host_consts(core):
    bf = ml_dtypes.bfloat16
    c = {}
    c["c_iota8"] = np.tile(np.arange(8, dtype=np.float32), (128, 1))
    c["c_iota"] = np.tile(np.arange(384, dtype=np.float32), (128, 1))
    c["c_piota"] = np.arange(128, dtype=np.float32)[:, None]
    k = np.arange(128)
    c["c_L128"] = (k[:, None] < k[None, :]).astype(bf)
    c["c_negid"] = (-1e9 * np.eye(128)).astype(bf)
    true = ((core + k // 16) % 8) * 16 + (k % 16)
    c["c_Lt"] = (true[:, None] < true[None, :]).astype(bf)
    c["c_ident"] = np.eye(128, dtype=np.float32)
    c["c_ones_bf"] = np.ones((1, 128), dtype=bf)
    c["c_ones_f"] = np.ones((1, 128), dtype=np.float32)
    c["c_onescol_f"] = np.ones((128, 1), dtype=np.float32)
    c["c_onescol_bf"] = np.ones((128, 1), dtype=bf)
    h0 = np.zeros((1, 128), dtype=bf); h0[0, :64] = 1
    h1 = np.zeros((1, 128), dtype=bf); h1[0, 64:] = 1
    c["c_half0"] = h0
    c["c_half1"] = h1
    return c


def gather_window(j):
    if j == 0:
        return 0, SLOTS
    lo = max(0, 16 * j - GM)
    hi = min(SLOTS, 16 * j + 16 + GM)
    if hi - lo > 208:
        hi = lo + 208
    return lo, hi - lo


def scatter_chunks(tc):
    lo = max(0, 16 * tc - SM)
    hi = min(SLOTS, 16 * tc + 16 + SM)
    return list(range(lo // 128, (hi + 127) // 128))


PHASES = None
DUMPS = False


def _on(name):
    return PHASES is None or name in PHASES


def build_program():
    nc = bass.Bass("TRN2", target_bir_lowering=False)
    tk = TK(nc)
    S, P = tk.sbuf, tk.psum

    inp = lambda n, s, d: nc.dram_tensor(n, s, d, kind="ExternalInput").ap()
    x_own = inp("x_own", [N, D], F32)
    x_rot = inp("x_rot", [B, N, D], F32)
    Wq = inp("Wq", [D, D], F32)
    Wkv = inp("Wkv", [D, 2 * D], F32)
    Wo = inp("Wo", [D, D], F32)
    bo = inp("bo", [1, D], F32)
    wg = inp("wg", [D, NE], F32)
    Wf = inp("Wf", [NE, D, D], F32)
    bfe = inp("bfe", [NE, D], F32)
    hc0 = host_consts(0)
    cns = {n: inp(n, list(v.shape), BF16 if v.dtype == ml_dtypes.bfloat16 else F32)
           for n, v in hc0.items()}
    out_t = nc.dram_tensor("out", [N, D], F32, kind="ExternalOutput").ap()

    xn_tm_t = S("xn_tm", [128, NT, 1024], BF16)
    xnfm_ar = S("xnfm_ar", [128, 16384], BF16)
    attn_t = S("attn", [128, 8, 2048], BF16)
    pairs = S("pairs", [128, 14336], BF16)
    ar2 = S("ar2", [128, 8192], BF16)
    pt_ar = S("pt_ar", [128, 3072], BF16)
    wt_ar = S("wt_ar", [128, 8192], BF16)
    gm_ar = S("gm_ar", [128, 6144], BF16)
    oh_bf = S("oh_bf", [128, 1024], BF16)
    loctok = S("loctok", [128, 128], F32)
    slotb = S("slotb", [128, 2048], F32)
    wb = S("wb", [128, 2048], BF16)
    scol = S("scol", [128, 16], F32)
    wcol = S("wcol", [128, 16], F32)
    cnttm = S("cnttm", [128, 8], BF16)
    tbtm = S("tbtm", [128, 8], F32)
    vecb = S("vecb", [128, 8], F32)
    v8row = S("v8row", [1, 8], F32)
    gstage = S("gstage", [16, 512], BF16)
    tr16 = S("tr16", [16, 128], F32)
    row2k = S("row2k", [1, 2048], F32)
    row2kb = row2k[:, :].bitcast(BF16)
    cntf = row2kb[0:1, 0:1024]
    tbf = row2k[0:1, 0:1024]
    sflat = row2k[0:1, 0:2048]
    wflat = row2k[0:1, 0:2048]
    rstd = S("rstd", [128, NT], F32)
    nmr = S("nmr", [128, NT], F32)
    bn6 = S("bn6", [128, 12], F32)
    bn2 = S("bn2", [128, 2], F32)
    lnv = S("lnv", [128, 1], F32)
    epsc = S("epsc", [128, 1], F32)
    eps8 = S("eps8", [1, 1], F32)
    l8 = S("l8", [128, 8], F32)
    mx8b = S("mx8b", [128, 8], BF16)
    mx8 = S("mx8", [128, 8], F32)
    ix8 = S("ix8", [128, 8], U32)
    ix8f = S("ix8f", [128, 1], F32)
    ex8 = S("ex8", [128, 8], F32)
    t8a = S("t8a", [128, 8], F32)
    zs = S("zs", [128, 4], F32)
    padj = S("padj", [128, 2], F32)
    s2c = row2kb[0:1, 2048:3072]
    rtmp = row2k[0:1, 0:1024]
    rrow = row2kb[0:1, 3072:4096]
    wgsb = S("wgsb", [128, 64], F32)
    wgtmp = S("wgtmp", [128, 8], F32)
    wgp = S("wgp", [128, 128], BF16)
    bo_bf = S("bo_bf", [1, 1024], BF16)
    be_bf = S("be_bf", [1, 1024], BF16)
    csb = {}
    for n, v in hc0.items():
        dt = BF16 if v.dtype == ml_dtypes.bfloat16 else F32
        csb[n] = S("s_" + n, list(v.shape), dt)

    psA = P("psA", [128, 1024], F32)
    psB = P("psB", [128, 1024], F32)
    psC = P("psC", [128, 512], F32)
    psD = P("psD", [128, 512], F32)
    psE = P("psE", [128, 512], F32)
    psF = P("psF", [128, 512], F32)

    xn_tm = xn_tm_t[:, :, :]
    xnfm = xnfm_ar[:, :].rearrange("p (dc t) -> p dc t", dc=8)
    attn = attn_t[:, :, :]
    gt = attn_t[:, 0, :].rearrange("p (j e) -> p j e", e=16)
    exclv = gm_ar[:, 0:2048].bitcast(F32)
    locv = gm_ar[:, 2048:4096].bitcast(F32)
    ohf = gm_ar[:, 4096:6144].bitcast(F32)
    stores = []

    qP = pairs[:, 0:2048]
    kP = pairs[:, 2048:4096]
    vS = pairs[:, 4096:6144]
    vT = pairs[:, 6144:8224].rearrange("p (j c) -> p j c", c=130)
    wqS = [pairs[:, 8224 + i * 1024: 8224 + (i + 1) * 1024] for i in range(2)]
    wkS = [pairs[:, 10272 + i * 1024: 10272 + (i + 1) * 1024] for i in range(2)]
    wvS = [pairs[:, 12320:13344], pairs[:, 12320:13344]]

    def mm(ps, lh, rh, start=True, stop=True, tag=""):
        return tk.op("tensor",
                     lambda t, a=ps, b=lh, c=rh, s1=start, s2=stop:
                     t.matmul(a, b, c, start=s1, stop=s2),
                     reads=[lh, rh], writes=[ps], tag=tag)

    def act(out, in_, func, bias=0.0, scale=1.0, tag=""):
        reads = [in_]
        if not isinstance(bias, float):
            reads.append(bias)
        if not isinstance(scale, float):
            reads.append(scale)
        return tk.op("scalar",
                     lambda s, o=out, i=in_, f=func, b=bias, sc=scale:
                     s.activation(o, i, f, bias=b, scale=sc),
                     reads=reads, writes=[out], tag=tag)

    def vcopy(out, in_, tag=""):
        return tk.op("vector", lambda v, o=out, i=in_: v.tensor_copy(o, i),
                     reads=[in_], writes=[out], tag=tag)

    def scopy(out, in_, tag=""):
        return act(out, in_, AF.Identity, tag=tag)

    def vtt(out, a, b, op, tag=""):
        return tk.op("vector",
                     lambda v, o=out, x=a, y=b, w=op: v.tensor_tensor(o, x, y, w),
                     reads=[a, b], writes=[out], tag=tag)

    def vts(out, in_, s1, op, tag=""):
        reads = [in_] + ([s1] if not isinstance(s1, float) else [])
        return tk.op("vector",
                     lambda v, o=out, i=in_, s=s1, w=op:
                     v.tensor_single_scalar(o, i, s, w),
                     reads=reads, writes=[out], tag=tag)

    def vstt(out, in0, s1, in1, op0, op1, tag=""):
        reads = [in0, in1] + ([s1] if not isinstance(s1, float) else [])
        return tk.op("vector",
                     lambda v, o=out, a=in0, s=s1, b=in1, w0=op0, w1=op1:
                     v.scalar_tensor_tensor(o, a, s, b, w0, w1),
                     reads=reads, writes=[out], tag=tag)

    for n in cns:
        tk.dma("sync", csb[n][:, :], cns[n], tag="c_" + n)
    tk.dma("sync", wgsb[:, :].rearrange("p (dc e) -> p dc e", dc=8),
           wg.rearrange("(dc p) e -> p dc e", p=128), tag="wg")
    tk.dma("gpsimd", bo_bf[:, :], bo, tag="bo")
    tk.op("vector", lambda v: v.memset(epsc[:, :], 1e-5), writes=[epsc[:, :]])
    tk.op("vector", lambda v: v.memset(eps8[:, :], 1e-8), writes=[eps8[:, :]])
    tk.op("vector", lambda v: v.memset(wgp[:, :], 0.0), writes=[wgp[:, :]])

    if _on("route1"):
        mm(psA[0:1, 0:64], csb["c_onescol_f"][:, :], wgsb[:, :], tag="wgsum")
        vcopy(tr16[0:1, 0:64], psA[0:1, 0:64], tag="wgsum_ev")
        tk.op("vector",
              lambda v: v.reduce_sum(
                  wgtmp[0:1, 0:8],
                  tr16[0:1, 0:64].rearrange("o (dc e) -> o e dc", e=8),
                  axis=AX.X),
              reads=[tr16[0:1, 0:64]], writes=[wgtmp[0:1, 0:8]], tag="wgred")
        vts(wgtmp[0:1, 0:8], wgtmp[0:1, 0:8], -1.0 / 1024.0, ALU.mult, tag="wgsc")
        mm(psB[:, 0:8], csb["c_ones_f"][:, :], wgtmp[0:1, 0:8], tag="wgbc")
        for dc in range(8):
            vtt(wgtmp[:, 0:8], psB[:, 0:8], wgsb[:, 8 * dc:8 * dc + 8],
                ALU.add, tag=f"wgp{dc}")
            vcopy(wgp[:, 16 * dc:16 * dc + 8], wgtmp[:, 0:8], tag=f"wgpc{dc}")

    if _on("ln"):
        for j in range(NT):
            xs = ar2[:, (j % 2) * 2048:(j % 2) * 2048 + 2048].bitcast(F32)
            tk.dma("sync", xs, x_own[128 * j:128 * j + 128, :], tag=f"xl{j}")
            tk.op("vector", lambda v, a=xs: v.bn_stats(bn6[:, 0:6], a[:, 0:512]),
                  reads=[xs[:, 0:512]], writes=[bn6[:, 0:6]], tag=f"bns0_{j}")
            tk.op("vector", lambda v, a=xs: v.bn_stats(bn6[:, 6:12], a[:, 512:1024]),
                  reads=[xs[:, 512:1024]], writes=[bn6[:, 6:12]], tag=f"bns1_{j}")
            tk.op("vector", lambda v: v.bn_aggr(bn2[:, :], bn6[:, :]),
                  reads=[bn6[:, :]], writes=[bn2[:, :]], tag=f"bna{j}")
            act(lnv[:, :], bn2[:, 1:2], AF.Ln, bias=epsc[:, :], tag=f"ln{j}")
            act(rstd[:, j:j + 1], lnv[:, :], AF.Exp, scale=-0.5, tag=f"rs{j}")
            vtt(nmr[:, j:j + 1], bn2[:, 0:1], rstd[:, j:j + 1], ALU.mult,
                tag=f"nm{j}")
            vts(nmr[:, j:j + 1], nmr[:, j:j + 1], -1.0, ALU.mult, tag=f"nmn{j}")
            act(xn_tm[:, j, :], xs, AF.Identity, bias=nmr[:, j:j + 1],
                scale=rstd[:, j:j + 1], tag=f"xn{j}")

    if _on("xnfm"):
        for j in range(NT):
            tk.dma("sync",
                   xnfm[:, :, 128 * j:128 * j + 128],
                   xn_tm[:, j, :].rearrange("p (dc q) -> p dc q", dc=8),
                   tag=f"xt{j}", transpose=True)

    if _on("route4"):
        for g in range(GT):
            bidx, jj = g // NT, g % NT
            sl = g % 2
            xh = ar2[:, 4096 + sl * 1024: 4096 + sl * 1024 + 1024]
            xf = ar2[:, 6144 + sl * 1024: 6144 + sl * 1024 + 1024]
            tk.dma("gpsimd", xh,
                   x_rot[bidx, 128 * jj:128 * jj + 128, :], tag=f"xh{g}")
            tk.dma("sync", xf.rearrange("p (dc q) -> p dc q", dc=8),
                   xh.rearrange("p (dc q) -> p dc q", dc=8),
                   tag=f"xf{g}", transpose=True)
            xfv = xf.rearrange("p (dc q) -> p dc q", dc=8)
            for dc in range(8):
                mm(psA[0:16, (g % 4) * 128:(g % 4) * 128 + 128],
                   wgp[:, 16 * dc:16 * dc + 16], xfv[:, dc, :],
                   start=(dc == 0), stop=(dc == 7), tag=f"g{g}_{dc}")
            if g % 4 == 3:
                g0 = g - 3
                scopy(gstage[:, :], psA[0:16, 0:512], tag=f"gev{g0}")
                tk.dma("sync",
                       gt[:, g0:g0 + 4, :],
                       gstage[:, :].rearrange("p (k q) -> p k q", k=4),
                       tag=f"gtr{g0}", transpose=True)

    if _on("route5"):
        for j in range(GT):
            gsl = gt[:, j, 0:8]
            tk.op("vector", lambda v, a=gsl: v.max(mx8b[:, :], a),
                  reads=[gsl], writes=[mx8b[:, :]], tag=f"mx{j}")
            tk.op("vector", lambda v, a=gsl: v.max_index(ix8[:, :], mx8b[:, :], a),
                  reads=[gsl, mx8b[:, :]], writes=[ix8[:, :]], tag=f"mi{j}")
            vcopy(ix8f[:, :], ix8[:, 0:1], tag=f"ic{j}")
            vts(oh_bf[:, 8 * j:8 * j + 8], csb["c_iota8"][:, :], ix8f[:, :],
                ALU.is_equal, tag=f"oh{j}")
            vts(ohf[:, 8 * j:8 * j + 8], csb["c_iota8"][:, :], ix8f[:, :],
                ALU.is_equal, tag=f"ohf{j}")
        for hhalf in range(2):
            sl = slice(512 * hhalf, 512 * hhalf + 512)
            mm(psA[:, sl], csb["c_L128"][:, :], oh_bf[:, sl], tag=f"ex{hhalf}")
            vcopy(exclv[:, sl], psA[:, sl], tag=f"exev{hhalf}")
        for hhalf in range(2):
            sl = slice(512 * hhalf, 512 * hhalf + 512)
            mm(psB[0:1, sl], csb["c_onescol_bf"][:, :], oh_bf[:, sl],
               tag=f"cnt{hhalf}")
        scopy(cntf, psB[0:1, 0:1024], tag="cntev")
        tk.dma("gpsimd", cnttm[:, :], cntf, tag="cntrs")
        mm(psB[:, 0:8], csb["c_Lt"][:, :], cnttm[:, :], tag="tbmm")
        vcopy(tbtm[:, :], psB[:, 0:8], tag="tbev")
        tk.dma("gpsimd", tbf, tbtm[:, :], tag="tbrs")
        for hhalf in range(2):
            sl = slice(512 * hhalf, 512 * hhalf + 512)
            mm(psA[:, sl], csb["c_ones_f"][:, :], tbf[0:1, sl], tag=f"tb{hhalf}")
            vstt(locv[:, sl], psA[:, sl], 1.0, exclv[:, sl], ALU.mult, ALU.add,
                 tag=f"loc{hhalf}")
        for hhalf in range(2):
            sl = slice(512 * hhalf, 512 * hhalf + 512)
            vtt(psB[:, sl], ohf[:, sl], locv[:, sl], ALU.mult, tag=f"lt{hhalf}")
        tk.op("vector",
              lambda v: v.reduce_sum(
                  loctok[:, :],
                  psB[:, 0:1024].rearrange("p (j e) -> p j e", e=8), axis=AX.X),
              reads=[psB[:, :]], writes=[loctok[:, :]], tag="ltred")

    if _on("route6"):
        vts(v8row[:, :], csb["c_iota8"][0:1, 0:8], float(SLOTS), ALU.mult,
            tag="v8m")
        vtt(v8row[:, :], v8row[:, :], tbf[0:1, 0:8], ALU.subtract, tag="v8s")
        mm(psB[:, 0:8], csb["c_ones_f"][:, :], v8row[0:1, 0:8], tag="v8bc")
        vcopy(vecb[:, :], psB[:, 0:8], tag="v8ev")
        for j in range(NT):
            gsl = gt[:, j, 0:8]
            tk.op("vector",
                  lambda v, a=gsl, r=rstd[:, j:j + 1]:
                  v.tensor_single_scalar(l8[:, :], a, r, ALU.mult),
                  reads=[gsl, rstd[:, j:j + 1]], writes=[l8[:, :]],
                  tag=f"l8{j}")
            tk.op("vector", lambda v: v.max(mx8[:, :], l8[:, :]),
                  reads=[l8[:, :]], writes=[mx8[:, :]], tag=f"lm{j}")
            vts(zs[:, 0:1], mx8[:, 0:1], -1.0, ALU.mult, tag=f"ln{j}")
            act(ex8[:, :], l8[:, :], AF.Exp, bias=zs[:, 0:1], tag=f"ge{j}")
            tk.op("vector",
                  lambda v: v.reduce_sum(zs[:, 1:2], ex8[:, :], axis=AX.X),
                  reads=[ex8[:, :]], writes=[zs[:, 1:2]], tag=f"gz{j}")
            tk.op("vector", lambda v: v.reciprocal(zs[:, 2:3], zs[:, 1:2]),
                  reads=[zs[:, 1:2]], writes=[zs[:, 2:3]], tag=f"gr{j}")
            vts(zs[:, 3:4], loctok[:, j:j + 1], float(CAP), ALU.is_lt,
                tag=f"kp{j}")
            vtt(wcol[:, j:j + 1], zs[:, 2:3], zs[:, 3:4], ALU.mult,
                tag=f"w{j}")
            vtt(t8a[:, :], locv[:, 8 * j:8 * j + 8], vecb[:, :], ALU.add,
                tag=f"sl{j}")
            vtt(t8a[:, :], t8a[:, :], ohf[:, 8 * j:8 * j + 8], ALU.mult,
                tag=f"slm{j}")
            tk.op("vector",
                  lambda v, o=scol[:, j:j + 1]: v.reduce_sum(o, t8a[:, :],
                                                             axis=AX.X),
                  reads=[t8a[:, :]], writes=[scol[:, j:j + 1]], tag=f"sr{j}")

    if _on("route7"):
        for (colsrc, flat, dstb, dt_out, tag) in (
                (scol, sflat, slotb, F32, "s"), (wcol, wflat, wb, BF16, "w")):
            tk.op("tensor",
                  lambda t, a=colsrc: t.transpose(psA[0:16, 0:128], a[:, :],
                                                  csb["c_ident"][:, :]),
                  reads=[colsrc[:, :], csb["c_ident"][:, :]],
                  writes=[psA[:, :]], tag=f"{tag}tr")
            vcopy(tr16[:, :], psA[0:16, 0:128], tag=f"{tag}trev")
            tk.dma("gpsimd", flat[0:1, 0:2048], tr16[:, :], tag=f"{tag}rs")
            for q in range(4):
                sl = slice(512 * q, 512 * q + 512)
                mm(psB[:, 0:512] if q % 2 == 0 else psB[:, 512:1024],
                   csb["c_ones_f"][:, :], flat[0:1, sl], tag=f"{tag}bc{q}")
                src = psB[:, 0:512] if q % 2 == 0 else psB[:, 512:1024]
                vcopy(dstb[:, sl], src, tag=f"{tag}be{q}")

    if _on("attn"):
        WqR = Wq.rearrange("(dc p) i -> p dc i", p=128)
        WkvR = Wkv.rearrange("(dc p) i -> p dc i", p=128)
        for pr in range(8):
            sl = pr % 2
            tk.dma("gpsimd", wqS[sl].rearrange("p (dc q) -> p dc q", dc=8),
                   WqR[:, :, 128 * pr:128 * pr + 128], tag=f"wq{pr}")
            tk.dma("gpsimd", wkS[sl].rearrange("p (dc q) -> p dc q", dc=8),
                   WkvR[:, :, 128 * pr:128 * pr + 128], tag=f"wk{pr}")
            tk.dma("gpsimd", wvS[sl].rearrange("p (dc q) -> p dc q", dc=8),
                   WkvR[:, :, 1024 + 128 * pr:1024 + 128 * pr + 128],
                   tag=f"wv{pr}")
            wqv = wqS[sl].rearrange("p (dc q) -> p dc q", dc=8)
            wkv = wkS[sl].rearrange("p (dc q) -> p dc q", dc=8)
            wvv = wvS[sl].rearrange("p (dc q) -> p dc q", dc=8)
            for (wsl, dst, scale, tg) in ((wqv, qP, 0.125, "q"),
                                          (wkv, kP, 1.0, "k"),
                                          (wvv, vS, 1.0, "v")):
                for nck in range(4):
                    ps = psB[:, 0:512] if nck % 2 == 0 else psB[:, 512:1024]
                    csl = slice(512 * nck, 512 * nck + 512)
                    for dc in range(8):
                        mm(ps, wsl[:, dc, :], xnfm[:, dc, csl],
                           start=(dc == 0), stop=(dc == 7),
                           tag=f"{tg}p{pr}_{nck}_{dc}")
                    act(dst[:, csl], ps, AF.Identity, scale=scale,
                        tag=f"{tg}e{pr}_{nck}")
            vSv = vS.rearrange("p (j q) -> p j q", j=NT)
            for h01t in range(2):
                slot = (2 * pr + h01t) % 4
                scrv = gm_ar[:, 1024 * slot: 1024 * slot + 1024].rearrange(
                    "p (j c) -> p j c", c=64)
                tk.dma("sync", scrv, vSv[64 * h01t:64 * h01t + 64, :, :],
                       tag=f"vt{h01t}_{pr}", transpose=True)
                vcopy(vT[:, :, 65 * h01t:65 * h01t + 64], scrv,
                      tag=f"vc{h01t}_{pr}")
            tk.op("vector", lambda v: v.memset(vT[:, :, 64:65], 1.0),
                  writes=[vT[:, :, 64:65]], tag=f"vo0_{pr}")
            tk.op("vector", lambda v: v.memset(vT[:, :, 129:130], 1.0),
                  writes=[vT[:, :, 129:130]], tag=f"vo1_{pr}")
            for c in range(4):
                for h01 in range(2):
                    hb = 64 * h01
                    sps = psC if h01 == 0 else psD
                    nps = psE if h01 == 0 else psF
                    for j in range(4 * c + 4):
                        voff = max(0, 128 * j - 512 * c)
                        diag = j >= 4 * c
                        mm(sps[:, voff:512],
                           kP[hb:hb + 64, 128 * j:128 * j + 128],
                           qP[hb:hb + 64, 512 * c + voff:512 * c + 512],
                           start=True, stop=not diag,
                           tag=f"s{pr}_{h01}_{c}_{j}")
                        if diag:
                            mm(sps[:, voff:voff + 128], csb["c_L128"][:, :],
                               csb["c_negid"][:, :], start=False, stop=True,
                               tag=f"m{pr}_{h01}_{c}_{j}")
                        pt = pt_ar[:, 512 * ((2 * j + h01) % 4):
                                   512 * ((2 * j + h01) % 4) + 512]
                        act(pt[:, voff:512], sps[:, voff:512], AF.Exp,
                            tag=f"e{pr}_{h01}_{c}_{j}")
                        mm(nps[0:65, voff:512],
                           vT[:, j, 65 * h01:65 * h01 + 65], pt[:, voff:512],
                           start=(j == 0), stop=(j == 4 * c + 3),
                           tag=f"n{pr}_{h01}_{c}_{j}")
                scopy(attn[0:64, pr, 512 * c:512 * c + 512], psE[0:64, :],
                      tag=f"ae{pr}_{c}")
                scopy(attn[64:128, pr, 512 * c:512 * c + 512], psF[0:64, :],
                      tag=f"ao{pr}_{c}")
                scopy(s2c[0:1, 0:512], psE[64:65, :], tag=f"se{pr}_{c}")
                scopy(s2c[0:1, 512:1024], psF[64:65, :], tag=f"so{pr}_{c}")
                act(rtmp[:, :], s2c[:, :], AF.Ln, bias=eps8[:, :],
                    tag=f"rl{pr}_{c}")
                act(rrow[:, :], rtmp[:, :], AF.Exp, scale=-1.0,
                    tag=f"rx{pr}_{c}")
                mm(psA[:, 0:512], csb["c_half0"][:, :], rrow[0:1, 0:512],
                   start=True, stop=False, tag=f"rb0{pr}_{c}")
                mm(psA[:, 0:512], csb["c_half1"][:, :], rrow[0:1, 512:1024],
                   start=False, stop=True, tag=f"rb1{pr}_{c}")
                rsb = oh_bf[:, 512 * (c % 2):512 * (c % 2) + 512]
                scopy(rsb, psA[:, 0:512], tag=f"re{pr}_{c}")
                vtt(attn[:, pr, 512 * c:512 * c + 512],
                    attn[:, pr, 512 * c:512 * c + 512], rsb, ALU.mult,
                    tag=f"dv{pr}_{c}")

    Y = []
    for e in range(NE):
        if e <= 4:
            Y.append(xnfm_ar[:, 3072 * e:3072 * e + 3072])
        elif e == 5:
            Y.append(pt_ar[:, 0:3072])
        else:
            Y.append(pairs[:, 3072 * (e - 6):3072 * (e - 6) + 3072])
    Yv = [y.rearrange("p (sc f) -> p sc f", sc=3) for y in Y]
    wstage = ar2[:, 0:8192].rearrange("p (fc h) -> p fc h", fc=8)
    wt = wt_ar[:, :].rearrange("p (hc f) -> p hc f", hc=8)

    if _on("ffn"):
        for e in range(NE):
            tk.dma("gpsimd", wstage,
                   Wf[e, :, :].rearrange("(fc p) h -> p fc h", p=128),
                   tag=f"wf{e}")
            for fc in range(8):
                tk.dma("sync",
                       wt[:, :, 128 * fc:128 * fc + 128],
                       wstage[:, fc, :],
                       tag=f"wt{e}_{fc}", transpose=True)
            tk.dma("gpsimd", be_bf[0:1, 0:1024], bfe[e:e + 1, :], tag=f"be{e}")
            eg, eo = e // 2, e % 2
            PT = []
            for j in range(NT):
                wlo, W = gather_window(j)
                off = 3584 * eo + (0 if j == 0 else 384 + 208 * (j - 1))
                ptile = pairs[:, off: off + W]
                vts(padj[:, 0:1], scol[:, j:j + 1], -float(SLOTS * e + wlo),
                    ALU.add, tag=f"pa{e}_{j}")
                vts(ptile, csb["c_iota"][:, 0:W], padj[:, 0:1], ALU.is_equal,
                    tag=f"pc{e}_{j}")
                PT.append((wlo, W, ptile))
            gsp = psC if eo == 0 else psD
            gdst = pairs[:, 7168 + 3072 * eo: 7168 + 3072 * eo + 3072]
            gv = gdst.rearrange("p (hc s) -> p hc s", hc=8)
            for hcx in range(8):
                for j in range(NT):
                    wlo, W, ptile = PT[j]
                    mm(gsp[:, wlo:wlo + W],
                       xn_tm[:, j, 128 * hcx:128 * hcx + 128], ptile,
                       start=(j == 0), stop=(j == NT - 1),
                       tag=f"gm{e}_{hcx}_{j}")
                vcopy(gv[:, hcx, 0:SLOTS], gsp[:, 0:SLOTS], tag=f"ge{e}_{hcx}")
            for sc in range(3):
                for fh in range(2):
                    yp = (psE if fh == 0 else psF)[:, :]
                    for hcx in range(8):
                        mm(yp, gv[:, hcx, 128 * sc:128 * sc + 128],
                           wt[:, hcx, 512 * fh:512 * fh + 512],
                           start=(hcx == 0), stop=False,
                           tag=f"ym{e}_{sc}_{fh}_{hcx}")
                    mm(yp, csb["c_ones_bf"][:, :],
                       be_bf[0:1, 512 * fh:512 * fh + 512],
                       start=False, stop=True, tag=f"yb{e}_{sc}_{fh}")
                    scopy(Yv[e][:, sc, 512 * fh:512 * fh + 512], yp,
                          tag=f"ye{e}_{sc}_{fh}")

    if _on("tail"):
        tk.dma("gpsimd", wt_ar[:, :].rearrange("p (ic d) -> p ic d", ic=8),
               Wo.rearrange("(ic p) d -> p ic d", p=128), tag="wo")
        wov = wt_ar[:, :].rearrange("p (ic d) -> p ic d", ic=8)
        for tc in range(NT):
            sl = tc % 2
            xr = ar2[:, 2048 * sl:2048 * sl + 2048].bitcast(F32)
            tk.dma("sync", xr, x_own[128 * tc:128 * tc + 128, :], tag=f"xr{tc}")
            rws = []
            k = 0
            for e in range(NE):
                for sc in scatter_chunks(tc):
                    rt = gm_ar[:, 3072 * sl + 128 * k: 3072 * sl + 128 * k + 128]
                    k += 1
                    vts(padj[:, 1:2], csb["c_piota"][:, :],
                        float(SLOTS * e + 128 * sc), ALU.add, tag=f"ra{tc}_{e}_{sc}")
                    vts(rt, slotb[:, 128 * tc:128 * tc + 128], padj[:, 1:2],
                        ALU.is_equal, tag=f"rc{tc}_{e}_{sc}")
                    vtt(rt, rt, wb[:, 128 * tc:128 * tc + 128], ALU.mult,
                        tag=f"rw{tc}_{e}_{sc}")
                    rws.append((e, sc, rt))
            ps = psA if sl == 0 else psB
            for dh in range(2):
                psd = ps[:, 512 * dh:512 * dh + 512]
                for ic in range(8):
                    mm(psd, attn[:, ic, 128 * tc:128 * tc + 128],
                       wov[:, ic, 512 * dh:512 * dh + 512],
                       start=(ic == 0), stop=False, tag=f"pj{tc}_{dh}_{ic}")
                mm(psd, csb["c_ones_bf"][:, :],
                   bo_bf[0:1, 512 * dh:512 * dh + 512],
                   start=False, stop=False, tag=f"pb{tc}_{dh}")
                for i, (e, sc, rt) in enumerate(rws):
                    mm(psd, rt, Yv[e][:, sc, 512 * dh:512 * dh + 512],
                       start=False, stop=(i == len(rws) - 1),
                       tag=f"sc{tc}_{dh}_{e}_{sc}")
                vstt(xr[:, 512 * dh:512 * dh + 512], psd, 1.0,
                     xr[:, 512 * dh:512 * dh + 512], ALU.mult, ALU.add,
                     tag=f"oev{tc}_{dh}")
            stores.append(tk.dma("sync", out_t[128 * tc:128 * tc + 128, :],
                                 xr, tag=f"st{tc}"))

    if DUMPS:
        def dump(nm, src, shape, dt):
            t = nc.dram_tensor(nm, shape, dt, kind="ExternalOutput").ap()
            stores.append(tk.dma("gpsimd", t, src, tag="d_" + nm))
        dump("d_xn", xn_tm_t[:, :, :], [128, NT, 1024], BF16)
        dump("d_attn", attn_t[:, :, :], [128, 8, 2048], BF16)
        dump("d_slotb", slotb[:, :], [128, 2048], F32)
        dump("d_wb", wb[:, :], [128, 2048], BF16)
        dump("d_loctok", loctok[:, :], [128, 128], F32)
        dump("d_scol", scol[:, :], [128, 16], F32)
        dump("d_wcol", wcol[:, :], [128, 16], F32)
        dump("d_xnfm", xnfm_ar[:, :], [128, 16384], BF16)
        dump("d_ptar", pt_ar[:, :], [128, 3072], BF16)
        dump("d_pairs", pairs[:, 0:13344], [128, 13344], BF16)
        dump("d_rstd", rstd[:, :], [128, NT], F32)

    tk.build(final_waits=stores, final_engine="sync")
    return nc, tk


_CACHE = {}


def _core_inputs(core, inputs):
    x = np.ascontiguousarray(inputs["L0_attn_x"])
    d = {
        "x_own": x[core],
        "x_rot": np.roll(x, -core, axis=0),
        "Wq": inputs["L0_attn_Wq"],
        "Wkv": inputs["L0_attn_Wkv"],
        "Wo": inputs["L0_attn_Wo"],
        "bo": np.ascontiguousarray(inputs["L0_attn_bo"]).reshape(1, D),
        "wg": inputs["L0_ffn_wg"],
        "Wf": inputs["L0_ffn_W"],
        "bfe": inputs["L0_ffn_b"],
    }
    d.update(host_consts(core))
    return {k: np.ascontiguousarray(v) for k, v in d.items()}


def kernel(**inputs):
    if "nc" not in _CACHE:
        _CACHE["nc"] = build_program()[0]
    nc = _CACHE["nc"]
    in_maps = [_core_inputs(c, inputs) for c in range(B)]
    res = run_bass_kernel_spmd(nc, in_maps, core_ids=list(range(B)))
    return np.stack([np.asarray(res.results[c]["out"], dtype=np.float32)
                     for c in range(B)], axis=0)


if __name__ == "__main__":
    nc, tk = build_program()
    ni = sum(len(s) for s in tk.streams.values())
    print(f"ops={ni} edges={tk.n_edges} waits={tk.n_waits}")


# version 03
# speedup vs baseline: 1.1645x; 1.1645x over previous
import contextlib
import numpy as np
import ml_dtypes
import concourse.bass as bass
import concourse.mybir as mybir
from concourse.bass_utils import run_bass_kernel_spmd

F32 = mybir.dt.float32
BF16 = mybir.dt.bfloat16
U32 = mybir.dt.uint32
AF = mybir.ActivationFunctionType
ALU = mybir.AluOpType
AX = mybir.AxisListType

B, N, D = 8, 2048, 1024
NH, DH = 16, 64
NE, CAP = 8, 2048
NT = N // 128
GT = B * NT
SLOTS = 384
GM = 96
SM = 128

ENGINES = ("tensor", "vector", "scalar", "gpsimd", "sync")
ESEM = {e: "sem_" + e for e in ENGINES}
_DT_SIZE = {"float32": 4, "float32r": 4, "bfloat16": 2, "float16": 2,
            "uint32": 4, "int32": 4, "uint16": 2, "int16": 2, "uint8": 1}


def _dt_size(dt):
    n = getattr(dt, "name", None) or str(dt).split(".")[-1]
    return _DT_SIZE.get(n, 4)


def _as_ap(x):
    return x.ap() if callable(getattr(x, "ap", None)) else x


def _ranges(x, out=None):
    if out is None:
        out = []
    if x is None or isinstance(x, (int, float, str)):
        return out
    if isinstance(x, (list, tuple)):
        for e in x:
            _ranges(e, out)
        return out
    ap = _as_ap(x)
    try:
        space = str(ap.space)
    except Exception:
        space = "?"
    name = ap.name
    if "SB" not in space.upper():
        out.append((name, 0, 1 << 20, 0, 1 << 60))
        return out
    try:
        pat = ap.ap
        offset = ap.offset
        esz = _dt_size(ap.dtype)
        pstride, pcount = pat[0]
        plo = (offset // pstride) if pstride else 0
        phi = plo + pcount
        row_off = offset - plo * pstride
        span = 1
        for step, count in pat[1:]:
            span += step * (count - 1)
        out.append((name, plo, phi, row_off * esz, (row_off + span) * esz))
    except Exception:
        out.append((name, 0, 1 << 20, 0, 1 << 60))
    return out


def _is_sbuf(x):
    try:
        return "SB" in str(_as_ap(x).space).upper()
    except Exception:
        return False


class Op:
    __slots__ = ("engine", "fn", "idx", "deps", "needs_inc", "fixed_inc",
                 "sem_val", "tag")

    def __init__(self, engine, fn, idx, tag=""):
        self.engine = engine
        self.fn = fn
        self.idx = idx
        self.deps = []
        self.needs_inc = False
        self.fixed_inc = None
        self.sem_val = None
        self.tag = tag


SERIALIZE = False


class TK:
    def __init__(self, nc, dma_queues=("gpsimd", "sync", "scalar")):
        self.nc = nc
        self.all_ops = []
        self.ctx = contextlib.ExitStack()
        self.streams = {e: [] for e in ENGINES}
        self.sem = {}
        self.cnt = {}
        self.access = {}
        self.n_edges = 0
        for e in ENGINES:
            self.semaphore(ESEM[e])
        self.dma_ring = {}
        self.dma_ring_pos = {}
        for q in dma_queues:
            depth = 6 if q != "scalar" else 2
            ring = []
            for i in range(depth):
                s = f"dma_{q}_{i}"
                self.semaphore(s)
                ring.append(s)
            self.dma_ring[q] = ring
            self.dma_ring_pos[q] = 0
        self.pending_T = []
        self.pending_P = []

    def semaphore(self, name):
        h = self.ctx.enter_context(self.nc.semaphore(name))
        self.sem[name] = h
        self.cnt[name] = 0
        return h

    def sbuf(self, name, shape, dtype):
        return self.ctx.enter_context(self.nc.sbuf_tensor(name, shape, dtype))

    def psum(self, name, shape, dtype):
        return self.ctx.enter_context(self.nc.psum_tensor(name, shape, dtype))

    def _add_access(self, op, kind, rects):
        for rect in rects:
            name, plo, phi, blo, bhi = rect
            hist = self.access.setdefault(name, [])
            for (p, k, r) in hist:
                if p is op:
                    continue
                if kind == "R" and k == "R":
                    continue
                if not (plo < r[2] and r[1] < phi and blo < r[4] and r[3] < bhi):
                    continue
                if p.engine == op.engine == "tensor":
                    continue
                op.deps.append(p)
                self.n_edges += 1
            hist.append((op, kind, rect))

    def op(self, eng, fn, reads=(), writes=(), tag=""):
        o = Op(eng, fn, len(self.streams[eng]), tag)
        self._add_access(o, "R", _ranges(list(reads)))
        self._add_access(o, "W", _ranges(list(writes)))
        if SERIALIZE and self.all_ops:
            o.deps.append(self.all_ops[-1])
        self.all_ops.append(o)
        self.streams[eng].append(o)
        return o

    def _dma_slot(self, queue, n_inc):
        ring = self.dma_ring[queue]
        i = self.dma_ring_pos[queue]
        self.dma_ring_pos[queue] = (i + 1) % len(ring)
        sname = ring[i]
        pre = self.cnt[sname]
        self.cnt[sname] += n_inc
        return sname, pre, self.cnt[sname]

    def dma(self, queue, out, in_, tag="", transpose=False):
        sname, pre, post = self._dma_slot(queue, 16)
        o = Op(queue, None, len(self.streams[queue]), tag)
        o.fixed_inc = (sname, 16, post)
        h = self.sem[sname]

        def fn(e, out=out, in_=in_, h=h, pre=pre, transpose=transpose):
            if pre:
                e.wait_ge(h, pre)
            if transpose:
                return e.dma_start_transpose(out, in_).then_inc(h, 16)
            return e.dma_start(out=out, in_=in_).then_inc(h, 16)
        o.fn = fn
        self._add_access(o, "R", _ranges(in_))
        self._add_access(o, "W", _ranges(out))
        if SERIALIZE and self.all_ops:
            o.deps.append(self.all_ops[-1])
        self.all_ops.append(o)
        if _is_sbuf(in_) and _is_sbuf(out):
            if transpose:
                o.deps.extend(self.pending_P)
                self.n_edges += len(self.pending_P)
                self.pending_P = []
                self.pending_T.append(o)
            else:
                o.deps.extend(self.pending_T)
                self.n_edges += len(self.pending_T)
                self.pending_T = []
                self.pending_P.append(o)
        self.streams[queue].append(o)
        return o

    def build(self, final_waits=(), final_engine="sync"):
        nc = self.nc
        for e in ENGINES:
            for o in self.streams[e]:
                for p in o.deps:
                    if p.fixed_inc is None:
                        p.needs_inc = True
        for e in ENGINES:
            for o in self.streams[e]:
                if o.needs_inc and o.fixed_inc is None:
                    self.cnt[ESEM[e]] += 1
                    o.sem_val = self.cnt[ESEM[e]]

        def token(p):
            if p.fixed_inc is not None:
                return (p.fixed_inc[0], p.fixed_inc[2])
            return (ESEM[p.engine], p.sem_val)

        if final_waits:
            fo = Op(final_engine, None, len(self.streams[final_engine]), "final")
            toks = [token(p) for p in final_waits]

            def ffn(e, toks=toks):
                best = {}
                for s, v in toks:
                    best[s] = max(best.get(s, 0), v)
                last = None
                for s, v in sorted(best.items()):
                    last = e.wait_ge(self.sem[s], v)
                return last
            fo.fn = ffn
            self.streams[final_engine].append(fo)

        emit_plan = {}
        n_waits = 0
        for e in ENGINES:
            water = {}
            plan = []
            for o in self.streams[e]:
                best = {}
                for p in o.deps:
                    s, v = token(p)
                    assert v is not None, f"unresolved producer {p.tag} <- {o.tag}"
                    if best.get(s, -1) < v:
                        best[s] = v
                waits = []
                for s, v in sorted(best.items()):
                    if water.get(s, 0) < v:
                        water[s] = v
                        waits.append((self.sem[s], v))
                n_waits += len(waits)
                plan.append((waits, o))
            emit_plan[e] = plan
        self.n_waits = n_waits

        with nc.Block() as block:
            for e in ENGINES:
                plan = emit_plan[e]
                if not plan:
                    continue

                def body(eng, plan=plan, e=e):
                    sem_h = self.sem[ESEM[e]]
                    for waits, o in plan:
                        for (h, v) in waits:
                            eng.wait_ge(h, v)
                        ins = o.fn(eng)
                        if o.needs_inc and o.fixed_inc is None:
                            assert ins is not None, f"op {o.tag} returned None"
                            ins.then_inc(sem_h, 1)
                getattr(block, e)(body)
        return nc


def host_consts(core):
    bf = ml_dtypes.bfloat16
    c = {}
    c["c_iota8"] = np.tile(np.arange(8, dtype=np.float32), (128, 1))
    c["c_iota8r"] = np.tile(8.0 - np.arange(8, dtype=np.float32), (128, 1))
    c["c_iota"] = np.tile(np.arange(384, dtype=np.float32), (128, 1))
    c["c_piota"] = np.arange(128, dtype=np.float32)[:, None]
    k = np.arange(128)
    c["c_L128"] = (k[:, None] < k[None, :]).astype(bf)
    c["c_negid"] = (-1e9 * np.eye(128)).astype(bf)
    true = ((core + k // 16) % 8) * 16 + (k % 16)
    c["c_Lt"] = (true[:, None] < true[None, :]).astype(bf)
    c["c_ident"] = np.eye(128, dtype=np.float32)
    c["c_ones_bf"] = np.ones((1, 128), dtype=bf)
    c["c_ones_f"] = np.ones((1, 128), dtype=np.float32)
    c["c_onescol_f"] = np.ones((128, 1), dtype=np.float32)
    c["c_onescol_bf"] = np.ones((128, 1), dtype=bf)
    h0 = np.zeros((1, 128), dtype=bf); h0[0, :64] = 1
    h1 = np.zeros((1, 128), dtype=bf); h1[0, 64:] = 1
    c["c_half0"] = h0
    c["c_half1"] = h1
    return c


def gather_window(j):
    if j == 0:
        return 0, SLOTS
    lo = max(0, 16 * j - GM)
    hi = min(SLOTS, 16 * j + 16 + GM)
    if hi - lo > 208:
        hi = lo + 208
    return lo, hi - lo


def scatter_chunks(tc):
    lo = max(0, 16 * tc - SM)
    hi = min(SLOTS, 16 * tc + 16 + SM)
    return list(range(lo // 128, (hi + 127) // 128))


PHASES = None
DUMPS = False


def _on(name):
    return PHASES is None or name in PHASES


def build_program():
    nc = bass.Bass("TRN2", target_bir_lowering=False)
    tk = TK(nc)
    S, P = tk.sbuf, tk.psum

    inp = lambda n, s, d: nc.dram_tensor(n, s, d, kind="ExternalInput").ap()
    x_own = inp("x_own", [N, D], F32)
    x_rot = inp("x_rot", [B, N, D], F32)
    Wq = inp("Wq", [D, D], F32)
    Wkv = inp("Wkv", [D, 2 * D], F32)
    Wo = inp("Wo", [D, D], F32)
    bo = inp("bo", [1, D], F32)
    wg = inp("wg", [D, NE], F32)
    Wf = inp("Wf", [NE, D, D], F32)
    bfe = inp("bfe", [NE, D], F32)
    hc0 = host_consts(0)
    cns = {n: inp(n, list(v.shape), BF16 if v.dtype == ml_dtypes.bfloat16 else F32)
           for n, v in hc0.items()}
    out_t = nc.dram_tensor("out", [N, D], F32, kind="ExternalOutput").ap()

    xn_tm_t = S("xn_tm", [128, NT, 1024], BF16)
    xnfm_ar = S("xnfm_ar", [128, 16384], BF16)
    attn_t = S("attn", [128, 8, 2048], BF16)
    pairs = S("pairs", [128, 14368], BF16)
    ar2 = S("ar2", [128, 8192], BF16)
    pt_ar = S("pt_ar", [128, 3072], BF16)
    wt_ar = S("wt_ar", [128, 8192], BF16)
    gm_ar = S("gm_ar", [128, 6144], BF16)
    oh_bf = S("oh_bf", [128, 1024], BF16)
    loctok = S("loctok", [128, 128], F32)
    slotb = S("slotb", [128, 2048], F32)
    wb = S("wb", [128, 2048], BF16)
    scol = S("scol", [128, 16], F32)
    wcol = S("wcol", [128, 16], F32)
    cnttm = S("cnttm", [128, 8], BF16)
    tbtm = S("tbtm", [128, 8], F32)
    vecb = S("vecb", [128, 8], F32)
    v8row = S("v8row", [1, 8], F32)
    tr16 = S("tr16", [16, 128], F32)
    row2k = S("row2k", [1, 2048], F32)
    row2kb = row2k[:, :].bitcast(BF16)
    cntf = row2kb[0:1, 0:1024]
    tbf = row2k[0:1, 0:1024]
    sflat = row2k[0:1, 0:2048]
    wflat = row2k[0:1, 0:2048]
    rstd = S("rstd", [128, NT], F32)
    nmr = S("nmr", [128, NT], F32)
    bn6 = S("bn6", [128, 12], F32)
    bn2 = S("bn2", [128, 2], F32)
    lnv = S("lnv", [128, 1], F32)
    epsc = S("epsc", [128, 1], F32)
    l8 = S("l8", [128, 8], F32)
    mx8 = S("mx8", [128, 8], F32)
    ex8 = S("ex8", [128, 8], F32)
    t8a = S("t8a", [128, 8], F32)
    zs = S("zs", [128, 4], F32)
    padj = S("padj", [128, 2], F32)
    s2cp = [row2kb[0:1, 0:1024], row2kb[0:1, 1024:2048]]
    rrowp = [row2kb[0:1, 2048:3072], row2kb[0:1, 3072:4096]]
    wgsb = S("wgsb", [128, 64], F32)
    wgtmp = S("wgtmp", [128, 8], F32)
    wgp = S("wgp", [128, 128], BF16)
    bo_bf = S("bo_bf", [1, 1024], BF16)
    be_bf = S("be_bf", [1, 1024], BF16)
    csb = {}
    for n, v in hc0.items():
        dt = BF16 if v.dtype == ml_dtypes.bfloat16 else F32
        csb[n] = S("s_" + n, list(v.shape), dt)

    psA = P("psA", [128, 1024], F32)
    psB = P("psB", [128, 1024], F32)
    psC = P("psC", [128, 512], F32)
    psD = P("psD", [128, 512], F32)
    psE = P("psE", [128, 512], F32)
    psF = P("psF", [128, 512], F32)

    xn_tm = xn_tm_t[:, :, :]
    xnfm = xnfm_ar[:, :].rearrange("p (dc t) -> p dc t", dc=8)
    attn = attn_t[:, :, :]
    gtb = attn_t[:, 0, 0:1024]
    exclv = gm_ar[:, 0:2048].bitcast(F32)
    locv = gm_ar[:, 2048:4096].bitcast(F32)
    ohf = gm_ar[:, 4096:6144].bitcast(F32)
    stores = []

    qP = pairs[:, 0:2048]
    kP = pairs[:, 2048:4096]
    vS = pairs[:, 4096:6144]
    vT = pairs[:, 6144:8224].rearrange("p (j c) -> p j c", c=130)
    wqS = [pairs[:, 8224 + i * 1024: 8224 + (i + 1) * 1024] for i in range(2)]
    wkS = [pairs[:, 10272 + i * 1024: 10272 + (i + 1) * 1024] for i in range(2)]
    wvS = [pairs[:, 12320:13344], pairs[:, 13344:14368]]

    def mm(ps, lh, rh, start=True, stop=True, tag=""):
        return tk.op("tensor",
                     lambda t, a=ps, b=lh, c=rh, s1=start, s2=stop:
                     t.matmul(a, b, c, start=s1, stop=s2),
                     reads=[lh, rh], writes=[ps], tag=tag)

    def act(out, in_, func, bias=0.0, scale=1.0, tag=""):
        reads = [in_]
        if not isinstance(bias, float):
            reads.append(bias)
        if not isinstance(scale, float):
            reads.append(scale)
        return tk.op("scalar",
                     lambda s, o=out, i=in_, f=func, b=bias, sc=scale:
                     s.activation(o, i, f, bias=b, scale=sc),
                     reads=reads, writes=[out], tag=tag)

    def vcopy(out, in_, tag=""):
        return tk.op("vector", lambda v, o=out, i=in_: v.tensor_copy(o, i),
                     reads=[in_], writes=[out], tag=tag)

    def scopy(out, in_, tag=""):
        return act(out, in_, AF.Identity, tag=tag)

    def vtt(out, a, b, op, tag=""):
        return tk.op("vector",
                     lambda v, o=out, x=a, y=b, w=op: v.tensor_tensor(o, x, y, w),
                     reads=[a, b], writes=[out], tag=tag)

    def vts(out, in_, s1, op, tag=""):
        reads = [in_] + ([s1] if not isinstance(s1, float) else [])
        return tk.op("vector",
                     lambda v, o=out, i=in_, s=s1, w=op:
                     v.tensor_single_scalar(o, i, s, w),
                     reads=reads, writes=[out], tag=tag)

    def vstt(out, in0, s1, in1, op0, op1, tag=""):
        reads = [in0, in1] + ([s1] if not isinstance(s1, float) else [])
        return tk.op("vector",
                     lambda v, o=out, a=in0, s=s1, b=in1, w0=op0, w1=op1:
                     v.scalar_tensor_tensor(o, a, s, b, w0, w1),
                     reads=reads, writes=[out], tag=tag)

    for n in cns:
        tk.dma("sync", csb[n][:, :], cns[n], tag="c_" + n)
    tk.dma("sync", wgsb[:, :].rearrange("p (dc e) -> p dc e", dc=8),
           wg.rearrange("(dc p) e -> p dc e", p=128), tag="wg")
    tk.dma("gpsimd", bo_bf[:, :], bo, tag="bo")
    tk.op("vector", lambda v: v.memset(epsc[:, :], 1e-5), writes=[epsc[:, :]])
    tk.op("vector", lambda v: v.memset(wgp[:, :], 0.0), writes=[wgp[:, :]])

    if _on("route1"):
        mm(psA[0:1, 0:64], csb["c_onescol_f"][:, :], wgsb[:, :], tag="wgsum")
        vcopy(tr16[0:1, 0:64], psA[0:1, 0:64], tag="wgsum_ev")
        tk.op("vector",
              lambda v: v.reduce_sum(
                  wgtmp[0:1, 0:8],
                  tr16[0:1, 0:64].rearrange("o (dc e) -> o e dc", e=8),
                  axis=AX.X),
              reads=[tr16[0:1, 0:64]], writes=[wgtmp[0:1, 0:8]], tag="wgred")
        vts(wgtmp[0:1, 0:8], wgtmp[0:1, 0:8], -1.0 / 1024.0, ALU.mult, tag="wgsc")
        mm(psB[:, 0:8], csb["c_ones_f"][:, :], wgtmp[0:1, 0:8], tag="wgbc")
        for dc in range(8):
            vtt(wgtmp[:, 0:8], psB[:, 0:8], wgsb[:, 8 * dc:8 * dc + 8],
                ALU.add, tag=f"wgp{dc}")
            vcopy(wgp[:, 16 * dc:16 * dc + 8], wgtmp[:, 0:8], tag=f"wgpc{dc}")

    if _on("ln"):
        for j in range(NT):
            xs = ar2[:, (j % 2) * 2048:(j % 2) * 2048 + 2048].bitcast(F32)
            tk.dma("sync", xs, x_own[128 * j:128 * j + 128, :], tag=f"xl{j}")
            tk.op("vector", lambda v, a=xs: v.bn_stats(bn6[:, 0:6], a[:, 0:512]),
                  reads=[xs[:, 0:512]], writes=[bn6[:, 0:6]], tag=f"bns0_{j}")
            tk.op("vector", lambda v, a=xs: v.bn_stats(bn6[:, 6:12], a[:, 512:1024]),
                  reads=[xs[:, 512:1024]], writes=[bn6[:, 6:12]], tag=f"bns1_{j}")
            tk.op("vector", lambda v: v.bn_aggr(bn2[:, :], bn6[:, :]),
                  reads=[bn6[:, :]], writes=[bn2[:, :]], tag=f"bna{j}")
            act(lnv[:, :], bn2[:, 1:2], AF.Ln, bias=epsc[:, :], tag=f"ln{j}")
            act(rstd[:, j:j + 1], lnv[:, :], AF.Exp, scale=-0.5, tag=f"rs{j}")
            vtt(nmr[:, j:j + 1], bn2[:, 0:1], rstd[:, j:j + 1], ALU.mult,
                tag=f"nm{j}")
            vts(nmr[:, j:j + 1], nmr[:, j:j + 1], -1.0, ALU.mult, tag=f"nmn{j}")
            act(xn_tm[:, j, :], xs, AF.Identity, bias=nmr[:, j:j + 1],
                scale=rstd[:, j:j + 1], tag=f"xn{j}")

    if _on("xnfm"):
        for j in range(NT):
            tk.dma("sync",
                   xnfm[:, :, 128 * j:128 * j + 128],
                   xn_tm[:, j, :].rearrange("p (dc q) -> p dc q", dc=8),
                   tag=f"xt{j}", transpose=True)

    if _on("route4"):
        xh2s = [ar2[:, 0:2048], ar2[:, 4096:6144]]
        xf2s = [ar2[:, 2048:4096], ar2[:, 6144:8192]]
        for gp in range(GT // 2):
            g = 2 * gp
            bidx, jj = g // NT, g % NT
            sl = gp % 2
            xh2 = xh2s[sl]
            xfv = xf2s[sl].rearrange("p (b q) -> p b q", b=16)
            tk.dma("gpsimd", xh2.rearrange("p (t d) -> p t d", t=2),
                   x_rot[bidx, 128 * jj:128 * jj + 256, :].rearrange(
                       "(t p) d -> p t d", p=128), tag=f"xh{g}")
            tk.dma("sync",
                   xfv, xh2.rearrange("p (b q) -> p b q", b=16),
                   tag=f"xf{g}", transpose=True)
            for t in range(2):
                for dc in range(8):
                    mm(psA[:, 8 * (g + t):8 * (g + t) + 8],
                       xfv[:, 8 * t + dc, :], wgp[:, 16 * dc:16 * dc + 8],
                       start=(dc == 0), stop=(dc == 7), tag=f"g{g + t}_{dc}")
        vcopy(gtb[:, 0:512], psA[:, 0:512], tag="gev0")
        vcopy(gtb[:, 512:1024], psA[:, 512:1024], tag="gev1")

    if _on("route5"):
        gt3v = gtb.rearrange("p (j e) -> p j e", e=8)
        mxt = loctok
        tk.op("vector", lambda v: v.reduce_max(mxt[:, :], gt3v, axis=AX.X),
              reads=[gtb], writes=[mxt[:, :]], tag="rmax")
        eq3 = exclv.rearrange("p (j e) -> p j e", e=8)
        tk.op("vector",
              lambda v: v.tensor_tensor(
                  eq3, gt3v, mxt[:, :].broadcast_to([128, 128, 8]),
                  ALU.is_equal),
              reads=[gtb, mxt[:, :]], writes=[exclv], tag="eq")
        iota_b = csb["c_iota8r"][:, :].__replace__(ap=[[8, 128], [0, 128], [1, 8]])
        u3 = locv.rearrange("p (j e) -> p j e", e=8)
        tk.op("vector",
              lambda v: v.tensor_tensor(u3, eq3, iota_b, ALU.mult),
              reads=[exclv, csb["c_iota8r"][:, :]], writes=[locv], tag="umul")
        tk.op("vector", lambda v: v.reduce_max(mxt[:, :], u3, axis=AX.X),
              reads=[locv], writes=[mxt[:, :]], tag="umax")
        umb = mxt[:, :].broadcast_to([128, 128, 8])
        oh3b = oh_bf[:, :].rearrange("p (j e) -> p j e", e=8)
        tk.op("vector",
              lambda v: v.tensor_tensor(oh3b, u3, umb, ALU.is_equal),
              reads=[locv, mxt[:, :]], writes=[oh_bf[:, :]], tag="ohb")
        oh3f = ohf.rearrange("p (j e) -> p j e", e=8)
        tk.op("vector",
              lambda v: v.tensor_tensor(oh3f, u3, umb, ALU.is_equal),
              reads=[locv, mxt[:, :]], writes=[ohf], tag="ohfb")
        for hhalf in range(2):
            sl = slice(512 * hhalf, 512 * hhalf + 512)
            mm(psA[:, sl], csb["c_L128"][:, :], oh_bf[:, sl], tag=f"ex{hhalf}")
            vcopy(exclv[:, sl], psA[:, sl], tag=f"exev{hhalf}")
        for hhalf in range(2):
            sl = slice(512 * hhalf, 512 * hhalf + 512)
            mm(psB[0:1, sl], csb["c_onescol_bf"][:, :], oh_bf[:, sl],
               tag=f"cnt{hhalf}")
        scopy(cntf, psB[0:1, 0:1024], tag="cntev")
        tk.dma("gpsimd", cnttm[:, :], cntf, tag="cntrs")
        mm(psB[:, 0:8], csb["c_Lt"][:, :], cnttm[:, :], tag="tbmm")
        vcopy(tbtm[:, :], psB[:, 0:8], tag="tbev")
        tk.dma("gpsimd", tbf, tbtm[:, :], tag="tbrs")
        for hhalf in range(2):
            sl = slice(512 * hhalf, 512 * hhalf + 512)
            mm(psA[:, sl], csb["c_ones_f"][:, :], tbf[0:1, sl], tag=f"tb{hhalf}")
            vstt(locv[:, sl], psA[:, sl], 1.0, exclv[:, sl], ALU.mult, ALU.add,
                 tag=f"loc{hhalf}")
        for hhalf in range(2):
            sl = slice(512 * hhalf, 512 * hhalf + 512)
            vtt(psB[:, sl], ohf[:, sl], locv[:, sl], ALU.mult, tag=f"lt{hhalf}")
        tk.op("vector",
              lambda v: v.reduce_sum(
                  loctok[:, :],
                  psB[:, 0:1024].rearrange("p (j e) -> p j e", e=8), axis=AX.X),
              reads=[psB[:, :]], writes=[loctok[:, :]], tag="ltred")

    if _on("route6"):
        vts(v8row[:, :], csb["c_iota8"][0:1, 0:8], float(SLOTS), ALU.mult,
            tag="v8m")
        vtt(v8row[:, :], v8row[:, :], tbf[0:1, 0:8], ALU.subtract, tag="v8s")
        mm(psB[:, 0:8], csb["c_ones_f"][:, :], v8row[0:1, 0:8], tag="v8bc")
        vcopy(vecb[:, :], psB[:, 0:8], tag="v8ev")
        for j in range(NT):
            gsl = gtb[:, 8 * j:8 * j + 8]
            tk.op("vector",
                  lambda v, a=gsl, r=rstd[:, j:j + 1]:
                  v.tensor_single_scalar(l8[:, :], a, r, ALU.mult),
                  reads=[gsl, rstd[:, j:j + 1]], writes=[l8[:, :]],
                  tag=f"l8{j}")
            tk.op("vector", lambda v: v.max(mx8[:, :], l8[:, :]),
                  reads=[l8[:, :]], writes=[mx8[:, :]], tag=f"lm{j}")
            vts(zs[:, 0:1], mx8[:, 0:1], -1.0, ALU.mult, tag=f"ln{j}")
            act(ex8[:, :], l8[:, :], AF.Exp, bias=zs[:, 0:1], tag=f"ge{j}")
            tk.op("vector",
                  lambda v: v.reduce_sum(zs[:, 1:2], ex8[:, :], axis=AX.X),
                  reads=[ex8[:, :]], writes=[zs[:, 1:2]], tag=f"gz{j}")
            tk.op("vector", lambda v: v.reciprocal(zs[:, 2:3], zs[:, 1:2]),
                  reads=[zs[:, 1:2]], writes=[zs[:, 2:3]], tag=f"gr{j}")
            vts(zs[:, 3:4], loctok[:, j:j + 1], float(CAP), ALU.is_lt,
                tag=f"kp{j}")
            vtt(wcol[:, j:j + 1], zs[:, 2:3], zs[:, 3:4], ALU.mult,
                tag=f"w{j}")
            vtt(t8a[:, :], locv[:, 8 * j:8 * j + 8], vecb[:, :], ALU.add,
                tag=f"sl{j}")
            vtt(t8a[:, :], t8a[:, :], ohf[:, 8 * j:8 * j + 8], ALU.mult,
                tag=f"slm{j}")
            tk.op("vector",
                  lambda v, o=scol[:, j:j + 1]: v.reduce_sum(o, t8a[:, :],
                                                             axis=AX.X),
                  reads=[t8a[:, :]], writes=[scol[:, j:j + 1]], tag=f"sr{j}")

    if _on("route7"):
        for (colsrc, flat, dstb, dt_out, tag) in (
                (scol, sflat, slotb, F32, "s"), (wcol, wflat, wb, BF16, "w")):
            tk.op("tensor",
                  lambda t, a=colsrc: t.transpose(psA[0:16, 0:128], a[:, :],
                                                  csb["c_ident"][:, :]),
                  reads=[colsrc[:, :], csb["c_ident"][:, :]],
                  writes=[psA[:, :]], tag=f"{tag}tr")
            vcopy(tr16[:, :], psA[0:16, 0:128], tag=f"{tag}trev")
            tk.dma("gpsimd", flat[0:1, 0:2048], tr16[:, :], tag=f"{tag}rs")
            for q in range(4):
                sl = slice(512 * q, 512 * q + 512)
                mm(psB[:, 0:512] if q % 2 == 0 else psB[:, 512:1024],
                   csb["c_ones_f"][:, :], flat[0:1, sl], tag=f"{tag}bc{q}")
                src = psB[:, 0:512] if q % 2 == 0 else psB[:, 512:1024]
                vcopy(dstb[:, sl], src, tag=f"{tag}be{q}")

    if _on("attn"):
        WqR = Wq.rearrange("(dc p) i -> p dc i", p=128)
        WkvR = Wkv.rearrange("(dc p) i -> p dc i", p=128)
        for pr in range(8):
            sl = pr % 2
            tk.dma("gpsimd", wqS[sl].rearrange("p (dc q) -> p dc q", dc=8),
                   WqR[:, :, 128 * pr:128 * pr + 128], tag=f"wq{pr}")
            tk.dma("gpsimd", wkS[sl].rearrange("p (dc q) -> p dc q", dc=8),
                   WkvR[:, :, 128 * pr:128 * pr + 128], tag=f"wk{pr}")
            tk.dma("gpsimd", wvS[sl].rearrange("p (dc q) -> p dc q", dc=8),
                   WkvR[:, :, 1024 + 128 * pr:1024 + 128 * pr + 128],
                   tag=f"wv{pr}")
            wqv = wqS[sl].rearrange("p (dc q) -> p dc q", dc=8)
            wkv = wkS[sl].rearrange("p (dc q) -> p dc q", dc=8)
            wvv = wvS[sl].rearrange("p (dc q) -> p dc q", dc=8)
            for (wsl, dst, tg) in ((wqv, qP, "q"), (wkv, kP, "k"),
                                   (wvv, vS, "v")):
                for nck in range(4):
                    ps = psB[:, 0:512] if nck % 2 == 0 else psB[:, 512:1024]
                    csl = slice(512 * nck, 512 * nck + 512)
                    for dc in range(8):
                        mm(ps, wsl[:, dc, :], xnfm[:, dc, csl],
                           start=(dc == 0), stop=(dc == 7),
                           tag=f"{tg}p{pr}_{nck}_{dc}")
                    vcopy(dst[:, csl], ps, tag=f"{tg}e{pr}_{nck}")
            vSv = vS.rearrange("p (j q) -> p j q", j=NT)
            for h01t in range(2):
                slot = (2 * pr + h01t) % 4
                scrv = gm_ar[:, 1024 * slot: 1024 * slot + 1024].rearrange(
                    "p (j c) -> p j c", c=64)
                tk.dma("sync", scrv, vSv[64 * h01t:64 * h01t + 64, :, :],
                       tag=f"vt{h01t}_{pr}", transpose=True)
                vcopy(vT[:, :, 65 * h01t:65 * h01t + 64], scrv,
                      tag=f"vc{h01t}_{pr}")
            if pr == 0:
                tk.op("vector", lambda v: v.memset(vT[:, :, 64:65], 1.0),
                      writes=[vT[:, :, 64:65]], tag="vo0")
                tk.op("vector", lambda v: v.memset(vT[:, :, 129:130], 1.0),
                      writes=[vT[:, :, 129:130]], tag="vo1")
            pending_div = None
            for c in range(4):
                for j in range(4 * c + 4):
                    voff = max(0, 128 * j - 512 * c)
                    diag = j >= 4 * c
                    pts = []
                    for h01 in range(2):
                        hb = 64 * h01
                        sps = psC if h01 == 0 else psD
                        mm(sps[:, voff:512],
                           kP[hb:hb + 64, 128 * j:128 * j + 128],
                           qP[hb:hb + 64, 512 * c + voff:512 * c + 512],
                           start=True, stop=not diag,
                           tag=f"s{pr}_{h01}_{c}_{j}")
                        if diag:
                            mm(sps[:, voff:voff + 128], csb["c_L128"][:, :],
                               csb["c_negid"][:, :], start=False, stop=True,
                               tag=f"m{pr}_{h01}_{c}_{j}")
                        pt = pt_ar[:, 512 * ((2 * j + h01) % 4):
                                   512 * ((2 * j + h01) % 4) + 512]
                        act(pt[:, voff:512], sps[:, voff:512], AF.Exp,
                            scale=0.125, tag=f"e{pr}_{h01}_{c}_{j}")
                        pts.append(pt)
                    for h01 in range(2):
                        nps = psE if h01 == 0 else psF
                        mm(nps[0:65, voff:512],
                           vT[:, j, 65 * h01:65 * h01 + 65],
                           pts[h01][:, voff:512],
                           start=(j == 0), stop=(j == 4 * c + 3),
                           tag=f"n{pr}_{h01}_{c}_{j}")
                par = c % 2
                vcopy(attn[0:64, pr, 512 * c:512 * c + 512], psE[0:64, :],
                      tag=f"ae{pr}_{c}")
                vcopy(attn[64:128, pr, 512 * c:512 * c + 512], psF[0:64, :],
                      tag=f"ao{pr}_{c}")
                vcopy(s2cp[par][0:1, 0:512], psE[64:65, :], tag=f"se{pr}_{c}")
                vcopy(s2cp[par][0:1, 512:1024], psF[64:65, :],
                      tag=f"so{pr}_{c}")

                def mkdiv(pr=pr, c=c, par=par):
                    def emit():
                        rr = rrowp[par]
                        def _rr(v, o=rr, i=s2cp[par]):
                            with nc.allow_low_precision(reason="bf16 1/S row"):
                                return v.reciprocal(o[0:1, :], i[0:1, :])
                        tk.op("vector", _rr,
                              reads=[s2cp[par]], writes=[rr], tag=f"rr{pr}_{c}")
                        mm(psA[:, 0:512], csb["c_half0"][:, :],
                           rr[0:1, 0:512], start=True, stop=False,
                           tag=f"rb0{pr}_{c}")
                        mm(psA[:, 0:512], csb["c_half1"][:, :],
                           rr[0:1, 512:1024], start=False, stop=True,
                           tag=f"rb1{pr}_{c}")
                        rsb = oh_bf[:, 512 * par:512 * par + 512]
                        vcopy(rsb, psA[:, 0:512], tag=f"re{pr}_{c}")
                        vtt(attn[:, pr, 512 * c:512 * c + 512],
                            attn[:, pr, 512 * c:512 * c + 512], rsb, ALU.mult,
                            tag=f"dv{pr}_{c}")
                    return emit

                if pending_div is not None:
                    pending_div()
                pending_div = mkdiv()
            pending_div()

    Y = []
    for e in range(NE):
        if e <= 4:
            Y.append(xnfm_ar[:, 3072 * e:3072 * e + 3072])
        elif e == 5:
            Y.append(pt_ar[:, 0:3072])
        else:
            Y.append(pairs[:, 3072 * (e - 6):3072 * (e - 6) + 3072])
    Yv = [y.rearrange("p (sc f) -> p sc f", sc=3) for y in Y]
    wstage = ar2[:, 0:8192].rearrange("p (fc h) -> p fc h", fc=8)
    wt = wt_ar[:, :].rearrange("p (hc f) -> p hc f", hc=8)

    if _on("ffn"):
        for e in range(NE):
            tk.dma("gpsimd", wstage,
                   Wf[e, :, :].rearrange("(fc p) h -> p fc h", p=128),
                   tag=f"wf{e}")
            for fc in range(8):
                tk.dma("sync",
                       wt[:, :, 128 * fc:128 * fc + 128],
                       wstage[:, fc, :],
                       tag=f"wt{e}_{fc}", transpose=True)
            tk.dma("gpsimd", be_bf[0:1, 0:1024], bfe[e:e + 1, :], tag=f"be{e}")
            eg, eo = e // 2, e % 2
            PT = []
            for j in range(NT):
                wlo, W = gather_window(j)
                off = 3584 * eo + (0 if j == 0 else 384 + 208 * (j - 1))
                ptile = pairs[:, off: off + W]
                vts(padj[:, 0:1], scol[:, j:j + 1], -float(SLOTS * e + wlo),
                    ALU.add, tag=f"pa{e}_{j}")
                vts(ptile, csb["c_iota"][:, 0:W], padj[:, 0:1], ALU.is_equal,
                    tag=f"pc{e}_{j}")
                PT.append((wlo, W, ptile))
            gsp = psC if eo == 0 else psD
            gdst = pairs[:, 7168 + 3072 * eo: 7168 + 3072 * eo + 3072]
            gv = gdst.rearrange("p (hc s) -> p hc s", hc=8)
            for hcx in range(8):
                for j in range(NT):
                    wlo, W, ptile = PT[j]
                    mm(gsp[:, wlo:wlo + W],
                       xn_tm[:, j, 128 * hcx:128 * hcx + 128], ptile,
                       start=(j == 0), stop=(j == NT - 1),
                       tag=f"gm{e}_{hcx}_{j}")
                vcopy(gv[:, hcx, 0:SLOTS], gsp[:, 0:SLOTS], tag=f"ge{e}_{hcx}")
            for sc in range(3):
                for fh in range(2):
                    yp = (psE if fh == 0 else psF)[:, :]
                    for hcx in range(8):
                        mm(yp, gv[:, hcx, 128 * sc:128 * sc + 128],
                           wt[:, hcx, 512 * fh:512 * fh + 512],
                           start=(hcx == 0), stop=False,
                           tag=f"ym{e}_{sc}_{fh}_{hcx}")
                    mm(yp, csb["c_ones_bf"][:, :],
                       be_bf[0:1, 512 * fh:512 * fh + 512],
                       start=False, stop=True, tag=f"yb{e}_{sc}_{fh}")
                    scopy(Yv[e][:, sc, 512 * fh:512 * fh + 512], yp,
                          tag=f"ye{e}_{sc}_{fh}")

    if _on("tail"):
        tk.dma("gpsimd", wt_ar[:, :].rearrange("p (ic d) -> p ic d", ic=8),
               Wo.rearrange("(ic p) d -> p ic d", p=128), tag="wo")
        wov = wt_ar[:, :].rearrange("p (ic d) -> p ic d", ic=8)
        for tc in range(NT):
            sl = tc % 2
            xr = ar2[:, 2048 * sl:2048 * sl + 2048].bitcast(F32)
            tk.dma("sync", xr, x_own[128 * tc:128 * tc + 128, :], tag=f"xr{tc}")
            rws = []
            k = 0
            for e in range(NE):
                for sc in scatter_chunks(tc):
                    rt = gm_ar[:, 3072 * sl + 128 * k: 3072 * sl + 128 * k + 128]
                    k += 1
                    vts(padj[:, 1:2], csb["c_piota"][:, :],
                        float(SLOTS * e + 128 * sc), ALU.add, tag=f"ra{tc}_{e}_{sc}")
                    vts(rt, slotb[:, 128 * tc:128 * tc + 128], padj[:, 1:2],
                        ALU.is_equal, tag=f"rc{tc}_{e}_{sc}")
                    vtt(rt, rt, wb[:, 128 * tc:128 * tc + 128], ALU.mult,
                        tag=f"rw{tc}_{e}_{sc}")
                    rws.append((e, sc, rt))
            ps = psA if sl == 0 else psB
            for dh in range(2):
                psd = ps[:, 512 * dh:512 * dh + 512]
                for ic in range(8):
                    mm(psd, attn[:, ic, 128 * tc:128 * tc + 128],
                       wov[:, ic, 512 * dh:512 * dh + 512],
                       start=(ic == 0), stop=False, tag=f"pj{tc}_{dh}_{ic}")
                mm(psd, csb["c_ones_bf"][:, :],
                   bo_bf[0:1, 512 * dh:512 * dh + 512],
                   start=False, stop=False, tag=f"pb{tc}_{dh}")
                for i, (e, sc, rt) in enumerate(rws):
                    mm(psd, rt, Yv[e][:, sc, 512 * dh:512 * dh + 512],
                       start=False, stop=(i == len(rws) - 1),
                       tag=f"sc{tc}_{dh}_{e}_{sc}")
                vstt(xr[:, 512 * dh:512 * dh + 512], psd, 1.0,
                     xr[:, 512 * dh:512 * dh + 512], ALU.mult, ALU.add,
                     tag=f"oev{tc}_{dh}")
            stores.append(tk.dma("sync", out_t[128 * tc:128 * tc + 128, :],
                                 xr, tag=f"st{tc}"))

    if DUMPS:
        def dump(nm, src, shape, dt):
            t = nc.dram_tensor(nm, shape, dt, kind="ExternalOutput").ap()
            stores.append(tk.dma("gpsimd", t, src, tag="d_" + nm))
        dump("d_xn", xn_tm_t[:, :, :], [128, NT, 1024], BF16)
        dump("d_attn", attn_t[:, :, :], [128, 8, 2048], BF16)
        dump("d_slotb", slotb[:, :], [128, 2048], F32)
        dump("d_wb", wb[:, :], [128, 2048], BF16)
        dump("d_loctok", loctok[:, :], [128, 128], F32)
        dump("d_scol", scol[:, :], [128, 16], F32)
        dump("d_wcol", wcol[:, :], [128, 16], F32)
        dump("d_xnfm", xnfm_ar[:, :], [128, 16384], BF16)
        dump("d_ptar", pt_ar[:, :], [128, 3072], BF16)
        dump("d_pairs", pairs[:, 0:13344], [128, 13344], BF16)
        dump("d_rstd", rstd[:, :], [128, NT], F32)

    tk.build(final_waits=stores, final_engine="sync")
    return nc, tk


_CACHE = {}


def _core_inputs(core, inputs):
    x = np.ascontiguousarray(inputs["L0_attn_x"])
    d = {
        "x_own": x[core],
        "x_rot": np.roll(x, -core, axis=0),
        "Wq": inputs["L0_attn_Wq"],
        "Wkv": inputs["L0_attn_Wkv"],
        "Wo": inputs["L0_attn_Wo"],
        "bo": np.ascontiguousarray(inputs["L0_attn_bo"]).reshape(1, D),
        "wg": inputs["L0_ffn_wg"],
        "Wf": inputs["L0_ffn_W"],
        "bfe": inputs["L0_ffn_b"],
    }
    d.update(host_consts(core))
    return {k: np.ascontiguousarray(v) for k, v in d.items()}


def kernel(**inputs):
    if "nc" not in _CACHE:
        _CACHE["nc"] = build_program()[0]
    nc = _CACHE["nc"]
    in_maps = [_core_inputs(c, inputs) for c in range(B)]
    res = run_bass_kernel_spmd(nc, in_maps, core_ids=list(range(B)))
    return np.stack([np.asarray(res.results[c]["out"], dtype=np.float32)
                     for c in range(B)], axis=0)


if __name__ == "__main__":
    nc, tk = build_program()
    ni = sum(len(s) for s in tk.streams.values())
    print(f"ops={ni} edges={tk.n_edges} waits={tk.n_waits}")


# version 04
# speedup vs baseline: 1.3313x; 1.1432x over previous
import contextlib
import numpy as np
import ml_dtypes
import concourse.bass as bass
import concourse.mybir as mybir
from concourse.bass_utils import run_bass_kernel_spmd

F32 = mybir.dt.float32
BF16 = mybir.dt.bfloat16
U32 = mybir.dt.uint32
AF = mybir.ActivationFunctionType
ALU = mybir.AluOpType
AX = mybir.AxisListType

B, N, D = 8, 2048, 1024
NH, DH = 16, 64
NE, CAP = 8, 2048
NT = N // 128
GT = B * NT
SLOTS = 384
GM = 96
SM = 128

ENGINES = ("tensor", "vector", "scalar", "gpsimd", "sync")
ESEM = {e: "sem_" + e for e in ENGINES}
_DT_SIZE = {"float32": 4, "float32r": 4, "bfloat16": 2, "float16": 2,
            "uint32": 4, "int32": 4, "uint16": 2, "int16": 2, "uint8": 1}


def _dt_size(dt):
    n = getattr(dt, "name", None) or str(dt).split(".")[-1]
    return _DT_SIZE.get(n, 4)


def _as_ap(x):
    return x.ap() if callable(getattr(x, "ap", None)) else x


def _ranges(x, out=None):
    if out is None:
        out = []
    if x is None or isinstance(x, (int, float, str)):
        return out
    if isinstance(x, (list, tuple)):
        for e in x:
            _ranges(e, out)
        return out
    ap = _as_ap(x)
    try:
        space = str(ap.space)
    except Exception:
        space = "?"
    name = ap.name
    if "SB" not in space.upper():
        out.append((name, 0, 1 << 20, 0, 1 << 60))
        return out
    try:
        pat = ap.ap
        offset = ap.offset
        esz = _dt_size(ap.dtype)
        pstride, pcount = pat[0]
        plo = (offset // pstride) if pstride else 0
        phi = plo + pcount
        row_off = offset - plo * pstride
        span = 1
        for step, count in pat[1:]:
            span += step * (count - 1)
        out.append((name, plo, phi, row_off * esz, (row_off + span) * esz))
    except Exception:
        out.append((name, 0, 1 << 20, 0, 1 << 60))
    return out


def _is_sbuf(x):
    try:
        return "SB" in str(_as_ap(x).space).upper()
    except Exception:
        return False


class Op:
    __slots__ = ("engine", "fn", "idx", "deps", "needs_inc", "fixed_inc",
                 "sem_val", "tag")

    def __init__(self, engine, fn, idx, tag=""):
        self.engine = engine
        self.fn = fn
        self.idx = idx
        self.deps = []
        self.needs_inc = False
        self.fixed_inc = None
        self.sem_val = None
        self.tag = tag


SERIALIZE = False


class TK:
    def __init__(self, nc, dma_queues=("gpsimd", "sync", "scalar")):
        self.nc = nc
        self.all_ops = []
        self.ctx = contextlib.ExitStack()
        self.streams = {e: [] for e in ENGINES}
        self.sem = {}
        self.cnt = {}
        self.access = {}
        self.n_edges = 0
        for e in ENGINES:
            self.semaphore(ESEM[e])
        self.dma_ring = {}
        self.dma_ring_pos = {}
        for q in dma_queues:
            depth = 6 if q != "scalar" else 2
            ring = []
            for i in range(depth):
                s = f"dma_{q}_{i}"
                self.semaphore(s)
                ring.append(s)
            self.dma_ring[q] = ring
            self.dma_ring_pos[q] = 0
        self.pending_T = []
        self.pending_P = []

    def semaphore(self, name):
        h = self.ctx.enter_context(self.nc.semaphore(name))
        self.sem[name] = h
        self.cnt[name] = 0
        return h

    def sbuf(self, name, shape, dtype):
        return self.ctx.enter_context(self.nc.sbuf_tensor(name, shape, dtype))

    def psum(self, name, shape, dtype):
        return self.ctx.enter_context(self.nc.psum_tensor(name, shape, dtype))

    def _add_access(self, op, kind, rects):
        for rect in rects:
            name, plo, phi, blo, bhi = rect
            hist = self.access.setdefault(name, [])
            for (p, k, r) in hist:
                if p is op:
                    continue
                if kind == "R" and k == "R":
                    continue
                if not (plo < r[2] and r[1] < phi and blo < r[4] and r[3] < bhi):
                    continue
                if p.engine == op.engine == "tensor":
                    continue
                op.deps.append(p)
                self.n_edges += 1
            hist.append((op, kind, rect))

    def op(self, eng, fn, reads=(), writes=(), tag=""):
        o = Op(eng, fn, len(self.streams[eng]), tag)
        self._add_access(o, "R", _ranges(list(reads)))
        self._add_access(o, "W", _ranges(list(writes)))
        if SERIALIZE and self.all_ops:
            o.deps.append(self.all_ops[-1])
        self.all_ops.append(o)
        self.streams[eng].append(o)
        return o

    def _dma_slot(self, queue, n_inc):
        ring = self.dma_ring[queue]
        i = self.dma_ring_pos[queue]
        self.dma_ring_pos[queue] = (i + 1) % len(ring)
        sname = ring[i]
        pre = self.cnt[sname]
        self.cnt[sname] += n_inc
        return sname, pre, self.cnt[sname]

    def dma(self, queue, out, in_, tag="", transpose=False):
        sname, pre, post = self._dma_slot(queue, 16)
        o = Op(queue, None, len(self.streams[queue]), tag)
        o.fixed_inc = (sname, 16, post)
        h = self.sem[sname]

        def fn(e, out=out, in_=in_, h=h, pre=pre, transpose=transpose):
            if pre:
                e.wait_ge(h, pre)
            if transpose:
                return e.dma_start_transpose(out, in_).then_inc(h, 16)
            return e.dma_start(out=out, in_=in_).then_inc(h, 16)
        o.fn = fn
        self._add_access(o, "R", _ranges(in_))
        self._add_access(o, "W", _ranges(out))
        if SERIALIZE and self.all_ops:
            o.deps.append(self.all_ops[-1])
        self.all_ops.append(o)
        if _is_sbuf(in_) and _is_sbuf(out):
            if transpose:
                o.deps.extend(self.pending_P)
                self.n_edges += len(self.pending_P)
                self.pending_P = []
                self.pending_T.append(o)
            else:
                o.deps.extend(self.pending_T)
                self.n_edges += len(self.pending_T)
                self.pending_T = []
                self.pending_P.append(o)
        self.streams[queue].append(o)
        return o

    def build(self, final_waits=(), final_engine="sync"):
        nc = self.nc
        for e in ENGINES:
            for o in self.streams[e]:
                for p in o.deps:
                    if p.fixed_inc is None:
                        p.needs_inc = True
        for e in ENGINES:
            for o in self.streams[e]:
                if o.needs_inc and o.fixed_inc is None:
                    self.cnt[ESEM[e]] += 1
                    o.sem_val = self.cnt[ESEM[e]]

        def token(p):
            if p.fixed_inc is not None:
                return (p.fixed_inc[0], p.fixed_inc[2])
            return (ESEM[p.engine], p.sem_val)

        if final_waits:
            fo = Op(final_engine, None, len(self.streams[final_engine]), "final")
            toks = [token(p) for p in final_waits]

            def ffn(e, toks=toks):
                best = {}
                for s, v in toks:
                    best[s] = max(best.get(s, 0), v)
                last = None
                for s, v in sorted(best.items()):
                    last = e.wait_ge(self.sem[s], v)
                return last
            fo.fn = ffn
            self.streams[final_engine].append(fo)

        emit_plan = {}
        n_waits = 0
        for e in ENGINES:
            water = {}
            plan = []
            for o in self.streams[e]:
                best = {}
                for p in o.deps:
                    s, v = token(p)
                    assert v is not None, f"unresolved producer {p.tag} <- {o.tag}"
                    if best.get(s, -1) < v:
                        best[s] = v
                waits = []
                for s, v in sorted(best.items()):
                    if water.get(s, 0) < v:
                        water[s] = v
                        waits.append((self.sem[s], v))
                n_waits += len(waits)
                plan.append((waits, o))
            emit_plan[e] = plan
        self.n_waits = n_waits

        with nc.Block() as block:
            for e in ENGINES:
                plan = emit_plan[e]
                if not plan:
                    continue

                def body(eng, plan=plan, e=e):
                    sem_h = self.sem[ESEM[e]]
                    for waits, o in plan:
                        for (h, v) in waits:
                            eng.wait_ge(h, v)
                        ins = o.fn(eng)
                        if o.needs_inc and o.fixed_inc is None:
                            assert ins is not None, f"op {o.tag} returned None"
                            ins.then_inc(sem_h, 1)
                getattr(block, e)(body)
        return nc


def host_consts(core):
    bf = ml_dtypes.bfloat16
    c = {}
    c["c_iota8"] = np.tile(np.arange(8, dtype=np.float32), (128, 1))
    c["c_iota8r"] = np.tile(8.0 - np.arange(8, dtype=np.float32), (128, 1))
    c["c_iota"] = np.tile(np.arange(384, dtype=np.float32), (128, 1))
    c["c_piota"] = np.arange(128, dtype=np.float32)[:, None]
    wlo = np.minimum(np.maximum(16 * np.arange(16) - 96, 0), 144)
    c["c_wlo"] = np.tile(wlo.astype(np.float32), (128, 1))
    eds = (384 * np.arange(8)[:, None] + 128 * np.arange(3)[None, :]).reshape(-1)
    c["c_eds"] = np.tile(eds.astype(np.float32), (128, 1))
    k = np.arange(128)
    c["c_L128"] = (k[:, None] < k[None, :]).astype(bf)
    c["c_negid"] = (-1e9 * np.eye(128)).astype(bf)
    true = ((core + k // 16) % 8) * 16 + (k % 16)
    c["c_Lt"] = (true[:, None] < true[None, :]).astype(bf)
    c["c_ident"] = np.eye(128, dtype=np.float32)
    c["c_ones_bf"] = np.ones((1, 128), dtype=bf)
    c["c_ones_f"] = np.ones((1, 128), dtype=np.float32)
    c["c_onescol_f"] = np.ones((128, 1), dtype=np.float32)
    c["c_onescol_bf"] = np.ones((128, 1), dtype=bf)
    h0 = np.zeros((1, 128), dtype=bf); h0[0, :64] = 1
    h1 = np.zeros((1, 128), dtype=bf); h1[0, 64:] = 1
    c["c_half0"] = h0
    c["c_half1"] = h1
    return c


GW = 240


def gather_window(j):
    return min(max(16 * j - 96, 0), SLOTS - GW), GW


def scatter_chunks(tc):
    lo = max(0, 16 * tc - SM)
    hi = min(SLOTS, 16 * tc + 16 + SM)
    return list(range(lo // 128, (hi + 127) // 128))


PHASES = None
DUMPS = False


def _on(name):
    return PHASES is None or name in PHASES


def build_program():
    nc = bass.Bass("TRN2", target_bir_lowering=False)
    tk = TK(nc)
    S, P = tk.sbuf, tk.psum

    inp = lambda n, s, d: nc.dram_tensor(n, s, d, kind="ExternalInput").ap()
    x_own = inp("x_own", [N, D], F32)
    x_rot = inp("x_rot", [B, N, D], F32)
    Wq = inp("Wq", [D, D], F32)
    Wkv = inp("Wkv", [D, 2 * D], F32)
    Wo = inp("Wo", [D, D], F32)
    bo = inp("bo", [1, D], F32)
    wg = inp("wg", [D, NE], F32)
    Wf = inp("Wf", [NE, D, D], F32)
    bfe = inp("bfe", [NE, D], F32)
    hc0 = host_consts(0)
    cns = {n: inp(n, list(v.shape), BF16 if v.dtype == ml_dtypes.bfloat16 else F32)
           for n, v in hc0.items()}
    out_t = nc.dram_tensor("out", [N, D], F32, kind="ExternalOutput").ap()

    xn_tm_t = S("xn_tm", [128, NT, 1024], BF16)
    xnfm_ar = S("xnfm_ar", [128, 16384], BF16)
    attn_t = S("attn", [128, 8, 2048], BF16)
    pairs = S("pairs", [128, 14368], BF16)
    ar2 = S("ar2", [128, 8192], BF16)
    pt_ar = S("pt_ar", [128, 3072], BF16)
    wt_ar = S("wt_ar", [128, 8192], BF16)
    gm_ar = S("gm_ar", [128, 6144], BF16)
    oh_bf = S("oh_bf", [128, 1024], BF16)
    loctok = S("loctok", [128, 128], F32)
    slotb = S("slotb", [128, 2048], F32)
    wb = S("wb", [128, 2048], BF16)
    scol = S("scol", [128, 16], F32)
    wcol = S("wcol", [128, 16], F32)
    cnttm = S("cnttm", [128, 8], BF16)
    tbtm = S("tbtm", [128, 8], F32)
    vecb = S("vecb", [128, 8], F32)
    v8row = S("v8row", [1, 8], F32)
    tr16 = S("tr16", [16, 128], F32)
    row2k = S("row2k", [1, 2048], F32)
    row2kb = row2k[:, :].bitcast(BF16)
    cntf = row2kb[0:1, 0:1024]
    tbf = row2k[0:1, 0:1024]
    sflat = row2k[0:1, 0:2048]
    wflat = row2k[0:1, 0:2048]
    rstd = S("rstd", [128, NT], F32)
    nmr = S("nmr", [128, NT], F32)
    bn6 = S("bn6", [128, 12], F32)
    bn2 = S("bn2", [128, 2], F32)
    lnv = S("lnv", [128, 1], F32)
    epsc = S("epsc", [128, 1], F32)
    l8 = S("l8", [128, 8], F32)
    mx8 = S("mx8", [128, 8], F32)
    ex8 = S("ex8", [128, 8], F32)
    t8a = S("t8a", [128, 8], F32)
    zs = S("zs", [128, 4], F32)
    padj = S("padj", [128, 2], F32)
    sml = S("sml", [128, 16], F32)
    padjE = S("padjE", [128, 16], F32)
    edsp = S("edsp", [128, 24], F32)
    s2cp = [row2kb[0:1, 0:1024], row2kb[0:1, 1024:2048]]
    rrowp = [row2kb[0:1, 2048:3072], row2kb[0:1, 3072:4096]]
    wgsb = S("wgsb", [128, 64], F32)
    wgtmp = S("wgtmp", [128, 8], F32)
    wgp = S("wgp", [128, 128], BF16)
    bo_bf = S("bo_bf", [1, 1024], BF16)
    be_bf = S("be_bf", [1, 1024], BF16)
    csb = {}
    for n, v in hc0.items():
        dt = BF16 if v.dtype == ml_dtypes.bfloat16 else F32
        csb[n] = S("s_" + n, list(v.shape), dt)

    psA = P("psA", [128, 1024], F32)
    psB = P("psB", [128, 1024], F32)
    psC = P("psC", [128, 512], F32)
    psD = P("psD", [128, 512], F32)
    psE = P("psE", [128, 512], F32)
    psF = P("psF", [128, 512], F32)

    xn_tm = xn_tm_t[:, :, :]
    xnfm = xnfm_ar[:, :].rearrange("p (dc t) -> p dc t", dc=8)
    attn = attn_t[:, :, :]
    gtb = attn_t[:, 0, 0:1024]
    exclv = gm_ar[:, 0:2048].bitcast(F32)
    locv = gm_ar[:, 2048:4096].bitcast(F32)
    ohf = gm_ar[:, 4096:6144].bitcast(F32)
    stores = []

    qP = pairs[:, 0:2048]
    kP = pairs[:, 2048:4096]
    vS = pairs[:, 4096:6144]
    vT = pairs[:, 6144:8224].rearrange("p (j c) -> p j c", c=130)
    wqS = [pairs[:, 8224 + i * 1024: 8224 + (i + 1) * 1024] for i in range(2)]
    wkS = [pairs[:, 10272 + i * 1024: 10272 + (i + 1) * 1024] for i in range(2)]
    wvS = [pairs[:, 12320:13344], pairs[:, 13344:14368]]

    def mm(ps, lh, rh, start=True, stop=True, tag=""):
        return tk.op("tensor",
                     lambda t, a=ps, b=lh, c=rh, s1=start, s2=stop:
                     t.matmul(a, b, c, start=s1, stop=s2),
                     reads=[lh, rh], writes=[ps], tag=tag)

    def act(out, in_, func, bias=0.0, scale=1.0, tag=""):
        reads = [in_]
        if not isinstance(bias, float):
            reads.append(bias)
        if not isinstance(scale, float):
            reads.append(scale)
        return tk.op("scalar",
                     lambda s, o=out, i=in_, f=func, b=bias, sc=scale:
                     s.activation(o, i, f, bias=b, scale=sc),
                     reads=reads, writes=[out], tag=tag)

    def vcopy(out, in_, tag=""):
        return tk.op("vector", lambda v, o=out, i=in_: v.tensor_copy(o, i),
                     reads=[in_], writes=[out], tag=tag)

    def scopy(out, in_, tag=""):
        return act(out, in_, AF.Identity, tag=tag)

    def vtt(out, a, b, op, tag=""):
        return tk.op("vector",
                     lambda v, o=out, x=a, y=b, w=op: v.tensor_tensor(o, x, y, w),
                     reads=[a, b], writes=[out], tag=tag)

    def vts(out, in_, s1, op, tag=""):
        reads = [in_] + ([s1] if not isinstance(s1, float) else [])
        return tk.op("vector",
                     lambda v, o=out, i=in_, s=s1, w=op:
                     v.tensor_single_scalar(o, i, s, w),
                     reads=reads, writes=[out], tag=tag)

    def vstt(out, in0, s1, in1, op0, op1, tag=""):
        reads = [in0, in1] + ([s1] if not isinstance(s1, float) else [])
        return tk.op("vector",
                     lambda v, o=out, a=in0, s=s1, b=in1, w0=op0, w1=op1:
                     v.scalar_tensor_tensor(o, a, s, b, w0, w1),
                     reads=reads, writes=[out], tag=tag)

    for n in cns:
        tk.dma("sync", csb[n][:, :], cns[n], tag="c_" + n)
    tk.dma("sync", wgsb[:, :].rearrange("p (dc e) -> p dc e", dc=8),
           wg.rearrange("(dc p) e -> p dc e", p=128), tag="wg")
    tk.dma("gpsimd", bo_bf[:, :], bo, tag="bo")
    tk.op("vector", lambda v: v.memset(epsc[:, :], 1e-5), writes=[epsc[:, :]])
    tk.op("vector", lambda v: v.memset(wgp[:, :], 0.0), writes=[wgp[:, :]])

    if _on("route1"):
        mm(psA[0:1, 0:64], csb["c_onescol_f"][:, :], wgsb[:, :], tag="wgsum")
        vcopy(tr16[0:1, 0:64], psA[0:1, 0:64], tag="wgsum_ev")
        tk.op("vector",
              lambda v: v.reduce_sum(
                  wgtmp[0:1, 0:8],
                  tr16[0:1, 0:64].rearrange("o (dc e) -> o e dc", e=8),
                  axis=AX.X),
              reads=[tr16[0:1, 0:64]], writes=[wgtmp[0:1, 0:8]], tag="wgred")
        vts(wgtmp[0:1, 0:8], wgtmp[0:1, 0:8], -1.0 / 1024.0, ALU.mult, tag="wgsc")
        mm(psB[:, 0:8], csb["c_ones_f"][:, :], wgtmp[0:1, 0:8], tag="wgbc")
        for dc in range(8):
            vtt(wgtmp[:, 0:8], psB[:, 0:8], wgsb[:, 8 * dc:8 * dc + 8],
                ALU.add, tag=f"wgp{dc}")
            vcopy(wgp[:, 16 * dc:16 * dc + 8], wgtmp[:, 0:8], tag=f"wgpc{dc}")

    if _on("ln"):
        for j in range(NT):
            xs = ar2[:, (j % 2) * 2048:(j % 2) * 2048 + 2048].bitcast(F32)
            tk.dma("sync", xs, x_own[128 * j:128 * j + 128, :], tag=f"xl{j}")
            tk.op("vector", lambda v, a=xs: v.bn_stats(bn6[:, 0:6], a[:, 0:512]),
                  reads=[xs[:, 0:512]], writes=[bn6[:, 0:6]], tag=f"bns0_{j}")
            tk.op("vector", lambda v, a=xs: v.bn_stats(bn6[:, 6:12], a[:, 512:1024]),
                  reads=[xs[:, 512:1024]], writes=[bn6[:, 6:12]], tag=f"bns1_{j}")
            tk.op("vector", lambda v: v.bn_aggr(bn2[:, :], bn6[:, :]),
                  reads=[bn6[:, :]], writes=[bn2[:, :]], tag=f"bna{j}")
            act(lnv[:, :], bn2[:, 1:2], AF.Ln, bias=epsc[:, :], tag=f"ln{j}")
            act(rstd[:, j:j + 1], lnv[:, :], AF.Exp, scale=-0.5, tag=f"rs{j}")
            vtt(nmr[:, j:j + 1], bn2[:, 0:1], rstd[:, j:j + 1], ALU.mult,
                tag=f"nm{j}")
            vts(nmr[:, j:j + 1], nmr[:, j:j + 1], -1.0, ALU.mult, tag=f"nmn{j}")
            act(xn_tm[:, j, :], xs, AF.Identity, bias=nmr[:, j:j + 1],
                scale=rstd[:, j:j + 1], tag=f"xn{j}")

    if _on("xnfm"):
        for j in range(NT):
            tk.dma("sync",
                   xnfm[:, :, 128 * j:128 * j + 128],
                   xn_tm[:, j, :].rearrange("p (dc q) -> p dc q", dc=8),
                   tag=f"xt{j}", transpose=True)

    if _on("route4"):
        xh2s = [ar2[:, 0:2048], ar2[:, 4096:6144]]
        xf2s = [ar2[:, 2048:4096], ar2[:, 6144:8192]]
        for gp in range(GT // 2):
            g = 2 * gp
            bidx, jj = g // NT, g % NT
            sl = gp % 2
            xh2 = xh2s[sl]
            xfv = xf2s[sl].rearrange("p (b q) -> p b q", b=16)
            tk.dma("gpsimd", xh2.rearrange("p (t d) -> p t d", t=2),
                   x_rot[bidx, 128 * jj:128 * jj + 256, :].rearrange(
                       "(t p) d -> p t d", p=128), tag=f"xh{g}")
            tk.dma("sync",
                   xfv, xh2.rearrange("p (b q) -> p b q", b=16),
                   tag=f"xf{g}", transpose=True)
            for t in range(2):
                for dc in range(8):
                    mm(psA[:, 8 * (g + t):8 * (g + t) + 8],
                       xfv[:, 8 * t + dc, :], wgp[:, 16 * dc:16 * dc + 8],
                       start=(dc == 0), stop=(dc == 7), tag=f"g{g + t}_{dc}")
        vcopy(gtb[:, 0:512], psA[:, 0:512], tag="gev0")
        vcopy(gtb[:, 512:1024], psA[:, 512:1024], tag="gev1")

    if _on("route5"):
        gt3v = gtb.rearrange("p (j e) -> p j e", e=8)
        mxt = loctok
        tk.op("vector", lambda v: v.reduce_max(mxt[:, :], gt3v, axis=AX.X),
              reads=[gtb], writes=[mxt[:, :]], tag="rmax")
        eq3 = exclv.rearrange("p (j e) -> p j e", e=8)
        tk.op("vector",
              lambda v: v.tensor_tensor(
                  eq3, gt3v, mxt[:, :].broadcast_to([128, 128, 8]),
                  ALU.is_equal),
              reads=[gtb, mxt[:, :]], writes=[exclv], tag="eq")
        iota_b = csb["c_iota8r"][:, :].__replace__(ap=[[8, 128], [0, 128], [1, 8]])
        u3 = locv.rearrange("p (j e) -> p j e", e=8)
        tk.op("vector",
              lambda v: v.tensor_tensor(u3, eq3, iota_b, ALU.mult),
              reads=[exclv, csb["c_iota8r"][:, :]], writes=[locv], tag="umul")
        tk.op("vector", lambda v: v.reduce_max(mxt[:, :], u3, axis=AX.X),
              reads=[locv], writes=[mxt[:, :]], tag="umax")
        umb = mxt[:, :].broadcast_to([128, 128, 8])
        oh3b = oh_bf[:, :].rearrange("p (j e) -> p j e", e=8)
        tk.op("vector",
              lambda v: v.tensor_tensor(oh3b, u3, umb, ALU.is_equal),
              reads=[locv, mxt[:, :]], writes=[oh_bf[:, :]], tag="ohb")
        oh3f = ohf.rearrange("p (j e) -> p j e", e=8)
        tk.op("vector",
              lambda v: v.tensor_tensor(oh3f, u3, umb, ALU.is_equal),
              reads=[locv, mxt[:, :]], writes=[ohf], tag="ohfb")
        for hhalf in range(2):
            sl = slice(512 * hhalf, 512 * hhalf + 512)
            mm(psA[:, sl], csb["c_L128"][:, :], oh_bf[:, sl], tag=f"ex{hhalf}")
            vcopy(exclv[:, sl], psA[:, sl], tag=f"exev{hhalf}")
        for hhalf in range(2):
            sl = slice(512 * hhalf, 512 * hhalf + 512)
            mm(psB[0:1, sl], csb["c_onescol_bf"][:, :], oh_bf[:, sl],
               tag=f"cnt{hhalf}")
        scopy(cntf, psB[0:1, 0:1024], tag="cntev")
        tk.dma("gpsimd", cnttm[:, :], cntf, tag="cntrs")
        mm(psB[:, 0:8], csb["c_Lt"][:, :], cnttm[:, :], tag="tbmm")
        vcopy(tbtm[:, :], psB[:, 0:8], tag="tbev")
        tk.dma("gpsimd", tbf, tbtm[:, :], tag="tbrs")
        for hhalf in range(2):
            sl = slice(512 * hhalf, 512 * hhalf + 512)
            mm(psA[:, sl], csb["c_ones_f"][:, :], tbf[0:1, sl], tag=f"tb{hhalf}")
            vstt(locv[:, sl], psA[:, sl], 1.0, exclv[:, sl], ALU.mult, ALU.add,
                 tag=f"loc{hhalf}")
        for hhalf in range(2):
            sl = slice(512 * hhalf, 512 * hhalf + 512)
            vtt(psB[:, sl], ohf[:, sl], locv[:, sl], ALU.mult, tag=f"lt{hhalf}")
        tk.op("vector",
              lambda v: v.reduce_sum(
                  loctok[:, :],
                  psB[:, 0:1024].rearrange("p (j e) -> p j e", e=8), axis=AX.X),
              reads=[psB[:, :]], writes=[loctok[:, :]], tag="ltred")

    if _on("route6"):
        vts(v8row[:, :], csb["c_iota8"][0:1, 0:8], float(SLOTS), ALU.mult,
            tag="v8m")
        vtt(v8row[:, :], v8row[:, :], tbf[0:1, 0:8], ALU.subtract, tag="v8s")
        mm(psB[:, 0:8], csb["c_ones_f"][:, :], v8row[0:1, 0:8], tag="v8bc")
        vcopy(vecb[:, :], psB[:, 0:8], tag="v8ev")
        for j in range(NT):
            gsl = gtb[:, 8 * j:8 * j + 8]
            tk.op("vector",
                  lambda v, a=gsl, r=rstd[:, j:j + 1]:
                  v.tensor_single_scalar(l8[:, :], a, r, ALU.mult),
                  reads=[gsl, rstd[:, j:j + 1]], writes=[l8[:, :]],
                  tag=f"l8{j}")
            tk.op("vector", lambda v: v.max(mx8[:, :], l8[:, :]),
                  reads=[l8[:, :]], writes=[mx8[:, :]], tag=f"lm{j}")
            vts(zs[:, 0:1], mx8[:, 0:1], -1.0, ALU.mult, tag=f"ln{j}")
            act(ex8[:, :], l8[:, :], AF.Exp, bias=zs[:, 0:1], tag=f"ge{j}")
            tk.op("vector",
                  lambda v: v.reduce_sum(zs[:, 1:2], ex8[:, :], axis=AX.X),
                  reads=[ex8[:, :]], writes=[zs[:, 1:2]], tag=f"gz{j}")
            tk.op("vector", lambda v: v.reciprocal(zs[:, 2:3], zs[:, 1:2]),
                  reads=[zs[:, 1:2]], writes=[zs[:, 2:3]], tag=f"gr{j}")
            vts(zs[:, 3:4], loctok[:, j:j + 1], float(CAP), ALU.is_lt,
                tag=f"kp{j}")
            vtt(wcol[:, j:j + 1], zs[:, 2:3], zs[:, 3:4], ALU.mult,
                tag=f"w{j}")
            vtt(t8a[:, :], locv[:, 8 * j:8 * j + 8], vecb[:, :], ALU.add,
                tag=f"sl{j}")
            vtt(t8a[:, :], t8a[:, :], ohf[:, 8 * j:8 * j + 8], ALU.mult,
                tag=f"slm{j}")
            tk.op("vector",
                  lambda v, o=scol[:, j:j + 1]: v.reduce_sum(o, t8a[:, :],
                                                             axis=AX.X),
                  reads=[t8a[:, :]], writes=[scol[:, j:j + 1]], tag=f"sr{j}")

    if _on("route7"):
        for (colsrc, flat, dstb, dt_out, tag) in (
                (scol, sflat, slotb, F32, "s"), (wcol, wflat, wb, BF16, "w")):
            tk.op("tensor",
                  lambda t, a=colsrc: t.transpose(psA[0:16, 0:128], a[:, :],
                                                  csb["c_ident"][:, :]),
                  reads=[colsrc[:, :], csb["c_ident"][:, :]],
                  writes=[psA[:, :]], tag=f"{tag}tr")
            vcopy(tr16[:, :], psA[0:16, 0:128], tag=f"{tag}trev")
            tk.dma("gpsimd", flat[0:1, 0:2048], tr16[:, :], tag=f"{tag}rs")
            for q in range(4):
                sl = slice(512 * q, 512 * q + 512)
                mm(psB[:, 0:512] if q % 2 == 0 else psB[:, 512:1024],
                   csb["c_ones_f"][:, :], flat[0:1, sl], tag=f"{tag}bc{q}")
                src = psB[:, 0:512] if q % 2 == 0 else psB[:, 512:1024]
                vcopy(dstb[:, sl], src, tag=f"{tag}be{q}")

    if _on("attn"):
        WqR = Wq.rearrange("(dc p) i -> p dc i", p=128)
        WkvR = Wkv.rearrange("(dc p) i -> p dc i", p=128)
        for pr in range(8):
            sl = pr % 2
            tk.dma("gpsimd", wqS[sl].rearrange("p (dc q) -> p dc q", dc=8),
                   WqR[:, :, 128 * pr:128 * pr + 128], tag=f"wq{pr}")
            tk.dma("gpsimd", wkS[sl].rearrange("p (dc q) -> p dc q", dc=8),
                   WkvR[:, :, 128 * pr:128 * pr + 128], tag=f"wk{pr}")
            tk.dma("gpsimd", wvS[sl].rearrange("p (dc q) -> p dc q", dc=8),
                   WkvR[:, :, 1024 + 128 * pr:1024 + 128 * pr + 128],
                   tag=f"wv{pr}")
            wqv = wqS[sl].rearrange("p (dc q) -> p dc q", dc=8)
            wkv = wkS[sl].rearrange("p (dc q) -> p dc q", dc=8)
            wvv = wvS[sl].rearrange("p (dc q) -> p dc q", dc=8)
            for (wsl, dst, tg) in ((wqv, qP, "q"), (wkv, kP, "k"),
                                   (wvv, vS, "v")):
                for nck in range(4):
                    ps = psB[:, 0:512] if nck % 2 == 0 else psB[:, 512:1024]
                    csl = slice(512 * nck, 512 * nck + 512)
                    for dc in range(8):
                        mm(ps, wsl[:, dc, :], xnfm[:, dc, csl],
                           start=(dc == 0), stop=(dc == 7),
                           tag=f"{tg}p{pr}_{nck}_{dc}")
                    vcopy(dst[:, csl], ps, tag=f"{tg}e{pr}_{nck}")
            vSv = vS.rearrange("p (j q) -> p j q", j=NT)
            for h01t in range(2):
                slot = (2 * pr + h01t) % 4
                scrv = gm_ar[:, 1024 * slot: 1024 * slot + 1024].rearrange(
                    "p (j c) -> p j c", c=64)
                tk.dma("sync", scrv, vSv[64 * h01t:64 * h01t + 64, :, :],
                       tag=f"vt{h01t}_{pr}", transpose=True)
                vcopy(vT[:, :, 65 * h01t:65 * h01t + 64], scrv,
                      tag=f"vc{h01t}_{pr}")
            if pr == 0:
                tk.op("vector", lambda v: v.memset(vT[:, :, 64:65], 1.0),
                      writes=[vT[:, :, 64:65]], tag="vo0")
                tk.op("vector", lambda v: v.memset(vT[:, :, 129:130], 1.0),
                      writes=[vT[:, :, 129:130]], tag="vo1")
            pending_div = None
            for c in range(4):
                for j in range(4 * c + 4):
                    voff = max(0, 128 * j - 512 * c)
                    diag = j >= 4 * c
                    pts = []
                    for h01 in range(2):
                        hb = 64 * h01
                        sps = psC if h01 == 0 else psD
                        mm(sps[:, voff:512],
                           kP[hb:hb + 64, 128 * j:128 * j + 128],
                           qP[hb:hb + 64, 512 * c + voff:512 * c + 512],
                           start=True, stop=not diag,
                           tag=f"s{pr}_{h01}_{c}_{j}")
                        if diag:
                            mm(sps[:, voff:voff + 128], csb["c_L128"][:, :],
                               csb["c_negid"][:, :], start=False, stop=True,
                               tag=f"m{pr}_{h01}_{c}_{j}")
                        pt = pt_ar[:, 512 * ((2 * j + h01) % 4):
                                   512 * ((2 * j + h01) % 4) + 512]
                        act(pt[:, voff:512], sps[:, voff:512], AF.Exp,
                            scale=0.125, tag=f"e{pr}_{h01}_{c}_{j}")
                        pts.append(pt)
                    for h01 in range(2):
                        nps = psE if h01 == 0 else psF
                        mm(nps[0:65, voff:512],
                           vT[:, j, 65 * h01:65 * h01 + 65],
                           pts[h01][:, voff:512],
                           start=(j == 0), stop=(j == 4 * c + 3),
                           tag=f"n{pr}_{h01}_{c}_{j}")
                par = c % 2
                vcopy(attn[0:64, pr, 512 * c:512 * c + 512], psE[0:64, :],
                      tag=f"ae{pr}_{c}")
                vcopy(attn[64:128, pr, 512 * c:512 * c + 512], psF[0:64, :],
                      tag=f"ao{pr}_{c}")
                vcopy(s2cp[par][0:1, 0:512], psE[64:65, :], tag=f"se{pr}_{c}")
                vcopy(s2cp[par][0:1, 512:1024], psF[64:65, :],
                      tag=f"so{pr}_{c}")

                def mkdiv(pr=pr, c=c, par=par):
                    def emit():
                        rr = rrowp[par]
                        lsc = psA[0:1, 0:1024]
                        act(lsc, s2cp[par][0:1, :], AF.Ln, tag=f"rl{pr}_{c}")
                        act(rr[0:1, :], lsc, AF.Exp, scale=-1.0,
                            tag=f"rx{pr}_{c}")
                        mm(psA[:, 0:512], csb["c_half0"][:, :],
                           rr[0:1, 0:512], start=True, stop=False,
                           tag=f"rb0{pr}_{c}")
                        mm(psA[:, 0:512], csb["c_half1"][:, :],
                           rr[0:1, 512:1024], start=False, stop=True,
                           tag=f"rb1{pr}_{c}")
                        rsb = oh_bf[:, 512 * par:512 * par + 512]
                        vcopy(rsb, psA[:, 0:512], tag=f"re{pr}_{c}")
                        vtt(attn[:, pr, 512 * c:512 * c + 512],
                            attn[:, pr, 512 * c:512 * c + 512], rsb, ALU.mult,
                            tag=f"dv{pr}_{c}")
                    return emit

                if pending_div is not None:
                    pending_div()
                pending_div = mkdiv()
            pending_div()

    Y = []
    for e in range(NE):
        if e <= 4:
            Y.append(xnfm_ar[:, 3072 * e:3072 * e + 3072])
        elif e == 5:
            Y.append(pt_ar[:, 0:3072])
        else:
            Y.append(pairs[:, 3072 * (e - 6):3072 * (e - 6) + 3072])
    Yv = [y.rearrange("p (sc f) -> p sc f", sc=3) for y in Y]
    wstage = ar2[:, 0:8192].rearrange("p (fc h) -> p fc h", fc=8)
    wt = wt_ar[:, :].rearrange("p (hc f) -> p hc f", hc=8)

    if _on("ffn"):
        for e in range(NE):
            tk.dma("gpsimd", wstage,
                   Wf[e, :, :].rearrange("(fc p) h -> p fc h", p=128),
                   tag=f"wf{e}")
            for fc in range(8):
                tk.dma("sync",
                       wt[:, :, 128 * fc:128 * fc + 128],
                       wstage[:, fc, :],
                       tag=f"wt{e}_{fc}", transpose=True)
            tk.dma("gpsimd", be_bf[0:1, 0:1024], bfe[e:e + 1, :], tag=f"be{e}")
            eg, eo = e // 2, e % 2
            if e == 0:
                vtt(sml[:, :], scol[:, :], csb["c_wlo"][:, :], ALU.subtract,
                    tag="sml")
            vts(padjE[:, :], sml[:, :], -float(SLOTS * e), ALU.add,
                tag=f"pa{e}")
            P0 = 3984 * eo
            pt0 = pairs[:, P0:P0 + 384]
            ptr = pairs[:, P0 + 384: P0 + 3984].rearrange(
                "p (j w) -> p j w", w=GW)
            vts(pt0, csb["c_iota"][:, 0:384], padjE[:, 0:1], ALU.is_equal,
                tag=f"p0{e}")
            iota_w = csb["c_iota"][:, 0:GW].__replace__(
                ap=[[384, 128], [0, 15], [1, GW]])
            tk.op("vector",
                  lambda v, o=ptr, i=iota_w, b=padjE:
                  v.tensor_tensor(o, i,
                                  b[:, 1:16].broadcast_to([128, 15, GW]),
                                  ALU.is_equal),
                  reads=[csb["c_iota"][:, 0:GW], padjE[:, :]],
                  writes=[pairs[:, P0 + 384: P0 + 3984]],
                  tag=f"pc{e}")
            gsp = psC if eo == 0 else psD
            gdst = pairs[:, 7968 + 3072 * eo: 7968 + 3072 * eo + 3072]
            gv = gdst.rearrange("p (hc s) -> p hc s", hc=8)
            for hcx in range(8):
                for j in range(NT):
                    if j == 0:
                        wlo, W, ptile = 0, 384, pt0
                    else:
                        wlo, W = gather_window(j)
                        ptile = ptr[:, j - 1, :]
                    mm(gsp[:, wlo:wlo + W],
                       xn_tm[:, j, 128 * hcx:128 * hcx + 128], ptile,
                       start=(j == 0), stop=(j == NT - 1),
                       tag=f"gm{e}_{hcx}_{j}")
                vcopy(gv[:, hcx, 0:SLOTS], gsp[:, 0:SLOTS], tag=f"ge{e}_{hcx}")
            for sc in range(3):
                for fh in range(2):
                    yp = (psE if fh == 0 else psF)[:, :]
                    for hcx in range(8):
                        mm(yp, gv[:, hcx, 128 * sc:128 * sc + 128],
                           wt[:, hcx, 512 * fh:512 * fh + 512],
                           start=(hcx == 0), stop=False,
                           tag=f"ym{e}_{sc}_{fh}_{hcx}")
                    mm(yp, csb["c_ones_bf"][:, :],
                       be_bf[0:1, 512 * fh:512 * fh + 512],
                       start=False, stop=True, tag=f"yb{e}_{sc}_{fh}")
                    scopy(Yv[e][:, sc, 512 * fh:512 * fh + 512], yp,
                          tag=f"ye{e}_{sc}_{fh}")

    if _on("tail"):
        tk.dma("gpsimd", wt_ar[:, :].rearrange("p (ic d) -> p ic d", ic=8),
               Wo.rearrange("(ic p) d -> p ic d", p=128), tag="wo")
        wov = wt_ar[:, :].rearrange("p (ic d) -> p ic d", ic=8)
        for tc in range(NT):
            sl = tc % 2
            xr = ar2[:, 2048 * sl:2048 * sl + 2048].bitcast(F32)
            tk.dma("sync", xr, x_own[128 * tc:128 * tc + 128, :], tag=f"xr{tc}")
            if tc == 0:
                vts(edsp[:, :], csb["c_eds"][:, :], csb["c_piota"][:, :],
                    ALU.add, tag="edsp")
            nchunk = len(scatter_chunks(tc))
            K = NE * nchunk
            rt4 = gm_ar[:, 3072 * sl: 3072 * sl + 128 * K].rearrange(
                "p (e d t) -> p e d t", d=nchunk, t=128)
            sb = slotb[:, 128 * tc:128 * tc + 128]
            slot_b = sb.__replace__(
                ap=[[2048, 128], [0, NE], [0, nchunk], [1, 128]],
                offset=sb.offset)
            ev4 = edsp[:, :].rearrange("p (e d) -> p e d", d=3)[:, :, 0:nchunk]
            tk.op("vector",
                  lambda v, o=rt4, i=slot_b, b=ev4, nch=nchunk:
                  v.tensor_tensor(o, i,
                                  b.broadcast_to([128, NE, nch, 128]),
                                  ALU.is_equal),
                  reads=[sb, edsp[:, :]],
                  writes=[gm_ar[:, 3072 * sl: 3072 * sl + 128 * K]],
                  tag=f"rc{tc}")
            wbsl = wb[:, 128 * tc:128 * tc + 128]
            wb_b = wbsl.__replace__(
                ap=[[2048, 128], [0, K], [1, 128]], offset=wbsl.offset)
            rt3 = gm_ar[:, 3072 * sl: 3072 * sl + 128 * K].rearrange(
                "p (k t) -> p k t", t=128)
            tk.op("vector",
                  lambda v, o=rt3, b=wb_b:
                  v.tensor_tensor(o, o, b, ALU.mult),
                  reads=[wbsl, gm_ar[:, 3072 * sl: 3072 * sl + 128 * K]],
                  writes=[gm_ar[:, 3072 * sl: 3072 * sl + 128 * K]],
                  tag=f"rw{tc}")
            rws = []
            k = 0
            for e in range(NE):
                for sc in scatter_chunks(tc):
                    rt = gm_ar[:, 3072 * sl + 128 * k: 3072 * sl + 128 * k + 128]
                    k += 1
                    rws.append((e, sc, rt))
            ps = psA if sl == 0 else psB
            for dh in range(2):
                psd = ps[:, 512 * dh:512 * dh + 512]
                for ic in range(8):
                    mm(psd, attn[:, ic, 128 * tc:128 * tc + 128],
                       wov[:, ic, 512 * dh:512 * dh + 512],
                       start=(ic == 0), stop=False, tag=f"pj{tc}_{dh}_{ic}")
                mm(psd, csb["c_ones_bf"][:, :],
                   bo_bf[0:1, 512 * dh:512 * dh + 512],
                   start=False, stop=False, tag=f"pb{tc}_{dh}")
                for i, (e, sc, rt) in enumerate(rws):
                    mm(psd, rt, Yv[e][:, sc, 512 * dh:512 * dh + 512],
                       start=False, stop=(i == len(rws) - 1),
                       tag=f"sc{tc}_{dh}_{e}_{sc}")
                vstt(xr[:, 512 * dh:512 * dh + 512], psd, 1.0,
                     xr[:, 512 * dh:512 * dh + 512], ALU.mult, ALU.add,
                     tag=f"oev{tc}_{dh}")
            stores.append(tk.dma("sync", out_t[128 * tc:128 * tc + 128, :],
                                 xr, tag=f"st{tc}"))

    if DUMPS:
        def dump(nm, src, shape, dt):
            t = nc.dram_tensor(nm, shape, dt, kind="ExternalOutput").ap()
            stores.append(tk.dma("gpsimd", t, src, tag="d_" + nm))
        dump("d_xn", xn_tm_t[:, :, :], [128, NT, 1024], BF16)
        dump("d_attn", attn_t[:, :, :], [128, 8, 2048], BF16)
        dump("d_slotb", slotb[:, :], [128, 2048], F32)
        dump("d_wb", wb[:, :], [128, 2048], BF16)
        dump("d_loctok", loctok[:, :], [128, 128], F32)
        dump("d_scol", scol[:, :], [128, 16], F32)
        dump("d_wcol", wcol[:, :], [128, 16], F32)
        dump("d_xnfm", xnfm_ar[:, :], [128, 16384], BF16)
        dump("d_ptar", pt_ar[:, :], [128, 3072], BF16)
        dump("d_pairs", pairs[:, 0:13344], [128, 13344], BF16)
        dump("d_rstd", rstd[:, :], [128, NT], F32)

    tk.build(final_waits=stores, final_engine="sync")
    return nc, tk


_CACHE = {}


def _core_inputs(core, inputs):
    x = np.ascontiguousarray(inputs["L0_attn_x"])
    d = {
        "x_own": x[core],
        "x_rot": np.roll(x, -core, axis=0),
        "Wq": inputs["L0_attn_Wq"],
        "Wkv": inputs["L0_attn_Wkv"],
        "Wo": inputs["L0_attn_Wo"],
        "bo": np.ascontiguousarray(inputs["L0_attn_bo"]).reshape(1, D),
        "wg": inputs["L0_ffn_wg"],
        "Wf": inputs["L0_ffn_W"],
        "bfe": inputs["L0_ffn_b"],
    }
    d.update(host_consts(core))
    return {k: np.ascontiguousarray(v) for k, v in d.items()}


def kernel(**inputs):
    if "nc" not in _CACHE:
        _CACHE["nc"] = build_program()[0]
    nc = _CACHE["nc"]
    in_maps = [_core_inputs(c, inputs) for c in range(B)]
    res = run_bass_kernel_spmd(nc, in_maps, core_ids=list(range(B)))
    return np.stack([np.asarray(res.results[c]["out"], dtype=np.float32)
                     for c in range(B)], axis=0)


if __name__ == "__main__":
    nc, tk = build_program()
    ni = sum(len(s) for s in tk.streams.values())
    print(f"ops={ni} edges={tk.n_edges} waits={tk.n_waits}")


# version 05
# speedup vs baseline: 1.4536x; 1.0919x over previous
import contextlib
import numpy as np
import ml_dtypes
import concourse.bass as bass
import concourse.mybir as mybir
from concourse.bass_utils import run_bass_kernel_spmd

F32 = mybir.dt.float32
BF16 = mybir.dt.bfloat16
U32 = mybir.dt.uint32
AF = mybir.ActivationFunctionType
ALU = mybir.AluOpType
AX = mybir.AxisListType

B, N, D = 8, 2048, 1024
NH, DH = 16, 64
NE, CAP = 8, 2048
NT = N // 128
GT = B * NT
SLOTS = 384
GM = 96
SM = 128

ENGINES = ("tensor", "vector", "scalar", "gpsimd", "sync")
ESEM = {e: "sem_" + e for e in ENGINES}
_DT_SIZE = {"float32": 4, "float32r": 4, "bfloat16": 2, "float16": 2,
            "uint32": 4, "int32": 4, "uint16": 2, "int16": 2, "uint8": 1}


def _dt_size(dt):
    n = getattr(dt, "name", None) or str(dt).split(".")[-1]
    return _DT_SIZE.get(n, 4)


def _as_ap(x):
    return x.ap() if callable(getattr(x, "ap", None)) else x


def _ranges(x, out=None):
    if out is None:
        out = []
    if x is None or isinstance(x, (int, float, str)):
        return out
    if isinstance(x, (list, tuple)):
        for e in x:
            _ranges(e, out)
        return out
    ap = _as_ap(x)
    try:
        space = str(ap.space)
    except Exception:
        space = "?"
    name = ap.name
    if "SB" not in space.upper():
        out.append((name, 0, 1 << 20, 0, 1 << 60))
        return out
    try:
        pat = ap.ap
        offset = ap.offset
        esz = _dt_size(ap.dtype)
        pstride, pcount = pat[0]
        plo = (offset // pstride) if pstride else 0
        phi = plo + pcount
        row_off = offset - plo * pstride
        span = 1
        for step, count in pat[1:]:
            span += step * (count - 1)
        out.append((name, plo, phi, row_off * esz, (row_off + span) * esz))
    except Exception:
        out.append((name, 0, 1 << 20, 0, 1 << 60))
    return out


def _is_sbuf(x):
    try:
        return "SB" in str(_as_ap(x).space).upper()
    except Exception:
        return False


class Op:
    __slots__ = ("engine", "fn", "idx", "deps", "needs_inc", "fixed_inc",
                 "sem_val", "tag")

    def __init__(self, engine, fn, idx, tag=""):
        self.engine = engine
        self.fn = fn
        self.idx = idx
        self.deps = []
        self.needs_inc = False
        self.fixed_inc = None
        self.sem_val = None
        self.tag = tag


SERIALIZE = False


class TK:
    def __init__(self, nc, dma_queues=("gpsimd", "sync", "scalar")):
        self.nc = nc
        self.all_ops = []
        self.ctx = contextlib.ExitStack()
        self.streams = {e: [] for e in ENGINES}
        self.sem = {}
        self.cnt = {}
        self.access = {}
        self.n_edges = 0
        for e in ENGINES:
            self.semaphore(ESEM[e])
        self.dma_ring = {}
        self.dma_ring_pos = {}
        for q in dma_queues:
            depth = 6 if q != "scalar" else 2
            ring = []
            for i in range(depth):
                s = f"dma_{q}_{i}"
                self.semaphore(s)
                ring.append(s)
            self.dma_ring[q] = ring
            self.dma_ring_pos[q] = 0
        self.pending_T = []
        self.pending_P = []

    def semaphore(self, name):
        h = self.ctx.enter_context(self.nc.semaphore(name))
        self.sem[name] = h
        self.cnt[name] = 0
        return h

    def sbuf(self, name, shape, dtype):
        return self.ctx.enter_context(self.nc.sbuf_tensor(name, shape, dtype))

    def psum(self, name, shape, dtype):
        return self.ctx.enter_context(self.nc.psum_tensor(name, shape, dtype))

    def _add_access(self, op, kind, rects):
        for rect in rects:
            name, plo, phi, blo, bhi = rect
            hist = self.access.setdefault(name, [])
            for (p, k, r) in hist:
                if p is op:
                    continue
                if kind == "R" and k == "R":
                    continue
                if not (plo < r[2] and r[1] < phi and blo < r[4] and r[3] < bhi):
                    continue
                if p.engine == op.engine == "tensor":
                    continue
                op.deps.append(p)
                self.n_edges += 1
            hist.append((op, kind, rect))

    def op(self, eng, fn, reads=(), writes=(), tag=""):
        o = Op(eng, fn, len(self.streams[eng]), tag)
        self._add_access(o, "R", _ranges(list(reads)))
        self._add_access(o, "W", _ranges(list(writes)))
        if SERIALIZE and self.all_ops:
            o.deps.append(self.all_ops[-1])
        self.all_ops.append(o)
        self.streams[eng].append(o)
        return o

    def _dma_slot(self, queue, n_inc):
        ring = self.dma_ring[queue]
        i = self.dma_ring_pos[queue]
        self.dma_ring_pos[queue] = (i + 1) % len(ring)
        sname = ring[i]
        pre = self.cnt[sname]
        self.cnt[sname] += n_inc
        return sname, pre, self.cnt[sname]

    def dma(self, queue, out, in_, tag="", transpose=False):
        sname, pre, post = self._dma_slot(queue, 16)
        o = Op(queue, None, len(self.streams[queue]), tag)
        o.fixed_inc = (sname, 16, post)
        h = self.sem[sname]

        def fn(e, out=out, in_=in_, h=h, pre=pre, transpose=transpose):
            if pre:
                e.wait_ge(h, pre)
            if transpose:
                return e.dma_start_transpose(out, in_).then_inc(h, 16)
            return e.dma_start(out=out, in_=in_).then_inc(h, 16)
        o.fn = fn
        self._add_access(o, "R", _ranges(in_))
        self._add_access(o, "W", _ranges(out))
        if SERIALIZE and self.all_ops:
            o.deps.append(self.all_ops[-1])
        self.all_ops.append(o)
        if _is_sbuf(in_) and _is_sbuf(out):
            if transpose:
                o.deps.extend(self.pending_P)
                self.n_edges += len(self.pending_P)
                self.pending_P = []
                self.pending_T.append(o)
            else:
                o.deps.extend(self.pending_T)
                self.n_edges += len(self.pending_T)
                self.pending_T = []
                self.pending_P.append(o)
        self.streams[queue].append(o)
        return o

    def build(self, final_waits=(), final_engine="sync"):
        nc = self.nc
        for e in ENGINES:
            for o in self.streams[e]:
                for p in o.deps:
                    if p.fixed_inc is None:
                        p.needs_inc = True
        for e in ENGINES:
            for o in self.streams[e]:
                if o.needs_inc and o.fixed_inc is None:
                    self.cnt[ESEM[e]] += 1
                    o.sem_val = self.cnt[ESEM[e]]

        def token(p):
            if p.fixed_inc is not None:
                return (p.fixed_inc[0], p.fixed_inc[2])
            return (ESEM[p.engine], p.sem_val)

        if final_waits:
            fo = Op(final_engine, None, len(self.streams[final_engine]), "final")
            toks = [token(p) for p in final_waits]

            def ffn(e, toks=toks):
                best = {}
                for s, v in toks:
                    best[s] = max(best.get(s, 0), v)
                last = None
                for s, v in sorted(best.items()):
                    last = e.wait_ge(self.sem[s], v)
                return last
            fo.fn = ffn
            self.streams[final_engine].append(fo)

        emit_plan = {}
        n_waits = 0
        for e in ENGINES:
            water = {}
            plan = []
            for o in self.streams[e]:
                best = {}
                for p in o.deps:
                    s, v = token(p)
                    assert v is not None, f"unresolved producer {p.tag} <- {o.tag}"
                    if best.get(s, -1) < v:
                        best[s] = v
                waits = []
                for s, v in sorted(best.items()):
                    if water.get(s, 0) < v:
                        water[s] = v
                        waits.append((self.sem[s], v))
                n_waits += len(waits)
                plan.append((waits, o))
            emit_plan[e] = plan
        self.n_waits = n_waits

        with nc.Block() as block:
            for e in ENGINES:
                plan = emit_plan[e]
                if not plan:
                    continue

                def body(eng, plan=plan, e=e):
                    sem_h = self.sem[ESEM[e]]
                    for waits, o in plan:
                        for (h, v) in waits:
                            eng.wait_ge(h, v)
                        ins = o.fn(eng)
                        if o.needs_inc and o.fixed_inc is None:
                            assert ins is not None, f"op {o.tag} returned None"
                            ins.then_inc(sem_h, 1)
                getattr(block, e)(body)
        return nc


def host_consts(core):
    bf = ml_dtypes.bfloat16
    c = {}
    c["c_iota8"] = np.tile(np.arange(8, dtype=np.float32), (128, 1))
    c["c_iota8r"] = np.tile(8.0 - np.arange(8, dtype=np.float32), (128, 1))
    c["c_iota"] = np.tile(np.arange(384, dtype=np.float32), (128, 1))
    c["c_piota"] = np.arange(128, dtype=np.float32)[:, None]
    wlo = np.minimum(np.maximum(16 * np.arange(16) - 96, 0), 144)
    c["c_wlo"] = np.tile(wlo.astype(np.float32), (128, 1))
    eds = (384 * np.arange(8)[:, None] + 128 * np.arange(3)[None, :]).reshape(-1)
    c["c_eds"] = np.tile(eds.astype(np.float32), (128, 1))
    k = np.arange(128)
    c["c_L128"] = (k[:, None] < k[None, :]).astype(bf)
    c["c_negid"] = (-1e9 * np.eye(128)).astype(bf)
    true = ((core + k // 16) % 8) * 16 + (k % 16)
    c["c_Lt"] = (true[:, None] < true[None, :]).astype(bf)
    c["c_ident"] = np.eye(128, dtype=np.float32)
    c["c_ones_bf"] = np.ones((1, 128), dtype=bf)
    c["c_ones_f"] = np.ones((1, 128), dtype=np.float32)
    c["c_onescol_f"] = np.ones((128, 1), dtype=np.float32)
    c["c_onescol_bf"] = np.ones((128, 1), dtype=bf)
    h0 = np.zeros((1, 128), dtype=bf); h0[0, :64] = 1
    h1 = np.zeros((1, 128), dtype=bf); h1[0, 64:] = 1
    c["c_half0"] = h0
    c["c_half1"] = h1
    return c


GW = 240


def gather_window(j):
    return min(max(16 * j - 96, 0), SLOTS - GW), GW


def scatter_chunks(tc):
    lo = max(0, 16 * tc - SM)
    hi = min(SLOTS, 16 * tc + 16 + SM)
    return list(range(lo // 128, (hi + 127) // 128))


PHASES = None
DUMPS = False


def _on(name):
    return PHASES is None or name in PHASES


def build_program():
    nc = bass.Bass("TRN2", target_bir_lowering=False)
    tk = TK(nc)
    S, P = tk.sbuf, tk.psum

    inp = lambda n, s, d: nc.dram_tensor(n, s, d, kind="ExternalInput").ap()
    x_own = inp("x_own", [N, D], F32)
    x_rot = inp("x_rot", [B, N, D], F32)
    Wq = inp("Wq", [D, D], F32)
    Wkv = inp("Wkv", [D, 2 * D], F32)
    Wo = inp("Wo", [D, D], F32)
    bo = inp("bo", [1, D], F32)
    wg = inp("wg", [D, NE], F32)
    Wf = inp("Wf", [NE, D, D], F32)
    bfe = inp("bfe", [NE, D], F32)
    hc0 = host_consts(0)
    cns = {n: inp(n, list(v.shape), BF16 if v.dtype == ml_dtypes.bfloat16 else F32)
           for n, v in hc0.items()}
    out_t = nc.dram_tensor("out", [N, D], F32, kind="ExternalOutput").ap()

    xn_tm_t = S("xn_tm", [128, NT, 1024], BF16)
    xnfm_ar = S("xnfm_ar", [128, 16384], BF16)
    attn_t = S("attn", [128, 8, 2048], BF16)
    pairs = S("pairs", [128, 14368], BF16)
    ar2 = S("ar2", [128, 8192], BF16)
    pt_ar = S("pt_ar", [128, 3072], BF16)
    wt_ar = S("wt_ar", [128, 8192], BF16)
    gm_ar = S("gm_ar", [128, 6144], BF16)
    oh_bf = S("oh_bf", [128, 1024], BF16)
    loctok = S("loctok", [128, 128], F32)
    slotb = S("slotb", [128, 2048], F32)
    wb = S("wb", [128, 2048], BF16)
    scol = S("scol", [128, 16], F32)
    wcol = S("wcol", [128, 16], F32)
    cnttm = S("cnttm", [128, 8], BF16)
    tbtm = S("tbtm", [128, 8], F32)
    vecb = S("vecb", [128, 8], F32)
    v8row = S("v8row", [1, 8], F32)
    tr16 = S("tr16", [16, 128], F32)
    row2k = S("row2k", [1, 2048], F32)
    row2kb = row2k[:, :].bitcast(BF16)
    cntf = row2kb[0:1, 0:1024]
    tbf = row2k[0:1, 0:1024]
    sflat = row2k[0:1, 0:2048]
    wflat = row2k[0:1, 0:2048]
    rstd = S("rstd", [128, NT], F32)
    nmr = S("nmr", [128, NT], F32)
    bn6 = S("bn6", [128, 12], F32)
    bn2 = S("bn2", [128, 2], F32)
    lnv = S("lnv", [128, 1], F32)
    epsc = S("epsc", [128, 1], F32)
    l8 = S("l8", [128, 8], F32)
    mx8 = S("mx8", [128, 8], F32)
    ex8 = S("ex8", [128, 8], F32)
    t8a = S("t8a", [128, 8], F32)
    zs = S("zs", [128, 4], F32)
    padj = S("padj", [128, 2], F32)
    sml = S("sml", [128, 16], F32)
    padjE = S("padjE", [128, 16], F32)
    edsp = S("edsp", [128, 24], F32)
    s2cp = [row2kb[0:1, 0:1024], row2kb[0:1, 1024:2048]]
    rrowp = [row2kb[0:1, 2048:3072], row2kb[0:1, 3072:4096]]
    wgsb = S("wgsb", [128, 64], F32)
    wgtmp = S("wgtmp", [128, 8], F32)
    wgp = S("wgp", [128, 128], BF16)
    bo_bf = S("bo_bf", [1, 1024], BF16)
    be_bf = S("be_bf", [1, 1024], BF16)
    csb = {}
    for n, v in hc0.items():
        dt = BF16 if v.dtype == ml_dtypes.bfloat16 else F32
        csb[n] = S("s_" + n, list(v.shape), dt)

    psA = P("psA", [128, 1024], F32)
    psB = P("psB", [128, 1024], F32)
    psC = P("psC", [128, 512], F32)
    psD = P("psD", [128, 512], F32)
    psE = P("psE", [128, 512], F32)
    psF = P("psF", [128, 512], F32)

    xn_tm = xn_tm_t[:, :, :]
    xnfm = xnfm_ar[:, :].rearrange("p (dc t) -> p dc t", dc=8)
    attn = attn_t[:, :, :]
    gtb = attn_t[:, 0, 0:1024]
    exclv = gm_ar[:, 0:2048].bitcast(F32)
    locv = gm_ar[:, 2048:4096].bitcast(F32)
    ohf = gm_ar[:, 4096:6144].bitcast(F32)
    stores = []

    qP = pairs[:, 0:2048]
    kP = pairs[:, 2048:4096]
    vS = pairs[:, 4096:6144]
    vT = pairs[:, 6144:8224].rearrange("p (j c) -> p j c", c=130)
    wqS = [pairs[:, 8224 + i * 1024: 8224 + (i + 1) * 1024] for i in range(2)]
    wkS = [pairs[:, 10272 + i * 1024: 10272 + (i + 1) * 1024] for i in range(2)]
    wvS = [pairs[:, 12320:13344], pairs[:, 13344:14368]]

    def mm(ps, lh, rh, start=True, stop=True, tag=""):
        return tk.op("tensor",
                     lambda t, a=ps, b=lh, c=rh, s1=start, s2=stop:
                     t.matmul(a, b, c, start=s1, stop=s2),
                     reads=[lh, rh], writes=[ps], tag=tag)

    def act(out, in_, func, bias=0.0, scale=1.0, tag=""):
        reads = [in_]
        if not isinstance(bias, float):
            reads.append(bias)
        if not isinstance(scale, float):
            reads.append(scale)
        return tk.op("scalar",
                     lambda s, o=out, i=in_, f=func, b=bias, sc=scale:
                     s.activation(o, i, f, bias=b, scale=sc),
                     reads=reads, writes=[out], tag=tag)

    def vcopy(out, in_, tag=""):
        return tk.op("vector", lambda v, o=out, i=in_: v.tensor_copy(o, i),
                     reads=[in_], writes=[out], tag=tag)

    def scopy(out, in_, tag=""):
        return act(out, in_, AF.Identity, tag=tag)

    def vtt(out, a, b, op, tag=""):
        return tk.op("vector",
                     lambda v, o=out, x=a, y=b, w=op: v.tensor_tensor(o, x, y, w),
                     reads=[a, b], writes=[out], tag=tag)

    def vts(out, in_, s1, op, tag=""):
        reads = [in_] + ([s1] if not isinstance(s1, float) else [])
        return tk.op("vector",
                     lambda v, o=out, i=in_, s=s1, w=op:
                     v.tensor_single_scalar(o, i, s, w),
                     reads=reads, writes=[out], tag=tag)

    def vstt(out, in0, s1, in1, op0, op1, tag=""):
        reads = [in0, in1] + ([s1] if not isinstance(s1, float) else [])
        return tk.op("vector",
                     lambda v, o=out, a=in0, s=s1, b=in1, w0=op0, w1=op1:
                     v.scalar_tensor_tensor(o, a, s, b, w0, w1),
                     reads=reads, writes=[out], tag=tag)

    for n in cns:
        tk.dma("sync", csb[n][:, :], cns[n], tag="c_" + n)
    tk.dma("sync", wgsb[:, :].rearrange("p (dc e) -> p dc e", dc=8),
           wg.rearrange("(dc p) e -> p dc e", p=128), tag="wg")
    tk.dma("gpsimd", bo_bf[:, :], bo, tag="bo")
    tk.op("vector", lambda v: v.memset(epsc[:, :], 1e-5), writes=[epsc[:, :]])
    tk.op("vector", lambda v: v.memset(wgp[:, :], 0.0), writes=[wgp[:, :]])

    if _on("route1"):
        mm(psA[0:1, 0:64], csb["c_onescol_f"][:, :], wgsb[:, :], tag="wgsum")
        vcopy(tr16[0:1, 0:64], psA[0:1, 0:64], tag="wgsum_ev")
        tk.op("vector",
              lambda v: v.reduce_sum(
                  wgtmp[0:1, 0:8],
                  tr16[0:1, 0:64].rearrange("o (dc e) -> o e dc", e=8),
                  axis=AX.X),
              reads=[tr16[0:1, 0:64]], writes=[wgtmp[0:1, 0:8]], tag="wgred")
        vts(wgtmp[0:1, 0:8], wgtmp[0:1, 0:8], -1.0 / 1024.0, ALU.mult, tag="wgsc")
        mm(psB[:, 0:8], csb["c_ones_f"][:, :], wgtmp[0:1, 0:8], tag="wgbc")
        for dc in range(8):
            vtt(wgtmp[:, 0:8], psB[:, 0:8], wgsb[:, 8 * dc:8 * dc + 8],
                ALU.add, tag=f"wgp{dc}")
            vcopy(wgp[:, 16 * dc:16 * dc + 8], wgtmp[:, 0:8], tag=f"wgpc{dc}")

    if _on("ln"):
        for j in range(NT):
            xs = ar2[:, (j % 2) * 2048:(j % 2) * 2048 + 2048].bitcast(F32)
            tk.dma("sync", xs, x_own[128 * j:128 * j + 128, :], tag=f"xl{j}")
            tk.op("vector", lambda v, a=xs: v.bn_stats(bn6[:, 0:6], a[:, 0:512]),
                  reads=[xs[:, 0:512]], writes=[bn6[:, 0:6]], tag=f"bns0_{j}")
            tk.op("vector", lambda v, a=xs: v.bn_stats(bn6[:, 6:12], a[:, 512:1024]),
                  reads=[xs[:, 512:1024]], writes=[bn6[:, 6:12]], tag=f"bns1_{j}")
            tk.op("vector", lambda v: v.bn_aggr(bn2[:, :], bn6[:, :]),
                  reads=[bn6[:, :]], writes=[bn2[:, :]], tag=f"bna{j}")
            act(lnv[:, :], bn2[:, 1:2], AF.Ln, bias=epsc[:, :], tag=f"ln{j}")
            act(rstd[:, j:j + 1], lnv[:, :], AF.Exp, scale=-0.5, tag=f"rs{j}")
            vtt(nmr[:, j:j + 1], bn2[:, 0:1], rstd[:, j:j + 1], ALU.mult,
                tag=f"nm{j}")
            vts(nmr[:, j:j + 1], nmr[:, j:j + 1], -1.0, ALU.mult, tag=f"nmn{j}")
            act(xn_tm[:, j, :], xs, AF.Identity, bias=nmr[:, j:j + 1],
                scale=rstd[:, j:j + 1], tag=f"xn{j}")

    if _on("xnfm"):
        for j in range(NT):
            tk.dma("sync",
                   xnfm[:, :, 128 * j:128 * j + 128],
                   xn_tm[:, j, :].rearrange("p (dc q) -> p dc q", dc=8),
                   tag=f"xt{j}", transpose=True)

    if _on("route4"):
        xh2s = [ar2[:, 0:2048], ar2[:, 4096:6144]]
        xf2s = [ar2[:, 2048:4096], ar2[:, 6144:8192]]
        for gp in range(GT // 2):
            g = 2 * gp
            bidx, jj = g // NT, g % NT
            sl = gp % 2
            xh2 = xh2s[sl]
            xfv = xf2s[sl].rearrange("p (b q) -> p b q", b=16)
            tk.dma("gpsimd", xh2.rearrange("p (t d) -> p t d", t=2),
                   x_rot[bidx, 128 * jj:128 * jj + 256, :].rearrange(
                       "(t p) d -> p t d", p=128), tag=f"xh{g}")
            tk.dma("sync",
                   xfv, xh2.rearrange("p (b q) -> p b q", b=16),
                   tag=f"xf{g}", transpose=True)
            for t in range(2):
                for dc in range(8):
                    mm(psA[:, 8 * (g + t):8 * (g + t) + 8],
                       xfv[:, 8 * t + dc, :], wgp[:, 16 * dc:16 * dc + 8],
                       start=(dc == 0), stop=(dc == 7), tag=f"g{g + t}_{dc}")
        vcopy(gtb[:, 0:512], psA[:, 0:512], tag="gev0")
        vcopy(gtb[:, 512:1024], psA[:, 512:1024], tag="gev1")

    if _on("route5"):
        gt3v = gtb.rearrange("p (j e) -> p j e", e=8)
        mxt = loctok
        tk.op("vector", lambda v: v.reduce_max(mxt[:, :], gt3v, axis=AX.X),
              reads=[gtb], writes=[mxt[:, :]], tag="rmax")
        eq3 = exclv.rearrange("p (j e) -> p j e", e=8)
        tk.op("vector",
              lambda v: v.tensor_tensor(
                  eq3, gt3v, mxt[:, :].broadcast_to([128, 128, 8]),
                  ALU.is_equal),
              reads=[gtb, mxt[:, :]], writes=[exclv], tag="eq")
        iota_b = csb["c_iota8r"][:, :].__replace__(ap=[[8, 128], [0, 128], [1, 8]])
        u3 = locv.rearrange("p (j e) -> p j e", e=8)
        tk.op("vector",
              lambda v: v.tensor_tensor(u3, eq3, iota_b, ALU.mult),
              reads=[exclv, csb["c_iota8r"][:, :]], writes=[locv], tag="umul")
        tk.op("vector", lambda v: v.reduce_max(mxt[:, :], u3, axis=AX.X),
              reads=[locv], writes=[mxt[:, :]], tag="umax")
        umb = mxt[:, :].broadcast_to([128, 128, 8])
        oh3b = oh_bf[:, :].rearrange("p (j e) -> p j e", e=8)
        tk.op("vector",
              lambda v: v.tensor_tensor(oh3b, u3, umb, ALU.is_equal),
              reads=[locv, mxt[:, :]], writes=[oh_bf[:, :]], tag="ohb")
        oh3f = ohf.rearrange("p (j e) -> p j e", e=8)
        tk.op("vector",
              lambda v: v.tensor_tensor(oh3f, u3, umb, ALU.is_equal),
              reads=[locv, mxt[:, :]], writes=[ohf], tag="ohfb")
        for hhalf in range(2):
            sl = slice(512 * hhalf, 512 * hhalf + 512)
            mm(psA[:, sl], csb["c_L128"][:, :], oh_bf[:, sl], tag=f"ex{hhalf}")
            vcopy(exclv[:, sl], psA[:, sl], tag=f"exev{hhalf}")
        for hhalf in range(2):
            sl = slice(512 * hhalf, 512 * hhalf + 512)
            mm(psB[0:1, sl], csb["c_onescol_bf"][:, :], oh_bf[:, sl],
               tag=f"cnt{hhalf}")
        scopy(cntf, psB[0:1, 0:1024], tag="cntev")
        tk.dma("gpsimd", cnttm[:, :], cntf, tag="cntrs")
        mm(psB[:, 0:8], csb["c_Lt"][:, :], cnttm[:, :], tag="tbmm")
        vcopy(tbtm[:, :], psB[:, 0:8], tag="tbev")
        tk.dma("gpsimd", tbf, tbtm[:, :], tag="tbrs")
        for hhalf in range(2):
            sl = slice(512 * hhalf, 512 * hhalf + 512)
            mm(psA[:, sl], csb["c_ones_f"][:, :], tbf[0:1, sl], tag=f"tb{hhalf}")
            vstt(locv[:, sl], psA[:, sl], 1.0, exclv[:, sl], ALU.mult, ALU.add,
                 tag=f"loc{hhalf}")
        for hhalf in range(2):
            sl = slice(512 * hhalf, 512 * hhalf + 512)
            vtt(psB[:, sl], ohf[:, sl], locv[:, sl], ALU.mult, tag=f"lt{hhalf}")
        tk.op("vector",
              lambda v: v.reduce_sum(
                  loctok[:, :],
                  psB[:, 0:1024].rearrange("p (j e) -> p j e", e=8), axis=AX.X),
              reads=[psB[:, :]], writes=[loctok[:, :]], tag="ltred")

    if _on("route6"):
        vts(v8row[:, :], csb["c_iota8"][0:1, 0:8], float(SLOTS), ALU.mult,
            tag="v8m")
        vtt(v8row[:, :], v8row[:, :], tbf[0:1, 0:8], ALU.subtract, tag="v8s")
        mm(psB[:, 0:8], csb["c_ones_f"][:, :], v8row[0:1, 0:8], tag="v8bc")
        vcopy(vecb[:, :], psB[:, 0:8], tag="v8ev")
        for j in range(NT):
            gsl = gtb[:, 8 * j:8 * j + 8]
            tk.op("vector",
                  lambda v, a=gsl, r=rstd[:, j:j + 1]:
                  v.tensor_single_scalar(l8[:, :], a, r, ALU.mult),
                  reads=[gsl, rstd[:, j:j + 1]], writes=[l8[:, :]],
                  tag=f"l8{j}")
            tk.op("vector", lambda v: v.max(mx8[:, :], l8[:, :]),
                  reads=[l8[:, :]], writes=[mx8[:, :]], tag=f"lm{j}")
            vts(zs[:, 0:1], mx8[:, 0:1], -1.0, ALU.mult, tag=f"ln{j}")
            act(ex8[:, :], l8[:, :], AF.Exp, bias=zs[:, 0:1], tag=f"ge{j}")
            tk.op("vector",
                  lambda v: v.reduce_sum(zs[:, 1:2], ex8[:, :], axis=AX.X),
                  reads=[ex8[:, :]], writes=[zs[:, 1:2]], tag=f"gz{j}")
            tk.op("vector", lambda v: v.reciprocal(zs[:, 2:3], zs[:, 1:2]),
                  reads=[zs[:, 1:2]], writes=[zs[:, 2:3]], tag=f"gr{j}")
            vts(zs[:, 3:4], loctok[:, j:j + 1], float(CAP), ALU.is_lt,
                tag=f"kp{j}")
            vtt(wcol[:, j:j + 1], zs[:, 2:3], zs[:, 3:4], ALU.mult,
                tag=f"w{j}")
            vtt(t8a[:, :], locv[:, 8 * j:8 * j + 8], vecb[:, :], ALU.add,
                tag=f"sl{j}")
            vtt(t8a[:, :], t8a[:, :], ohf[:, 8 * j:8 * j + 8], ALU.mult,
                tag=f"slm{j}")
            tk.op("vector",
                  lambda v, o=scol[:, j:j + 1]: v.reduce_sum(o, t8a[:, :],
                                                             axis=AX.X),
                  reads=[t8a[:, :]], writes=[scol[:, j:j + 1]], tag=f"sr{j}")

    if _on("route7"):
        for (colsrc, flat, dstb, dt_out, tag) in (
                (scol, sflat, slotb, F32, "s"), (wcol, wflat, wb, BF16, "w")):
            tk.op("tensor",
                  lambda t, a=colsrc: t.transpose(psA[0:16, 0:128], a[:, :],
                                                  csb["c_ident"][:, :]),
                  reads=[colsrc[:, :], csb["c_ident"][:, :]],
                  writes=[psA[:, :]], tag=f"{tag}tr")
            vcopy(tr16[:, :], psA[0:16, 0:128], tag=f"{tag}trev")
            tk.dma("gpsimd", flat[0:1, 0:2048], tr16[:, :], tag=f"{tag}rs")
            for q in range(4):
                sl = slice(512 * q, 512 * q + 512)
                mm(psB[:, 0:512] if q % 2 == 0 else psB[:, 512:1024],
                   csb["c_ones_f"][:, :], flat[0:1, sl], tag=f"{tag}bc{q}")
                src = psB[:, 0:512] if q % 2 == 0 else psB[:, 512:1024]
                vcopy(dstb[:, sl], src, tag=f"{tag}be{q}")

    if _on("attn"):
        WqR = Wq.rearrange("(dc p) i -> p dc i", p=128)
        WkvR = Wkv.rearrange("(dc p) i -> p dc i", p=128)
        for pr in range(8):
            sl = pr % 2
            tk.dma("gpsimd", wqS[sl].rearrange("p (dc q) -> p dc q", dc=8),
                   WqR[:, :, 128 * pr:128 * pr + 128], tag=f"wq{pr}")
            tk.dma("gpsimd", wkS[sl].rearrange("p (dc q) -> p dc q", dc=8),
                   WkvR[:, :, 128 * pr:128 * pr + 128], tag=f"wk{pr}")
            tk.dma("gpsimd", wvS[sl].rearrange("p (dc q) -> p dc q", dc=8),
                   WkvR[:, :, 1024 + 128 * pr:1024 + 128 * pr + 128],
                   tag=f"wv{pr}")
            wqv = wqS[sl].rearrange("p (dc q) -> p dc q", dc=8)
            wkv = wkS[sl].rearrange("p (dc q) -> p dc q", dc=8)
            wvv = wvS[sl].rearrange("p (dc q) -> p dc q", dc=8)
            for (wsl, dst, tg) in ((wqv, qP, "q"), (wkv, kP, "k"),
                                   (wvv, vS, "v")):
                for nck in range(4):
                    ps = psB[:, 0:512] if nck % 2 == 0 else psB[:, 512:1024]
                    csl = slice(512 * nck, 512 * nck + 512)
                    for dc in range(8):
                        mm(ps, wsl[:, dc, :], xnfm[:, dc, csl],
                           start=(dc == 0), stop=(dc == 7),
                           tag=f"{tg}p{pr}_{nck}_{dc}")
                    vcopy(dst[:, csl], ps, tag=f"{tg}e{pr}_{nck}")
            vSv = vS.rearrange("p (j q) -> p j q", j=NT)
            for h01t in range(2):
                slot = (2 * pr + h01t) % 4
                scrv = gm_ar[:, 1024 * slot: 1024 * slot + 1024].rearrange(
                    "p (j c) -> p j c", c=64)
                tk.dma("sync", scrv, vSv[64 * h01t:64 * h01t + 64, :, :],
                       tag=f"vt{h01t}_{pr}", transpose=True)
                vcopy(vT[:, :, 65 * h01t:65 * h01t + 64], scrv,
                      tag=f"vc{h01t}_{pr}")
            if pr == 0:
                tk.op("vector", lambda v: v.memset(vT[:, :, 64:65], 1.0),
                      writes=[vT[:, :, 64:65]], tag="vo0")
                tk.op("vector", lambda v: v.memset(vT[:, :, 129:130], 1.0),
                      writes=[vT[:, :, 129:130]], tag="vo1")
            pending_div = None
            for c in range(4):
                nj = 4 * c + 4

                def emit_n(j, c=c, nj=nj):
                    voff = max(0, 128 * j - 512 * c)
                    for h01 in range(2):
                        nps = psE if h01 == 0 else psF
                        pt = pt_ar[:, 512 * ((2 * j + h01) % 6):
                                   512 * ((2 * j + h01) % 6) + 512]
                        mm(nps[0:65, voff:512],
                           vT[:, j, 65 * h01:65 * h01 + 65],
                           pt[:, voff:512],
                           start=(j == 0), stop=(j == nj - 1),
                           tag=f"n{pr}_{h01}_{c}_{j}")

                for j in range(nj):
                    voff = max(0, 128 * j - 512 * c)
                    diag = j >= 4 * c
                    for h01 in range(2):
                        hb = 64 * h01
                        sps = psC if h01 == 0 else psD
                        mm(sps[:, voff:512],
                           kP[hb:hb + 64, 128 * j:128 * j + 128],
                           qP[hb:hb + 64, 512 * c + voff:512 * c + 512],
                           start=True, stop=not diag,
                           tag=f"s{pr}_{h01}_{c}_{j}")
                        if diag:
                            mm(sps[:, voff:voff + 128], csb["c_L128"][:, :],
                               csb["c_negid"][:, :], start=False, stop=True,
                               tag=f"m{pr}_{h01}_{c}_{j}")
                        pt = pt_ar[:, 512 * ((2 * j + h01) % 6):
                                   512 * ((2 * j + h01) % 6) + 512]
                        act(pt[:, voff:512], sps[:, voff:512], AF.Exp,
                            scale=0.125, tag=f"e{pr}_{h01}_{c}_{j}")
                    if j > 0:
                        emit_n(j - 1)
                emit_n(nj - 1)
                par = c % 2
                vcopy(attn[0:64, pr, 512 * c:512 * c + 512], psE[0:64, :],
                      tag=f"ae{pr}_{c}")
                vcopy(attn[64:128, pr, 512 * c:512 * c + 512], psF[0:64, :],
                      tag=f"ao{pr}_{c}")
                vcopy(s2cp[par][0:1, 0:512], psE[64:65, :], tag=f"se{pr}_{c}")
                vcopy(s2cp[par][0:1, 512:1024], psF[64:65, :],
                      tag=f"so{pr}_{c}")

                def mkdiv(pr=pr, c=c, par=par):
                    def emit():
                        rr = rrowp[par]
                        lsc = psA[0:1, 0:1024]
                        act(lsc, s2cp[par][0:1, :], AF.Ln, tag=f"rl{pr}_{c}")
                        act(rr[0:1, :], lsc, AF.Exp, scale=-1.0,
                            tag=f"rx{pr}_{c}")
                        mm(psA[:, 0:512], csb["c_half0"][:, :],
                           rr[0:1, 0:512], start=True, stop=False,
                           tag=f"rb0{pr}_{c}")
                        mm(psA[:, 0:512], csb["c_half1"][:, :],
                           rr[0:1, 512:1024], start=False, stop=True,
                           tag=f"rb1{pr}_{c}")
                        rsb = oh_bf[:, 512 * par:512 * par + 512]
                        vcopy(rsb, psA[:, 0:512], tag=f"re{pr}_{c}")
                        vtt(attn[:, pr, 512 * c:512 * c + 512],
                            attn[:, pr, 512 * c:512 * c + 512], rsb, ALU.mult,
                            tag=f"dv{pr}_{c}")
                    return emit

                if pending_div is not None:
                    pending_div()
                pending_div = mkdiv()
            pending_div()

    Y = []
    for e in range(NE):
        if e <= 4:
            Y.append(xnfm_ar[:, 3072 * e:3072 * e + 3072])
        elif e == 5:
            Y.append(pt_ar[:, 0:3072])
        else:
            Y.append(pairs[:, 3072 * (e - 6):3072 * (e - 6) + 3072])
    Yv = [y.rearrange("p (sc f) -> p sc f", sc=3) for y in Y]
    wstage = ar2[:, 0:8192].rearrange("p (fc h) -> p fc h", fc=8)
    wt = wt_ar[:, :].rearrange("p (hc f) -> p hc f", hc=8)

    if _on("ffn"):
        for e in range(NE):
            tk.dma("gpsimd", wstage,
                   Wf[e, :, :].rearrange("(fc p) h -> p fc h", p=128),
                   tag=f"wf{e}")
            for fc in range(8):
                tk.dma("sync",
                       wt[:, :, 128 * fc:128 * fc + 128],
                       wstage[:, fc, :],
                       tag=f"wt{e}_{fc}", transpose=True)
            tk.dma("gpsimd", be_bf[0:1, 0:1024], bfe[e:e + 1, :], tag=f"be{e}")
            eg, eo = e // 2, e % 2
            if e == 0:
                vtt(sml[:, :], scol[:, :], csb["c_wlo"][:, :], ALU.subtract,
                    tag="sml")
            vts(padjE[:, :], sml[:, :], -float(SLOTS * e), ALU.add,
                tag=f"pa{e}")
            P0 = 3984 * eo
            pt0 = pairs[:, P0:P0 + 384]
            ptr = pairs[:, P0 + 384: P0 + 3984].rearrange(
                "p (j w) -> p j w", w=GW)
            vts(pt0, csb["c_iota"][:, 0:384], padjE[:, 0:1], ALU.is_equal,
                tag=f"p0{e}")
            iota_w = csb["c_iota"][:, 0:GW].__replace__(
                ap=[[384, 128], [0, 15], [1, GW]])
            tk.op("vector",
                  lambda v, o=ptr, i=iota_w, b=padjE:
                  v.tensor_tensor(o, i,
                                  b[:, 1:16].broadcast_to([128, 15, GW]),
                                  ALU.is_equal),
                  reads=[csb["c_iota"][:, 0:GW], padjE[:, :]],
                  writes=[pairs[:, P0 + 384: P0 + 3984]],
                  tag=f"pc{e}")
            gsp = psC if eo == 0 else psD
            gdst = pairs[:, 7968 + 3072 * eo: 7968 + 3072 * eo + 3072]
            gv = gdst.rearrange("p (hc s) -> p hc s", hc=8)
            for hcx in range(8):
                for j in range(NT):
                    if j == 0:
                        wlo, W, ptile = 0, 384, pt0
                    else:
                        wlo, W = gather_window(j)
                        ptile = ptr[:, j - 1, :]
                    mm(gsp[:, wlo:wlo + W],
                       xn_tm[:, j, 128 * hcx:128 * hcx + 128], ptile,
                       start=(j == 0), stop=(j == NT - 1),
                       tag=f"gm{e}_{hcx}_{j}")
                vcopy(gv[:, hcx, 0:SLOTS], gsp[:, 0:SLOTS], tag=f"ge{e}_{hcx}")
            for sc in range(3):
                for fh in range(2):
                    yp = (psE if fh == 0 else psF)[:, :]
                    for hcx in range(8):
                        mm(yp, gv[:, hcx, 128 * sc:128 * sc + 128],
                           wt[:, hcx, 512 * fh:512 * fh + 512],
                           start=(hcx == 0), stop=False,
                           tag=f"ym{e}_{sc}_{fh}_{hcx}")
                    mm(yp, csb["c_ones_bf"][:, :],
                       be_bf[0:1, 512 * fh:512 * fh + 512],
                       start=False, stop=True, tag=f"yb{e}_{sc}_{fh}")
                    scopy(Yv[e][:, sc, 512 * fh:512 * fh + 512], yp,
                          tag=f"ye{e}_{sc}_{fh}")

    if _on("tail"):
        tk.dma("gpsimd", wt_ar[:, :].rearrange("p (ic d) -> p ic d", ic=8),
               Wo.rearrange("(ic p) d -> p ic d", p=128), tag="wo")
        wov = wt_ar[:, :].rearrange("p (ic d) -> p ic d", ic=8)
        for tc in range(NT):
            sl = tc % 2
            xr = ar2[:, 2048 * sl:2048 * sl + 2048].bitcast(F32)
            tk.dma("sync", xr, x_own[128 * tc:128 * tc + 128, :], tag=f"xr{tc}")
            if tc == 0:
                vts(edsp[:, :], csb["c_eds"][:, :], csb["c_piota"][:, :],
                    ALU.add, tag="edsp")
            nchunk = len(scatter_chunks(tc))
            K = NE * nchunk
            rt4 = gm_ar[:, 3072 * sl: 3072 * sl + 128 * K].rearrange(
                "p (e d t) -> p e d t", d=nchunk, t=128)
            sb = slotb[:, 128 * tc:128 * tc + 128]
            slot_b = sb.__replace__(
                ap=[[2048, 128], [0, NE], [0, nchunk], [1, 128]],
                offset=sb.offset)
            ev4 = edsp[:, :].rearrange("p (e d) -> p e d", d=3)[:, :, 0:nchunk]
            tk.op("vector",
                  lambda v, o=rt4, i=slot_b, b=ev4, nch=nchunk:
                  v.tensor_tensor(o, i,
                                  b.broadcast_to([128, NE, nch, 128]),
                                  ALU.is_equal),
                  reads=[sb, edsp[:, :]],
                  writes=[gm_ar[:, 3072 * sl: 3072 * sl + 128 * K]],
                  tag=f"rc{tc}")
            wbsl = wb[:, 128 * tc:128 * tc + 128]
            wb_b = wbsl.__replace__(
                ap=[[2048, 128], [0, K], [1, 128]], offset=wbsl.offset)
            rt3 = gm_ar[:, 3072 * sl: 3072 * sl + 128 * K].rearrange(
                "p (k t) -> p k t", t=128)
            tk.op("vector",
                  lambda v, o=rt3, b=wb_b:
                  v.tensor_tensor(o, o, b, ALU.mult),
                  reads=[wbsl, gm_ar[:, 3072 * sl: 3072 * sl + 128 * K]],
                  writes=[gm_ar[:, 3072 * sl: 3072 * sl + 128 * K]],
                  tag=f"rw{tc}")
            rws = []
            k = 0
            for e in range(NE):
                for sc in scatter_chunks(tc):
                    rt = gm_ar[:, 3072 * sl + 128 * k: 3072 * sl + 128 * k + 128]
                    k += 1
                    rws.append((e, sc, rt))
            ps = psA if sl == 0 else psB
            for dh in range(2):
                psd = ps[:, 512 * dh:512 * dh + 512]
                for ic in range(8):
                    mm(psd, attn[:, ic, 128 * tc:128 * tc + 128],
                       wov[:, ic, 512 * dh:512 * dh + 512],
                       start=(ic == 0), stop=False, tag=f"pj{tc}_{dh}_{ic}")
                mm(psd, csb["c_ones_bf"][:, :],
                   bo_bf[0:1, 512 * dh:512 * dh + 512],
                   start=False, stop=False, tag=f"pb{tc}_{dh}")
                for i, (e, sc, rt) in enumerate(rws):
                    mm(psd, rt, Yv[e][:, sc, 512 * dh:512 * dh + 512],
                       start=False, stop=(i == len(rws) - 1),
                       tag=f"sc{tc}_{dh}_{e}_{sc}")
                vstt(xr[:, 512 * dh:512 * dh + 512], psd, 1.0,
                     xr[:, 512 * dh:512 * dh + 512], ALU.mult, ALU.add,
                     tag=f"oev{tc}_{dh}")
            stores.append(tk.dma("sync", out_t[128 * tc:128 * tc + 128, :],
                                 xr, tag=f"st{tc}"))

    if DUMPS:
        def dump(nm, src, shape, dt):
            t = nc.dram_tensor(nm, shape, dt, kind="ExternalOutput").ap()
            stores.append(tk.dma("gpsimd", t, src, tag="d_" + nm))
        dump("d_xn", xn_tm_t[:, :, :], [128, NT, 1024], BF16)
        dump("d_attn", attn_t[:, :, :], [128, 8, 2048], BF16)
        dump("d_slotb", slotb[:, :], [128, 2048], F32)
        dump("d_wb", wb[:, :], [128, 2048], BF16)
        dump("d_loctok", loctok[:, :], [128, 128], F32)
        dump("d_scol", scol[:, :], [128, 16], F32)
        dump("d_wcol", wcol[:, :], [128, 16], F32)
        dump("d_xnfm", xnfm_ar[:, :], [128, 16384], BF16)
        dump("d_ptar", pt_ar[:, :], [128, 3072], BF16)
        dump("d_pairs", pairs[:, 0:13344], [128, 13344], BF16)
        dump("d_rstd", rstd[:, :], [128, NT], F32)

    tk.build(final_waits=stores, final_engine="sync")
    return nc, tk


_CACHE = {}


def _core_inputs(core, inputs):
    x = np.ascontiguousarray(inputs["L0_attn_x"])
    d = {
        "x_own": x[core],
        "x_rot": np.roll(x, -core, axis=0),
        "Wq": inputs["L0_attn_Wq"],
        "Wkv": inputs["L0_attn_Wkv"],
        "Wo": inputs["L0_attn_Wo"],
        "bo": np.ascontiguousarray(inputs["L0_attn_bo"]).reshape(1, D),
        "wg": inputs["L0_ffn_wg"],
        "Wf": inputs["L0_ffn_W"],
        "bfe": inputs["L0_ffn_b"],
    }
    d.update(host_consts(core))
    return {k: np.ascontiguousarray(v) for k, v in d.items()}


def kernel(**inputs):
    if "nc" not in _CACHE:
        _CACHE["nc"] = build_program()[0]
    nc = _CACHE["nc"]
    in_maps = [_core_inputs(c, inputs) for c in range(B)]
    res = run_bass_kernel_spmd(nc, in_maps, core_ids=list(range(B)))
    return np.stack([np.asarray(res.results[c]["out"], dtype=np.float32)
                     for c in range(B)], axis=0)


if __name__ == "__main__":
    nc, tk = build_program()
    ni = sum(len(s) for s in tk.streams.values())
    print(f"ops={ni} edges={tk.n_edges} waits={tk.n_waits}")


# version 06
# speedup vs baseline: 1.6053x; 1.1043x over previous
import contextlib
import numpy as np
import ml_dtypes
import concourse.bass as bass
import concourse.mybir as mybir
from concourse.bass_utils import run_bass_kernel_spmd

F32 = mybir.dt.float32
BF16 = mybir.dt.bfloat16
U32 = mybir.dt.uint32
AF = mybir.ActivationFunctionType
ALU = mybir.AluOpType
AX = mybir.AxisListType

B, N, D = 8, 2048, 1024
NH, DH = 16, 64
NE, CAP = 8, 2048
NT = N // 128
GT = B * NT
SLOTS = 384
GM = 96
SM = 128

ENGINES = ("tensor", "vector", "scalar", "gpsimd", "sync")
ESEM = {e: "sem_" + e for e in ENGINES}
_DT_SIZE = {"float32": 4, "float32r": 4, "bfloat16": 2, "float16": 2,
            "uint32": 4, "int32": 4, "uint16": 2, "int16": 2, "uint8": 1}


def _dt_size(dt):
    n = getattr(dt, "name", None) or str(dt).split(".")[-1]
    return _DT_SIZE.get(n, 4)


def _as_ap(x):
    return x.ap() if callable(getattr(x, "ap", None)) else x


def _ranges(x, out=None):
    if out is None:
        out = []
    if x is None or isinstance(x, (int, float, str)):
        return out
    if isinstance(x, (list, tuple)):
        for e in x:
            _ranges(e, out)
        return out
    ap = _as_ap(x)
    try:
        space = str(ap.space)
    except Exception:
        space = "?"
    name = ap.name
    su = space.upper()
    if "SB" not in su and "PSUM" not in su:
        out.append((name, 0, 1 << 20, 0, 1 << 60))
        return out
    try:
        pat = ap.ap
        offset = ap.offset
        esz = _dt_size(ap.dtype)
        pstride, pcount = pat[0]
        plo = (offset // pstride) if pstride else 0
        phi = plo + pcount
        row_off = offset - plo * pstride
        span = 1
        for step, count in pat[1:]:
            span += step * (count - 1)
        out.append((name, plo, phi, row_off * esz, (row_off + span) * esz))
    except Exception:
        out.append((name, 0, 1 << 20, 0, 1 << 60))
    return out


def _is_sbuf(x):
    try:
        return "SB" in str(_as_ap(x).space).upper()
    except Exception:
        return False


class Op:
    __slots__ = ("engine", "fn", "idx", "deps", "needs_inc", "fixed_inc",
                 "sem_val", "tag")

    def __init__(self, engine, fn, idx, tag=""):
        self.engine = engine
        self.fn = fn
        self.idx = idx
        self.deps = []
        self.needs_inc = False
        self.fixed_inc = None
        self.sem_val = None
        self.tag = tag


SERIALIZE = False


class TK:
    def __init__(self, nc, dma_queues=("gpsimd", "sync", "scalar")):
        self.nc = nc
        self.all_ops = []
        self.ctx = contextlib.ExitStack()
        self.streams = {e: [] for e in ENGINES}
        self.sem = {}
        self.cnt = {}
        self.access = {}
        self.n_edges = 0
        for e in ENGINES:
            self.semaphore(ESEM[e])
        self.dma_ring = {}
        self.dma_ring_pos = {}
        for q in dma_queues:
            depth = 6 if q != "scalar" else 2
            ring = []
            for i in range(depth):
                s = f"dma_{q}_{i}"
                self.semaphore(s)
                ring.append(s)
            self.dma_ring[q] = ring
            self.dma_ring_pos[q] = 0
        self.pending_T = []
        self.pending_P = []

    def semaphore(self, name):
        h = self.ctx.enter_context(self.nc.semaphore(name))
        self.sem[name] = h
        self.cnt[name] = 0
        return h

    def sbuf(self, name, shape, dtype):
        return self.ctx.enter_context(self.nc.sbuf_tensor(name, shape, dtype))

    def psum(self, name, shape, dtype):
        return self.ctx.enter_context(self.nc.psum_tensor(name, shape, dtype))

    def _add_access(self, op, kind, rects):
        for rect in rects:
            name, plo, phi, blo, bhi = rect
            hist = self.access.setdefault(name, [])
            for (p, k, r) in hist:
                if p is op:
                    continue
                if kind == "R" and k == "R":
                    continue
                if not (plo < r[2] and r[1] < phi and blo < r[4] and r[3] < bhi):
                    continue
                if p.engine == op.engine == "tensor":
                    continue
                op.deps.append(p)
                self.n_edges += 1
            hist.append((op, kind, rect))

    def op(self, eng, fn, reads=(), writes=(), tag=""):
        o = Op(eng, fn, len(self.streams[eng]), tag)
        self._add_access(o, "R", _ranges(list(reads)))
        self._add_access(o, "W", _ranges(list(writes)))
        if SERIALIZE and self.all_ops:
            o.deps.append(self.all_ops[-1])
        self.all_ops.append(o)
        self.streams[eng].append(o)
        return o

    def _dma_slot(self, queue, n_inc):
        ring = self.dma_ring[queue]
        i = self.dma_ring_pos[queue]
        self.dma_ring_pos[queue] = (i + 1) % len(ring)
        sname = ring[i]
        pre = self.cnt[sname]
        self.cnt[sname] += n_inc
        return sname, pre, self.cnt[sname]

    def dma(self, queue, out, in_, tag="", transpose=False):
        sname, pre, post = self._dma_slot(queue, 16)
        o = Op(queue, None, len(self.streams[queue]), tag)
        o.fixed_inc = (sname, 16, post)
        h = self.sem[sname]

        def fn(e, out=out, in_=in_, h=h, pre=pre, transpose=transpose):
            if pre:
                e.wait_ge(h, pre)
            if transpose:
                return e.dma_start_transpose(out, in_).then_inc(h, 16)
            return e.dma_start(out=out, in_=in_).then_inc(h, 16)
        o.fn = fn
        self._add_access(o, "R", _ranges(in_))
        self._add_access(o, "W", _ranges(out))
        if SERIALIZE and self.all_ops:
            o.deps.append(self.all_ops[-1])
        self.all_ops.append(o)
        if _is_sbuf(in_) and _is_sbuf(out):
            if transpose:
                o.deps.extend(self.pending_P)
                self.n_edges += len(self.pending_P)
                self.pending_P = []
                self.pending_T.append(o)
            else:
                o.deps.extend(self.pending_T)
                self.n_edges += len(self.pending_T)
                self.pending_T = []
                self.pending_P.append(o)
        self.streams[queue].append(o)
        return o

    def build(self, final_waits=(), final_engine="sync"):
        nc = self.nc
        for e in ENGINES:
            for o in self.streams[e]:
                for p in o.deps:
                    if p.fixed_inc is None:
                        p.needs_inc = True
        for e in ENGINES:
            for o in self.streams[e]:
                if o.needs_inc and o.fixed_inc is None:
                    self.cnt[ESEM[e]] += 1
                    o.sem_val = self.cnt[ESEM[e]]

        def token(p):
            if p.fixed_inc is not None:
                return (p.fixed_inc[0], p.fixed_inc[2])
            return (ESEM[p.engine], p.sem_val)

        if final_waits:
            fo = Op(final_engine, None, len(self.streams[final_engine]), "final")
            toks = [token(p) for p in final_waits]

            def ffn(e, toks=toks):
                best = {}
                for s, v in toks:
                    best[s] = max(best.get(s, 0), v)
                last = None
                for s, v in sorted(best.items()):
                    last = e.wait_ge(self.sem[s], v)
                return last
            fo.fn = ffn
            self.streams[final_engine].append(fo)

        emit_plan = {}
        n_waits = 0
        for e in ENGINES:
            water = {}
            plan = []
            for o in self.streams[e]:
                best = {}
                for p in o.deps:
                    s, v = token(p)
                    assert v is not None, f"unresolved producer {p.tag} <- {o.tag}"
                    if best.get(s, -1) < v:
                        best[s] = v
                waits = []
                for s, v in sorted(best.items()):
                    if water.get(s, 0) < v:
                        water[s] = v
                        waits.append((self.sem[s], v))
                n_waits += len(waits)
                plan.append((waits, o))
            emit_plan[e] = plan
        self.n_waits = n_waits

        with nc.Block() as block:
            for e in ENGINES:
                plan = emit_plan[e]
                if not plan:
                    continue

                def body(eng, plan=plan, e=e):
                    sem_h = self.sem[ESEM[e]]
                    for waits, o in plan:
                        for (h, v) in waits:
                            eng.wait_ge(h, v)
                        ins = o.fn(eng)
                        if o.needs_inc and o.fixed_inc is None:
                            assert ins is not None, f"op {o.tag} returned None"
                            ins.then_inc(sem_h, 1)
                getattr(block, e)(body)
        return nc


def host_consts(core):
    bf = ml_dtypes.bfloat16
    c = {}
    c["c_iota8"] = np.tile(np.arange(8, dtype=np.float32), (128, 1))
    c["c_iota8r"] = np.tile(8.0 - np.arange(8, dtype=np.float32), (128, 1))
    c["c_iota"] = np.tile(np.arange(384, dtype=np.float32), (128, 1))
    c["c_piota"] = np.arange(128, dtype=np.float32)[:, None]
    wlo = np.minimum(np.maximum(16 * np.arange(16) - 96, 0), 144)
    c["c_wlo"] = np.tile(wlo.astype(np.float32), (128, 1))
    eds = (384 * np.arange(8)[:, None] + 128 * np.arange(3)[None, :]).reshape(-1)
    c["c_eds"] = np.tile(eds.astype(np.float32), (128, 1))
    k = np.arange(128)
    c["c_L128"] = (k[:, None] < k[None, :]).astype(bf)
    c["c_negid"] = (-1e9 * np.eye(128)).astype(bf)
    true = ((core + k // 16) % 8) * 16 + (k % 16)
    c["c_Lt"] = (true[:, None] < true[None, :]).astype(bf)
    c["c_ident"] = np.eye(128, dtype=np.float32)
    c["c_ones_bf"] = np.ones((1, 128), dtype=bf)
    c["c_ones_f"] = np.ones((1, 128), dtype=np.float32)
    c["c_onescol_f"] = np.ones((128, 1), dtype=np.float32)
    c["c_onescol_bf"] = np.ones((128, 1), dtype=bf)
    h0 = np.zeros((1, 128), dtype=bf); h0[0, :64] = 1
    h1 = np.zeros((1, 128), dtype=bf); h1[0, 64:] = 1
    c["c_half0"] = h0
    c["c_half1"] = h1
    return c


GW = 240


def gather_window(j):
    return min(max(16 * j - 96, 0), SLOTS - GW), GW


def scatter_chunks(tc):
    lo = max(0, 16 * tc - SM)
    hi = min(SLOTS, 16 * tc + 16 + SM)
    return list(range(lo // 128, (hi + 127) // 128))


PHASES = None
DUMPS = False


def _on(name):
    return PHASES is None or name in PHASES


def build_program():
    nc = bass.Bass("TRN2", target_bir_lowering=False)
    tk = TK(nc)
    S, P = tk.sbuf, tk.psum

    inp = lambda n, s, d: nc.dram_tensor(n, s, d, kind="ExternalInput").ap()
    x_own = inp("x_own", [N, D], F32)
    x_rot = inp("x_rot", [B, N, D], F32)
    Wq = inp("Wq", [D, D], F32)
    Wkv = inp("Wkv", [D, 2 * D], F32)
    Wo = inp("Wo", [D, D], F32)
    bo = inp("bo", [1, D], F32)
    wg = inp("wg", [D, NE], F32)
    Wf = inp("Wf", [NE, D, D], F32)
    bfe = inp("bfe", [NE, D], F32)
    hc0 = host_consts(0)
    cns = {n: inp(n, list(v.shape), BF16 if v.dtype == ml_dtypes.bfloat16 else F32)
           for n, v in hc0.items()}
    out_t = nc.dram_tensor("out", [N, D], F32, kind="ExternalOutput").ap()

    xn_tm_t = S("xn_tm", [128, NT, 1024], BF16)
    xnfm_ar = S("xnfm_ar", [128, 16384], BF16)
    attn_t = S("attn", [128, 8, 2048], BF16)
    pairs = S("pairs", [128, 14368], BF16)
    ar2 = S("ar2", [128, 8192], BF16)
    pt_ar = S("pt_ar", [128, 3072], BF16)
    wt_ar = S("wt_ar", [128, 8192], BF16)
    gm_ar = S("gm_ar", [128, 6144], BF16)
    oh_bf = S("oh_bf", [128, 1024], BF16)
    loctok = S("loctok", [128, 128], F32)
    slotb = S("slotb", [128, 2048], F32)
    wb = S("wb", [128, 2048], BF16)
    scol = S("scol", [128, 16], F32)
    wcol = S("wcol", [128, 16], F32)
    cnttm = S("cnttm", [128, 8], BF16)
    tbtm = S("tbtm", [128, 8], F32)
    vecb = S("vecb", [128, 8], F32)
    v8row = S("v8row", [1, 8], F32)
    tr16 = S("tr16", [16, 128], F32)
    row2k = S("row2k", [1, 2048], F32)
    row2kb = row2k[:, :].bitcast(BF16)
    cntf = row2kb[0:1, 0:1024]
    tbf = row2k[0:1, 0:1024]
    sflat = row2k[0:1, 0:2048]
    wflat = row2k[0:1, 0:2048]
    rstd = S("rstd", [128, NT], F32)
    nmr = S("nmr", [128, NT], F32)
    bn6 = S("bn6", [128, 12], F32)
    bn2 = S("bn2", [128, 2], F32)
    lnv = S("lnv", [128, 1], F32)
    epsc = S("epsc", [128, 1], F32)
    l8 = S("l8", [128, 8], F32)
    mx8 = S("mx8", [128, 8], F32)
    ex8 = S("ex8", [128, 8], F32)
    t8a = S("t8a", [128, 8], F32)
    zs = S("zs", [128, 4], F32)
    padj = S("padj", [128, 2], F32)
    sml = S("sml", [128, 16], F32)
    padjE = S("padjE", [128, 16], F32)
    edsp = S("edsp", [128, 24], F32)
    s2cp = [row2kb[0:1, 0:1024], row2kb[0:1, 1024:2048]]
    rrowp = [row2kb[0:1, 2048:3072], row2kb[0:1, 3072:4096]]
    wgsb = S("wgsb", [128, 64], F32)
    wgtmp = S("wgtmp", [128, 8], F32)
    wgp = S("wgp", [128, 128], BF16)
    bo_bf = S("bo_bf", [1, 1024], BF16)
    be_bf = S("be_bf", [1, 1024], BF16)
    csb = {}
    for n, v in hc0.items():
        dt = BF16 if v.dtype == ml_dtypes.bfloat16 else F32
        csb[n] = S("s_" + n, list(v.shape), dt)

    psA = P("psA", [128, 1024], F32)
    psB = P("psB", [128, 1024], F32)
    psC = P("psC", [128, 512], F32)
    psD = P("psD", [128, 512], F32)
    psE = P("psE", [128, 512], F32)
    psF = P("psF", [128, 512], F32)

    xn_tm = xn_tm_t[:, :, :]
    xnfm = xnfm_ar[:, :].rearrange("p (dc t) -> p dc t", dc=8)
    attn = attn_t[:, :, :]
    gtb = slotb[:, 0:512].bitcast(BF16)
    exclv = gm_ar[:, 0:2048].bitcast(F32)
    locv = gm_ar[:, 2048:4096].bitcast(F32)
    ohf = gm_ar[:, 4096:6144].bitcast(F32)
    stores = []

    qP = pairs[:, 0:2048]
    kP = pairs[:, 2048:4096]
    vS = pairs[:, 4096:6144]
    vT = pairs[:, 6144:8224].rearrange("p (j c) -> p j c", c=130)
    wqS = [pairs[:, 8224 + i * 1024: 8224 + (i + 1) * 1024] for i in range(2)]
    wkS = [pairs[:, 10272 + i * 1024: 10272 + (i + 1) * 1024] for i in range(2)]
    wvS = [pairs[:, 12320:13344], pairs[:, 13344:14368]]

    def mm(ps, lh, rh, start=True, stop=True, tag=""):
        return tk.op("tensor",
                     lambda t, a=ps, b=lh, c=rh, s1=start, s2=stop:
                     t.matmul(a, b, c, start=s1, stop=s2),
                     reads=[lh, rh], writes=[ps], tag=tag)

    def act(out, in_, func, bias=0.0, scale=1.0, tag=""):
        reads = [in_]
        if not isinstance(bias, float):
            reads.append(bias)
        if not isinstance(scale, float):
            reads.append(scale)
        return tk.op("scalar",
                     lambda s, o=out, i=in_, f=func, b=bias, sc=scale:
                     s.activation(o, i, f, bias=b, scale=sc),
                     reads=reads, writes=[out], tag=tag)

    def vcopy(out, in_, tag=""):
        return tk.op("vector", lambda v, o=out, i=in_: v.tensor_copy(o, i),
                     reads=[in_], writes=[out], tag=tag)

    def scopy(out, in_, tag=""):
        return act(out, in_, AF.Identity, tag=tag)

    def vtt(out, a, b, op, tag=""):
        return tk.op("vector",
                     lambda v, o=out, x=a, y=b, w=op: v.tensor_tensor(o, x, y, w),
                     reads=[a, b], writes=[out], tag=tag)

    def vts(out, in_, s1, op, tag=""):
        reads = [in_] + ([s1] if not isinstance(s1, float) else [])
        return tk.op("vector",
                     lambda v, o=out, i=in_, s=s1, w=op:
                     v.tensor_single_scalar(o, i, s, w),
                     reads=reads, writes=[out], tag=tag)

    def vstt(out, in0, s1, in1, op0, op1, tag=""):
        reads = [in0, in1] + ([s1] if not isinstance(s1, float) else [])
        return tk.op("vector",
                     lambda v, o=out, a=in0, s=s1, b=in1, w0=op0, w1=op1:
                     v.scalar_tensor_tensor(o, a, s, b, w0, w1),
                     reads=reads, writes=[out], tag=tag)

    for n in cns:
        tk.dma("sync", csb[n][:, :], cns[n], tag="c_" + n)
    tk.dma("sync", wgsb[:, :].rearrange("p (dc e) -> p dc e", dc=8),
           wg.rearrange("(dc p) e -> p dc e", p=128), tag="wg")
    tk.dma("gpsimd", bo_bf[:, :], bo, tag="bo")
    tk.op("vector", lambda v: v.memset(epsc[:, :], 1e-5), writes=[epsc[:, :]])
    tk.op("vector", lambda v: v.memset(wgp[:, :], 0.0), writes=[wgp[:, :]])

    if _on("route1"):
        mm(psA[0:1, 0:64], csb["c_onescol_f"][:, :], wgsb[:, :], tag="wgsum")
        vcopy(tr16[0:1, 0:64], psA[0:1, 0:64], tag="wgsum_ev")
        tk.op("vector",
              lambda v: v.reduce_sum(
                  wgtmp[0:1, 0:8],
                  tr16[0:1, 0:64].rearrange("o (dc e) -> o e dc", e=8),
                  axis=AX.X),
              reads=[tr16[0:1, 0:64]], writes=[wgtmp[0:1, 0:8]], tag="wgred")
        vts(wgtmp[0:1, 0:8], wgtmp[0:1, 0:8], -1.0 / 1024.0, ALU.mult, tag="wgsc")
        mm(psB[:, 0:8], csb["c_ones_f"][:, :], wgtmp[0:1, 0:8], tag="wgbc")
        for dc in range(8):
            vtt(wgtmp[:, 0:8], psB[:, 0:8], wgsb[:, 8 * dc:8 * dc + 8],
                ALU.add, tag=f"wgp{dc}")
            vcopy(wgp[:, 16 * dc:16 * dc + 8], wgtmp[:, 0:8], tag=f"wgpc{dc}")

    if _on("ln"):
        for j in range(NT):
            xs = ar2[:, (j % 2) * 2048:(j % 2) * 2048 + 2048].bitcast(F32)
            tk.dma("sync", xs, x_own[128 * j:128 * j + 128, :], tag=f"xl{j}")
            tk.op("vector", lambda v, a=xs: v.bn_stats(bn6[:, 0:6], a[:, 0:512]),
                  reads=[xs[:, 0:512]], writes=[bn6[:, 0:6]], tag=f"bns0_{j}")
            tk.op("vector", lambda v, a=xs: v.bn_stats(bn6[:, 6:12], a[:, 512:1024]),
                  reads=[xs[:, 512:1024]], writes=[bn6[:, 6:12]], tag=f"bns1_{j}")
            tk.op("vector", lambda v: v.bn_aggr(bn2[:, :], bn6[:, :]),
                  reads=[bn6[:, :]], writes=[bn2[:, :]], tag=f"bna{j}")
            act(lnv[:, :], bn2[:, 1:2], AF.Ln, bias=epsc[:, :], tag=f"ln{j}")
            act(rstd[:, j:j + 1], lnv[:, :], AF.Exp, scale=-0.5, tag=f"rs{j}")
            vtt(nmr[:, j:j + 1], bn2[:, 0:1], rstd[:, j:j + 1], ALU.mult,
                tag=f"nm{j}")
            vts(nmr[:, j:j + 1], nmr[:, j:j + 1], -1.0, ALU.mult, tag=f"nmn{j}")
            act(xn_tm[:, j, :], xs, AF.Identity, bias=nmr[:, j:j + 1],
                scale=rstd[:, j:j + 1], tag=f"xn{j}")

    if _on("xnfm"):
        for j in range(NT):
            tk.dma("sync",
                   xnfm[:, :, 128 * j:128 * j + 128],
                   xn_tm[:, j, :].rearrange("p (dc q) -> p dc q", dc=8),
                   tag=f"xt{j}", transpose=True)

    xh2s = [ar2[:, 0:2048], ar2[:, 4096:6144]]
    xf2s = [ar2[:, 2048:4096], ar2[:, 6144:8192]]

    def emit_route_group(gp):
        g = 2 * gp
        bidx, jj = g // NT, g % NT
        sl = gp % 2
        xh2 = xh2s[sl]
        xfv = xf2s[sl].rearrange("p (b q) -> p b q", b=16)
        tk.dma("gpsimd", xh2.rearrange("p (t d) -> p t d", t=2),
               x_rot[bidx, 128 * jj:128 * jj + 256, :].rearrange(
                   "(t p) d -> p t d", p=128), tag=f"xh{g}")
        tk.dma("sync",
               xfv, xh2.rearrange("p (b q) -> p b q", b=16),
               tag=f"xf{g}", transpose=True)
        for t in range(2):
            w = 8 * ((g + t) % 64)
            for dc in range(8):
                mm(psA[:, w:w + 8],
                   xfv[:, 8 * t + dc, :], wgp[:, 16 * dc:16 * dc + 8],
                   start=(dc == 0), stop=(dc == 7), tag=f"g{g + t}_{dc}")

    def emit_gev(half):
        vcopy(gtb[:, 512 * half:512 * half + 512], psA[:, 0:512],
              tag=f"gev{half}")

    def emit_ph5():
        gt3v = gtb.rearrange("p (j e) -> p j e", e=8)
        mxt = loctok
        tk.op("vector", lambda v: v.reduce_max(mxt[:, :], gt3v, axis=AX.X),
              reads=[gtb], writes=[mxt[:, :]], tag="rmax")
        eq3 = exclv.rearrange("p (j e) -> p j e", e=8)
        tk.op("vector",
              lambda v: v.tensor_tensor(
                  eq3, gt3v, mxt[:, :].broadcast_to([128, 128, 8]),
                  ALU.is_equal),
              reads=[gtb, mxt[:, :]], writes=[exclv], tag="eq")
        iota_b = csb["c_iota8r"][:, :].__replace__(
            ap=[[8, 128], [0, 128], [1, 8]])
        u3 = locv.rearrange("p (j e) -> p j e", e=8)
        tk.op("vector",
              lambda v: v.tensor_tensor(u3, eq3, iota_b, ALU.mult),
              reads=[exclv, csb["c_iota8r"][:, :]], writes=[locv], tag="umul")
        tk.op("vector", lambda v: v.reduce_max(mxt[:, :], u3, axis=AX.X),
              reads=[locv], writes=[mxt[:, :]], tag="umax")
        umb = mxt[:, :].broadcast_to([128, 128, 8])
        oh3b = oh_bf[:, :].rearrange("p (j e) -> p j e", e=8)
        tk.op("vector",
              lambda v: v.tensor_tensor(oh3b, u3, umb, ALU.is_equal),
              reads=[locv, mxt[:, :]], writes=[oh_bf[:, :]], tag="ohb")
        oh3f = ohf.rearrange("p (j e) -> p j e", e=8)
        tk.op("vector",
              lambda v: v.tensor_tensor(oh3f, u3, umb, ALU.is_equal),
              reads=[locv, mxt[:, :]], writes=[ohf], tag="ohfb")
        for hhalf in range(2):
            sl = slice(512 * hhalf, 512 * hhalf + 512)
            mm(psA[:, sl], csb["c_L128"][:, :], oh_bf[:, sl], tag=f"ex{hhalf}")
            vcopy(exclv[:, sl], psA[:, sl], tag=f"exev{hhalf}")
        for hhalf in range(2):
            sl = slice(512 * hhalf, 512 * hhalf + 512)
            mm(psB[0:1, sl], csb["c_onescol_bf"][:, :], oh_bf[:, sl],
               tag=f"cnt{hhalf}")
        scopy(cntf, psB[0:1, 0:1024], tag="cntev")
        tk.dma("gpsimd", cnttm[:, :], cntf, tag="cntrs")
        mm(psB[:, 0:8], csb["c_Lt"][:, :], cnttm[:, :], tag="tbmm")
        vcopy(tbtm[:, :], psB[:, 0:8], tag="tbev")
        tk.dma("gpsimd", tbf, tbtm[:, :], tag="tbrs")
        for hhalf in range(2):
            sl = slice(512 * hhalf, 512 * hhalf + 512)
            mm(psA[:, sl], csb["c_ones_f"][:, :], tbf[0:1, sl], tag=f"tb{hhalf}")
            vstt(locv[:, sl], psA[:, sl], 1.0, exclv[:, sl], ALU.mult, ALU.add,
                 tag=f"loc{hhalf}")
        for hhalf in range(2):
            sl = slice(512 * hhalf, 512 * hhalf + 512)
            vtt(psB[:, sl], ohf[:, sl], locv[:, sl], ALU.mult, tag=f"lt{hhalf}")
        tk.op("vector",
              lambda v: v.reduce_sum(
                  loctok[:, :],
                  psB[:, 0:1024].rearrange("p (j e) -> p j e", e=8), axis=AX.X),
              reads=[psB[:, :]], writes=[loctok[:, :]], tag="ltred")

    def emit_ph6():
        vts(v8row[:, :], csb["c_iota8"][0:1, 0:8], float(SLOTS), ALU.mult,
            tag="v8m")
        vtt(v8row[:, :], v8row[:, :], tbf[0:1, 0:8], ALU.subtract, tag="v8s")
        mm(psB[:, 0:8], csb["c_ones_f"][:, :], v8row[0:1, 0:8], tag="v8bc")
        vcopy(vecb[:, :], psB[:, 0:8], tag="v8ev")
        for j in range(NT):
            gsl = gtb[:, 8 * j:8 * j + 8]
            tk.op("vector",
                  lambda v, a=gsl, r=rstd[:, j:j + 1]:
                  v.tensor_single_scalar(l8[:, :], a, r, ALU.mult),
                  reads=[gsl, rstd[:, j:j + 1]], writes=[l8[:, :]],
                  tag=f"l8{j}")
            tk.op("vector", lambda v: v.max(mx8[:, :], l8[:, :]),
                  reads=[l8[:, :]], writes=[mx8[:, :]], tag=f"lm{j}")
            vts(zs[:, 0:1], mx8[:, 0:1], -1.0, ALU.mult, tag=f"ln{j}")
            act(ex8[:, :], l8[:, :], AF.Exp, bias=zs[:, 0:1], tag=f"ge{j}")
            tk.op("vector",
                  lambda v: v.reduce_sum(zs[:, 1:2], ex8[:, :], axis=AX.X),
                  reads=[ex8[:, :]], writes=[zs[:, 1:2]], tag=f"gz{j}")
            tk.op("vector", lambda v: v.reciprocal(zs[:, 2:3], zs[:, 1:2]),
                  reads=[zs[:, 1:2]], writes=[zs[:, 2:3]], tag=f"gr{j}")
            vts(zs[:, 3:4], loctok[:, j:j + 1], float(CAP), ALU.is_lt,
                tag=f"kp{j}")
            vtt(wcol[:, j:j + 1], zs[:, 2:3], zs[:, 3:4], ALU.mult,
                tag=f"w{j}")
            vtt(t8a[:, :], locv[:, 8 * j:8 * j + 8], vecb[:, :], ALU.add,
                tag=f"sl{j}")
            vtt(t8a[:, :], t8a[:, :], ohf[:, 8 * j:8 * j + 8], ALU.mult,
                tag=f"slm{j}")
            tk.op("vector",
                  lambda v, o=scol[:, j:j + 1]: v.reduce_sum(o, t8a[:, :],
                                                             axis=AX.X),
                  reads=[t8a[:, :]], writes=[scol[:, j:j + 1]], tag=f"sr{j}")

    def emit_ph7():
        for (colsrc, flat, dstb, dt_out, tag) in (
                (scol, sflat, slotb, F32, "s"), (wcol, wflat, wb, BF16, "w")):
            tk.op("tensor",
                  lambda t, a=colsrc: t.transpose(psA[0:16, 0:128], a[:, :],
                                                  csb["c_ident"][:, :]),
                  reads=[colsrc[:, :], csb["c_ident"][:, :]],
                  writes=[psA[:, 0:128]], tag=f"{tag}tr")
            vcopy(tr16[:, :], psA[0:16, 0:128], tag=f"{tag}trev")
            tk.dma("gpsimd", flat[0:1, 0:2048], tr16[:, :], tag=f"{tag}rs")
            for q in range(4):
                sl = slice(512 * q, 512 * q + 512)
                mm(psB[:, 0:512] if q % 2 == 0 else psB[:, 512:1024],
                   csb["c_ones_f"][:, :], flat[0:1, sl], tag=f"{tag}bc{q}")
                src = psB[:, 0:512] if q % 2 == 0 else psB[:, 512:1024]
                vcopy(dstb[:, sl], src, tag=f"{tag}be{q}")

    if _on("attn"):
        WqR = Wq.rearrange("(dc p) i -> p dc i", p=128)
        WkvR = Wkv.rearrange("(dc p) i -> p dc i", p=128)
        for pr in range(8):
            for gp in range(8 * pr, 8 * pr + 8):
                emit_route_group(gp)
            if pr == 3:
                emit_gev(0)
            if pr == 7:
                emit_gev(1)
                emit_ph5()
                emit_ph6()
                emit_ph7()
            sl = pr % 2
            tk.dma("gpsimd", wqS[sl].rearrange("p (dc q) -> p dc q", dc=8),
                   WqR[:, :, 128 * pr:128 * pr + 128], tag=f"wq{pr}")
            tk.dma("gpsimd", wkS[sl].rearrange("p (dc q) -> p dc q", dc=8),
                   WkvR[:, :, 128 * pr:128 * pr + 128], tag=f"wk{pr}")
            tk.dma("gpsimd", wvS[sl].rearrange("p (dc q) -> p dc q", dc=8),
                   WkvR[:, :, 1024 + 128 * pr:1024 + 128 * pr + 128],
                   tag=f"wv{pr}")
            wqv = wqS[sl].rearrange("p (dc q) -> p dc q", dc=8)
            wkv = wkS[sl].rearrange("p (dc q) -> p dc q", dc=8)
            wvv = wvS[sl].rearrange("p (dc q) -> p dc q", dc=8)
            for (wsl, dst, tg) in ((wqv, qP, "q"), (wkv, kP, "k"),
                                   (wvv, vS, "v")):
                for nck in range(4):
                    ps = psB[:, 0:512] if nck % 2 == 0 else psB[:, 512:1024]
                    csl = slice(512 * nck, 512 * nck + 512)
                    for dc in range(8):
                        mm(ps, wsl[:, dc, :], xnfm[:, dc, csl],
                           start=(dc == 0), stop=(dc == 7),
                           tag=f"{tg}p{pr}_{nck}_{dc}")
                    vcopy(dst[:, csl], ps, tag=f"{tg}e{pr}_{nck}")
            vSv = vS.rearrange("p (j q) -> p j q", j=NT)
            for h01t in range(2):
                scrv = attn_t[:, 7, 1024 * h01t: 1024 * h01t + 1024].rearrange(
                    "p (j c) -> p j c", c=64)
                tk.dma("sync", scrv, vSv[64 * h01t:64 * h01t + 64, :, :],
                       tag=f"vt{h01t}_{pr}", transpose=True)
                vcopy(vT[:, :, 65 * h01t:65 * h01t + 64], scrv,
                      tag=f"vc{h01t}_{pr}")
            if pr == 0:
                tk.op("vector", lambda v: v.memset(vT[:, :, 64:65], 1.0),
                      writes=[vT[:, :, 64:65]], tag="vo0")
                tk.op("vector", lambda v: v.memset(vT[:, :, 129:130], 1.0),
                      writes=[vT[:, :, 129:130]], tag="vo1")
            pending_div = None
            for c in range(4):
                nj = 4 * c + 4

                def emit_n(j, c=c, nj=nj):
                    voff = max(0, 128 * j - 512 * c)
                    for h01 in range(2):
                        nps = psE if h01 == 0 else psF
                        pt = pt_ar[:, 512 * ((2 * j + h01) % 6):
                                   512 * ((2 * j + h01) % 6) + 512]
                        mm(nps[0:65, voff:512],
                           vT[:, j, 65 * h01:65 * h01 + 65],
                           pt[:, voff:512],
                           start=(j == 0), stop=(j == nj - 1),
                           tag=f"n{pr}_{h01}_{c}_{j}")

                for j in range(nj):
                    voff = max(0, 128 * j - 512 * c)
                    diag = j >= 4 * c
                    for h01 in range(2):
                        hb = 64 * h01
                        sps = psC if h01 == 0 else psD
                        mm(sps[:, voff:512],
                           kP[hb:hb + 64, 128 * j:128 * j + 128],
                           qP[hb:hb + 64, 512 * c + voff:512 * c + 512],
                           start=True, stop=not diag,
                           tag=f"s{pr}_{h01}_{c}_{j}")
                        if diag:
                            mm(sps[:, voff:voff + 128], csb["c_L128"][:, :],
                               csb["c_negid"][:, :], start=False, stop=True,
                               tag=f"m{pr}_{h01}_{c}_{j}")
                        pt = pt_ar[:, 512 * ((2 * j + h01) % 6):
                                   512 * ((2 * j + h01) % 6) + 512]
                        act(pt[:, voff:512], sps[:, voff:512], AF.Exp,
                            scale=0.125, tag=f"e{pr}_{h01}_{c}_{j}")
                    if j > 0:
                        emit_n(j - 1)
                emit_n(nj - 1)
                par = c % 2
                vcopy(attn[0:64, pr, 512 * c:512 * c + 512], psE[0:64, :],
                      tag=f"ae{pr}_{c}")
                vcopy(attn[64:128, pr, 512 * c:512 * c + 512], psF[0:64, :],
                      tag=f"ao{pr}_{c}")
                vcopy(s2cp[par][0:1, 0:512], psE[64:65, :], tag=f"se{pr}_{c}")
                vcopy(s2cp[par][0:1, 512:1024], psF[64:65, :],
                      tag=f"so{pr}_{c}")

                def mkdiv(pr=pr, c=c, par=par):
                    def emit():
                        rr = rrowp[par]
                        lsc = psA[0:1, 512:1024]
                        for hf in range(2):
                            act(lsc, s2cp[par][0:1, 512 * hf:512 * hf + 512],
                                AF.Ln, tag=f"rl{pr}_{c}_{hf}")
                            act(rr[0:1, 512 * hf:512 * hf + 512], lsc,
                                AF.Exp, scale=-1.0, tag=f"rx{pr}_{c}_{hf}")
                        mm(psA[:, 512:1024], csb["c_half0"][:, :],
                           rr[0:1, 0:512], start=True, stop=False,
                           tag=f"rb0{pr}_{c}")
                        mm(psA[:, 512:1024], csb["c_half1"][:, :],
                           rr[0:1, 512:1024], start=False, stop=True,
                           tag=f"rb1{pr}_{c}")
                        rsb = oh_bf[:, 512 * par:512 * par + 512]
                        vcopy(rsb, psA[:, 512:1024], tag=f"re{pr}_{c}")
                        vtt(attn[:, pr, 512 * c:512 * c + 512],
                            attn[:, pr, 512 * c:512 * c + 512], rsb, ALU.mult,
                            tag=f"dv{pr}_{c}")
                    return emit

                if pending_div is not None:
                    pending_div()
                pending_div = mkdiv()
            pending_div()

    Y = []
    for e in range(NE):
        if e <= 4:
            Y.append(xnfm_ar[:, 3072 * e:3072 * e + 3072])
        elif e == 5:
            Y.append(pt_ar[:, 0:3072])
        else:
            Y.append(pairs[:, 3072 * (e - 6):3072 * (e - 6) + 3072])
    Yv = [y.rearrange("p (sc f) -> p sc f", sc=3) for y in Y]
    wstage = ar2[:, 0:8192].rearrange("p (fc h) -> p fc h", fc=8)
    wt = wt_ar[:, :].rearrange("p (hc f) -> p hc f", hc=8)

    if _on("ffn"):
        for e in range(NE):
            tk.dma("gpsimd", wstage,
                   Wf[e, :, :].rearrange("(fc p) h -> p fc h", p=128),
                   tag=f"wf{e}")
            for fc in range(8):
                tk.dma("sync",
                       wt[:, :, 128 * fc:128 * fc + 128],
                       wstage[:, fc, :],
                       tag=f"wt{e}_{fc}", transpose=True)
            tk.dma("gpsimd", be_bf[0:1, 0:1024], bfe[e:e + 1, :], tag=f"be{e}")
            eg, eo = e // 2, e % 2
            if e == 0:
                vtt(sml[:, :], scol[:, :], csb["c_wlo"][:, :], ALU.subtract,
                    tag="sml")
            vts(padjE[:, :], sml[:, :], -float(SLOTS * e), ALU.add,
                tag=f"pa{e}")
            P0 = 3984 * eo
            pt0 = pairs[:, P0:P0 + 384]
            ptr = pairs[:, P0 + 384: P0 + 3984].rearrange(
                "p (j w) -> p j w", w=GW)
            vts(pt0, csb["c_iota"][:, 0:384], padjE[:, 0:1], ALU.is_equal,
                tag=f"p0{e}")
            iota_w = csb["c_iota"][:, 0:GW].__replace__(
                ap=[[384, 128], [0, 15], [1, GW]])
            tk.op("vector",
                  lambda v, o=ptr, i=iota_w, b=padjE:
                  v.tensor_tensor(o, i,
                                  b[:, 1:16].broadcast_to([128, 15, GW]),
                                  ALU.is_equal),
                  reads=[csb["c_iota"][:, 0:GW], padjE[:, :]],
                  writes=[pairs[:, P0 + 384: P0 + 3984]],
                  tag=f"pc{e}")
            gsp = psC if eo == 0 else psD
            gdst = pairs[:, 7968 + 3072 * eo: 7968 + 3072 * eo + 3072]
            gv = gdst.rearrange("p (hc s) -> p hc s", hc=8)
            for hcx in range(8):
                for j in range(NT):
                    if j == 0:
                        wlo, W, ptile = 0, 384, pt0
                    else:
                        wlo, W = gather_window(j)
                        ptile = ptr[:, j - 1, :]
                    mm(gsp[:, wlo:wlo + W],
                       xn_tm[:, j, 128 * hcx:128 * hcx + 128], ptile,
                       start=(j == 0), stop=(j == NT - 1),
                       tag=f"gm{e}_{hcx}_{j}")
                vcopy(gv[:, hcx, 0:SLOTS], gsp[:, 0:SLOTS], tag=f"ge{e}_{hcx}")
            for sc in range(3):
                for fh in range(2):
                    yp = (psE if fh == 0 else psF)[:, :]
                    for hcx in range(8):
                        mm(yp, gv[:, hcx, 128 * sc:128 * sc + 128],
                           wt[:, hcx, 512 * fh:512 * fh + 512],
                           start=(hcx == 0), stop=False,
                           tag=f"ym{e}_{sc}_{fh}_{hcx}")
                    mm(yp, csb["c_ones_bf"][:, :],
                       be_bf[0:1, 512 * fh:512 * fh + 512],
                       start=False, stop=True, tag=f"yb{e}_{sc}_{fh}")
                    scopy(Yv[e][:, sc, 512 * fh:512 * fh + 512], yp,
                          tag=f"ye{e}_{sc}_{fh}")

    if _on("tail"):
        tk.dma("gpsimd", wt_ar[:, :].rearrange("p (ic d) -> p ic d", ic=8),
               Wo.rearrange("(ic p) d -> p ic d", p=128), tag="wo")
        wov = wt_ar[:, :].rearrange("p (ic d) -> p ic d", ic=8)
        for tc in range(NT):
            sl = tc % 2
            xr = ar2[:, 2048 * sl:2048 * sl + 2048].bitcast(F32)
            tk.dma("sync", xr, x_own[128 * tc:128 * tc + 128, :], tag=f"xr{tc}")
            if tc == 0:
                vts(edsp[:, :], csb["c_eds"][:, :], csb["c_piota"][:, :],
                    ALU.add, tag="edsp")
            nchunk = len(scatter_chunks(tc))
            K = NE * nchunk
            rt4 = gm_ar[:, 3072 * sl: 3072 * sl + 128 * K].rearrange(
                "p (e d t) -> p e d t", d=nchunk, t=128)
            sb = slotb[:, 128 * tc:128 * tc + 128]
            slot_b = sb.__replace__(
                ap=[[2048, 128], [0, NE], [0, nchunk], [1, 128]],
                offset=sb.offset)
            ev4 = edsp[:, :].rearrange("p (e d) -> p e d", d=3)[:, :, 0:nchunk]
            tk.op("vector",
                  lambda v, o=rt4, i=slot_b, b=ev4, nch=nchunk:
                  v.tensor_tensor(o, i,
                                  b.broadcast_to([128, NE, nch, 128]),
                                  ALU.is_equal),
                  reads=[sb, edsp[:, :]],
                  writes=[gm_ar[:, 3072 * sl: 3072 * sl + 128 * K]],
                  tag=f"rc{tc}")
            wbsl = wb[:, 128 * tc:128 * tc + 128]
            wb_b = wbsl.__replace__(
                ap=[[2048, 128], [0, K], [1, 128]], offset=wbsl.offset)
            rt3 = gm_ar[:, 3072 * sl: 3072 * sl + 128 * K].rearrange(
                "p (k t) -> p k t", t=128)
            tk.op("vector",
                  lambda v, o=rt3, b=wb_b:
                  v.tensor_tensor(o, o, b, ALU.mult),
                  reads=[wbsl, gm_ar[:, 3072 * sl: 3072 * sl + 128 * K]],
                  writes=[gm_ar[:, 3072 * sl: 3072 * sl + 128 * K]],
                  tag=f"rw{tc}")
            rws = []
            k = 0
            for e in range(NE):
                for sc in scatter_chunks(tc):
                    rt = gm_ar[:, 3072 * sl + 128 * k: 3072 * sl + 128 * k + 128]
                    k += 1
                    rws.append((e, sc, rt))
            ps = psA if sl == 0 else psB
            for dh in range(2):
                psd = ps[:, 512 * dh:512 * dh + 512]
                for ic in range(8):
                    mm(psd, attn[:, ic, 128 * tc:128 * tc + 128],
                       wov[:, ic, 512 * dh:512 * dh + 512],
                       start=(ic == 0), stop=False, tag=f"pj{tc}_{dh}_{ic}")
                mm(psd, csb["c_ones_bf"][:, :],
                   bo_bf[0:1, 512 * dh:512 * dh + 512],
                   start=False, stop=False, tag=f"pb{tc}_{dh}")
                for i, (e, sc, rt) in enumerate(rws):
                    mm(psd, rt, Yv[e][:, sc, 512 * dh:512 * dh + 512],
                       start=False, stop=(i == len(rws) - 1),
                       tag=f"sc{tc}_{dh}_{e}_{sc}")
                vstt(xr[:, 512 * dh:512 * dh + 512], psd, 1.0,
                     xr[:, 512 * dh:512 * dh + 512], ALU.mult, ALU.add,
                     tag=f"oev{tc}_{dh}")
            stores.append(tk.dma("sync", out_t[128 * tc:128 * tc + 128, :],
                                 xr, tag=f"st{tc}"))

    if DUMPS:
        def dump(nm, src, shape, dt):
            t = nc.dram_tensor(nm, shape, dt, kind="ExternalOutput").ap()
            stores.append(tk.dma("gpsimd", t, src, tag="d_" + nm))
        dump("d_xn", xn_tm_t[:, :, :], [128, NT, 1024], BF16)
        dump("d_attn", attn_t[:, :, :], [128, 8, 2048], BF16)
        dump("d_slotb", slotb[:, :], [128, 2048], F32)
        dump("d_wb", wb[:, :], [128, 2048], BF16)
        dump("d_loctok", loctok[:, :], [128, 128], F32)
        dump("d_scol", scol[:, :], [128, 16], F32)
        dump("d_wcol", wcol[:, :], [128, 16], F32)
        dump("d_xnfm", xnfm_ar[:, :], [128, 16384], BF16)
        dump("d_ptar", pt_ar[:, :], [128, 3072], BF16)
        dump("d_pairs", pairs[:, 0:13344], [128, 13344], BF16)
        dump("d_rstd", rstd[:, :], [128, NT], F32)

    tk.build(final_waits=stores, final_engine="sync")
    return nc, tk


_CACHE = {}


def _core_inputs(core, inputs):
    x = np.ascontiguousarray(inputs["L0_attn_x"])
    d = {
        "x_own": x[core],
        "x_rot": np.roll(x, -core, axis=0),
        "Wq": inputs["L0_attn_Wq"],
        "Wkv": inputs["L0_attn_Wkv"],
        "Wo": inputs["L0_attn_Wo"],
        "bo": np.ascontiguousarray(inputs["L0_attn_bo"]).reshape(1, D),
        "wg": inputs["L0_ffn_wg"],
        "Wf": inputs["L0_ffn_W"],
        "bfe": inputs["L0_ffn_b"],
    }
    d.update(host_consts(core))
    return {k: np.ascontiguousarray(v) for k, v in d.items()}


def kernel(**inputs):
    if "nc" not in _CACHE:
        _CACHE["nc"] = build_program()[0]
    nc = _CACHE["nc"]
    in_maps = [_core_inputs(c, inputs) for c in range(B)]
    res = run_bass_kernel_spmd(nc, in_maps, core_ids=list(range(B)))
    return np.stack([np.asarray(res.results[c]["out"], dtype=np.float32)
                     for c in range(B)], axis=0)


if __name__ == "__main__":
    nc, tk = build_program()
    ni = sum(len(s) for s in tk.streams.values())
    print(f"ops={ni} edges={tk.n_edges} waits={tk.n_waits}")


# version 09
# speedup vs baseline: 1.9713x; 1.2280x over previous
import contextlib
import numpy as np
import ml_dtypes
import concourse.bass as bass
import concourse.mybir as mybir
from concourse.bass_utils import run_bass_kernel_spmd

F32 = mybir.dt.float32
BF16 = mybir.dt.bfloat16
U32 = mybir.dt.uint32
AF = mybir.ActivationFunctionType
ALU = mybir.AluOpType
AX = mybir.AxisListType

B, N, D = 8, 2048, 1024
NH, DH = 16, 64
NE, CAP = 8, 2048
NT = N // 128
GT = B * NT
SLOTS = 384
GM = 96
SM = 128

ENGINES = ("tensor", "vector", "scalar", "gpsimd", "sync")
ESEM = {e: "sem_" + e for e in ENGINES}
_DT_SIZE = {"float32": 4, "float32r": 4, "bfloat16": 2, "float16": 2,
            "uint32": 4, "int32": 4, "uint16": 2, "int16": 2, "uint8": 1}


def _dt_size(dt):
    n = getattr(dt, "name", None) or str(dt).split(".")[-1]
    return _DT_SIZE.get(n, 4)


def _as_ap(x):
    return x.ap() if callable(getattr(x, "ap", None)) else x


def _ranges(x, out=None):
    if out is None:
        out = []
    if x is None or isinstance(x, (int, float, str)):
        return out
    if isinstance(x, (list, tuple)):
        for e in x:
            _ranges(e, out)
        return out
    ap = _as_ap(x)
    try:
        space = str(ap.space)
    except Exception:
        space = "?"
    name = ap.name
    su = space.upper()
    if "SB" not in su and "PSUM" not in su:
        out.append((name, 0, 1 << 20, 0, 1 << 60))
        return out
    try:
        pat = ap.ap
        offset = ap.offset
        esz = _dt_size(ap.dtype)
        pstride, pcount = pat[0]
        plo = (offset // pstride) if pstride else 0
        phi = plo + pcount
        row_off = offset - plo * pstride
        span = 1
        for step, count in pat[1:]:
            span += step * (count - 1)
        out.append((name, plo, phi, row_off * esz, (row_off + span) * esz))
    except Exception:
        out.append((name, 0, 1 << 20, 0, 1 << 60))
    return out


def _is_sbuf(x):
    try:
        return "SB" in str(_as_ap(x).space).upper()
    except Exception:
        return False


class Op:
    __slots__ = ("engine", "fn", "idx", "deps", "needs_inc", "fixed_inc",
                 "sem_val", "tag")

    def __init__(self, engine, fn, idx, tag=""):
        self.engine = engine
        self.fn = fn
        self.idx = idx
        self.deps = []
        self.needs_inc = False
        self.fixed_inc = None
        self.sem_val = None
        self.tag = tag


SERIALIZE = False


class TK:
    def __init__(self, nc, dma_queues=("gpsimd", "sync", "scalar")):
        self.nc = nc
        self.all_ops = []
        self.ctx = contextlib.ExitStack()
        self.streams = {e: [] for e in ENGINES}
        self.sem = {}
        self.cnt = {}
        self.access = {}
        self.n_edges = 0
        for e in ENGINES:
            self.semaphore(ESEM[e])
        self.dma_ring = {}
        self.dma_ring_pos = {}
        for q in dma_queues:
            depth = 6
            ring = []
            for i in range(depth):
                s = f"dma_{q}_{i}"
                self.semaphore(s)
                ring.append(s)
            self.dma_ring[q] = ring
            self.dma_ring_pos[q] = 0
        self.pending_T = []
        self.pending_P = []

    def semaphore(self, name):
        h = self.ctx.enter_context(self.nc.semaphore(name))
        self.sem[name] = h
        self.cnt[name] = 0
        return h

    def sbuf(self, name, shape, dtype):
        return self.ctx.enter_context(self.nc.sbuf_tensor(name, shape, dtype))

    def psum(self, name, shape, dtype):
        return self.ctx.enter_context(self.nc.psum_tensor(name, shape, dtype))

    def _add_access(self, op, kind, rects):
        for rect in rects:
            name, plo, phi, blo, bhi = rect
            hist = self.access.setdefault(name, [])
            for (p, k, r) in hist:
                if p is op:
                    continue
                if kind == "R" and k == "R":
                    continue
                if not (plo < r[2] and r[1] < phi and blo < r[4] and r[3] < bhi):
                    continue
                if p.engine == op.engine == "tensor":
                    continue
                op.deps.append(p)
                self.n_edges += 1
            hist.append((op, kind, rect))

    def op(self, eng, fn, reads=(), writes=(), tag=""):
        o = Op(eng, fn, len(self.streams[eng]), tag)
        self._add_access(o, "R", _ranges(list(reads)))
        self._add_access(o, "W", _ranges(list(writes)))
        if SERIALIZE and self.all_ops:
            o.deps.append(self.all_ops[-1])
        self.all_ops.append(o)
        self.streams[eng].append(o)
        return o

    def _dma_slot(self, queue, n_inc):
        ring = self.dma_ring[queue]
        i = self.dma_ring_pos[queue]
        self.dma_ring_pos[queue] = (i + 1) % len(ring)
        sname = ring[i]
        pre = self.cnt[sname]
        self.cnt[sname] += n_inc
        return sname, pre, self.cnt[sname]

    def dma(self, queue, out, in_, tag="", transpose=False):
        sname, pre, post = self._dma_slot(queue, 16)
        o = Op(queue, None, len(self.streams[queue]), tag)
        o.fixed_inc = (sname, 16, post)
        h = self.sem[sname]

        def fn(e, out=out, in_=in_, h=h, pre=pre, transpose=transpose):
            if pre:
                e.wait_ge(h, pre)
            if transpose:
                return e.dma_start_transpose(out, in_).then_inc(h, 16)
            return e.dma_start(out=out, in_=in_).then_inc(h, 16)
        o.fn = fn
        self._add_access(o, "R", _ranges(in_))
        self._add_access(o, "W", _ranges(out))
        if SERIALIZE and self.all_ops:
            o.deps.append(self.all_ops[-1])
        self.all_ops.append(o)
        if _is_sbuf(in_) and _is_sbuf(out):
            if transpose:
                o.deps.extend(self.pending_P)
                self.n_edges += len(self.pending_P)
                self.pending_P = []
                self.pending_T.append(o)
            else:
                o.deps.extend(self.pending_T)
                self.n_edges += len(self.pending_T)
                self.pending_T = []
                self.pending_P.append(o)
        self.streams[queue].append(o)
        return o

    def build(self, final_waits=(), final_engine="sync"):
        nc = self.nc
        for e in ENGINES:
            for o in self.streams[e]:
                for p in o.deps:
                    if p.fixed_inc is None:
                        p.needs_inc = True
        for e in ENGINES:
            for o in self.streams[e]:
                if o.needs_inc and o.fixed_inc is None:
                    self.cnt[ESEM[e]] += 1
                    o.sem_val = self.cnt[ESEM[e]]

        def token(p):
            if p.fixed_inc is not None:
                return (p.fixed_inc[0], p.fixed_inc[2])
            return (ESEM[p.engine], p.sem_val)

        if final_waits:
            fo = Op(final_engine, None, len(self.streams[final_engine]), "final")
            toks = [token(p) for p in final_waits]

            def ffn(e, toks=toks):
                best = {}
                for s, v in toks:
                    best[s] = max(best.get(s, 0), v)
                last = None
                for s, v in sorted(best.items()):
                    last = e.wait_ge(self.sem[s], v)
                return last
            fo.fn = ffn
            self.streams[final_engine].append(fo)

        emit_plan = {}
        n_waits = 0
        for e in ENGINES:
            water = {}
            plan = []
            for o in self.streams[e]:
                best = {}
                for p in o.deps:
                    s, v = token(p)
                    assert v is not None, f"unresolved producer {p.tag} <- {o.tag}"
                    if best.get(s, -1) < v:
                        best[s] = v
                waits = []
                for s, v in sorted(best.items()):
                    if water.get(s, 0) < v:
                        water[s] = v
                        waits.append((self.sem[s], v))
                n_waits += len(waits)
                plan.append((waits, o))
            emit_plan[e] = plan
        self.n_waits = n_waits

        with nc.Block() as block:
            for e in ENGINES:
                plan = emit_plan[e]
                if not plan:
                    continue

                def body(eng, plan=plan, e=e):
                    sem_h = self.sem[ESEM[e]]
                    for waits, o in plan:
                        for (h, v) in waits:
                            eng.wait_ge(h, v)
                        ins = o.fn(eng)
                        if o.needs_inc and o.fixed_inc is None:
                            assert ins is not None, f"op {o.tag} returned None"
                            ins.then_inc(sem_h, 1)
                getattr(block, e)(body)
        return nc


def host_consts(core):
    bf = ml_dtypes.bfloat16
    c = {}
    c["c_iota8"] = np.tile(np.arange(8, dtype=np.float32), (128, 1))
    c["c_iota8r"] = np.tile(8.0 - np.arange(8, dtype=np.float32), (128, 1))
    c["c_iota"] = np.tile(np.arange(384, dtype=np.float32), (128, 1))
    c["c_piota"] = np.arange(128, dtype=np.float32)[:, None]
    wlo = np.minimum(np.maximum(16 * np.arange(16) - 96, 0), 144)
    c["c_wlo"] = np.tile(wlo.astype(np.float32), (128, 1))
    eds = (384 * np.arange(8)[:, None] + 128 * np.arange(3)[None, :]).reshape(-1)
    c["c_eds"] = np.tile(eds.astype(np.float32), (128, 1))
    k = np.arange(128)
    c["c_L128"] = (k[:, None] < k[None, :]).astype(bf)
    c["c_negid"] = (-1e9 * np.eye(128)).astype(bf)
    true = ((core + k // 16) % 8) * 16 + (k % 16)
    c["c_Lt"] = (true[:, None] < true[None, :]).astype(bf)
    c["c_ident"] = np.eye(128, dtype=np.float32)
    c["c_ones_bf"] = np.ones((1, 128), dtype=bf)
    c["c_ones_f"] = np.ones((1, 128), dtype=np.float32)
    c["c_onescol_f"] = np.ones((128, 1), dtype=np.float32)
    c["c_onescol_bf"] = np.ones((128, 1), dtype=bf)
    h0 = np.zeros((1, 128), dtype=bf); h0[0, :64] = 1
    h1 = np.zeros((1, 128), dtype=bf); h1[0, 64:] = 1
    c["c_half0"] = h0
    c["c_half1"] = h1
    return c


GW = 240


def gather_window(j):
    return min(max(16 * j - 96, 0), SLOTS - GW), GW


def scatter_chunks(tc):
    lo = max(0, 16 * tc - SM)
    hi = min(SLOTS, 16 * tc + 16 + SM)
    return list(range(lo // 128, (hi + 127) // 128))


PHASES = None
DUMPS = False


def _on(name):
    return PHASES is None or name in PHASES


def build_program():
    nc = bass.Bass("TRN2", target_bir_lowering=False)
    tk = TK(nc)
    S, P = tk.sbuf, tk.psum

    inp = lambda n, s, d: nc.dram_tensor(n, s, d, kind="ExternalInput").ap()
    x_own = inp("x_own", [N, D], F32)
    x_rot = inp("x_rot", [B, N, D], F32)
    Wq = inp("Wq", [D, D], F32)
    Wkv = inp("Wkv", [D, 2 * D], F32)
    Wo = inp("Wo", [D, D], F32)
    bo = inp("bo", [1, D], F32)
    wg = inp("wg", [D, NE], F32)
    Wf = inp("Wf", [NE, D, D], F32)
    bfe = inp("bfe", [NE, D], F32)
    hc0 = host_consts(0)
    cns = {n: inp(n, list(v.shape), BF16 if v.dtype == ml_dtypes.bfloat16 else F32)
           for n, v in hc0.items()}
    out_t = nc.dram_tensor("out", [N, D], F32, kind="ExternalOutput").ap()

    xn_tm_t = S("xn_tm", [128, NT, 1024], BF16)
    xnfm_ar = S("xnfm_ar", [128, 16384], BF16)
    attn_t = S("attn", [128, 8, 2048], BF16)
    pairs = S("pairs", [128, 14368], BF16)
    ar2 = S("ar2", [128, 8192], BF16)
    pt_ar = S("pt_ar", [128, 3072], BF16)
    wt_ar = S("wt_ar", [128, 8192], BF16)
    gm_ar = S("gm_ar", [128, 6144], BF16)
    oh_bf = S("oh_bf", [128, 1024], BF16)
    loctok = S("loctok", [128, 128], F32)
    slotb = S("slotb", [128, 2048], F32)
    wb = S("wb", [128, 2048], BF16)
    scol = S("scol", [128, 16], F32)
    wcol = S("wcol", [128, 16], F32)
    cnttm = S("cnttm", [128, 8], BF16)
    tbtm = S("tbtm", [128, 8], F32)
    vecb = S("vecb", [128, 8], F32)
    v8row = S("v8row", [1, 8], F32)
    tr16 = S("tr16", [16, 128], F32)
    row2k = S("row2k", [1, 2048], F32)
    row2kb = row2k[:, :].bitcast(BF16)
    cntf = row2kb[0:1, 0:1024]
    tbf = row2k[0:1, 0:1024]
    sflat = row2k[0:1, 0:2048]
    wflat = row2k[0:1, 0:2048]
    rstd = S("rstd", [128, NT], F32)
    nmr = S("nmr", [128, NT], F32)
    bn6 = S("bn6", [128, 12], F32)
    bn2 = S("bn2", [128, 2], F32)
    lnv = S("lnv", [128, 1], F32)
    epsc = S("epsc", [128, 1], F32)
    l8 = S("l8", [128, 8], F32)
    mx8 = S("mx8", [128, 8], F32)
    ex8 = S("ex8", [128, 8], F32)
    t8a = S("t8a", [128, 8], F32)
    zs = S("zs", [128, 4], F32)
    padj = S("padj", [128, 2], F32)
    sml = S("sml", [128, 16], F32)
    padjE = S("padjE", [128, 16], F32)
    edsp = S("edsp", [128, 24], F32)
    s2cp = [row2kb[0:1, 0:1024], row2kb[0:1, 1024:2048]]
    rrowp = [row2kb[0:1, 2048:3072], row2kb[0:1, 3072:4096]]
    wgsb = S("wgsb", [128, 64], F32)
    wgtmp = S("wgtmp", [128, 8], F32)
    wgp = S("wgp", [128, 128], BF16)
    bo_bf = S("bo_bf", [1, 1024], BF16)
    be_bf = S("be_bf", [1, 1024], BF16)
    csb = {}
    for n, v in hc0.items():
        dt = BF16 if v.dtype == ml_dtypes.bfloat16 else F32
        csb[n] = S("s_" + n, list(v.shape), dt)

    psA = P("psA", [128, 1024], F32)
    psB = P("psB", [128, 1024], F32)
    psC = P("psC", [128, 512], F32)
    psD = P("psD", [128, 512], F32)
    psE = P("psE", [128, 512], F32)
    psF = P("psF", [128, 512], F32)

    xn_tm = xn_tm_t[:, :, :]
    xnfm = xnfm_ar[:, :].rearrange("p (dc t) -> p dc t", dc=8)
    attn = attn_t[:, :, :]
    gtb = slotb[:, 0:512].bitcast(BF16)
    exclv = gm_ar[:, 0:2048].bitcast(F32)
    locv = gm_ar[:, 2048:4096].bitcast(F32)
    ohf = gm_ar[:, 4096:6144].bitcast(F32)
    stores = []

    qP = pairs[:, 0:2048]
    kP = pairs[:, 2048:4096]
    vS = pairs[:, 4096:6144]
    vT = pairs[:, 6144:8224].rearrange("p (j c) -> p j c", c=130)
    wqS = [pairs[:, 8224 + i * 1024: 8224 + (i + 1) * 1024] for i in range(2)]
    wkS = [pairs[:, 10272 + i * 1024: 10272 + (i + 1) * 1024] for i in range(2)]
    wvS = [pairs[:, 12320:13344], pairs[:, 13344:14368]]

    def mm(ps, lh, rh, start=True, stop=True, tag=""):
        return tk.op("tensor",
                     lambda t, a=ps, b=lh, c=rh, s1=start, s2=stop:
                     t.matmul(a, b, c, start=s1, stop=s2),
                     reads=[lh, rh], writes=[ps], tag=tag)

    def act(out, in_, func, bias=0.0, scale=1.0, tag=""):
        reads = [in_]
        if not isinstance(bias, float):
            reads.append(bias)
        if not isinstance(scale, float):
            reads.append(scale)
        return tk.op("scalar",
                     lambda s, o=out, i=in_, f=func, b=bias, sc=scale:
                     s.activation(o, i, f, bias=b, scale=sc),
                     reads=reads, writes=[out], tag=tag)

    def vcopy(out, in_, tag=""):
        return tk.op("vector", lambda v, o=out, i=in_: v.tensor_copy(o, i),
                     reads=[in_], writes=[out], tag=tag)

    def scopy(out, in_, tag=""):
        return act(out, in_, AF.Identity, tag=tag)

    def vtt(out, a, b, op, tag=""):
        return tk.op("vector",
                     lambda v, o=out, x=a, y=b, w=op: v.tensor_tensor(o, x, y, w),
                     reads=[a, b], writes=[out], tag=tag)

    def vts(out, in_, s1, op, tag=""):
        reads = [in_] + ([s1] if not isinstance(s1, float) else [])
        return tk.op("vector",
                     lambda v, o=out, i=in_, s=s1, w=op:
                     v.tensor_single_scalar(o, i, s, w),
                     reads=reads, writes=[out], tag=tag)

    def vstt(out, in0, s1, in1, op0, op1, tag=""):
        reads = [in0, in1] + ([s1] if not isinstance(s1, float) else [])
        return tk.op("vector",
                     lambda v, o=out, a=in0, s=s1, b=in1, w0=op0, w1=op1:
                     v.scalar_tensor_tensor(o, a, s, b, w0, w1),
                     reads=reads, writes=[out], tag=tag)

    for n in cns:
        tk.dma("sync", csb[n][:, :], cns[n], tag="c_" + n)
    tk.dma("sync", wgsb[:, :].rearrange("p (dc e) -> p dc e", dc=8),
           wg.rearrange("(dc p) e -> p dc e", p=128), tag="wg")
    tk.dma("gpsimd", bo_bf[:, :], bo, tag="bo")
    tk.op("vector", lambda v: v.memset(epsc[:, :], 1e-5), writes=[epsc[:, :]])
    tk.op("vector", lambda v: v.memset(wgp[:, :], 0.0), writes=[wgp[:, :]])

    if _on("route1"):
        mm(psA[0:1, 0:64], csb["c_onescol_f"][:, :], wgsb[:, :], tag="wgsum")
        vcopy(tr16[0:1, 0:64], psA[0:1, 0:64], tag="wgsum_ev")
        tk.op("vector",
              lambda v: v.reduce_sum(
                  wgtmp[0:1, 0:8],
                  tr16[0:1, 0:64].rearrange("o (dc e) -> o e dc", e=8),
                  axis=AX.X),
              reads=[tr16[0:1, 0:64]], writes=[wgtmp[0:1, 0:8]], tag="wgred")
        vts(wgtmp[0:1, 0:8], wgtmp[0:1, 0:8], -1.0 / 1024.0, ALU.mult, tag="wgsc")
        mm(psB[:, 0:8], csb["c_ones_f"][:, :], wgtmp[0:1, 0:8], tag="wgbc")
        for dc in range(8):
            vtt(wgtmp[:, 0:8], psB[:, 0:8], wgsb[:, 8 * dc:8 * dc + 8],
                ALU.add, tag=f"wgp{dc}")
            vcopy(wgp[:, 16 * dc:16 * dc + 8], wgtmp[:, 0:8], tag=f"wgpc{dc}")

    if _on("ln"):
        for j in range(NT):
            xs = ar2[:, (j % 2) * 2048:(j % 2) * 2048 + 2048].bitcast(F32)
            tk.dma("sync", xs, x_own[128 * j:128 * j + 128, :], tag=f"xl{j}")
            tk.op("vector", lambda v, a=xs: v.bn_stats(bn6[:, 0:6], a[:, 0:512]),
                  reads=[xs[:, 0:512]], writes=[bn6[:, 0:6]], tag=f"bns0_{j}")
            tk.op("vector", lambda v, a=xs: v.bn_stats(bn6[:, 6:12], a[:, 512:1024]),
                  reads=[xs[:, 512:1024]], writes=[bn6[:, 6:12]], tag=f"bns1_{j}")
            tk.op("vector", lambda v: v.bn_aggr(bn2[:, :], bn6[:, :]),
                  reads=[bn6[:, :]], writes=[bn2[:, :]], tag=f"bna{j}")
            act(lnv[:, :], bn2[:, 1:2], AF.Ln, bias=epsc[:, :], tag=f"ln{j}")
            act(rstd[:, j:j + 1], lnv[:, :], AF.Exp, scale=-0.5, tag=f"rs{j}")
            vtt(nmr[:, j:j + 1], bn2[:, 0:1], rstd[:, j:j + 1], ALU.mult,
                tag=f"nm{j}")
            vts(nmr[:, j:j + 1], nmr[:, j:j + 1], -1.0, ALU.mult, tag=f"nmn{j}")
            act(xn_tm[:, j, :], xs, AF.Identity, bias=nmr[:, j:j + 1],
                scale=rstd[:, j:j + 1], tag=f"xn{j}")

    if _on("xnfm"):
        for j in range(NT):
            tk.dma("sync",
                   xnfm[:, :, 128 * j:128 * j + 128],
                   xn_tm[:, j, :].rearrange("p (dc q) -> p dc q", dc=8),
                   tag=f"xt{j}", transpose=True)

    xh2s = [ar2[:, 0:2048], ar2[:, 4096:6144]]
    xf2s = [ar2[:, 2048:4096], ar2[:, 6144:8192]]

    def emit_route_group(gp):
        g = 2 * gp
        bidx, jj = g // NT, g % NT
        sl = gp % 2
        xh2 = xh2s[sl]
        xfv = xf2s[sl].rearrange("p (b q) -> p b q", b=16)
        tk.dma("gpsimd", xh2.rearrange("p (t d) -> p t d", t=2),
               x_rot[bidx, 128 * jj:128 * jj + 256, :].rearrange(
                   "(t p) d -> p t d", p=128), tag=f"xh{g}")
        tk.dma("sync",
               xfv, xh2.rearrange("p (b q) -> p b q", b=16),
               tag=f"xf{g}", transpose=True)
        for t in range(2):
            w = 8 * ((g + t) % 64)
            for dc in range(8):
                mm(psA[:, w:w + 8],
                   xfv[:, 8 * t + dc, :], wgp[:, 16 * dc:16 * dc + 8],
                   start=(dc == 0), stop=(dc == 7), tag=f"g{g + t}_{dc}")

    def emit_gev(half):
        vcopy(gtb[:, 512 * half:512 * half + 512], psA[:, 0:512],
              tag=f"gev{half}")

    def emit_ph5():
        gt3v = gtb.rearrange("p (j e) -> p j e", e=8)
        mxt = loctok
        tk.op("vector", lambda v: v.reduce_max(mxt[:, :], gt3v, axis=AX.X),
              reads=[gtb], writes=[mxt[:, :]], tag="rmax")
        eq3 = exclv.rearrange("p (j e) -> p j e", e=8)
        tk.op("vector",
              lambda v: v.tensor_tensor(
                  eq3, gt3v, mxt[:, :].broadcast_to([128, 128, 8]),
                  ALU.is_equal),
              reads=[gtb, mxt[:, :]], writes=[exclv], tag="eq")
        iota_b = csb["c_iota8r"][:, :].__replace__(
            ap=[[8, 128], [0, 128], [1, 8]])
        u3 = locv.rearrange("p (j e) -> p j e", e=8)
        tk.op("vector",
              lambda v: v.tensor_tensor(u3, eq3, iota_b, ALU.mult),
              reads=[exclv, csb["c_iota8r"][:, :]], writes=[locv], tag="umul")
        tk.op("vector", lambda v: v.reduce_max(mxt[:, :], u3, axis=AX.X),
              reads=[locv], writes=[mxt[:, :]], tag="umax")
        umb = mxt[:, :].broadcast_to([128, 128, 8])
        oh3b = oh_bf[:, :].rearrange("p (j e) -> p j e", e=8)
        tk.op("vector",
              lambda v: v.tensor_tensor(oh3b, u3, umb, ALU.is_equal),
              reads=[locv, mxt[:, :]], writes=[oh_bf[:, :]], tag="ohb")
        oh3f = ohf.rearrange("p (j e) -> p j e", e=8)
        tk.op("vector",
              lambda v: v.tensor_tensor(oh3f, u3, umb, ALU.is_equal),
              reads=[locv, mxt[:, :]], writes=[ohf], tag="ohfb")
        for hhalf in range(2):
            sl = slice(512 * hhalf, 512 * hhalf + 512)
            mm(psA[:, sl], csb["c_L128"][:, :], oh_bf[:, sl], tag=f"ex{hhalf}")
            vcopy(exclv[:, sl], psA[:, sl], tag=f"exev{hhalf}")
        for hhalf in range(2):
            sl = slice(512 * hhalf, 512 * hhalf + 512)
            mm(psB[0:1, sl], csb["c_onescol_bf"][:, :], oh_bf[:, sl],
               tag=f"cnt{hhalf}")
        scopy(cntf, psB[0:1, 0:1024], tag="cntev")
        tk.dma("gpsimd", cnttm[:, :], cntf, tag="cntrs")
        mm(psB[:, 0:8], csb["c_Lt"][:, :], cnttm[:, :], tag="tbmm")
        vcopy(tbtm[:, :], psB[:, 0:8], tag="tbev")
        tk.dma("gpsimd", tbf, tbtm[:, :], tag="tbrs")
        for hhalf in range(2):
            sl = slice(512 * hhalf, 512 * hhalf + 512)
            mm(psA[:, sl], csb["c_ones_f"][:, :], tbf[0:1, sl], tag=f"tb{hhalf}")
            vstt(locv[:, sl], psA[:, sl], 1.0, exclv[:, sl], ALU.mult, ALU.add,
                 tag=f"loc{hhalf}")
        for hhalf in range(2):
            sl = slice(512 * hhalf, 512 * hhalf + 512)
            vtt(psB[:, sl], ohf[:, sl], locv[:, sl], ALU.mult, tag=f"lt{hhalf}")
        tk.op("vector",
              lambda v: v.reduce_sum(
                  loctok[:, :],
                  psB[:, 0:1024].rearrange("p (j e) -> p j e", e=8), axis=AX.X),
              reads=[psB[:, :]], writes=[loctok[:, :]], tag="ltred")

    def emit_ph6():
        vts(v8row[:, :], csb["c_iota8"][0:1, 0:8], float(SLOTS), ALU.mult,
            tag="v8m")
        vtt(v8row[:, :], v8row[:, :], tbf[0:1, 0:8], ALU.subtract, tag="v8s")
        mm(psB[:, 0:8], csb["c_ones_f"][:, :], v8row[0:1, 0:8], tag="v8bc")
        vcopy(vecb[:, :], psB[:, 0:8], tag="v8ev")
        for j in range(NT):
            gsl = gtb[:, 8 * j:8 * j + 8]
            tk.op("vector",
                  lambda v, a=gsl, r=rstd[:, j:j + 1]:
                  v.tensor_single_scalar(l8[:, :], a, r, ALU.mult),
                  reads=[gsl, rstd[:, j:j + 1]], writes=[l8[:, :]],
                  tag=f"l8{j}")
            tk.op("vector", lambda v: v.max(mx8[:, :], l8[:, :]),
                  reads=[l8[:, :]], writes=[mx8[:, :]], tag=f"lm{j}")
            vts(zs[:, 0:1], mx8[:, 0:1], -1.0, ALU.mult, tag=f"ln{j}")
            act(ex8[:, :], l8[:, :], AF.Exp, bias=zs[:, 0:1], tag=f"ge{j}")
            tk.op("vector",
                  lambda v: v.reduce_sum(zs[:, 1:2], ex8[:, :], axis=AX.X),
                  reads=[ex8[:, :]], writes=[zs[:, 1:2]], tag=f"gz{j}")
            tk.op("vector", lambda v: v.reciprocal(zs[:, 2:3], zs[:, 1:2]),
                  reads=[zs[:, 1:2]], writes=[zs[:, 2:3]], tag=f"gr{j}")
            vts(zs[:, 3:4], loctok[:, j:j + 1], float(CAP), ALU.is_lt,
                tag=f"kp{j}")
            vtt(wcol[:, j:j + 1], zs[:, 2:3], zs[:, 3:4], ALU.mult,
                tag=f"w{j}")
            vtt(t8a[:, :], locv[:, 8 * j:8 * j + 8], vecb[:, :], ALU.add,
                tag=f"sl{j}")
            vtt(t8a[:, :], t8a[:, :], ohf[:, 8 * j:8 * j + 8], ALU.mult,
                tag=f"slm{j}")
            tk.op("vector",
                  lambda v, o=scol[:, j:j + 1]: v.reduce_sum(o, t8a[:, :],
                                                             axis=AX.X),
                  reads=[t8a[:, :]], writes=[scol[:, j:j + 1]], tag=f"sr{j}")

    def emit_ph7():
        for (colsrc, flat, dstb, dt_out, tag) in (
                (scol, sflat, slotb, F32, "s"), (wcol, wflat, wb, BF16, "w")):
            tk.op("tensor",
                  lambda t, a=colsrc: t.transpose(psA[0:16, 0:128], a[:, :],
                                                  csb["c_ident"][:, :]),
                  reads=[colsrc[:, :], csb["c_ident"][:, :]],
                  writes=[psA[:, 0:128]], tag=f"{tag}tr")
            vcopy(tr16[:, :], psA[0:16, 0:128], tag=f"{tag}trev")
            tk.dma("gpsimd", flat[0:1, 0:2048], tr16[:, :], tag=f"{tag}rs")
            for q in range(4):
                sl = slice(512 * q, 512 * q + 512)
                mm(psB[:, 0:512] if q % 2 == 0 else psB[:, 512:1024],
                   csb["c_ones_f"][:, :], flat[0:1, sl], tag=f"{tag}bc{q}")
                src = psB[:, 0:512] if q % 2 == 0 else psB[:, 512:1024]
                vcopy(dstb[:, sl], src, tag=f"{tag}be{q}")

    if _on("attn"):
        WqR = Wq.rearrange("(dc p) i -> p dc i", p=128)
        WkvR = Wkv.rearrange("(dc p) i -> p dc i", p=128)
        for pr in range(8):
            sl = pr % 2
            tk.dma("gpsimd", wqS[sl].rearrange("p (dc q) -> p dc q", dc=8),
                   WqR[:, :, 128 * pr:128 * pr + 128], tag=f"wq{pr}")
            tk.dma("gpsimd", wkS[sl].rearrange("p (dc q) -> p dc q", dc=8),
                   WkvR[:, :, 128 * pr:128 * pr + 128], tag=f"wk{pr}")
            tk.dma("gpsimd", wvS[sl].rearrange("p (dc q) -> p dc q", dc=8),
                   WkvR[:, :, 1024 + 128 * pr:1024 + 128 * pr + 128],
                   tag=f"wv{pr}")
            wqv = wqS[sl].rearrange("p (dc q) -> p dc q", dc=8)
            wkv = wkS[sl].rearrange("p (dc q) -> p dc q", dc=8)
            wvv = wvS[sl].rearrange("p (dc q) -> p dc q", dc=8)
            for (wsl, dst, tg) in ((wqv, qP, "q"), (wkv, kP, "k"),
                                   (wvv, vS, "v")):
                for nck in range(4):
                    ps = psB[:, 0:512] if nck % 2 == 0 else psB[:, 512:1024]
                    csl = slice(512 * nck, 512 * nck + 512)
                    for dc in range(8):
                        mm(ps, wsl[:, dc, :], xnfm[:, dc, csl],
                           start=(dc == 0), stop=(dc == 7),
                           tag=f"{tg}p{pr}_{nck}_{dc}")
                    vcopy(dst[:, csl], ps, tag=f"{tg}e{pr}_{nck}")
            vSv = vS.rearrange("p (j q) -> p j q", j=NT)
            for h01t in range(2):
                scrv = attn_t[:, 7, 1024 * h01t: 1024 * h01t + 1024].rearrange(
                    "p (j c) -> p j c", c=64)
                tk.dma("sync", scrv, vSv[64 * h01t:64 * h01t + 64, :, :],
                       tag=f"vt{h01t}_{pr}", transpose=True)
                vcopy(vT[:, :, 65 * h01t:65 * h01t + 64], scrv,
                      tag=f"vc{h01t}_{pr}")
            if pr == 0:
                tk.op("vector", lambda v: v.memset(vT[:, :, 64:65], 1.0),
                      writes=[vT[:, :, 64:65]], tag="vo0")
                tk.op("vector", lambda v: v.memset(vT[:, :, 129:130], 1.0),
                      writes=[vT[:, :, 129:130]], tag="vo1")
            pending_div = None
            for c in range(4):
                nj = 4 * c + 4

                def emit_n(j, c=c, nj=nj):
                    voff = max(0, 128 * j - 512 * c)
                    for h01 in range(2):
                        nps = psE if h01 == 0 else psF
                        pt = pt_ar[:, 512 * ((2 * j + h01) % 6):
                                   512 * ((2 * j + h01) % 6) + 512]
                        mm(nps[0:65, voff:512],
                           vT[:, j, 65 * h01:65 * h01 + 65],
                           pt[:, voff:512],
                           start=(j == 0), stop=(j == nj - 1),
                           tag=f"n{pr}_{h01}_{c}_{j}")

                for j in range(nj):
                    voff = max(0, 128 * j - 512 * c)
                    diag = j >= 4 * c
                    for h01 in range(2):
                        hb = 64 * h01
                        if j % 2 == 0:
                            sps = psC if h01 == 0 else psD
                        else:
                            sps = psB[:, 0:512] if h01 == 0 else psB[:, 512:1024]
                        mm(sps[:, voff:512],
                           kP[hb:hb + 64, 128 * j:128 * j + 128],
                           qP[hb:hb + 64, 512 * c + voff:512 * c + 512],
                           start=True, stop=not diag,
                           tag=f"s{pr}_{h01}_{c}_{j}")
                        if diag:
                            mm(sps[:, voff:voff + 128], csb["c_L128"][:, :],
                               csb["c_negid"][:, :], start=False, stop=True,
                               tag=f"m{pr}_{h01}_{c}_{j}")
                        pt = pt_ar[:, 512 * ((2 * j + h01) % 6):
                                   512 * ((2 * j + h01) % 6) + 512]
                        act(pt[:, voff:512], sps[:, voff:512], AF.Exp,
                            scale=0.125, tag=f"e{pr}_{h01}_{c}_{j}")
                    if j > 0:
                        emit_n(j - 1)
                emit_n(nj - 1)
                par = c % 2
                vcopy(attn[0:64, pr, 512 * c:512 * c + 512], psE[0:64, :],
                      tag=f"ae{pr}_{c}")
                vcopy(attn[64:128, pr, 512 * c:512 * c + 512], psF[0:64, :],
                      tag=f"ao{pr}_{c}")
                vcopy(s2cp[par][0:1, 0:512], psE[64:65, :], tag=f"se{pr}_{c}")
                vcopy(s2cp[par][0:1, 512:1024], psF[64:65, :],
                      tag=f"so{pr}_{c}")

                def mkdiv(pr=pr, c=c, par=par):
                    def emit():
                        rr = rrowp[par]
                        lsc = psA[0:1, 512:1024]
                        for hf in range(2):
                            act(lsc, s2cp[par][0:1, 512 * hf:512 * hf + 512],
                                AF.Ln, tag=f"rl{pr}_{c}_{hf}")
                            act(rr[0:1, 512 * hf:512 * hf + 512], lsc,
                                AF.Exp, scale=-1.0, tag=f"rx{pr}_{c}_{hf}")
                        mm(psA[:, 512:1024], csb["c_half0"][:, :],
                           rr[0:1, 0:512], start=True, stop=False,
                           tag=f"rb0{pr}_{c}")
                        mm(psA[:, 512:1024], csb["c_half1"][:, :],
                           rr[0:1, 512:1024], start=False, stop=True,
                           tag=f"rb1{pr}_{c}")
                        rsb = oh_bf[:, 512 * par:512 * par + 512]
                        vcopy(rsb, psA[:, 512:1024], tag=f"re{pr}_{c}")
                        vtt(attn[:, pr, 512 * c:512 * c + 512],
                            attn[:, pr, 512 * c:512 * c + 512], rsb, ALU.mult,
                            tag=f"dv{pr}_{c}")
                    return emit

                if pending_div is not None:
                    pending_div()
                pending_div = mkdiv()
                for gp in range(8 * pr + 2 * c, 8 * pr + 2 * c + 2):
                    emit_route_group(gp)
                if pr == 3 and c == 3:
                    emit_gev(0)
            pending_div()
            if pr == 7:
                emit_gev(1)
                emit_ph5()
                emit_ph6()
                emit_ph7()

    Y = []
    for e in range(NE):
        if e <= 4:
            Y.append(xnfm_ar[:, 3072 * e:3072 * e + 3072])
        elif e == 5:
            Y.append(pt_ar[:, 0:3072])
        else:
            Y.append(pairs[:, 3072 * (e - 6):3072 * (e - 6) + 3072])
    Yv = [y.rearrange("p (sc f) -> p sc f", sc=3) for y in Y]
    wstage = ar2[:, 0:8192].rearrange("p (fc h) -> p fc h", fc=8)
    wt = wt_ar[:, :].rearrange("p (hc f) -> p hc f", hc=8)

    if _on("ffn"):
        for e in range(NE):
            tk.dma("gpsimd", wstage,
                   Wf[e, :, :].rearrange("(fc p) h -> p fc h", p=128),
                   tag=f"wf{e}")
            for fc in range(8):
                tk.dma("sync" if fc % 2 == 0 else "scalar",
                       wt[:, :, 128 * fc:128 * fc + 128],
                       wstage[:, fc, :],
                       tag=f"wt{e}_{fc}", transpose=True)
            tk.dma("gpsimd", be_bf[0:1, 0:1024], bfe[e:e + 1, :], tag=f"be{e}")
            eg, eo = e // 2, e % 2
            if e == 0:
                vtt(sml[:, :], scol[:, :], csb["c_wlo"][:, :], ALU.subtract,
                    tag="sml")
            vts(padjE[:, :], sml[:, :], -float(SLOTS * e), ALU.add,
                tag=f"pa{e}")
            P0 = 3984 * eo
            pt0 = pairs[:, P0:P0 + 384]
            ptr = pairs[:, P0 + 384: P0 + 3984].rearrange(
                "p (j w) -> p j w", w=GW)
            vts(pt0, csb["c_iota"][:, 0:384], padjE[:, 0:1], ALU.is_equal,
                tag=f"p0{e}")
            iota_w = csb["c_iota"][:, 0:GW].__replace__(
                ap=[[384, 128], [0, 15], [1, GW]])
            tk.op("vector",
                  lambda v, o=ptr, i=iota_w, b=padjE:
                  v.tensor_tensor(o, i,
                                  b[:, 1:16].broadcast_to([128, 15, GW]),
                                  ALU.is_equal),
                  reads=[csb["c_iota"][:, 0:GW], padjE[:, :]],
                  writes=[pairs[:, P0 + 384: P0 + 3984]],
                  tag=f"pc{e}")
            gsp = psC if eo == 0 else psD
            gdst = pairs[:, 7968 + 3072 * eo: 7968 + 3072 * eo + 3072]
            gv = gdst.rearrange("p (hc s) -> p hc s", hc=8)
            for hcx in range(8):
                for j in range(NT):
                    if j == 0:
                        wlo, W, ptile = 0, 384, pt0
                    else:
                        wlo, W = gather_window(j)
                        ptile = ptr[:, j - 1, :]
                    mm(gsp[:, wlo:wlo + W],
                       xn_tm[:, j, 128 * hcx:128 * hcx + 128], ptile,
                       start=(j == 0), stop=(j == NT - 1),
                       tag=f"gm{e}_{hcx}_{j}")
                vcopy(gv[:, hcx, 0:SLOTS], gsp[:, 0:SLOTS], tag=f"ge{e}_{hcx}")
            for sc in range(3):
                for fh in range(2):
                    yp = (psE if fh == 0 else psF)[:, :]
                    for hcx in range(8):
                        mm(yp, gv[:, hcx, 128 * sc:128 * sc + 128],
                           wt[:, hcx, 512 * fh:512 * fh + 512],
                           start=(hcx == 0), stop=False,
                           tag=f"ym{e}_{sc}_{fh}_{hcx}")
                    mm(yp, csb["c_ones_bf"][:, :],
                       be_bf[0:1, 512 * fh:512 * fh + 512],
                       start=False, stop=True, tag=f"yb{e}_{sc}_{fh}")
                    scopy(Yv[e][:, sc, 512 * fh:512 * fh + 512], yp,
                          tag=f"ye{e}_{sc}_{fh}")

    if _on("tail"):
        tk.dma("gpsimd", wt_ar[:, :].rearrange("p (ic d) -> p ic d", ic=8),
               Wo.rearrange("(ic p) d -> p ic d", p=128), tag="wo")
        wov = wt_ar[:, :].rearrange("p (ic d) -> p ic d", ic=8)
        for tc in range(NT):
            sl = tc % 2
            xr = ar2[:, 2048 * sl:2048 * sl + 2048].bitcast(F32)
            tk.dma("sync", xr, x_own[128 * tc:128 * tc + 128, :], tag=f"xr{tc}")
            if tc == 0:
                vts(edsp[:, :], csb["c_eds"][:, :], csb["c_piota"][:, :],
                    ALU.add, tag="edsp")
            nchunk = len(scatter_chunks(tc))
            K = NE * nchunk
            rt4 = gm_ar[:, 3072 * sl: 3072 * sl + 128 * K].rearrange(
                "p (e d t) -> p e d t", d=nchunk, t=128)
            sb = slotb[:, 128 * tc:128 * tc + 128]
            slot_b = sb.__replace__(
                ap=[[2048, 128], [0, NE], [0, nchunk], [1, 128]],
                offset=sb.offset)
            ev4 = edsp[:, :].rearrange("p (e d) -> p e d", d=3)[:, :, 0:nchunk]
            tk.op("vector",
                  lambda v, o=rt4, i=slot_b, b=ev4, nch=nchunk:
                  v.tensor_tensor(o, i,
                                  b.broadcast_to([128, NE, nch, 128]),
                                  ALU.is_equal),
                  reads=[sb, edsp[:, :]],
                  writes=[gm_ar[:, 3072 * sl: 3072 * sl + 128 * K]],
                  tag=f"rc{tc}")
            wbsl = wb[:, 128 * tc:128 * tc + 128]
            wb_b = wbsl.__replace__(
                ap=[[2048, 128], [0, K], [1, 128]], offset=wbsl.offset)
            rt3 = gm_ar[:, 3072 * sl: 3072 * sl + 128 * K].rearrange(
                "p (k t) -> p k t", t=128)
            tk.op("vector",
                  lambda v, o=rt3, b=wb_b:
                  v.tensor_tensor(o, o, b, ALU.mult),
                  reads=[wbsl, gm_ar[:, 3072 * sl: 3072 * sl + 128 * K]],
                  writes=[gm_ar[:, 3072 * sl: 3072 * sl + 128 * K]],
                  tag=f"rw{tc}")
            rws = []
            k = 0
            for e in range(NE):
                for sc in scatter_chunks(tc):
                    rt = gm_ar[:, 3072 * sl + 128 * k: 3072 * sl + 128 * k + 128]
                    k += 1
                    rws.append((e, sc, rt))
            ps = psA if sl == 0 else psB
            for dh in range(2):
                psd = ps[:, 512 * dh:512 * dh + 512]
                for ic in range(8):
                    mm(psd, attn[:, ic, 128 * tc:128 * tc + 128],
                       wov[:, ic, 512 * dh:512 * dh + 512],
                       start=(ic == 0), stop=False, tag=f"pj{tc}_{dh}_{ic}")
                mm(psd, csb["c_ones_bf"][:, :],
                   bo_bf[0:1, 512 * dh:512 * dh + 512],
                   start=False, stop=False, tag=f"pb{tc}_{dh}")
                for i, (e, sc, rt) in enumerate(rws):
                    mm(psd, rt, Yv[e][:, sc, 512 * dh:512 * dh + 512],
                       start=False, stop=(i == len(rws) - 1),
                       tag=f"sc{tc}_{dh}_{e}_{sc}")
                vstt(xr[:, 512 * dh:512 * dh + 512], psd, 1.0,
                     xr[:, 512 * dh:512 * dh + 512], ALU.mult, ALU.add,
                     tag=f"oev{tc}_{dh}")
            stores.append(tk.dma("sync", out_t[128 * tc:128 * tc + 128, :],
                                 xr, tag=f"st{tc}"))

    if DUMPS:
        def dump(nm, src, shape, dt):
            t = nc.dram_tensor(nm, shape, dt, kind="ExternalOutput").ap()
            stores.append(tk.dma("gpsimd", t, src, tag="d_" + nm))
        dump("d_xn", xn_tm_t[:, :, :], [128, NT, 1024], BF16)
        dump("d_attn", attn_t[:, :, :], [128, 8, 2048], BF16)
        dump("d_slotb", slotb[:, :], [128, 2048], F32)
        dump("d_wb", wb[:, :], [128, 2048], BF16)
        dump("d_loctok", loctok[:, :], [128, 128], F32)
        dump("d_scol", scol[:, :], [128, 16], F32)
        dump("d_wcol", wcol[:, :], [128, 16], F32)
        dump("d_xnfm", xnfm_ar[:, :], [128, 16384], BF16)
        dump("d_ptar", pt_ar[:, :], [128, 3072], BF16)
        dump("d_pairs", pairs[:, 0:13344], [128, 13344], BF16)
        dump("d_rstd", rstd[:, :], [128, NT], F32)

    tk.build(final_waits=stores, final_engine="sync")
    return nc, tk


_CACHE = {}


def _core_inputs(core, inputs):
    x = np.ascontiguousarray(inputs["L0_attn_x"])
    d = {
        "x_own": x[core],
        "x_rot": np.roll(x, -core, axis=0),
        "Wq": inputs["L0_attn_Wq"],
        "Wkv": inputs["L0_attn_Wkv"],
        "Wo": inputs["L0_attn_Wo"],
        "bo": np.ascontiguousarray(inputs["L0_attn_bo"]).reshape(1, D),
        "wg": inputs["L0_ffn_wg"],
        "Wf": inputs["L0_ffn_W"],
        "bfe": inputs["L0_ffn_b"],
    }
    d.update(host_consts(core))
    return {k: np.ascontiguousarray(v) for k, v in d.items()}


def kernel(**inputs):
    if "nc" not in _CACHE:
        _CACHE["nc"] = build_program()[0]
    nc = _CACHE["nc"]
    in_maps = [_core_inputs(c, inputs) for c in range(B)]
    res = run_bass_kernel_spmd(nc, in_maps, core_ids=list(range(B)))
    return np.stack([np.asarray(res.results[c]["out"], dtype=np.float32)
                     for c in range(B)], axis=0)


if __name__ == "__main__":
    nc, tk = build_program()
    ni = sum(len(s) for s in tk.streams.values())
    print(f"ops={ni} edges={tk.n_edges} waits={tk.n_waits}")


# version 10
# speedup vs baseline: 2.0197x; 1.0246x over previous
import contextlib
import numpy as np
import ml_dtypes
import concourse.bass as bass
import concourse.mybir as mybir
from concourse.bass_utils import run_bass_kernel_spmd

F32 = mybir.dt.float32
BF16 = mybir.dt.bfloat16
U32 = mybir.dt.uint32
AF = mybir.ActivationFunctionType
ALU = mybir.AluOpType
AX = mybir.AxisListType

B, N, D = 8, 2048, 1024
NH, DH = 16, 64
NE, CAP = 8, 2048
NT = N // 128
GT = B * NT
SLOTS = 384
GM = 96
SM = 128

ENGINES = ("tensor", "vector", "scalar", "gpsimd", "sync")
ESEM = {e: "sem_" + e for e in ENGINES}
_DT_SIZE = {"float32": 4, "float32r": 4, "bfloat16": 2, "float16": 2,
            "uint32": 4, "int32": 4, "uint16": 2, "int16": 2, "uint8": 1}


def _dt_size(dt):
    n = getattr(dt, "name", None) or str(dt).split(".")[-1]
    return _DT_SIZE.get(n, 4)


def _as_ap(x):
    return x.ap() if callable(getattr(x, "ap", None)) else x


def _ranges(x, out=None):
    if out is None:
        out = []
    if x is None or isinstance(x, (int, float, str)):
        return out
    if isinstance(x, (list, tuple)):
        for e in x:
            _ranges(e, out)
        return out
    ap = _as_ap(x)
    try:
        space = str(ap.space)
    except Exception:
        space = "?"
    name = ap.name
    su = space.upper()
    if "SB" not in su and "PSUM" not in su:
        out.append((name, 0, 1 << 20, 0, 1 << 60))
        return out
    try:
        pat = ap.ap
        offset = ap.offset
        esz = _dt_size(ap.dtype)
        pstride, pcount = pat[0]
        plo = (offset // pstride) if pstride else 0
        phi = plo + pcount
        row_off = offset - plo * pstride
        span = 1
        for step, count in pat[1:]:
            span += step * (count - 1)
        out.append((name, plo, phi, row_off * esz, (row_off + span) * esz))
    except Exception:
        out.append((name, 0, 1 << 20, 0, 1 << 60))
    return out


def _is_sbuf(x):
    try:
        return "SB" in str(_as_ap(x).space).upper()
    except Exception:
        return False


class Op:
    __slots__ = ("engine", "fn", "idx", "deps", "needs_inc", "fixed_inc",
                 "sem_val", "tag")

    def __init__(self, engine, fn, idx, tag=""):
        self.engine = engine
        self.fn = fn
        self.idx = idx
        self.deps = []
        self.needs_inc = False
        self.fixed_inc = None
        self.sem_val = None
        self.tag = tag


SERIALIZE = False


class TK:
    def __init__(self, nc, dma_queues=("gpsimd", "sync", "scalar")):
        self.nc = nc
        self.all_ops = []
        self.ctx = contextlib.ExitStack()
        self.streams = {e: [] for e in ENGINES}
        self.sem = {}
        self.cnt = {}
        self.access = {}
        self.n_edges = 0
        for e in ENGINES:
            self.semaphore(ESEM[e])
        self.dma_ring = {}
        self.dma_ring_pos = {}
        for q in dma_queues:
            depth = 6
            ring = []
            for i in range(depth):
                s = f"dma_{q}_{i}"
                self.semaphore(s)
                ring.append(s)
            self.dma_ring[q] = ring
            self.dma_ring_pos[q] = 0
        self.pending_T = []
        self.pending_P = []

    def semaphore(self, name):
        h = self.ctx.enter_context(self.nc.semaphore(name))
        self.sem[name] = h
        self.cnt[name] = 0
        return h

    def sbuf(self, name, shape, dtype):
        return self.ctx.enter_context(self.nc.sbuf_tensor(name, shape, dtype))

    def psum(self, name, shape, dtype):
        return self.ctx.enter_context(self.nc.psum_tensor(name, shape, dtype))

    def _add_access(self, op, kind, rects):
        for rect in rects:
            name, plo, phi, blo, bhi = rect
            hist = self.access.setdefault(name, [])
            for (p, k, r) in hist:
                if p is op:
                    continue
                if kind == "R" and k == "R":
                    continue
                if not (plo < r[2] and r[1] < phi and blo < r[4] and r[3] < bhi):
                    continue
                if p.engine == op.engine == "tensor":
                    continue
                op.deps.append(p)
                self.n_edges += 1
            hist.append((op, kind, rect))

    def op(self, eng, fn, reads=(), writes=(), tag=""):
        o = Op(eng, fn, len(self.streams[eng]), tag)
        self._add_access(o, "R", _ranges(list(reads)))
        self._add_access(o, "W", _ranges(list(writes)))
        if SERIALIZE and self.all_ops:
            o.deps.append(self.all_ops[-1])
        self.all_ops.append(o)
        self.streams[eng].append(o)
        return o

    def _dma_slot(self, queue, n_inc):
        ring = self.dma_ring[queue]
        i = self.dma_ring_pos[queue]
        self.dma_ring_pos[queue] = (i + 1) % len(ring)
        sname = ring[i]
        pre = self.cnt[sname]
        self.cnt[sname] += n_inc
        return sname, pre, self.cnt[sname]

    def dma(self, queue, out, in_, tag="", transpose=False):
        sname, pre, post = self._dma_slot(queue, 16)
        o = Op(queue, None, len(self.streams[queue]), tag)
        o.fixed_inc = (sname, 16, post)
        h = self.sem[sname]

        def fn(e, out=out, in_=in_, h=h, pre=pre, transpose=transpose):
            if pre:
                e.wait_ge(h, pre)
            if transpose:
                return e.dma_start_transpose(out, in_).then_inc(h, 16)
            return e.dma_start(out=out, in_=in_).then_inc(h, 16)
        o.fn = fn
        self._add_access(o, "R", _ranges(in_))
        self._add_access(o, "W", _ranges(out))
        if SERIALIZE and self.all_ops:
            o.deps.append(self.all_ops[-1])
        self.all_ops.append(o)
        if _is_sbuf(in_) and _is_sbuf(out):
            if transpose:
                o.deps.extend(self.pending_P)
                self.n_edges += len(self.pending_P)
                self.pending_P = []
                self.pending_T.append(o)
            else:
                o.deps.extend(self.pending_T)
                self.n_edges += len(self.pending_T)
                self.pending_T = []
                self.pending_P.append(o)
        self.streams[queue].append(o)
        return o

    def build(self, final_waits=(), final_engine="sync"):
        nc = self.nc
        for e in ENGINES:
            for o in self.streams[e]:
                for p in o.deps:
                    if p.fixed_inc is None:
                        p.needs_inc = True
        for e in ENGINES:
            for o in self.streams[e]:
                if o.needs_inc and o.fixed_inc is None:
                    self.cnt[ESEM[e]] += 1
                    o.sem_val = self.cnt[ESEM[e]]

        def token(p):
            if p.fixed_inc is not None:
                return (p.fixed_inc[0], p.fixed_inc[2])
            return (ESEM[p.engine], p.sem_val)

        if final_waits:
            fo = Op(final_engine, None, len(self.streams[final_engine]), "final")
            toks = [token(p) for p in final_waits]

            def ffn(e, toks=toks):
                best = {}
                for s, v in toks:
                    best[s] = max(best.get(s, 0), v)
                last = None
                for s, v in sorted(best.items()):
                    last = e.wait_ge(self.sem[s], v)
                return last
            fo.fn = ffn
            self.streams[final_engine].append(fo)

        emit_plan = {}
        n_waits = 0
        for e in ENGINES:
            water = {}
            plan = []
            for o in self.streams[e]:
                best = {}
                for p in o.deps:
                    s, v = token(p)
                    assert v is not None, f"unresolved producer {p.tag} <- {o.tag}"
                    if best.get(s, -1) < v:
                        best[s] = v
                waits = []
                for s, v in sorted(best.items()):
                    if water.get(s, 0) < v:
                        water[s] = v
                        waits.append((self.sem[s], v))
                n_waits += len(waits)
                plan.append((waits, o))
            emit_plan[e] = plan
        self.n_waits = n_waits

        with nc.Block() as block:
            for e in ENGINES:
                plan = emit_plan[e]
                if not plan:
                    continue

                def body(eng, plan=plan, e=e):
                    sem_h = self.sem[ESEM[e]]
                    for waits, o in plan:
                        for (h, v) in waits:
                            eng.wait_ge(h, v)
                        ins = o.fn(eng)
                        if o.needs_inc and o.fixed_inc is None:
                            assert ins is not None, f"op {o.tag} returned None"
                            ins.then_inc(sem_h, 1)
                getattr(block, e)(body)
        return nc


def host_consts(core):
    bf = ml_dtypes.bfloat16
    c = {}
    c["c_iota8"] = np.tile(np.arange(8, dtype=np.float32), (128, 1))
    c["c_iota8r"] = np.tile(8.0 - np.arange(8, dtype=np.float32), (128, 1))
    c["c_iota"] = np.tile(np.arange(384, dtype=np.float32), (128, 1))
    c["c_piota"] = np.arange(128, dtype=np.float32)[:, None]
    wlo = np.minimum(np.maximum(16 * np.arange(16) - 96, 0), 144)
    c["c_wlo"] = np.tile(wlo.astype(np.float32), (128, 1))
    eds = (384 * np.arange(8)[:, None] + 128 * np.arange(3)[None, :]).reshape(-1)
    c["c_eds"] = np.tile(eds.astype(np.float32), (128, 1))
    k = np.arange(128)
    c["c_L128"] = (k[:, None] < k[None, :]).astype(bf)
    c["c_negid"] = (-1e9 * np.eye(128)).astype(bf)
    true = ((core + k // 16) % 8) * 16 + (k % 16)
    c["c_Lt"] = (true[:, None] < true[None, :]).astype(bf)
    c["c_ident"] = np.eye(128, dtype=np.float32)
    c["c_ones_bf"] = np.ones((1, 128), dtype=bf)
    c["c_ones_f"] = np.ones((1, 128), dtype=np.float32)
    c["c_onescol_f"] = np.ones((128, 1), dtype=np.float32)
    c["c_onescol_bf"] = np.ones((128, 1), dtype=bf)
    h0 = np.zeros((1, 128), dtype=bf); h0[0, :64] = 1
    h1 = np.zeros((1, 128), dtype=bf); h1[0, 64:] = 1
    c["c_half0"] = h0
    c["c_half1"] = h1
    return c


GW = 240


def gather_window(j):
    return min(max(16 * j - 96, 0), SLOTS - GW), GW


def scatter_chunks(tc):
    lo = max(0, 16 * tc - SM)
    hi = min(SLOTS, 16 * tc + 16 + SM)
    return list(range(lo // 128, (hi + 127) // 128))


PHASES = None
DUMPS = False


def _on(name):
    return PHASES is None or name in PHASES


def build_program():
    nc = bass.Bass("TRN2", target_bir_lowering=False)
    tk = TK(nc)
    S, P = tk.sbuf, tk.psum

    inp = lambda n, s, d: nc.dram_tensor(n, s, d, kind="ExternalInput").ap()
    x_own = inp("x_own", [N, D], F32)
    x_rot = inp("x_rot", [B, N, D], F32)
    Wq = inp("Wq", [D, D], F32)
    Wkv = inp("Wkv", [D, 2 * D], F32)
    Wo = inp("Wo", [D, D], F32)
    bo = inp("bo", [1, D], F32)
    wg = inp("wg", [D, NE], F32)
    Wf = inp("Wf", [NE, D, D], F32)
    bfe = inp("bfe", [NE, D], F32)
    hc0 = host_consts(0)
    cns = {n: inp(n, list(v.shape), BF16 if v.dtype == ml_dtypes.bfloat16 else F32)
           for n, v in hc0.items()}
    out_t = nc.dram_tensor("out", [N, D], F32, kind="ExternalOutput").ap()

    xn_tm_t = S("xn_tm", [128, NT, 1024], BF16)
    xnfm_ar = S("xnfm_ar", [128, 16384], BF16)
    attn_t = S("attn", [128, 8, 2048], BF16)
    pairs = S("pairs", [128, 14368], BF16)
    ar2 = S("ar2", [128, 8192], BF16)
    pt_ar = S("pt_ar", [128, 3072], BF16)
    wt_ar = S("wt_ar", [128, 8192], BF16)
    gm_ar = S("gm_ar", [128, 6144], BF16)
    oh_bf = S("oh_bf", [128, 1024], BF16)
    loctok = S("loctok", [128, 128], F32)
    slotb = S("slotb", [128, 2048], F32)
    wb = S("wb", [128, 2048], BF16)
    scol = S("scol", [128, 16], F32)
    wcol = S("wcol", [128, 16], F32)
    cnttm = S("cnttm", [128, 8], BF16)
    tbtm = S("tbtm", [128, 8], F32)
    vecb = S("vecb", [128, 8], F32)
    v8row = S("v8row", [1, 8], F32)
    tr16 = S("tr16", [16, 128], F32)
    row2k = S("row2k", [1, 2048], F32)
    row2kb = row2k[:, :].bitcast(BF16)
    cntf = row2kb[0:1, 0:1024]
    tbf = row2k[0:1, 0:1024]
    sflat = row2k[0:1, 0:2048]
    wflat = row2k[0:1, 0:2048]
    rstd = S("rstd", [128, NT], F32)
    nmr = S("nmr", [128, NT], F32)
    bn6 = S("bn6", [128, 12], F32)
    bn2 = S("bn2", [128, 2], F32)
    lnv = S("lnv", [128, 1], F32)
    epsc = S("epsc", [128, 1], F32)
    l8 = S("l8", [128, 8], F32)
    mx8 = S("mx8", [128, 8], F32)
    ex8 = S("ex8", [128, 8], F32)
    t8a = S("t8a", [128, 8], F32)
    zs = S("zs", [128, 4], F32)
    padj = S("padj", [128, 2], F32)
    sml = S("sml", [128, 16], F32)
    padjE = S("padjE", [128, 16], F32)
    edsp = S("edsp", [128, 24], F32)
    s2cp = [row2kb[0:1, 0:1024], row2kb[0:1, 1024:2048]]
    rrowp = [row2kb[0:1, 2048:3072], row2kb[0:1, 3072:4096]]
    wgsb = S("wgsb", [128, 64], F32)
    wgtmp = S("wgtmp", [128, 8], F32)
    wgp = S("wgp", [128, 128], BF16)
    bo_bf = S("bo_bf", [1, 1024], BF16)
    be_bf = S("be_bf", [1, 1024], BF16)
    csb = {}
    for n, v in hc0.items():
        dt = BF16 if v.dtype == ml_dtypes.bfloat16 else F32
        csb[n] = S("s_" + n, list(v.shape), dt)

    psA = P("psA", [128, 1024], F32)
    psB = P("psB", [128, 1024], F32)
    psC = P("psC", [128, 512], F32)
    psD = P("psD", [128, 512], F32)
    psE = P("psE", [128, 512], F32)
    psF = P("psF", [128, 512], F32)

    xn_tm = xn_tm_t[:, :, :]
    xnfm = xnfm_ar[:, :].rearrange("p (dc t) -> p dc t", dc=8)
    attn = attn_t[:, :, :]
    gtb = slotb[:, 0:512].bitcast(BF16)
    exclv = gm_ar[:, 0:2048].bitcast(F32)
    locv = gm_ar[:, 2048:4096].bitcast(F32)
    ohf = gm_ar[:, 4096:6144].bitcast(F32)
    stores = []

    qP = pairs[:, 0:2048]
    kP = pairs[:, 2048:4096]
    vS = pairs[:, 4096:6144]
    vT = pairs[:, 6144:8224].rearrange("p (j c) -> p j c", c=130)
    wqS = [pairs[:, 8224 + i * 1024: 8224 + (i + 1) * 1024] for i in range(2)]
    wkS = [pairs[:, 10272 + i * 1024: 10272 + (i + 1) * 1024] for i in range(2)]
    wvS = [pairs[:, 12320:13344], pairs[:, 13344:14368]]

    def mm(ps, lh, rh, start=True, stop=True, tag=""):
        return tk.op("tensor",
                     lambda t, a=ps, b=lh, c=rh, s1=start, s2=stop:
                     t.matmul(a, b, c, start=s1, stop=s2),
                     reads=[lh, rh], writes=[ps], tag=tag)

    def act(out, in_, func, bias=0.0, scale=1.0, tag=""):
        reads = [in_]
        if not isinstance(bias, float):
            reads.append(bias)
        if not isinstance(scale, float):
            reads.append(scale)
        return tk.op("scalar",
                     lambda s, o=out, i=in_, f=func, b=bias, sc=scale:
                     s.activation(o, i, f, bias=b, scale=sc),
                     reads=reads, writes=[out], tag=tag)

    def vcopy(out, in_, tag=""):
        return tk.op("vector", lambda v, o=out, i=in_: v.tensor_copy(o, i),
                     reads=[in_], writes=[out], tag=tag)

    def scopy(out, in_, tag=""):
        return act(out, in_, AF.Identity, tag=tag)

    def vtt(out, a, b, op, tag=""):
        return tk.op("vector",
                     lambda v, o=out, x=a, y=b, w=op: v.tensor_tensor(o, x, y, w),
                     reads=[a, b], writes=[out], tag=tag)

    def vts(out, in_, s1, op, tag=""):
        reads = [in_] + ([s1] if not isinstance(s1, float) else [])
        return tk.op("vector",
                     lambda v, o=out, i=in_, s=s1, w=op:
                     v.tensor_single_scalar(o, i, s, w),
                     reads=reads, writes=[out], tag=tag)

    def vstt(out, in0, s1, in1, op0, op1, tag=""):
        reads = [in0, in1] + ([s1] if not isinstance(s1, float) else [])
        return tk.op("vector",
                     lambda v, o=out, a=in0, s=s1, b=in1, w0=op0, w1=op1:
                     v.scalar_tensor_tensor(o, a, s, b, w0, w1),
                     reads=reads, writes=[out], tag=tag)

    for n in cns:
        tk.dma("sync", csb[n][:, :], cns[n], tag="c_" + n)
    tk.dma("sync", wgsb[:, :].rearrange("p (dc e) -> p dc e", dc=8),
           wg.rearrange("(dc p) e -> p dc e", p=128), tag="wg")
    tk.dma("gpsimd", bo_bf[:, :], bo, tag="bo")
    tk.op("vector", lambda v: v.memset(epsc[:, :], 1e-5), writes=[epsc[:, :]])
    tk.op("vector", lambda v: v.memset(wgp[:, :], 0.0), writes=[wgp[:, :]])

    if _on("route1"):
        mm(psA[0:1, 0:64], csb["c_onescol_f"][:, :], wgsb[:, :], tag="wgsum")
        vcopy(tr16[0:1, 0:64], psA[0:1, 0:64], tag="wgsum_ev")
        tk.op("vector",
              lambda v: v.reduce_sum(
                  wgtmp[0:1, 0:8],
                  tr16[0:1, 0:64].rearrange("o (dc e) -> o e dc", e=8),
                  axis=AX.X),
              reads=[tr16[0:1, 0:64]], writes=[wgtmp[0:1, 0:8]], tag="wgred")
        vts(wgtmp[0:1, 0:8], wgtmp[0:1, 0:8], -1.0 / 1024.0, ALU.mult, tag="wgsc")
        mm(psB[:, 0:8], csb["c_ones_f"][:, :], wgtmp[0:1, 0:8], tag="wgbc")
        for dc in range(8):
            vtt(wgtmp[:, 0:8], psB[:, 0:8], wgsb[:, 8 * dc:8 * dc + 8],
                ALU.add, tag=f"wgp{dc}")
            vcopy(wgp[:, 16 * dc:16 * dc + 8], wgtmp[:, 0:8], tag=f"wgpc{dc}")

    if _on("ln"):
        for j in range(NT):
            xs = ar2[:, (j % 2) * 2048:(j % 2) * 2048 + 2048].bitcast(F32)
            tk.dma("sync", xs, x_own[128 * j:128 * j + 128, :], tag=f"xl{j}")
            tk.op("vector", lambda v, a=xs: v.bn_stats(bn6[:, 0:6], a[:, 0:512]),
                  reads=[xs[:, 0:512]], writes=[bn6[:, 0:6]], tag=f"bns0_{j}")
            tk.op("vector", lambda v, a=xs: v.bn_stats(bn6[:, 6:12], a[:, 512:1024]),
                  reads=[xs[:, 512:1024]], writes=[bn6[:, 6:12]], tag=f"bns1_{j}")
            tk.op("vector", lambda v: v.bn_aggr(bn2[:, :], bn6[:, :]),
                  reads=[bn6[:, :]], writes=[bn2[:, :]], tag=f"bna{j}")
            act(lnv[:, :], bn2[:, 1:2], AF.Ln, bias=epsc[:, :], tag=f"ln{j}")
            act(rstd[:, j:j + 1], lnv[:, :], AF.Exp, scale=-0.5, tag=f"rs{j}")
            vtt(nmr[:, j:j + 1], bn2[:, 0:1], rstd[:, j:j + 1], ALU.mult,
                tag=f"nm{j}")
            vts(nmr[:, j:j + 1], nmr[:, j:j + 1], -1.0, ALU.mult, tag=f"nmn{j}")
            act(xn_tm[:, j, :], xs, AF.Identity, bias=nmr[:, j:j + 1],
                scale=rstd[:, j:j + 1], tag=f"xn{j}")

    if _on("xnfm"):
        for j in range(NT):
            tk.dma("sync",
                   xnfm[:, :, 128 * j:128 * j + 128],
                   xn_tm[:, j, :].rearrange("p (dc q) -> p dc q", dc=8),
                   tag=f"xt{j}", transpose=True)

    xh2s = [ar2[:, 0:2048], ar2[:, 4096:6144]]
    xf2s = [ar2[:, 2048:4096], ar2[:, 6144:8192]]

    def emit_route_group(gp):
        g = 2 * gp
        bidx, jj = g // NT, g % NT
        sl = gp % 2
        xh2 = xh2s[sl]
        xfv = xf2s[sl].rearrange("p (b q) -> p b q", b=16)
        tk.dma("gpsimd", xh2.rearrange("p (t d) -> p t d", t=2),
               x_rot[bidx, 128 * jj:128 * jj + 256, :].rearrange(
                   "(t p) d -> p t d", p=128), tag=f"xh{g}")
        tk.dma("sync",
               xfv, xh2.rearrange("p (b q) -> p b q", b=16),
               tag=f"xf{g}", transpose=True)
        for t in range(2):
            w = 8 * ((g + t) % 64)
            for dc in range(8):
                mm(psA[:, w:w + 8],
                   xfv[:, 8 * t + dc, :], wgp[:, 16 * dc:16 * dc + 8],
                   start=(dc == 0), stop=(dc == 7), tag=f"g{g + t}_{dc}")

    def emit_gev(half):
        vcopy(gtb[:, 512 * half:512 * half + 512], psA[:, 0:512],
              tag=f"gev{half}")

    def emit_ph5():
        gt3v = gtb.rearrange("p (j e) -> p j e", e=8)
        mxt = loctok
        tk.op("vector", lambda v: v.reduce_max(mxt[:, :], gt3v, axis=AX.X),
              reads=[gtb], writes=[mxt[:, :]], tag="rmax")
        eq3 = exclv.rearrange("p (j e) -> p j e", e=8)
        tk.op("vector",
              lambda v: v.tensor_tensor(
                  eq3, gt3v, mxt[:, :].broadcast_to([128, 128, 8]),
                  ALU.is_equal),
              reads=[gtb, mxt[:, :]], writes=[exclv], tag="eq")
        iota_b = csb["c_iota8r"][:, :].__replace__(
            ap=[[8, 128], [0, 128], [1, 8]])
        u3 = locv.rearrange("p (j e) -> p j e", e=8)
        tk.op("vector",
              lambda v: v.tensor_tensor(u3, eq3, iota_b, ALU.mult),
              reads=[exclv, csb["c_iota8r"][:, :]], writes=[locv], tag="umul")
        tk.op("vector", lambda v: v.reduce_max(mxt[:, :], u3, axis=AX.X),
              reads=[locv], writes=[mxt[:, :]], tag="umax")
        umb = mxt[:, :].broadcast_to([128, 128, 8])
        oh3b = oh_bf[:, :].rearrange("p (j e) -> p j e", e=8)
        tk.op("vector",
              lambda v: v.tensor_tensor(oh3b, u3, umb, ALU.is_equal),
              reads=[locv, mxt[:, :]], writes=[oh_bf[:, :]], tag="ohb")
        oh3f = ohf.rearrange("p (j e) -> p j e", e=8)
        tk.op("vector",
              lambda v: v.tensor_tensor(oh3f, u3, umb, ALU.is_equal),
              reads=[locv, mxt[:, :]], writes=[ohf], tag="ohfb")
        for hhalf in range(2):
            sl = slice(512 * hhalf, 512 * hhalf + 512)
            mm(psA[:, sl], csb["c_L128"][:, :], oh_bf[:, sl], tag=f"ex{hhalf}")
            vcopy(exclv[:, sl], psA[:, sl], tag=f"exev{hhalf}")
        for hhalf in range(2):
            sl = slice(512 * hhalf, 512 * hhalf + 512)
            mm(psB[0:1, sl], csb["c_onescol_bf"][:, :], oh_bf[:, sl],
               tag=f"cnt{hhalf}")
        scopy(cntf, psB[0:1, 0:1024], tag="cntev")
        tk.dma("gpsimd", cnttm[:, :], cntf, tag="cntrs")
        mm(psB[:, 0:8], csb["c_Lt"][:, :], cnttm[:, :], tag="tbmm")
        vcopy(tbtm[:, :], psB[:, 0:8], tag="tbev")
        tk.dma("gpsimd", tbf, tbtm[:, :], tag="tbrs")
        for hhalf in range(2):
            sl = slice(512 * hhalf, 512 * hhalf + 512)
            mm(psA[:, sl], csb["c_ones_f"][:, :], tbf[0:1, sl], tag=f"tb{hhalf}")
            vstt(locv[:, sl], psA[:, sl], 1.0, exclv[:, sl], ALU.mult, ALU.add,
                 tag=f"loc{hhalf}")
        for hhalf in range(2):
            sl = slice(512 * hhalf, 512 * hhalf + 512)
            vtt(psB[:, sl], ohf[:, sl], locv[:, sl], ALU.mult, tag=f"lt{hhalf}")
        tk.op("vector",
              lambda v: v.reduce_sum(
                  loctok[:, :],
                  psB[:, 0:1024].rearrange("p (j e) -> p j e", e=8), axis=AX.X),
              reads=[psB[:, :]], writes=[loctok[:, :]], tag="ltred")

    def emit_ph6():
        vts(v8row[:, :], csb["c_iota8"][0:1, 0:8], float(SLOTS), ALU.mult,
            tag="v8m")
        vtt(v8row[:, :], v8row[:, :], tbf[0:1, 0:8], ALU.subtract, tag="v8s")
        mm(psB[:, 0:8], csb["c_ones_f"][:, :], v8row[0:1, 0:8], tag="v8bc")
        vcopy(vecb[:, :], psB[:, 0:8], tag="v8ev")
        for j in range(NT):
            gsl = gtb[:, 8 * j:8 * j + 8]
            tk.op("vector",
                  lambda v, a=gsl, r=rstd[:, j:j + 1]:
                  v.tensor_single_scalar(l8[:, :], a, r, ALU.mult),
                  reads=[gsl, rstd[:, j:j + 1]], writes=[l8[:, :]],
                  tag=f"l8{j}")
            tk.op("vector", lambda v: v.max(mx8[:, :], l8[:, :]),
                  reads=[l8[:, :]], writes=[mx8[:, :]], tag=f"lm{j}")
            vts(zs[:, 0:1], mx8[:, 0:1], -1.0, ALU.mult, tag=f"ln{j}")
            act(ex8[:, :], l8[:, :], AF.Exp, bias=zs[:, 0:1], tag=f"ge{j}")
            tk.op("vector",
                  lambda v: v.reduce_sum(zs[:, 1:2], ex8[:, :], axis=AX.X),
                  reads=[ex8[:, :]], writes=[zs[:, 1:2]], tag=f"gz{j}")
            tk.op("vector", lambda v: v.reciprocal(zs[:, 2:3], zs[:, 1:2]),
                  reads=[zs[:, 1:2]], writes=[zs[:, 2:3]], tag=f"gr{j}")
            vts(zs[:, 3:4], loctok[:, j:j + 1], float(CAP), ALU.is_lt,
                tag=f"kp{j}")
            vtt(wcol[:, j:j + 1], zs[:, 2:3], zs[:, 3:4], ALU.mult,
                tag=f"w{j}")
            vtt(t8a[:, :], locv[:, 8 * j:8 * j + 8], vecb[:, :], ALU.add,
                tag=f"sl{j}")
            vtt(t8a[:, :], t8a[:, :], ohf[:, 8 * j:8 * j + 8], ALU.mult,
                tag=f"slm{j}")
            tk.op("vector",
                  lambda v, o=scol[:, j:j + 1]: v.reduce_sum(o, t8a[:, :],
                                                             axis=AX.X),
                  reads=[t8a[:, :]], writes=[scol[:, j:j + 1]], tag=f"sr{j}")

    def emit_ph7():
        for (colsrc, flat, dstb, dt_out, tag) in (
                (scol, sflat, slotb, F32, "s"), (wcol, wflat, wb, BF16, "w")):
            tk.op("tensor",
                  lambda t, a=colsrc: t.transpose(psA[0:16, 0:128], a[:, :],
                                                  csb["c_ident"][:, :]),
                  reads=[colsrc[:, :], csb["c_ident"][:, :]],
                  writes=[psA[:, 0:128]], tag=f"{tag}tr")
            vcopy(tr16[:, :], psA[0:16, 0:128], tag=f"{tag}trev")
            tk.dma("gpsimd", flat[0:1, 0:2048], tr16[:, :], tag=f"{tag}rs")
            for q in range(4):
                sl = slice(512 * q, 512 * q + 512)
                mm(psB[:, 0:512] if q % 2 == 0 else psB[:, 512:1024],
                   csb["c_ones_f"][:, :], flat[0:1, sl], tag=f"{tag}bc{q}")
                src = psB[:, 0:512] if q % 2 == 0 else psB[:, 512:1024]
                vcopy(dstb[:, sl], src, tag=f"{tag}be{q}")

    if _on("attn"):
        WqR = Wq.rearrange("(dc p) i -> p dc i", p=128)
        WkvR = Wkv.rearrange("(dc p) i -> p dc i", p=128)
        for pr in range(8):
            sl = pr % 2
            tk.dma("gpsimd", wqS[sl].rearrange("p (dc q) -> p dc q", dc=8),
                   WqR[:, :, 128 * pr:128 * pr + 128], tag=f"wq{pr}")
            tk.dma("gpsimd", wkS[sl].rearrange("p (dc q) -> p dc q", dc=8),
                   WkvR[:, :, 128 * pr:128 * pr + 128], tag=f"wk{pr}")
            tk.dma("gpsimd", wvS[sl].rearrange("p (dc q) -> p dc q", dc=8),
                   WkvR[:, :, 1024 + 128 * pr:1024 + 128 * pr + 128],
                   tag=f"wv{pr}")
            wqv = wqS[sl].rearrange("p (dc q) -> p dc q", dc=8)
            wkv = wkS[sl].rearrange("p (dc q) -> p dc q", dc=8)
            wvv = wvS[sl].rearrange("p (dc q) -> p dc q", dc=8)
            for (wsl, dst, tg) in ((wqv, qP, "q"), (wkv, kP, "k"),
                                   (wvv, vS, "v")):
                for nck in range(4):
                    ps = psB[:, 0:512] if nck % 2 == 0 else psB[:, 512:1024]
                    csl = slice(512 * nck, 512 * nck + 512)
                    for dc in range(8):
                        mm(ps, wsl[:, dc, :], xnfm[:, dc, csl],
                           start=(dc == 0), stop=(dc == 7),
                           tag=f"{tg}p{pr}_{nck}_{dc}")
                    vcopy(dst[:, csl], ps, tag=f"{tg}e{pr}_{nck}")
            vSv = vS.rearrange("p (j q) -> p j q", j=NT)
            for h01t in range(2):
                scrv = attn_t[:, 7, 1024 * h01t: 1024 * h01t + 1024].rearrange(
                    "p (j c) -> p j c", c=64)
                tk.dma("sync", scrv, vSv[64 * h01t:64 * h01t + 64, :, :],
                       tag=f"vt{h01t}_{pr}", transpose=True)
                vcopy(vT[:, :, 65 * h01t:65 * h01t + 64], scrv,
                      tag=f"vc{h01t}_{pr}")
            if pr == 0:
                tk.op("vector", lambda v: v.memset(vT[:, :, 64:65], 1.0),
                      writes=[vT[:, :, 64:65]], tag="vo0")
                tk.op("vector", lambda v: v.memset(vT[:, :, 129:130], 1.0),
                      writes=[vT[:, :, 129:130]], tag="vo1")
            pending_div = None
            for c in range(4):
                nj = 4 * c + 4

                def emit_n(j, c=c, nj=nj):
                    voff = max(0, 128 * j - 512 * c)
                    for h01 in range(2):
                        nps = psE if h01 == 0 else psF
                        pt = pt_ar[:, 512 * ((2 * j + h01) % 6):
                                   512 * ((2 * j + h01) % 6) + 512]
                        mm(nps[0:65, voff:512],
                           vT[:, j, 65 * h01:65 * h01 + 65],
                           pt[:, voff:512],
                           start=(j == 0), stop=(j == nj - 1),
                           tag=f"n{pr}_{h01}_{c}_{j}")

                for j in range(nj):
                    voff = max(0, 128 * j - 512 * c)
                    diag = j >= 4 * c
                    for h01 in range(2):
                        hb = 64 * h01
                        if j % 2 == 0:
                            sps = psC if h01 == 0 else psD
                        else:
                            sps = psB[:, 0:512] if h01 == 0 else psB[:, 512:1024]
                        mm(sps[:, voff:512],
                           kP[hb:hb + 64, 128 * j:128 * j + 128],
                           qP[hb:hb + 64, 512 * c + voff:512 * c + 512],
                           start=True, stop=not diag,
                           tag=f"s{pr}_{h01}_{c}_{j}")
                        if diag:
                            mm(sps[:, voff:voff + 128], csb["c_L128"][:, :],
                               csb["c_negid"][:, :], start=False, stop=True,
                               tag=f"m{pr}_{h01}_{c}_{j}")
                        pt = pt_ar[:, 512 * ((2 * j + h01) % 6):
                                   512 * ((2 * j + h01) % 6) + 512]
                        act(pt[:, voff:512], sps[:, voff:512], AF.Exp,
                            scale=0.125, tag=f"e{pr}_{h01}_{c}_{j}")
                    if j > 0:
                        emit_n(j - 1)
                emit_n(nj - 1)
                par = c % 2
                vcopy(attn[0:64, pr, 512 * c:512 * c + 512], psE[0:64, :],
                      tag=f"ae{pr}_{c}")
                vcopy(attn[64:128, pr, 512 * c:512 * c + 512], psF[0:64, :],
                      tag=f"ao{pr}_{c}")
                vcopy(s2cp[par][0:1, 0:512], psE[64:65, :], tag=f"se{pr}_{c}")
                vcopy(s2cp[par][0:1, 512:1024], psF[64:65, :],
                      tag=f"so{pr}_{c}")

                def mkdiv(pr=pr, c=c, par=par):
                    def emit():
                        rr = rrowp[par]
                        lsc = psA[0:1, 512:1024]
                        for hf in range(2):
                            act(lsc, s2cp[par][0:1, 512 * hf:512 * hf + 512],
                                AF.Ln, tag=f"rl{pr}_{c}_{hf}")
                            act(rr[0:1, 512 * hf:512 * hf + 512], lsc,
                                AF.Exp, scale=-1.0, tag=f"rx{pr}_{c}_{hf}")
                        mm(psA[:, 512:1024], csb["c_half0"][:, :],
                           rr[0:1, 0:512], start=True, stop=False,
                           tag=f"rb0{pr}_{c}")
                        mm(psA[:, 512:1024], csb["c_half1"][:, :],
                           rr[0:1, 512:1024], start=False, stop=True,
                           tag=f"rb1{pr}_{c}")
                        rsb = oh_bf[:, 512 * par:512 * par + 512]
                        vcopy(rsb, psA[:, 512:1024], tag=f"re{pr}_{c}")
                        vtt(attn[:, pr, 512 * c:512 * c + 512],
                            attn[:, pr, 512 * c:512 * c + 512], rsb, ALU.mult,
                            tag=f"dv{pr}_{c}")
                    return emit

                if pending_div is not None:
                    pending_div()
                pending_div = mkdiv()
                for gp in range(8 * pr + 2 * c, 8 * pr + 2 * c + 2):
                    emit_route_group(gp)
                if pr == 3 and c == 3:
                    emit_gev(0)
            pending_div()
            if pr == 7:
                emit_gev(1)
                emit_ph5()
                emit_ph6()
                emit_ph7()

    Y = []
    for e in range(NE):
        if e <= 4:
            Y.append(xnfm_ar[:, 3072 * e:3072 * e + 3072])
        elif e == 5:
            Y.append(pt_ar[:, 0:3072])
        else:
            Y.append(pairs[:, 3072 * (e - 6):3072 * (e - 6) + 3072])
    Yv = [y.rearrange("p (sc f) -> p sc f", sc=3) for y in Y]
    wstage = ar2[:, 0:8192].rearrange("p (fc h) -> p fc h", fc=8)
    wt = wt_ar[:, :].rearrange("p (hc f) -> p hc f", hc=8)

    if _on("ffn"):
        for e in range(NE):
            tk.dma("gpsimd", wstage,
                   Wf[e, :, :].rearrange("(fc p) h -> p fc h", p=128),
                   tag=f"wf{e}")
            for fc in range(8):
                tk.dma("sync" if fc % 2 == 0 else "scalar",
                       wt[:, :, 128 * fc:128 * fc + 128],
                       wstage[:, fc, :],
                       tag=f"wt{e}_{fc}", transpose=True)
            tk.dma("gpsimd", be_bf[0:1, 0:1024], bfe[e:e + 1, :], tag=f"be{e}")
            eg, eo = e // 2, e % 2
            if e == 0:
                vtt(sml[:, :], scol[:, :], csb["c_wlo"][:, :], ALU.subtract,
                    tag="sml")
            vts(padjE[:, :], sml[:, :], -float(SLOTS * e), ALU.add,
                tag=f"pa{e}")
            P0 = 3984 * eo
            pt0 = pairs[:, P0:P0 + 384]
            ptr = pairs[:, P0 + 384: P0 + 3984].rearrange(
                "p (j w) -> p j w", w=GW)
            vts(pt0, csb["c_iota"][:, 0:384], padjE[:, 0:1], ALU.is_equal,
                tag=f"p0{e}")
            iota_w = csb["c_iota"][:, 0:GW].__replace__(
                ap=[[384, 128], [0, 15], [1, GW]])
            tk.op("vector",
                  lambda v, o=ptr, i=iota_w, b=padjE:
                  v.tensor_tensor(o, i,
                                  b[:, 1:16].broadcast_to([128, 15, GW]),
                                  ALU.is_equal),
                  reads=[csb["c_iota"][:, 0:GW], padjE[:, :]],
                  writes=[pairs[:, P0 + 384: P0 + 3984]],
                  tag=f"pc{e}")
            gsp = psC if eo == 0 else psD
            gdst = pairs[:, 7968 + 3072 * eo: 7968 + 3072 * eo + 3072]
            gv = gdst.rearrange("p (hc s) -> p hc s", hc=8)
            for hcx in range(8):
                for j in range(NT):
                    if j == 0:
                        wlo, W, ptile = 0, 384, pt0
                    else:
                        wlo, W = gather_window(j)
                        ptile = ptr[:, j - 1, :]
                    mm(gsp[:, wlo:wlo + W],
                       xn_tm[:, j, 128 * hcx:128 * hcx + 128], ptile,
                       start=(j == 0), stop=(j == NT - 1),
                       tag=f"gm{e}_{hcx}_{j}")
                vcopy(gv[:, hcx, 0:SLOTS], gsp[:, 0:SLOTS], tag=f"ge{e}_{hcx}")
            for fh in range(2):
                for sc in range(3):
                    yp = (psE if fh == 0 else psF)[:, :]
                    for hcx in range(8):
                        mm(yp, gv[:, hcx, 128 * sc:128 * sc + 128],
                           wt[:, hcx, 512 * fh:512 * fh + 512],
                           start=(hcx == 0), stop=False,
                           tag=f"ym{e}_{sc}_{fh}_{hcx}")
                    mm(yp, csb["c_ones_bf"][:, :],
                       be_bf[0:1, 512 * fh:512 * fh + 512],
                       start=False, stop=True, tag=f"yb{e}_{sc}_{fh}")
                    scopy(Yv[e][:, sc, 512 * fh:512 * fh + 512], yp,
                          tag=f"ye{e}_{sc}_{fh}")

    if _on("tail"):
        tk.dma("gpsimd", wt_ar[:, :].rearrange("p (ic d) -> p ic d", ic=8),
               Wo.rearrange("(ic p) d -> p ic d", p=128), tag="wo")
        wov = wt_ar[:, :].rearrange("p (ic d) -> p ic d", ic=8)

        def emit_head(tc):
            sl = tc % 2
            xr = ar2[:, 2048 * sl:2048 * sl + 2048].bitcast(F32)
            tk.dma("sync", xr, x_own[128 * tc:128 * tc + 128, :], tag=f"xr{tc}")
            if tc == 0:
                vts(edsp[:, :], csb["c_eds"][:, :], csb["c_piota"][:, :],
                    ALU.add, tag="edsp")
            nchunk = len(scatter_chunks(tc))
            K = NE * nchunk
            rt4 = gm_ar[:, 3072 * sl: 3072 * sl + 128 * K].rearrange(
                "p (e d t) -> p e d t", d=nchunk, t=128)
            sb = slotb[:, 128 * tc:128 * tc + 128]
            slot_b = sb.__replace__(
                ap=[[2048, 128], [0, NE], [0, nchunk], [1, 128]],
                offset=sb.offset)
            ev4 = edsp[:, :].rearrange("p (e d) -> p e d", d=3)[:, :, 0:nchunk]
            tk.op("vector",
                  lambda v, o=rt4, i=slot_b, b=ev4, nch=nchunk:
                  v.tensor_tensor(o, i,
                                  b.broadcast_to([128, NE, nch, 128]),
                                  ALU.is_equal),
                  reads=[sb, edsp[:, :]],
                  writes=[gm_ar[:, 3072 * sl: 3072 * sl + 128 * K]],
                  tag=f"rc{tc}")
            wbsl = wb[:, 128 * tc:128 * tc + 128]
            wb_b = wbsl.__replace__(
                ap=[[2048, 128], [0, K], [1, 128]], offset=wbsl.offset)
            rt3 = gm_ar[:, 3072 * sl: 3072 * sl + 128 * K].rearrange(
                "p (k t) -> p k t", t=128)
            tk.op("vector",
                  lambda v, o=rt3, b=wb_b:
                  v.tensor_tensor(o, o, b, ALU.mult),
                  reads=[wbsl, gm_ar[:, 3072 * sl: 3072 * sl + 128 * K]],
                  writes=[gm_ar[:, 3072 * sl: 3072 * sl + 128 * K]],
                  tag=f"rw{tc}")

        def emit_body(tc):
            sl = tc % 2
            xr = ar2[:, 2048 * sl:2048 * sl + 2048].bitcast(F32)
            rws = []
            k = 0
            for e in range(NE):
                for sc in scatter_chunks(tc):
                    rt = gm_ar[:, 3072 * sl + 128 * k:
                               3072 * sl + 128 * k + 128]
                    k += 1
                    rws.append((e, sc, rt))
            ps = psA if sl == 0 else psB
            for dh in range(2):
                psd = ps[:, 512 * dh:512 * dh + 512]
                for ic in range(8):
                    mm(psd, attn[:, ic, 128 * tc:128 * tc + 128],
                       wov[:, ic, 512 * dh:512 * dh + 512],
                       start=(ic == 0), stop=False, tag=f"pj{tc}_{dh}_{ic}")
                mm(psd, csb["c_ones_bf"][:, :],
                   bo_bf[0:1, 512 * dh:512 * dh + 512],
                   start=False, stop=False, tag=f"pb{tc}_{dh}")
                for i, (e, sc, rt) in enumerate(rws):
                    mm(psd, rt, Yv[e][:, sc, 512 * dh:512 * dh + 512],
                       start=False, stop=(i == len(rws) - 1),
                       tag=f"sc{tc}_{dh}_{e}_{sc}")
                vstt(xr[:, 512 * dh:512 * dh + 512], psd, 1.0,
                     xr[:, 512 * dh:512 * dh + 512], ALU.mult, ALU.add,
                     tag=f"oev{tc}_{dh}")
            stores.append(tk.dma("sync", out_t[128 * tc:128 * tc + 128, :],
                                 xr, tag=f"st{tc}"))

        emit_head(0)
        for tc in range(NT):
            if tc + 1 < NT:
                emit_head(tc + 1)
            emit_body(tc)

    if DUMPS:
        def dump(nm, src, shape, dt):
            t = nc.dram_tensor(nm, shape, dt, kind="ExternalOutput").ap()
            stores.append(tk.dma("gpsimd", t, src, tag="d_" + nm))
        dump("d_xn", xn_tm_t[:, :, :], [128, NT, 1024], BF16)
        dump("d_attn", attn_t[:, :, :], [128, 8, 2048], BF16)
        dump("d_slotb", slotb[:, :], [128, 2048], F32)
        dump("d_wb", wb[:, :], [128, 2048], BF16)
        dump("d_loctok", loctok[:, :], [128, 128], F32)
        dump("d_scol", scol[:, :], [128, 16], F32)
        dump("d_wcol", wcol[:, :], [128, 16], F32)
        dump("d_xnfm", xnfm_ar[:, :], [128, 16384], BF16)
        dump("d_ptar", pt_ar[:, :], [128, 3072], BF16)
        dump("d_pairs", pairs[:, 0:13344], [128, 13344], BF16)
        dump("d_rstd", rstd[:, :], [128, NT], F32)

    tk.build(final_waits=stores, final_engine="sync")
    return nc, tk


_CACHE = {}


def _core_inputs(core, inputs):
    x = np.ascontiguousarray(inputs["L0_attn_x"])
    d = {
        "x_own": x[core],
        "x_rot": np.roll(x, -core, axis=0),
        "Wq": inputs["L0_attn_Wq"],
        "Wkv": inputs["L0_attn_Wkv"],
        "Wo": inputs["L0_attn_Wo"],
        "bo": np.ascontiguousarray(inputs["L0_attn_bo"]).reshape(1, D),
        "wg": inputs["L0_ffn_wg"],
        "Wf": inputs["L0_ffn_W"],
        "bfe": inputs["L0_ffn_b"],
    }
    d.update(host_consts(core))
    return {k: np.ascontiguousarray(v) for k, v in d.items()}


def kernel(**inputs):
    if "nc" not in _CACHE:
        _CACHE["nc"] = build_program()[0]
    nc = _CACHE["nc"]
    in_maps = [_core_inputs(c, inputs) for c in range(B)]
    res = run_bass_kernel_spmd(nc, in_maps, core_ids=list(range(B)))
    return np.stack([np.asarray(res.results[c]["out"], dtype=np.float32)
                     for c in range(B)], axis=0)


if __name__ == "__main__":
    nc, tk = build_program()
    ni = sum(len(s) for s in tk.streams.values())
    print(f"ops={ni} edges={tk.n_edges} waits={tk.n_waits}")


# version 11
# speedup vs baseline: 2.0821x; 1.0309x over previous
import contextlib
import numpy as np
import ml_dtypes
import concourse.bass as bass
import concourse.mybir as mybir
from concourse.bass_utils import run_bass_kernel_spmd

F32 = mybir.dt.float32
BF16 = mybir.dt.bfloat16
U32 = mybir.dt.uint32
AF = mybir.ActivationFunctionType
ALU = mybir.AluOpType
AX = mybir.AxisListType

B, N, D = 8, 2048, 1024
NH, DH = 16, 64
NE, CAP = 8, 2048
NT = N // 128
GT = B * NT
SLOTS = 384
GM = 96
SM = 128

ENGINES = ("tensor", "vector", "scalar", "gpsimd", "sync")
ESEM = {e: "sem_" + e for e in ENGINES}
_DT_SIZE = {"float32": 4, "float32r": 4, "bfloat16": 2, "float16": 2,
            "uint32": 4, "int32": 4, "uint16": 2, "int16": 2, "uint8": 1}


def _dt_size(dt):
    n = getattr(dt, "name", None) or str(dt).split(".")[-1]
    return _DT_SIZE.get(n, 4)


def _as_ap(x):
    return x.ap() if callable(getattr(x, "ap", None)) else x


def _ranges(x, out=None):
    if out is None:
        out = []
    if x is None or isinstance(x, (int, float, str)):
        return out
    if isinstance(x, (list, tuple)):
        for e in x:
            _ranges(e, out)
        return out
    ap = _as_ap(x)
    try:
        space = str(ap.space)
    except Exception:
        space = "?"
    name = ap.name
    su = space.upper()
    if "SB" not in su and "PSUM" not in su:
        out.append((name, 0, 1 << 20, 0, 1 << 60))
        return out
    try:
        pat = ap.ap
        offset = ap.offset
        esz = _dt_size(ap.dtype)
        pstride, pcount = pat[0]
        plo = (offset // pstride) if pstride else 0
        phi = plo + pcount
        row_off = offset - plo * pstride
        span = 1
        for step, count in pat[1:]:
            span += step * (count - 1)
        out.append((name, plo, phi, row_off * esz, (row_off + span) * esz))
    except Exception:
        out.append((name, 0, 1 << 20, 0, 1 << 60))
    return out


def _is_sbuf(x):
    try:
        return "SB" in str(_as_ap(x).space).upper()
    except Exception:
        return False


class Op:
    __slots__ = ("engine", "fn", "idx", "deps", "needs_inc", "fixed_inc",
                 "sem_val", "tag")

    def __init__(self, engine, fn, idx, tag=""):
        self.engine = engine
        self.fn = fn
        self.idx = idx
        self.deps = []
        self.needs_inc = False
        self.fixed_inc = None
        self.sem_val = None
        self.tag = tag


SERIALIZE = False


class TK:
    def __init__(self, nc, dma_queues=("gpsimd", "sync", "scalar")):
        self.nc = nc
        self.all_ops = []
        self.ctx = contextlib.ExitStack()
        self.streams = {e: [] for e in ENGINES}
        self.sem = {}
        self.cnt = {}
        self.access = {}
        self.n_edges = 0
        for e in ENGINES:
            self.semaphore(ESEM[e])
        self.dma_ring = {}
        self.dma_ring_pos = {}
        for q in dma_queues:
            depth = 6
            ring = []
            for i in range(depth):
                s = f"dma_{q}_{i}"
                self.semaphore(s)
                ring.append(s)
            self.dma_ring[q] = ring
            self.dma_ring_pos[q] = 0
        self.pending_T = []
        self.pending_P = []

    def semaphore(self, name):
        h = self.ctx.enter_context(self.nc.semaphore(name))
        self.sem[name] = h
        self.cnt[name] = 0
        return h

    def sbuf(self, name, shape, dtype):
        return self.ctx.enter_context(self.nc.sbuf_tensor(name, shape, dtype))

    def psum(self, name, shape, dtype):
        return self.ctx.enter_context(self.nc.psum_tensor(name, shape, dtype))

    def _add_access(self, op, kind, rects):
        for rect in rects:
            name, plo, phi, blo, bhi = rect
            hist = self.access.setdefault(name, [])
            for (p, k, r) in hist:
                if p is op:
                    continue
                if kind == "R" and k == "R":
                    continue
                if not (plo < r[2] and r[1] < phi and blo < r[4] and r[3] < bhi):
                    continue
                if p.engine == op.engine == "tensor":
                    continue
                op.deps.append(p)
                self.n_edges += 1
            hist.append((op, kind, rect))

    def op(self, eng, fn, reads=(), writes=(), tag=""):
        o = Op(eng, fn, len(self.streams[eng]), tag)
        self._add_access(o, "R", _ranges(list(reads)))
        self._add_access(o, "W", _ranges(list(writes)))
        if SERIALIZE and self.all_ops:
            o.deps.append(self.all_ops[-1])
        self.all_ops.append(o)
        self.streams[eng].append(o)
        return o

    def _dma_slot(self, queue, n_inc):
        ring = self.dma_ring[queue]
        i = self.dma_ring_pos[queue]
        self.dma_ring_pos[queue] = (i + 1) % len(ring)
        sname = ring[i]
        pre = self.cnt[sname]
        self.cnt[sname] += n_inc
        return sname, pre, self.cnt[sname]

    def dma(self, queue, out, in_, tag="", transpose=False):
        sname, pre, post = self._dma_slot(queue, 16)
        o = Op(queue, None, len(self.streams[queue]), tag)
        o.fixed_inc = (sname, 16, post)
        h = self.sem[sname]

        def fn(e, out=out, in_=in_, h=h, pre=pre, transpose=transpose):
            if pre:
                e.wait_ge(h, pre)
            if transpose:
                return e.dma_start_transpose(out, in_).then_inc(h, 16)
            return e.dma_start(out=out, in_=in_).then_inc(h, 16)
        o.fn = fn
        self._add_access(o, "R", _ranges(in_))
        self._add_access(o, "W", _ranges(out))
        if SERIALIZE and self.all_ops:
            o.deps.append(self.all_ops[-1])
        self.all_ops.append(o)
        if _is_sbuf(in_) and _is_sbuf(out):
            if transpose:
                o.deps.extend(self.pending_P)
                self.n_edges += len(self.pending_P)
                self.pending_P = []
                self.pending_T.append(o)
            else:
                o.deps.extend(self.pending_T)
                self.n_edges += len(self.pending_T)
                self.pending_T = []
                self.pending_P.append(o)
        self.streams[queue].append(o)
        return o

    def build(self, final_waits=(), final_engine="sync"):
        nc = self.nc
        for e in ENGINES:
            for o in self.streams[e]:
                for p in o.deps:
                    if p.fixed_inc is None:
                        p.needs_inc = True
        for e in ENGINES:
            for o in self.streams[e]:
                if o.needs_inc and o.fixed_inc is None:
                    self.cnt[ESEM[e]] += 1
                    o.sem_val = self.cnt[ESEM[e]]

        def token(p):
            if p.fixed_inc is not None:
                return (p.fixed_inc[0], p.fixed_inc[2])
            return (ESEM[p.engine], p.sem_val)

        if final_waits:
            fo = Op(final_engine, None, len(self.streams[final_engine]), "final")
            toks = [token(p) for p in final_waits]

            def ffn(e, toks=toks):
                best = {}
                for s, v in toks:
                    best[s] = max(best.get(s, 0), v)
                last = None
                for s, v in sorted(best.items()):
                    last = e.wait_ge(self.sem[s], v)
                return last
            fo.fn = ffn
            self.streams[final_engine].append(fo)

        emit_plan = {}
        n_waits = 0
        for e in ENGINES:
            water = {}
            plan = []
            for o in self.streams[e]:
                best = {}
                for p in o.deps:
                    s, v = token(p)
                    assert v is not None, f"unresolved producer {p.tag} <- {o.tag}"
                    if best.get(s, -1) < v:
                        best[s] = v
                waits = []
                for s, v in sorted(best.items()):
                    if water.get(s, 0) < v:
                        water[s] = v
                        waits.append((self.sem[s], v))
                n_waits += len(waits)
                plan.append((waits, o))
            emit_plan[e] = plan
        self.n_waits = n_waits

        with nc.Block() as block:
            for e in ENGINES:
                plan = emit_plan[e]
                if not plan:
                    continue

                def body(eng, plan=plan, e=e):
                    sem_h = self.sem[ESEM[e]]
                    for waits, o in plan:
                        for (h, v) in waits:
                            eng.wait_ge(h, v)
                        ins = o.fn(eng)
                        if o.needs_inc and o.fixed_inc is None:
                            assert ins is not None, f"op {o.tag} returned None"
                            ins.then_inc(sem_h, 1)
                getattr(block, e)(body)
        return nc


def host_consts(core):
    bf = ml_dtypes.bfloat16
    c = {}
    c["c_iota8"] = np.tile(np.arange(8, dtype=np.float32), (128, 1))
    c["c_iota8r"] = np.tile(8.0 - np.arange(8, dtype=np.float32), (128, 1))
    c["c_iota"] = np.tile(np.arange(384, dtype=np.float32), (128, 1))
    c["c_piota"] = np.arange(128, dtype=np.float32)[:, None]
    wlo = np.minimum(np.maximum(16 * np.arange(16) - 96, 0), 144)
    c["c_wlo"] = np.tile(wlo.astype(np.float32), (128, 1))
    eds = (384 * np.arange(8)[:, None] + 128 * np.arange(3)[None, :]).reshape(-1)
    c["c_eds"] = np.tile(eds.astype(np.float32), (128, 1))
    k = np.arange(128)
    c["c_L128"] = (k[:, None] < k[None, :]).astype(bf)
    c["c_negid"] = (-1e9 * np.eye(128)).astype(bf)
    true = ((core + k // 16) % 8) * 16 + (k % 16)
    c["c_Lt"] = (true[:, None] < true[None, :]).astype(bf)
    c["c_ident"] = np.eye(128, dtype=np.float32)
    c["c_ones_bf"] = np.ones((1, 128), dtype=bf)
    c["c_ones_f"] = np.ones((1, 128), dtype=np.float32)
    c["c_onescol_f"] = np.ones((128, 1), dtype=np.float32)
    c["c_onescol_bf"] = np.ones((128, 1), dtype=bf)
    h0 = np.zeros((1, 128), dtype=bf); h0[0, :64] = 1
    h1 = np.zeros((1, 128), dtype=bf); h1[0, 64:] = 1
    c["c_half0"] = h0
    c["c_half1"] = h1
    return c


GW = 240


def gather_window(j):
    return min(max(16 * j - 96, 0), SLOTS - GW), GW


def scatter_chunks(tc):
    lo = max(0, 16 * tc - SM)
    hi = min(SLOTS, 16 * tc + 16 + SM)
    return list(range(lo // 128, (hi + 127) // 128))


PHASES = None
DUMPS = False


def _on(name):
    return PHASES is None or name in PHASES


def build_program():
    nc = bass.Bass("TRN2", target_bir_lowering=False)
    tk = TK(nc)
    S, P = tk.sbuf, tk.psum

    inp = lambda n, s, d: nc.dram_tensor(n, s, d, kind="ExternalInput").ap()
    x_own = inp("x_own", [N, D], F32)
    x_rot = inp("x_rot", [B, N, D], F32)
    Wq = inp("Wq", [D, D], F32)
    Wkv = inp("Wkv", [D, 2 * D], F32)
    Wo = inp("Wo", [D, D], F32)
    bo = inp("bo", [1, D], F32)
    wg = inp("wg", [D, NE], F32)
    Wf = inp("Wf", [NE, D, D], F32)
    bfe = inp("bfe", [NE, D], F32)
    hc0 = host_consts(0)
    cns = {n: inp(n, list(v.shape), BF16 if v.dtype == ml_dtypes.bfloat16 else F32)
           for n, v in hc0.items()}
    out_t = nc.dram_tensor("out", [N, D], F32, kind="ExternalOutput").ap()

    xn_tm_t = S("xn_tm", [128, NT, 1024], BF16)
    xnfm_ar = S("xnfm_ar", [128, 16384], BF16)
    attn_t = S("attn", [128, 8, 2048], BF16)
    pairs = S("pairs", [128, 14368], BF16)
    ar2 = S("ar2", [128, 8192], BF16)
    pt_ar = S("pt_ar", [128, 3072], BF16)
    wt_ar = S("wt_ar", [128, 8192], BF16)
    gm_ar = S("gm_ar", [128, 6144], BF16)
    oh_bf = S("oh_bf", [128, 1024], BF16)
    loctok = S("loctok", [128, 128], F32)
    slotb = S("slotb", [128, 2048], F32)
    wb = S("wb", [128, 2048], BF16)
    scol = S("scol", [128, 16], F32)
    wcol = S("wcol", [128, 16], F32)
    cnttm = S("cnttm", [128, 8], BF16)
    tbtm = S("tbtm", [128, 8], F32)
    vecb = S("vecb", [128, 8], F32)
    v8row = S("v8row", [1, 8], F32)
    tr16 = S("tr16", [16, 128], F32)
    row2k = S("row2k", [1, 2048], F32)
    row2kb = row2k[:, :].bitcast(BF16)
    cntf = row2kb[0:1, 0:1024]
    tbf = row2k[0:1, 0:1024]
    sflat = row2k[0:1, 0:2048]
    wflat = row2k[0:1, 0:2048]
    rstd = S("rstd", [128, NT], F32)
    nmr = S("nmr", [128, NT], F32)
    bn6 = S("bn6", [128, 12], F32)
    bn2 = S("bn2", [128, 2], F32)
    lnv = S("lnv", [128, 1], F32)
    epsc = S("epsc", [128, 1], F32)
    l8 = S("l8", [128, 8], F32)
    mx8 = S("mx8", [128, 8], F32)
    ex8 = S("ex8", [128, 8], F32)
    t8a = S("t8a", [128, 8], F32)
    zs = S("zs", [128, 4], F32)
    padj = S("padj", [128, 2], F32)
    sml = S("sml", [128, 16], F32)
    padjE = S("padjE", [128, 16], F32)
    edsp = S("edsp", [128, 24], F32)
    s2cp = [row2kb[0:1, 0:1024], row2kb[0:1, 1024:2048]]
    rrowp = [row2kb[0:1, 2048:3072], row2kb[0:1, 3072:4096]]
    wgsb = S("wgsb", [128, 64], F32)
    wgtmp = S("wgtmp", [128, 8], F32)
    wgp = S("wgp", [128, 128], BF16)
    bo_bf = S("bo_bf", [1, 1024], BF16)
    be_bf = S("be_bf", [1, 1024], BF16)
    csb = {}
    for n, v in hc0.items():
        dt = BF16 if v.dtype == ml_dtypes.bfloat16 else F32
        csb[n] = S("s_" + n, list(v.shape), dt)

    psA = P("psA", [128, 1024], F32)
    psB = P("psB", [128, 1024], F32)
    psC = P("psC", [128, 512], F32)
    psD = P("psD", [128, 512], F32)
    psE = P("psE", [128, 512], F32)
    psF = P("psF", [128, 512], F32)

    xn_tm = xn_tm_t[:, :, :]
    xnfm = xnfm_ar[:, :].rearrange("p (dc t) -> p dc t", dc=8)
    attn = attn_t[:, :, :]
    gtb = slotb[:, 0:512].bitcast(BF16)
    exclv = gm_ar[:, 0:2048].bitcast(F32)
    locv = gm_ar[:, 2048:4096].bitcast(F32)
    ohf = gm_ar[:, 4096:6144].bitcast(F32)
    stores = []

    qP = pairs[:, 0:2048]
    kP = pairs[:, 2048:4096]
    vS = pairs[:, 4096:6144]
    vT = pairs[:, 6144:8224].rearrange("p (j c) -> p j c", c=130)
    wqS = [pairs[:, 8224 + i * 1024: 8224 + (i + 1) * 1024] for i in range(2)]
    wkS = [pairs[:, 10272 + i * 1024: 10272 + (i + 1) * 1024] for i in range(2)]
    wvS = [pairs[:, 12320:13344], pairs[:, 13344:14368]]

    def mm(ps, lh, rh, start=True, stop=True, tag=""):
        return tk.op("tensor",
                     lambda t, a=ps, b=lh, c=rh, s1=start, s2=stop:
                     t.matmul(a, b, c, start=s1, stop=s2),
                     reads=[lh, rh], writes=[ps], tag=tag)

    def act(out, in_, func, bias=0.0, scale=1.0, tag=""):
        reads = [in_]
        if not isinstance(bias, float):
            reads.append(bias)
        if not isinstance(scale, float):
            reads.append(scale)
        return tk.op("scalar",
                     lambda s, o=out, i=in_, f=func, b=bias, sc=scale:
                     s.activation(o, i, f, bias=b, scale=sc),
                     reads=reads, writes=[out], tag=tag)

    def vcopy(out, in_, tag=""):
        return tk.op("vector", lambda v, o=out, i=in_: v.tensor_copy(o, i),
                     reads=[in_], writes=[out], tag=tag)

    def scopy(out, in_, tag=""):
        return act(out, in_, AF.Identity, tag=tag)

    def vtt(out, a, b, op, tag=""):
        return tk.op("vector",
                     lambda v, o=out, x=a, y=b, w=op: v.tensor_tensor(o, x, y, w),
                     reads=[a, b], writes=[out], tag=tag)

    def vts(out, in_, s1, op, tag=""):
        reads = [in_] + ([s1] if not isinstance(s1, float) else [])
        return tk.op("vector",
                     lambda v, o=out, i=in_, s=s1, w=op:
                     v.tensor_single_scalar(o, i, s, w),
                     reads=reads, writes=[out], tag=tag)

    def vstt(out, in0, s1, in1, op0, op1, tag=""):
        reads = [in0, in1] + ([s1] if not isinstance(s1, float) else [])
        return tk.op("vector",
                     lambda v, o=out, a=in0, s=s1, b=in1, w0=op0, w1=op1:
                     v.scalar_tensor_tensor(o, a, s, b, w0, w1),
                     reads=reads, writes=[out], tag=tag)

    for n in cns:
        tk.dma("sync", csb[n][:, :], cns[n], tag="c_" + n)
    tk.dma("sync", wgsb[:, :].rearrange("p (dc e) -> p dc e", dc=8),
           wg.rearrange("(dc p) e -> p dc e", p=128), tag="wg")
    tk.dma("gpsimd", bo_bf[:, :], bo, tag="bo")
    tk.op("vector", lambda v: v.memset(epsc[:, :], 1e-5), writes=[epsc[:, :]])
    tk.op("vector", lambda v: v.memset(wgp[:, :], 0.0), writes=[wgp[:, :]])

    if _on("route1"):
        mm(psA[0:1, 0:64], csb["c_onescol_f"][:, :], wgsb[:, :], tag="wgsum")
        vcopy(tr16[0:1, 0:64], psA[0:1, 0:64], tag="wgsum_ev")
        tk.op("vector",
              lambda v: v.reduce_sum(
                  wgtmp[0:1, 0:8],
                  tr16[0:1, 0:64].rearrange("o (dc e) -> o e dc", e=8),
                  axis=AX.X),
              reads=[tr16[0:1, 0:64]], writes=[wgtmp[0:1, 0:8]], tag="wgred")
        vts(wgtmp[0:1, 0:8], wgtmp[0:1, 0:8], -1.0 / 1024.0, ALU.mult, tag="wgsc")
        mm(psB[:, 0:8], csb["c_ones_f"][:, :], wgtmp[0:1, 0:8], tag="wgbc")
        for dc in range(8):
            vtt(wgtmp[:, 0:8], psB[:, 0:8], wgsb[:, 8 * dc:8 * dc + 8],
                ALU.add, tag=f"wgp{dc}")
            vcopy(wgp[:, 16 * dc:16 * dc + 8], wgtmp[:, 0:8], tag=f"wgpc{dc}")

    if _on("ln"):
        for j in range(NT):
            xs = ar2[:, (j % 2) * 2048:(j % 2) * 2048 + 2048].bitcast(F32)
            tk.dma("sync", xs, x_own[128 * j:128 * j + 128, :], tag=f"xl{j}")
            tk.op("vector", lambda v, a=xs: v.bn_stats(bn6[:, 0:6], a[:, 0:512]),
                  reads=[xs[:, 0:512]], writes=[bn6[:, 0:6]], tag=f"bns0_{j}")
            tk.op("vector", lambda v, a=xs: v.bn_stats(bn6[:, 6:12], a[:, 512:1024]),
                  reads=[xs[:, 512:1024]], writes=[bn6[:, 6:12]], tag=f"bns1_{j}")
            tk.op("vector", lambda v: v.bn_aggr(bn2[:, :], bn6[:, :]),
                  reads=[bn6[:, :]], writes=[bn2[:, :]], tag=f"bna{j}")
            act(lnv[:, :], bn2[:, 1:2], AF.Ln, bias=epsc[:, :], tag=f"ln{j}")
            act(rstd[:, j:j + 1], lnv[:, :], AF.Exp, scale=-0.5, tag=f"rs{j}")
            vtt(nmr[:, j:j + 1], bn2[:, 0:1], rstd[:, j:j + 1], ALU.mult,
                tag=f"nm{j}")
            vts(nmr[:, j:j + 1], nmr[:, j:j + 1], -1.0, ALU.mult, tag=f"nmn{j}")
            act(xn_tm[:, j, :], xs, AF.Identity, bias=nmr[:, j:j + 1],
                scale=rstd[:, j:j + 1], tag=f"xn{j}")

    if _on("xnfm"):
        for j in range(NT):
            tk.dma("sync",
                   xnfm[:, :, 128 * j:128 * j + 128],
                   xn_tm[:, j, :].rearrange("p (dc q) -> p dc q", dc=8),
                   tag=f"xt{j}", transpose=True)

    xh2s = [ar2[:, 0:2048], ar2[:, 4096:6144]]
    xf2s = [ar2[:, 2048:4096], ar2[:, 6144:8192]]

    def emit_route_group(gp):
        g = 2 * gp
        bidx, jj = g // NT, g % NT
        sl = gp % 2
        xh2 = xh2s[sl]
        xfv = xf2s[sl].rearrange("p (b q) -> p b q", b=16)
        tk.dma("gpsimd", xh2.rearrange("p (t d) -> p t d", t=2),
               x_rot[bidx, 128 * jj:128 * jj + 256, :].rearrange(
                   "(t p) d -> p t d", p=128), tag=f"xh{g}")
        tk.dma("sync",
               xfv, xh2.rearrange("p (b q) -> p b q", b=16),
               tag=f"xf{g}", transpose=True)
        for t in range(2):
            w = 8 * ((g + t) % 64)
            for dc in range(8):
                mm(psA[:, w:w + 8],
                   xfv[:, 8 * t + dc, :], wgp[:, 16 * dc:16 * dc + 8],
                   start=(dc == 0), stop=(dc == 7), tag=f"g{g + t}_{dc}")

    def emit_gev(half):
        vcopy(gtb[:, 512 * half:512 * half + 512], psA[:, 0:512],
              tag=f"gev{half}")

    def emit_ph5():
        gt3v = gtb.rearrange("p (j e) -> p j e", e=8)
        mxt = loctok
        tk.op("vector", lambda v: v.reduce_max(mxt[:, :], gt3v, axis=AX.X),
              reads=[gtb], writes=[mxt[:, :]], tag="rmax")
        eq3 = exclv.rearrange("p (j e) -> p j e", e=8)
        tk.op("vector",
              lambda v: v.tensor_tensor(
                  eq3, gt3v, mxt[:, :].broadcast_to([128, 128, 8]),
                  ALU.is_equal),
              reads=[gtb, mxt[:, :]], writes=[exclv], tag="eq")
        iota_b = csb["c_iota8r"][:, :].__replace__(
            ap=[[8, 128], [0, 128], [1, 8]])
        u3 = locv.rearrange("p (j e) -> p j e", e=8)
        tk.op("vector",
              lambda v: v.tensor_tensor(u3, eq3, iota_b, ALU.mult),
              reads=[exclv, csb["c_iota8r"][:, :]], writes=[locv], tag="umul")
        tk.op("vector", lambda v: v.reduce_max(mxt[:, :], u3, axis=AX.X),
              reads=[locv], writes=[mxt[:, :]], tag="umax")
        umb = mxt[:, :].broadcast_to([128, 128, 8])
        oh3b = oh_bf[:, :].rearrange("p (j e) -> p j e", e=8)
        tk.op("vector",
              lambda v: v.tensor_tensor(oh3b, u3, umb, ALU.is_equal),
              reads=[locv, mxt[:, :]], writes=[oh_bf[:, :]], tag="ohb")
        oh3f = ohf.rearrange("p (j e) -> p j e", e=8)
        tk.op("vector",
              lambda v: v.tensor_tensor(oh3f, u3, umb, ALU.is_equal),
              reads=[locv, mxt[:, :]], writes=[ohf], tag="ohfb")
        for hhalf in range(2):
            sl = slice(512 * hhalf, 512 * hhalf + 512)
            mm(psA[:, sl], csb["c_L128"][:, :], oh_bf[:, sl], tag=f"ex{hhalf}")
            vcopy(exclv[:, sl], psA[:, sl], tag=f"exev{hhalf}")
        for hhalf in range(2):
            sl = slice(512 * hhalf, 512 * hhalf + 512)
            mm(psB[0:1, sl], csb["c_onescol_bf"][:, :], oh_bf[:, sl],
               tag=f"cnt{hhalf}")
        scopy(cntf, psB[0:1, 0:1024], tag="cntev")
        tk.dma("gpsimd", cnttm[:, :], cntf, tag="cntrs")
        mm(psB[:, 0:8], csb["c_Lt"][:, :], cnttm[:, :], tag="tbmm")
        vcopy(tbtm[:, :], psB[:, 0:8], tag="tbev")
        tk.dma("gpsimd", tbf, tbtm[:, :], tag="tbrs")
        for hhalf in range(2):
            sl = slice(512 * hhalf, 512 * hhalf + 512)
            mm(psA[:, sl], csb["c_ones_f"][:, :], tbf[0:1, sl], tag=f"tb{hhalf}")
            vstt(locv[:, sl], psA[:, sl], 1.0, exclv[:, sl], ALU.mult, ALU.add,
                 tag=f"loc{hhalf}")
        for hhalf in range(2):
            sl = slice(512 * hhalf, 512 * hhalf + 512)
            vtt(psB[:, sl], ohf[:, sl], locv[:, sl], ALU.mult, tag=f"lt{hhalf}")
        tk.op("vector",
              lambda v: v.reduce_sum(
                  loctok[:, :],
                  psB[:, 0:1024].rearrange("p (j e) -> p j e", e=8), axis=AX.X),
              reads=[psB[:, :]], writes=[loctok[:, :]], tag="ltred")

    def emit_ph6():
        vts(v8row[:, :], csb["c_iota8"][0:1, 0:8], float(SLOTS), ALU.mult,
            tag="v8m")
        vtt(v8row[:, :], v8row[:, :], tbf[0:1, 0:8], ALU.subtract, tag="v8s")
        mm(psB[:, 0:8], csb["c_ones_f"][:, :], v8row[0:1, 0:8], tag="v8bc")
        vcopy(vecb[:, :], psB[:, 0:8], tag="v8ev")
        for j in range(NT):
            gsl = gtb[:, 8 * j:8 * j + 8]
            tk.op("vector",
                  lambda v, a=gsl, r=rstd[:, j:j + 1]:
                  v.tensor_single_scalar(l8[:, :], a, r, ALU.mult),
                  reads=[gsl, rstd[:, j:j + 1]], writes=[l8[:, :]],
                  tag=f"l8{j}")
            tk.op("vector", lambda v: v.max(mx8[:, :], l8[:, :]),
                  reads=[l8[:, :]], writes=[mx8[:, :]], tag=f"lm{j}")
            vts(zs[:, 0:1], mx8[:, 0:1], -1.0, ALU.mult, tag=f"ln{j}")
            act(ex8[:, :], l8[:, :], AF.Exp, bias=zs[:, 0:1], tag=f"ge{j}")
            tk.op("vector",
                  lambda v: v.reduce_sum(zs[:, 1:2], ex8[:, :], axis=AX.X),
                  reads=[ex8[:, :]], writes=[zs[:, 1:2]], tag=f"gz{j}")
            tk.op("vector", lambda v: v.reciprocal(zs[:, 2:3], zs[:, 1:2]),
                  reads=[zs[:, 1:2]], writes=[zs[:, 2:3]], tag=f"gr{j}")
            vts(zs[:, 3:4], loctok[:, j:j + 1], float(CAP), ALU.is_lt,
                tag=f"kp{j}")
            vtt(wcol[:, j:j + 1], zs[:, 2:3], zs[:, 3:4], ALU.mult,
                tag=f"w{j}")
            vtt(t8a[:, :], locv[:, 8 * j:8 * j + 8], vecb[:, :], ALU.add,
                tag=f"sl{j}")
            vtt(t8a[:, :], t8a[:, :], ohf[:, 8 * j:8 * j + 8], ALU.mult,
                tag=f"slm{j}")
            tk.op("vector",
                  lambda v, o=scol[:, j:j + 1]: v.reduce_sum(o, t8a[:, :],
                                                             axis=AX.X),
                  reads=[t8a[:, :]], writes=[scol[:, j:j + 1]], tag=f"sr{j}")

    def emit_ph7():
        for (colsrc, flat, dstb, dt_out, tag) in (
                (scol, sflat, slotb, F32, "s"), (wcol, wflat, wb, BF16, "w")):
            tk.op("tensor",
                  lambda t, a=colsrc: t.transpose(psA[0:16, 0:128], a[:, :],
                                                  csb["c_ident"][:, :]),
                  reads=[colsrc[:, :], csb["c_ident"][:, :]],
                  writes=[psA[:, 0:128]], tag=f"{tag}tr")
            vcopy(tr16[:, :], psA[0:16, 0:128], tag=f"{tag}trev")
            tk.dma("gpsimd", flat[0:1, 0:2048], tr16[:, :], tag=f"{tag}rs")
            for q in range(4):
                sl = slice(512 * q, 512 * q + 512)
                mm(psB[:, 0:512] if q % 2 == 0 else psB[:, 512:1024],
                   csb["c_ones_f"][:, :], flat[0:1, sl], tag=f"{tag}bc{q}")
                src = psB[:, 0:512] if q % 2 == 0 else psB[:, 512:1024]
                vcopy(dstb[:, sl], src, tag=f"{tag}be{q}")

    if _on("attn"):
        WqR = Wq.rearrange("(dc p) i -> p dc i", p=128)
        WkvR = Wkv.rearrange("(dc p) i -> p dc i", p=128)
        for pr in range(8):
            sl = pr % 2
            tk.dma("gpsimd", wqS[sl].rearrange("p (dc q) -> p dc q", dc=8),
                   WqR[:, :, 128 * pr:128 * pr + 128], tag=f"wq{pr}")
            tk.dma("gpsimd", wkS[sl].rearrange("p (dc q) -> p dc q", dc=8),
                   WkvR[:, :, 128 * pr:128 * pr + 128], tag=f"wk{pr}")
            tk.dma("gpsimd", wvS[sl].rearrange("p (dc q) -> p dc q", dc=8),
                   WkvR[:, :, 1024 + 128 * pr:1024 + 128 * pr + 128],
                   tag=f"wv{pr}")
            wqv = wqS[sl].rearrange("p (dc q) -> p dc q", dc=8)
            wkv = wkS[sl].rearrange("p (dc q) -> p dc q", dc=8)
            wvv = wvS[sl].rearrange("p (dc q) -> p dc q", dc=8)
            for (wsl, dst, tg) in ((wqv, qP, "q"), (wkv, kP, "k"),
                                   (wvv, vS, "v")):
                for nck in range(4):
                    ps = psB[:, 0:512] if nck % 2 == 0 else psB[:, 512:1024]
                    csl = slice(512 * nck, 512 * nck + 512)
                    for dc in range(8):
                        mm(ps, wsl[:, dc, :], xnfm[:, dc, csl],
                           start=(dc == 0), stop=(dc == 7),
                           tag=f"{tg}p{pr}_{nck}_{dc}")
                    vcopy(dst[:, csl], ps, tag=f"{tg}e{pr}_{nck}")
            vSv = vS.rearrange("p (j q) -> p j q", j=NT)
            for h01t in range(2):
                scrv = attn_t[:, 7, 1024 * h01t: 1024 * h01t + 1024].rearrange(
                    "p (j c) -> p j c", c=64)
                tk.dma("sync", scrv, vSv[64 * h01t:64 * h01t + 64, :, :],
                       tag=f"vt{h01t}_{pr}", transpose=True)
                vcopy(vT[:, :, 65 * h01t:65 * h01t + 64], scrv,
                      tag=f"vc{h01t}_{pr}")
            if pr == 0:
                tk.op("vector", lambda v: v.memset(vT[:, :, 64:65], 1.0),
                      writes=[vT[:, :, 64:65]], tag="vo0")
                tk.op("vector", lambda v: v.memset(vT[:, :, 129:130], 1.0),
                      writes=[vT[:, :, 129:130]], tag="vo1")
            pending_div = None
            for c in range(4):
                nj = 4 * c + 4

                def emit_n(j, c=c, nj=nj):
                    voff = max(0, 128 * j - 512 * c)
                    for h01 in range(2):
                        nps = psE if h01 == 0 else psF
                        pt = pt_ar[:, 512 * ((2 * j + h01) % 6):
                                   512 * ((2 * j + h01) % 6) + 512]
                        mm(nps[0:65, voff:512],
                           vT[:, j, 65 * h01:65 * h01 + 65],
                           pt[:, voff:512],
                           start=(j == 0), stop=(j == nj - 1),
                           tag=f"n{pr}_{h01}_{c}_{j}")

                for j in range(nj):
                    voff = max(0, 128 * j - 512 * c)
                    diag = j >= 4 * c
                    for h01 in range(2):
                        hb = 64 * h01
                        if j % 2 == 0:
                            sps = psC if h01 == 0 else psD
                        else:
                            sps = psB[:, 0:512] if h01 == 0 else psB[:, 512:1024]
                        mm(sps[:, voff:512],
                           kP[hb:hb + 64, 128 * j:128 * j + 128],
                           qP[hb:hb + 64, 512 * c + voff:512 * c + 512],
                           start=True, stop=not diag,
                           tag=f"s{pr}_{h01}_{c}_{j}")
                        if diag:
                            mm(sps[:, voff:voff + 128], csb["c_L128"][:, :],
                               csb["c_negid"][:, :], start=False, stop=True,
                               tag=f"m{pr}_{h01}_{c}_{j}")
                        pt = pt_ar[:, 512 * ((2 * j + h01) % 6):
                                   512 * ((2 * j + h01) % 6) + 512]
                        act(pt[:, voff:512], sps[:, voff:512], AF.Exp,
                            scale=0.125, tag=f"e{pr}_{h01}_{c}_{j}")
                    if j > 0:
                        emit_n(j - 1)
                emit_n(nj - 1)
                par = c % 2
                vcopy(attn[0:64, pr, 512 * c:512 * c + 512], psE[0:64, :],
                      tag=f"ae{pr}_{c}")
                vcopy(attn[64:128, pr, 512 * c:512 * c + 512], psF[0:64, :],
                      tag=f"ao{pr}_{c}")
                vcopy(s2cp[par][0:1, 0:512], psE[64:65, :], tag=f"se{pr}_{c}")
                vcopy(s2cp[par][0:1, 512:1024], psF[64:65, :],
                      tag=f"so{pr}_{c}")

                def mkdiv(pr=pr, c=c, par=par):
                    def emit():
                        rr = rrowp[par]
                        lsc = psA[0:1, 512:1024]
                        for hf in range(2):
                            act(lsc, s2cp[par][0:1, 512 * hf:512 * hf + 512],
                                AF.Ln, tag=f"rl{pr}_{c}_{hf}")
                            act(rr[0:1, 512 * hf:512 * hf + 512], lsc,
                                AF.Exp, scale=-1.0, tag=f"rx{pr}_{c}_{hf}")
                        mm(psA[:, 512:1024], csb["c_half0"][:, :],
                           rr[0:1, 0:512], start=True, stop=False,
                           tag=f"rb0{pr}_{c}")
                        mm(psA[:, 512:1024], csb["c_half1"][:, :],
                           rr[0:1, 512:1024], start=False, stop=True,
                           tag=f"rb1{pr}_{c}")
                        rsb = oh_bf[:, 512 * par:512 * par + 512]
                        vcopy(rsb, psA[:, 512:1024], tag=f"re{pr}_{c}")
                        vtt(attn[:, pr, 512 * c:512 * c + 512],
                            attn[:, pr, 512 * c:512 * c + 512], rsb, ALU.mult,
                            tag=f"dv{pr}_{c}")
                    return emit

                if pending_div is not None:
                    pending_div()
                pending_div = mkdiv()
                for gp in range(8 * pr + 2 * c, 8 * pr + 2 * c + 2):
                    emit_route_group(gp)
                if pr == 3 and c == 3:
                    emit_gev(0)
            pending_div()
            if pr == 7:
                emit_gev(1)
                emit_ph5()
                emit_ph6()
                emit_ph7()

    Y = []
    for e in range(NE):
        if e <= 4:
            Y.append(xnfm_ar[:, 3072 * e:3072 * e + 3072])
        elif e == 5:
            Y.append(pt_ar[:, 0:3072])
        else:
            Y.append(pairs[:, 3072 * (e - 6):3072 * (e - 6) + 3072])
    Yv = [y.rearrange("p (sc f) -> p sc f", sc=3) for y in Y]
    wstage = ar2[:, 0:8192].rearrange("p (fc h) -> p fc h", fc=8)
    wt = wt_ar[:, :].rearrange("p (hc f) -> p hc f", hc=8)

    if _on("ffn"):
        for e in range(NE):
            tk.dma("gpsimd", wstage,
                   Wf[e, :, :].rearrange("(fc p) h -> p fc h", p=128),
                   tag=f"wf{e}")
            for fc in range(8):
                tk.dma("sync" if fc % 2 == 0 else "scalar",
                       wt[:, :, 128 * fc:128 * fc + 128],
                       wstage[:, fc, :],
                       tag=f"wt{e}_{fc}", transpose=True)
            tk.dma("gpsimd", be_bf[0:1, 0:1024], bfe[e:e + 1, :], tag=f"be{e}")
            eg, eo = e // 2, e % 2
            if e == 0:
                vtt(sml[:, :], scol[:, :], csb["c_wlo"][:, :], ALU.subtract,
                    tag="sml")
            vts(padjE[:, :], sml[:, :], -float(SLOTS * e), ALU.add,
                tag=f"pa{e}")
            P0 = 3984 * eo
            pt0 = pairs[:, P0:P0 + 384]
            ptr = pairs[:, P0 + 384: P0 + 3984].rearrange(
                "p (j w) -> p j w", w=GW)
            vts(pt0, csb["c_iota"][:, 0:384], padjE[:, 0:1], ALU.is_equal,
                tag=f"p0{e}")
            iota_w = csb["c_iota"][:, 0:GW].__replace__(
                ap=[[384, 128], [0, 15], [1, GW]])
            tk.op("vector",
                  lambda v, o=ptr, i=iota_w, b=padjE:
                  v.tensor_tensor(o, i,
                                  b[:, 1:16].broadcast_to([128, 15, GW]),
                                  ALU.is_equal),
                  reads=[csb["c_iota"][:, 0:GW], padjE[:, :]],
                  writes=[pairs[:, P0 + 384: P0 + 3984]],
                  tag=f"pc{e}")
            gsp = psC if eo == 0 else psD
            gdst = pairs[:, 7968 + 3072 * eo: 7968 + 3072 * eo + 3072]
            gv = gdst.rearrange("p (hc s) -> p hc s", hc=8)
            for hcx in range(8):
                for j in range(NT):
                    if j == 0:
                        wlo, W, ptile = 0, 384, pt0
                    else:
                        wlo, W = gather_window(j)
                        ptile = ptr[:, j - 1, :]
                    mm(gsp[:, wlo:wlo + W],
                       xn_tm[:, j, 128 * hcx:128 * hcx + 128], ptile,
                       start=(j == 0), stop=(j == NT - 1),
                       tag=f"gm{e}_{hcx}_{j}")
                vcopy(gv[:, hcx, 0:SLOTS], gsp[:, 0:SLOTS], tag=f"ge{e}_{hcx}")
            for sc in range(3):
                for fh in range(2):
                    yp = (psE if fh == 0 else psF)[:, :]
                    for hcx in range(8):
                        mm(yp, gv[:, hcx, 128 * sc:128 * sc + 128],
                           wt[:, hcx, 512 * fh:512 * fh + 512],
                           start=(hcx == 0), stop=False,
                           tag=f"ym{e}_{sc}_{fh}_{hcx}")
                    mm(yp, csb["c_ones_bf"][:, :],
                       be_bf[0:1, 512 * fh:512 * fh + 512],
                       start=False, stop=True, tag=f"yb{e}_{sc}_{fh}")
                    scopy(Yv[e][:, sc, 512 * fh:512 * fh + 512], yp,
                          tag=f"ye{e}_{sc}_{fh}")

    if _on("tail"):
        tk.dma("gpsimd", wt_ar[:, :].rearrange("p (ic d) -> p ic d", ic=8),
               Wo.rearrange("(ic p) d -> p ic d", p=128), tag="wo")
        wov = wt_ar[:, :].rearrange("p (ic d) -> p ic d", ic=8)

        def emit_head(tc):
            sl = tc % 2
            xr = ar2[:, 2048 * sl:2048 * sl + 2048].bitcast(F32)
            tk.dma("sync", xr, x_own[128 * tc:128 * tc + 128, :], tag=f"xr{tc}")
            if tc == 0:
                vts(edsp[:, :], csb["c_eds"][:, :], csb["c_piota"][:, :],
                    ALU.add, tag="edsp")
            nchunk = len(scatter_chunks(tc))
            K = NE * nchunk
            rt4 = gm_ar[:, 3072 * sl: 3072 * sl + 128 * K].rearrange(
                "p (e d t) -> p e d t", d=nchunk, t=128)
            sb = slotb[:, 128 * tc:128 * tc + 128]
            slot_b = sb.__replace__(
                ap=[[2048, 128], [0, NE], [0, nchunk], [1, 128]],
                offset=sb.offset)
            ev4 = edsp[:, :].rearrange("p (e d) -> p e d", d=3)[:, :, 0:nchunk]
            tk.op("vector",
                  lambda v, o=rt4, i=slot_b, b=ev4, nch=nchunk:
                  v.tensor_tensor(o, i,
                                  b.broadcast_to([128, NE, nch, 128]),
                                  ALU.is_equal),
                  reads=[sb, edsp[:, :]],
                  writes=[gm_ar[:, 3072 * sl: 3072 * sl + 128 * K]],
                  tag=f"rc{tc}")
            wbsl = wb[:, 128 * tc:128 * tc + 128]
            wb_b = wbsl.__replace__(
                ap=[[2048, 128], [0, K], [1, 128]], offset=wbsl.offset)
            rt3 = gm_ar[:, 3072 * sl: 3072 * sl + 128 * K].rearrange(
                "p (k t) -> p k t", t=128)
            tk.op("vector",
                  lambda v, o=rt3, b=wb_b:
                  v.tensor_tensor(o, o, b, ALU.mult),
                  reads=[wbsl, gm_ar[:, 3072 * sl: 3072 * sl + 128 * K]],
                  writes=[gm_ar[:, 3072 * sl: 3072 * sl + 128 * K]],
                  tag=f"rw{tc}")

        def emit_body(tc):
            sl = tc % 2
            xr = ar2[:, 2048 * sl:2048 * sl + 2048].bitcast(F32)
            rws = []
            k = 0
            for e in range(NE):
                for sc in scatter_chunks(tc):
                    rt = gm_ar[:, 3072 * sl + 128 * k:
                               3072 * sl + 128 * k + 128]
                    k += 1
                    rws.append((e, sc, rt))
            ps = psA if sl == 0 else psB
            for dh in range(2):
                psd = ps[:, 512 * dh:512 * dh + 512]
                for ic in range(8):
                    mm(psd, attn[:, ic, 128 * tc:128 * tc + 128],
                       wov[:, ic, 512 * dh:512 * dh + 512],
                       start=(ic == 0), stop=False, tag=f"pj{tc}_{dh}_{ic}")
                mm(psd, csb["c_ones_bf"][:, :],
                   bo_bf[0:1, 512 * dh:512 * dh + 512],
                   start=False, stop=False, tag=f"pb{tc}_{dh}")
                for i, (e, sc, rt) in enumerate(rws):
                    mm(psd, rt, Yv[e][:, sc, 512 * dh:512 * dh + 512],
                       start=False, stop=(i == len(rws) - 1),
                       tag=f"sc{tc}_{dh}_{e}_{sc}")
                vstt(xr[:, 512 * dh:512 * dh + 512], psd, 1.0,
                     xr[:, 512 * dh:512 * dh + 512], ALU.mult, ALU.add,
                     tag=f"oev{tc}_{dh}")
            stores.append(tk.dma("sync", out_t[128 * tc:128 * tc + 128, :],
                                 xr, tag=f"st{tc}"))

        emit_head(0)
        for tc in range(NT):
            if tc + 1 < NT:
                emit_head(tc + 1)
            emit_body(tc)

    if DUMPS:
        def dump(nm, src, shape, dt):
            t = nc.dram_tensor(nm, shape, dt, kind="ExternalOutput").ap()
            stores.append(tk.dma("gpsimd", t, src, tag="d_" + nm))
        dump("d_xn", xn_tm_t[:, :, :], [128, NT, 1024], BF16)
        dump("d_attn", attn_t[:, :, :], [128, 8, 2048], BF16)
        dump("d_slotb", slotb[:, :], [128, 2048], F32)
        dump("d_wb", wb[:, :], [128, 2048], BF16)
        dump("d_loctok", loctok[:, :], [128, 128], F32)
        dump("d_scol", scol[:, :], [128, 16], F32)
        dump("d_wcol", wcol[:, :], [128, 16], F32)
        dump("d_xnfm", xnfm_ar[:, :], [128, 16384], BF16)
        dump("d_ptar", pt_ar[:, :], [128, 3072], BF16)
        dump("d_pairs", pairs[:, 0:13344], [128, 13344], BF16)
        dump("d_rstd", rstd[:, :], [128, NT], F32)

    tk.build(final_waits=stores, final_engine="sync")
    return nc, tk


_CACHE = {}


def _core_inputs(core, inputs):
    x = np.ascontiguousarray(inputs["L0_attn_x"])
    d = {
        "x_own": x[core],
        "x_rot": np.roll(x, -core, axis=0),
        "Wq": inputs["L0_attn_Wq"],
        "Wkv": inputs["L0_attn_Wkv"],
        "Wo": inputs["L0_attn_Wo"],
        "bo": np.ascontiguousarray(inputs["L0_attn_bo"]).reshape(1, D),
        "wg": inputs["L0_ffn_wg"],
        "Wf": inputs["L0_ffn_W"],
        "bfe": inputs["L0_ffn_b"],
    }
    d.update(host_consts(core))
    return {k: np.ascontiguousarray(v) for k, v in d.items()}


def kernel(**inputs):
    if "nc" not in _CACHE:
        _CACHE["nc"] = build_program()[0]
    nc = _CACHE["nc"]
    in_maps = [_core_inputs(c, inputs) for c in range(B)]
    res = run_bass_kernel_spmd(nc, in_maps, core_ids=list(range(B)))
    return np.stack([np.asarray(res.results[c]["out"], dtype=np.float32)
                     for c in range(B)], axis=0)


if __name__ == "__main__":
    nc, tk = build_program()
    ni = sum(len(s) for s in tk.streams.values())
    print(f"ops={ni} edges={tk.n_edges} waits={tk.n_waits}")


# version 12 (final)
# speedup vs baseline: 2.0851x; 1.0015x over previous
import contextlib
import numpy as np
import ml_dtypes
import concourse.bass as bass
import concourse.mybir as mybir
from concourse.bass_utils import run_bass_kernel_spmd

F32 = mybir.dt.float32
BF16 = mybir.dt.bfloat16
U32 = mybir.dt.uint32
AF = mybir.ActivationFunctionType
ALU = mybir.AluOpType
AX = mybir.AxisListType

B, N, D = 8, 2048, 1024
NH, DH = 16, 64
NE, CAP = 8, 2048
NT = N // 128
GT = B * NT
SLOTS = 384
GM = 96
SM = 128

ENGINES = ("tensor", "vector", "scalar", "gpsimd", "sync")
ESEM = {e: "sem_" + e for e in ENGINES}
_DT_SIZE = {"float32": 4, "float32r": 4, "bfloat16": 2, "float16": 2,
            "uint32": 4, "int32": 4, "uint16": 2, "int16": 2, "uint8": 1}


def _dt_size(dt):
    n = getattr(dt, "name", None) or str(dt).split(".")[-1]
    return _DT_SIZE.get(n, 4)


def _as_ap(x):
    return x.ap() if callable(getattr(x, "ap", None)) else x


def _ranges(x, out=None):
    if out is None:
        out = []
    if x is None or isinstance(x, (int, float, str)):
        return out
    if isinstance(x, (list, tuple)):
        for e in x:
            _ranges(e, out)
        return out
    ap = _as_ap(x)
    try:
        space = str(ap.space)
    except Exception:
        space = "?"
    name = ap.name
    su = space.upper()
    if "SB" not in su and "PSUM" not in su:
        out.append((name, 0, 1 << 20, 0, 1 << 60))
        return out
    try:
        pat = ap.ap
        offset = ap.offset
        esz = _dt_size(ap.dtype)
        pstride, pcount = pat[0]
        plo = (offset // pstride) if pstride else 0
        phi = plo + pcount
        row_off = offset - plo * pstride
        span = 1
        for step, count in pat[1:]:
            span += step * (count - 1)
        out.append((name, plo, phi, row_off * esz, (row_off + span) * esz))
    except Exception:
        out.append((name, 0, 1 << 20, 0, 1 << 60))
    return out


def _is_sbuf(x):
    try:
        return "SB" in str(_as_ap(x).space).upper()
    except Exception:
        return False


class Op:
    __slots__ = ("engine", "fn", "idx", "deps", "needs_inc", "fixed_inc",
                 "sem_val", "tag")

    def __init__(self, engine, fn, idx, tag=""):
        self.engine = engine
        self.fn = fn
        self.idx = idx
        self.deps = []
        self.needs_inc = False
        self.fixed_inc = None
        self.sem_val = None
        self.tag = tag


SERIALIZE = False


class TK:
    def __init__(self, nc, dma_queues=("gpsimd", "sync", "scalar")):
        self.nc = nc
        self.all_ops = []
        self.ctx = contextlib.ExitStack()
        self.streams = {e: [] for e in ENGINES}
        self.sem = {}
        self.cnt = {}
        self.access = {}
        self.n_edges = 0
        for e in ENGINES:
            self.semaphore(ESEM[e])
        self.dma_ring = {}
        self.dma_ring_pos = {}
        for q in dma_queues:
            depth = 6
            ring = []
            for i in range(depth):
                s = f"dma_{q}_{i}"
                self.semaphore(s)
                ring.append(s)
            self.dma_ring[q] = ring
            self.dma_ring_pos[q] = 0
        self.pending_T = []
        self.pending_P = []

    def semaphore(self, name):
        h = self.ctx.enter_context(self.nc.semaphore(name))
        self.sem[name] = h
        self.cnt[name] = 0
        return h

    def sbuf(self, name, shape, dtype):
        return self.ctx.enter_context(self.nc.sbuf_tensor(name, shape, dtype))

    def psum(self, name, shape, dtype):
        return self.ctx.enter_context(self.nc.psum_tensor(name, shape, dtype))

    def _add_access(self, op, kind, rects):
        for rect in rects:
            name, plo, phi, blo, bhi = rect
            hist = self.access.setdefault(name, [])
            for (p, k, r) in hist:
                if p is op:
                    continue
                if kind == "R" and k == "R":
                    continue
                if not (plo < r[2] and r[1] < phi and blo < r[4] and r[3] < bhi):
                    continue
                if p.engine == op.engine == "tensor":
                    continue
                op.deps.append(p)
                self.n_edges += 1
            hist.append((op, kind, rect))

    def op(self, eng, fn, reads=(), writes=(), tag=""):
        o = Op(eng, fn, len(self.streams[eng]), tag)
        self._add_access(o, "R", _ranges(list(reads)))
        self._add_access(o, "W", _ranges(list(writes)))
        if SERIALIZE and self.all_ops:
            o.deps.append(self.all_ops[-1])
        self.all_ops.append(o)
        self.streams[eng].append(o)
        return o

    def _dma_slot(self, queue, n_inc):
        ring = self.dma_ring[queue]
        i = self.dma_ring_pos[queue]
        self.dma_ring_pos[queue] = (i + 1) % len(ring)
        sname = ring[i]
        pre = self.cnt[sname]
        self.cnt[sname] += n_inc
        return sname, pre, self.cnt[sname]

    def dma(self, queue, out, in_, tag="", transpose=False):
        sname, pre, post = self._dma_slot(queue, 16)
        o = Op(queue, None, len(self.streams[queue]), tag)
        o.fixed_inc = (sname, 16, post)
        h = self.sem[sname]

        def fn(e, out=out, in_=in_, h=h, pre=pre, transpose=transpose):
            if pre:
                e.wait_ge(h, pre)
            if transpose:
                return e.dma_start_transpose(out, in_).then_inc(h, 16)
            return e.dma_start(out=out, in_=in_).then_inc(h, 16)
        o.fn = fn
        self._add_access(o, "R", _ranges(in_))
        self._add_access(o, "W", _ranges(out))
        if SERIALIZE and self.all_ops:
            o.deps.append(self.all_ops[-1])
        self.all_ops.append(o)
        if _is_sbuf(in_) and _is_sbuf(out):
            if transpose:
                o.deps.extend(self.pending_P)
                self.n_edges += len(self.pending_P)
                self.pending_P = []
                self.pending_T.append(o)
            else:
                o.deps.extend(self.pending_T)
                self.n_edges += len(self.pending_T)
                self.pending_T = []
                self.pending_P.append(o)
        self.streams[queue].append(o)
        return o

    def build(self, final_waits=(), final_engine="sync"):
        nc = self.nc
        for e in ENGINES:
            for o in self.streams[e]:
                for p in o.deps:
                    if p.fixed_inc is None:
                        p.needs_inc = True
        for e in ENGINES:
            for o in self.streams[e]:
                if o.needs_inc and o.fixed_inc is None:
                    self.cnt[ESEM[e]] += 1
                    o.sem_val = self.cnt[ESEM[e]]

        def token(p):
            if p.fixed_inc is not None:
                return (p.fixed_inc[0], p.fixed_inc[2])
            return (ESEM[p.engine], p.sem_val)

        if final_waits:
            fo = Op(final_engine, None, len(self.streams[final_engine]), "final")
            toks = [token(p) for p in final_waits]

            def ffn(e, toks=toks):
                best = {}
                for s, v in toks:
                    best[s] = max(best.get(s, 0), v)
                last = None
                for s, v in sorted(best.items()):
                    last = e.wait_ge(self.sem[s], v)
                return last
            fo.fn = ffn
            self.streams[final_engine].append(fo)

        emit_plan = {}
        n_waits = 0
        for e in ENGINES:
            water = {}
            plan = []
            for o in self.streams[e]:
                best = {}
                for p in o.deps:
                    s, v = token(p)
                    assert v is not None, f"unresolved producer {p.tag} <- {o.tag}"
                    if best.get(s, -1) < v:
                        best[s] = v
                waits = []
                for s, v in sorted(best.items()):
                    if water.get(s, 0) < v:
                        water[s] = v
                        waits.append((self.sem[s], v))
                n_waits += len(waits)
                plan.append((waits, o))
            emit_plan[e] = plan
        self.n_waits = n_waits

        with nc.Block() as block:
            for e in ENGINES:
                plan = emit_plan[e]
                if not plan:
                    continue

                def body(eng, plan=plan, e=e):
                    sem_h = self.sem[ESEM[e]]
                    for waits, o in plan:
                        for (h, v) in waits:
                            eng.wait_ge(h, v)
                        ins = o.fn(eng)
                        if o.needs_inc and o.fixed_inc is None:
                            assert ins is not None, f"op {o.tag} returned None"
                            ins.then_inc(sem_h, 1)
                getattr(block, e)(body)
        return nc


def host_consts(core):
    bf = ml_dtypes.bfloat16
    c = {}
    c["c_iota8"] = np.tile(np.arange(8, dtype=np.float32), (128, 1))
    c["c_iota8r"] = np.tile(8.0 - np.arange(8, dtype=np.float32), (128, 1))
    c["c_iota"] = np.tile(np.arange(384, dtype=np.float32), (128, 1))
    c["c_piota"] = np.arange(128, dtype=np.float32)[:, None]
    wlo = np.minimum(np.maximum(16 * np.arange(16) - 96, 0), 144)
    c["c_wlo"] = np.tile(wlo.astype(np.float32), (128, 1))
    eds = (384 * np.arange(8)[:, None] + 128 * np.arange(3)[None, :]).reshape(-1)
    c["c_eds"] = np.tile(eds.astype(np.float32), (128, 1))
    k = np.arange(128)
    c["c_L128"] = (k[:, None] < k[None, :]).astype(bf)
    c["c_negid"] = (-1e9 * np.eye(128)).astype(bf)
    true = ((core + k // 16) % 8) * 16 + (k % 16)
    c["c_Lt"] = (true[:, None] < true[None, :]).astype(bf)
    c["c_ident"] = np.eye(128, dtype=np.float32)
    c["c_ones_bf"] = np.ones((1, 128), dtype=bf)
    c["c_ones_f"] = np.ones((1, 128), dtype=np.float32)
    c["c_onescol_f"] = np.ones((128, 1), dtype=np.float32)
    c["c_onescol_bf"] = np.ones((128, 1), dtype=bf)
    h0 = np.zeros((1, 128), dtype=bf); h0[0, :64] = 1
    h1 = np.zeros((1, 128), dtype=bf); h1[0, 64:] = 1
    c["c_half0"] = h0
    c["c_half1"] = h1
    return c


GW = 240


def gather_window(j):
    return min(max(16 * j - 96, 0), SLOTS - GW), GW


def scatter_chunks(tc):
    lo = max(0, 16 * tc - SM)
    hi = min(SLOTS, 16 * tc + 16 + SM)
    return list(range(lo // 128, (hi + 127) // 128))


PHASES = None
DUMPS = False


def _on(name):
    return PHASES is None or name in PHASES


def build_program():
    nc = bass.Bass("TRN2", target_bir_lowering=False)
    tk = TK(nc)
    S, P = tk.sbuf, tk.psum

    inp = lambda n, s, d: nc.dram_tensor(n, s, d, kind="ExternalInput").ap()
    x_own = inp("x_own", [N, D], F32)
    x_rot = inp("x_rot", [B, N, D], F32)
    Wq = inp("Wq", [D, D], F32)
    Wkv = inp("Wkv", [D, 2 * D], F32)
    Wo = inp("Wo", [D, D], F32)
    bo = inp("bo", [1, D], F32)
    wg = inp("wg", [D, NE], F32)
    Wf = inp("Wf", [NE, D, D], F32)
    bfe = inp("bfe", [NE, D], F32)
    hc0 = host_consts(0)
    cns = {n: inp(n, list(v.shape), BF16 if v.dtype == ml_dtypes.bfloat16 else F32)
           for n, v in hc0.items()}
    out_t = nc.dram_tensor("out", [N, D], F32, kind="ExternalOutput").ap()

    xn_tm_t = S("xn_tm", [128, NT, 1024], BF16)
    xnfm_ar = S("xnfm_ar", [128, 16384], BF16)
    attn_t = S("attn", [128, 8, 2048], BF16)
    pairs = S("pairs", [128, 14368], BF16)
    ar2 = S("ar2", [128, 8192], BF16)
    pt_ar = S("pt_ar", [128, 3072], BF16)
    wt_ar = S("wt_ar", [128, 8192], BF16)
    gm_ar = S("gm_ar", [128, 6144], BF16)
    oh_bf = S("oh_bf", [128, 1024], BF16)
    loctok = S("loctok", [128, 128], F32)
    slotb = S("slotb", [128, 2048], F32)
    wb = S("wb", [128, 2048], BF16)
    scol = S("scol", [128, 16], F32)
    wcol = S("wcol", [128, 16], F32)
    cnttm = S("cnttm", [128, 8], BF16)
    tbtm = S("tbtm", [128, 8], F32)
    vecb = S("vecb", [128, 8], F32)
    v8row = S("v8row", [1, 8], F32)
    tr16 = S("tr16", [16, 128], F32)
    row2k = S("row2k", [1, 2048], F32)
    row2kb = row2k[:, :].bitcast(BF16)
    cntf = row2kb[0:1, 0:1024]
    tbf = row2k[0:1, 0:1024]
    sflat = row2k[0:1, 0:2048]
    wflat = row2k[0:1, 0:2048]
    rstd = S("rstd", [128, NT], F32)
    nmr = S("nmr", [128, NT], F32)
    bn6 = S("bn6", [128, 12], F32)
    bn2 = S("bn2", [128, 2], F32)
    lnv = S("lnv", [128, 1], F32)
    epsc = S("epsc", [128, 1], F32)
    l8 = S("l8", [128, 8], F32)
    mx8 = S("mx8", [128, 8], F32)
    ex8 = S("ex8", [128, 8], F32)
    t8a = S("t8a", [128, 8], F32)
    zs = S("zs", [128, 4], F32)
    padj = S("padj", [128, 2], F32)
    sml = S("sml", [128, 16], F32)
    padjE = S("padjE", [128, 16], F32)
    edsp = S("edsp", [128, 24], F32)
    s2cp = [row2kb[0:1, 0:1024], row2kb[0:1, 1024:2048]]
    rrowp = [row2kb[0:1, 2048:3072], row2kb[0:1, 3072:4096]]
    wgsb = S("wgsb", [128, 64], F32)
    wgtmp = S("wgtmp", [128, 8], F32)
    wgp = S("wgp", [128, 128], BF16)
    bo_bf = S("bo_bf", [1, 1024], BF16)
    be_bf = S("be_bf", [1, 1024], BF16)
    csb = {}
    for n, v in hc0.items():
        dt = BF16 if v.dtype == ml_dtypes.bfloat16 else F32
        csb[n] = S("s_" + n, list(v.shape), dt)

    psA = P("psA", [128, 1024], F32)
    psB = P("psB", [128, 1024], F32)
    psC = P("psC", [128, 512], F32)
    psD = P("psD", [128, 512], F32)
    psE = P("psE", [128, 512], F32)
    psF = P("psF", [128, 512], F32)

    xn_tm = xn_tm_t[:, :, :]
    xnfm = xnfm_ar[:, :].rearrange("p (dc t) -> p dc t", dc=8)
    attn = attn_t[:, :, :]
    gtb = slotb[:, 0:512].bitcast(BF16)
    exclv = gm_ar[:, 0:2048].bitcast(F32)
    locv = gm_ar[:, 2048:4096].bitcast(F32)
    ohf = gm_ar[:, 4096:6144].bitcast(F32)
    stores = []

    qP = pairs[:, 0:2048]
    kP = pairs[:, 2048:4096]
    vS = pairs[:, 4096:6144]
    vT = pairs[:, 6144:8224].rearrange("p (j c) -> p j c", c=130)
    wqS = [pairs[:, 8224 + i * 1024: 8224 + (i + 1) * 1024] for i in range(2)]
    wkS = [pairs[:, 10272 + i * 1024: 10272 + (i + 1) * 1024] for i in range(2)]
    wvS = [pairs[:, 12320:13344], pairs[:, 13344:14368]]

    def mm(ps, lh, rh, start=True, stop=True, tag=""):
        return tk.op("tensor",
                     lambda t, a=ps, b=lh, c=rh, s1=start, s2=stop:
                     t.matmul(a, b, c, start=s1, stop=s2),
                     reads=[lh, rh], writes=[ps], tag=tag)

    def act(out, in_, func, bias=0.0, scale=1.0, tag=""):
        reads = [in_]
        if not isinstance(bias, float):
            reads.append(bias)
        if not isinstance(scale, float):
            reads.append(scale)
        return tk.op("scalar",
                     lambda s, o=out, i=in_, f=func, b=bias, sc=scale:
                     s.activation(o, i, f, bias=b, scale=sc),
                     reads=reads, writes=[out], tag=tag)

    def vcopy(out, in_, tag=""):
        return tk.op("vector", lambda v, o=out, i=in_: v.tensor_copy(o, i),
                     reads=[in_], writes=[out], tag=tag)

    def scopy(out, in_, tag=""):
        return act(out, in_, AF.Identity, tag=tag)

    def vtt(out, a, b, op, tag=""):
        return tk.op("vector",
                     lambda v, o=out, x=a, y=b, w=op: v.tensor_tensor(o, x, y, w),
                     reads=[a, b], writes=[out], tag=tag)

    def vts(out, in_, s1, op, tag=""):
        reads = [in_] + ([s1] if not isinstance(s1, float) else [])
        return tk.op("vector",
                     lambda v, o=out, i=in_, s=s1, w=op:
                     v.tensor_single_scalar(o, i, s, w),
                     reads=reads, writes=[out], tag=tag)

    def vstt(out, in0, s1, in1, op0, op1, tag=""):
        reads = [in0, in1] + ([s1] if not isinstance(s1, float) else [])
        return tk.op("vector",
                     lambda v, o=out, a=in0, s=s1, b=in1, w0=op0, w1=op1:
                     v.scalar_tensor_tensor(o, a, s, b, w0, w1),
                     reads=reads, writes=[out], tag=tag)

    for n in cns:
        tk.dma("sync", csb[n][:, :], cns[n], tag="c_" + n)
    tk.dma("sync", wgsb[:, :].rearrange("p (dc e) -> p dc e", dc=8),
           wg.rearrange("(dc p) e -> p dc e", p=128), tag="wg")
    tk.dma("gpsimd", bo_bf[:, :], bo, tag="bo")
    tk.op("vector", lambda v: v.memset(epsc[:, :], 1e-5), writes=[epsc[:, :]])
    tk.op("vector", lambda v: v.memset(wgp[:, :], 0.0), writes=[wgp[:, :]])

    if _on("route1"):
        mm(psA[0:1, 0:64], csb["c_onescol_f"][:, :], wgsb[:, :], tag="wgsum")
        vcopy(tr16[0:1, 0:64], psA[0:1, 0:64], tag="wgsum_ev")
        tk.op("vector",
              lambda v: v.reduce_sum(
                  wgtmp[0:1, 0:8],
                  tr16[0:1, 0:64].rearrange("o (dc e) -> o e dc", e=8),
                  axis=AX.X),
              reads=[tr16[0:1, 0:64]], writes=[wgtmp[0:1, 0:8]], tag="wgred")
        vts(wgtmp[0:1, 0:8], wgtmp[0:1, 0:8], -1.0 / 1024.0, ALU.mult, tag="wgsc")
        mm(psB[:, 0:8], csb["c_ones_f"][:, :], wgtmp[0:1, 0:8], tag="wgbc")
        for dc in range(8):
            vtt(wgtmp[:, 0:8], psB[:, 0:8], wgsb[:, 8 * dc:8 * dc + 8],
                ALU.add, tag=f"wgp{dc}")
            vcopy(wgp[:, 16 * dc:16 * dc + 8], wgtmp[:, 0:8], tag=f"wgpc{dc}")

    if _on("ln"):
        for j in range(NT):
            xs = ar2[:, (j % 4) * 2048:(j % 4) * 2048 + 2048].bitcast(F32)
            tk.dma("sync", xs, x_own[128 * j:128 * j + 128, :], tag=f"xl{j}")
            tk.op("vector", lambda v, a=xs: v.bn_stats(bn6[:, 0:6], a[:, 0:512]),
                  reads=[xs[:, 0:512]], writes=[bn6[:, 0:6]], tag=f"bns0_{j}")
            tk.op("vector", lambda v, a=xs: v.bn_stats(bn6[:, 6:12], a[:, 512:1024]),
                  reads=[xs[:, 512:1024]], writes=[bn6[:, 6:12]], tag=f"bns1_{j}")
            tk.op("vector", lambda v: v.bn_aggr(bn2[:, :], bn6[:, :]),
                  reads=[bn6[:, :]], writes=[bn2[:, :]], tag=f"bna{j}")
            act(lnv[:, :], bn2[:, 1:2], AF.Ln, bias=epsc[:, :], tag=f"ln{j}")
            act(rstd[:, j:j + 1], lnv[:, :], AF.Exp, scale=-0.5, tag=f"rs{j}")
            vtt(nmr[:, j:j + 1], bn2[:, 0:1], rstd[:, j:j + 1], ALU.mult,
                tag=f"nm{j}")
            vts(nmr[:, j:j + 1], nmr[:, j:j + 1], -1.0, ALU.mult, tag=f"nmn{j}")
            act(xn_tm[:, j, :], xs, AF.Identity, bias=nmr[:, j:j + 1],
                scale=rstd[:, j:j + 1], tag=f"xn{j}")

    if _on("xnfm"):
        for j in range(NT):
            tk.dma("sync",
                   xnfm[:, :, 128 * j:128 * j + 128],
                   xn_tm[:, j, :].rearrange("p (dc q) -> p dc q", dc=8),
                   tag=f"xt{j}", transpose=True)

    xh2s = [ar2[:, 0:2048], ar2[:, 4096:6144]]
    xf2s = [ar2[:, 2048:4096], ar2[:, 6144:8192]]

    def emit_route_group(gp):
        g = 2 * gp
        bidx, jj = g // NT, g % NT
        sl = gp % 2
        xh2 = xh2s[sl]
        xfv = xf2s[sl].rearrange("p (b q) -> p b q", b=16)
        tk.dma("gpsimd", xh2.rearrange("p (t d) -> p t d", t=2),
               x_rot[bidx, 128 * jj:128 * jj + 256, :].rearrange(
                   "(t p) d -> p t d", p=128), tag=f"xh{g}")
        tk.dma("sync",
               xfv, xh2.rearrange("p (b q) -> p b q", b=16),
               tag=f"xf{g}", transpose=True)
        for t in range(2):
            w = 8 * ((g + t) % 64)
            for dc in range(8):
                mm(psA[:, w:w + 8],
                   xfv[:, 8 * t + dc, :], wgp[:, 16 * dc:16 * dc + 8],
                   start=(dc == 0), stop=(dc == 7), tag=f"g{g + t}_{dc}")

    def emit_gev(half):
        vcopy(gtb[:, 512 * half:512 * half + 512], psA[:, 0:512],
              tag=f"gev{half}")

    def emit_ph5():
        gt3v = gtb.rearrange("p (j e) -> p j e", e=8)
        mxt = loctok
        tk.op("vector", lambda v: v.reduce_max(mxt[:, :], gt3v, axis=AX.X),
              reads=[gtb], writes=[mxt[:, :]], tag="rmax")
        eq3 = exclv.rearrange("p (j e) -> p j e", e=8)
        tk.op("vector",
              lambda v: v.tensor_tensor(
                  eq3, gt3v, mxt[:, :].broadcast_to([128, 128, 8]),
                  ALU.is_equal),
              reads=[gtb, mxt[:, :]], writes=[exclv], tag="eq")
        iota_b = csb["c_iota8r"][:, :].__replace__(
            ap=[[8, 128], [0, 128], [1, 8]])
        u3 = locv.rearrange("p (j e) -> p j e", e=8)
        tk.op("vector",
              lambda v: v.tensor_tensor(u3, eq3, iota_b, ALU.mult),
              reads=[exclv, csb["c_iota8r"][:, :]], writes=[locv], tag="umul")
        tk.op("vector", lambda v: v.reduce_max(mxt[:, :], u3, axis=AX.X),
              reads=[locv], writes=[mxt[:, :]], tag="umax")
        umb = mxt[:, :].broadcast_to([128, 128, 8])
        oh3b = oh_bf[:, :].rearrange("p (j e) -> p j e", e=8)
        tk.op("vector",
              lambda v: v.tensor_tensor(oh3b, u3, umb, ALU.is_equal),
              reads=[locv, mxt[:, :]], writes=[oh_bf[:, :]], tag="ohb")
        oh3f = ohf.rearrange("p (j e) -> p j e", e=8)
        tk.op("vector",
              lambda v: v.tensor_tensor(oh3f, u3, umb, ALU.is_equal),
              reads=[locv, mxt[:, :]], writes=[ohf], tag="ohfb")
        for hhalf in range(2):
            sl = slice(512 * hhalf, 512 * hhalf + 512)
            mm(psA[:, sl], csb["c_L128"][:, :], oh_bf[:, sl], tag=f"ex{hhalf}")
            vcopy(exclv[:, sl], psA[:, sl], tag=f"exev{hhalf}")
        for hhalf in range(2):
            sl = slice(512 * hhalf, 512 * hhalf + 512)
            mm(psB[0:1, sl], csb["c_onescol_bf"][:, :], oh_bf[:, sl],
               tag=f"cnt{hhalf}")
        scopy(cntf, psB[0:1, 0:1024], tag="cntev")
        tk.dma("gpsimd", cnttm[:, :], cntf, tag="cntrs")
        mm(psB[:, 0:8], csb["c_Lt"][:, :], cnttm[:, :], tag="tbmm")
        vcopy(tbtm[:, :], psB[:, 0:8], tag="tbev")
        tk.dma("gpsimd", tbf, tbtm[:, :], tag="tbrs")
        for hhalf in range(2):
            sl = slice(512 * hhalf, 512 * hhalf + 512)
            mm(psA[:, sl], csb["c_ones_f"][:, :], tbf[0:1, sl], tag=f"tb{hhalf}")
            vstt(locv[:, sl], psA[:, sl], 1.0, exclv[:, sl], ALU.mult, ALU.add,
                 tag=f"loc{hhalf}")
        for hhalf in range(2):
            sl = slice(512 * hhalf, 512 * hhalf + 512)
            vtt(psB[:, sl], ohf[:, sl], locv[:, sl], ALU.mult, tag=f"lt{hhalf}")
        tk.op("vector",
              lambda v: v.reduce_sum(
                  loctok[:, :],
                  psB[:, 0:1024].rearrange("p (j e) -> p j e", e=8), axis=AX.X),
              reads=[psB[:, :]], writes=[loctok[:, :]], tag="ltred")

    def emit_ph6():
        vts(v8row[:, :], csb["c_iota8"][0:1, 0:8], float(SLOTS), ALU.mult,
            tag="v8m")
        vtt(v8row[:, :], v8row[:, :], tbf[0:1, 0:8], ALU.subtract, tag="v8s")
        mm(psB[:, 0:8], csb["c_ones_f"][:, :], v8row[0:1, 0:8], tag="v8bc")
        vcopy(vecb[:, :], psB[:, 0:8], tag="v8ev")
        for j in range(NT):
            gsl = gtb[:, 8 * j:8 * j + 8]
            tk.op("vector",
                  lambda v, a=gsl, r=rstd[:, j:j + 1]:
                  v.tensor_single_scalar(l8[:, :], a, r, ALU.mult),
                  reads=[gsl, rstd[:, j:j + 1]], writes=[l8[:, :]],
                  tag=f"l8{j}")
            tk.op("vector", lambda v: v.max(mx8[:, :], l8[:, :]),
                  reads=[l8[:, :]], writes=[mx8[:, :]], tag=f"lm{j}")
            vts(zs[:, 0:1], mx8[:, 0:1], -1.0, ALU.mult, tag=f"ln{j}")
            act(ex8[:, :], l8[:, :], AF.Exp, bias=zs[:, 0:1], tag=f"ge{j}")
            tk.op("vector",
                  lambda v: v.reduce_sum(zs[:, 1:2], ex8[:, :], axis=AX.X),
                  reads=[ex8[:, :]], writes=[zs[:, 1:2]], tag=f"gz{j}")
            tk.op("vector", lambda v: v.reciprocal(zs[:, 2:3], zs[:, 1:2]),
                  reads=[zs[:, 1:2]], writes=[zs[:, 2:3]], tag=f"gr{j}")
            vts(zs[:, 3:4], loctok[:, j:j + 1], float(CAP), ALU.is_lt,
                tag=f"kp{j}")
            vtt(wcol[:, j:j + 1], zs[:, 2:3], zs[:, 3:4], ALU.mult,
                tag=f"w{j}")
            vtt(t8a[:, :], locv[:, 8 * j:8 * j + 8], vecb[:, :], ALU.add,
                tag=f"sl{j}")
            vtt(t8a[:, :], t8a[:, :], ohf[:, 8 * j:8 * j + 8], ALU.mult,
                tag=f"slm{j}")
            tk.op("vector",
                  lambda v, o=scol[:, j:j + 1]: v.reduce_sum(o, t8a[:, :],
                                                             axis=AX.X),
                  reads=[t8a[:, :]], writes=[scol[:, j:j + 1]], tag=f"sr{j}")

    def emit_ph7():
        for (colsrc, flat, dstb, dt_out, tag) in (
                (scol, sflat, slotb, F32, "s"), (wcol, wflat, wb, BF16, "w")):
            tk.op("tensor",
                  lambda t, a=colsrc: t.transpose(psA[0:16, 0:128], a[:, :],
                                                  csb["c_ident"][:, :]),
                  reads=[colsrc[:, :], csb["c_ident"][:, :]],
                  writes=[psA[:, 0:128]], tag=f"{tag}tr")
            vcopy(tr16[:, :], psA[0:16, 0:128], tag=f"{tag}trev")
            tk.dma("gpsimd", flat[0:1, 0:2048], tr16[:, :], tag=f"{tag}rs")
            for q in range(4):
                sl = slice(512 * q, 512 * q + 512)
                mm(psB[:, 0:512] if q % 2 == 0 else psB[:, 512:1024],
                   csb["c_ones_f"][:, :], flat[0:1, sl], tag=f"{tag}bc{q}")
                src = psB[:, 0:512] if q % 2 == 0 else psB[:, 512:1024]
                vcopy(dstb[:, sl], src, tag=f"{tag}be{q}")

    if _on("attn"):
        WqR = Wq.rearrange("(dc p) i -> p dc i", p=128)
        WkvR = Wkv.rearrange("(dc p) i -> p dc i", p=128)
        for pr in range(8):
            sl = pr % 2
            tk.dma("gpsimd", wqS[sl].rearrange("p (dc q) -> p dc q", dc=8),
                   WqR[:, :, 128 * pr:128 * pr + 128], tag=f"wq{pr}")
            tk.dma("gpsimd", wkS[sl].rearrange("p (dc q) -> p dc q", dc=8),
                   WkvR[:, :, 128 * pr:128 * pr + 128], tag=f"wk{pr}")
            tk.dma("gpsimd", wvS[sl].rearrange("p (dc q) -> p dc q", dc=8),
                   WkvR[:, :, 1024 + 128 * pr:1024 + 128 * pr + 128],
                   tag=f"wv{pr}")
            wqv = wqS[sl].rearrange("p (dc q) -> p dc q", dc=8)
            wkv = wkS[sl].rearrange("p (dc q) -> p dc q", dc=8)
            wvv = wvS[sl].rearrange("p (dc q) -> p dc q", dc=8)
            for (wsl, dst, tg) in ((wqv, qP, "q"), (wkv, kP, "k"),
                                   (wvv, vS, "v")):
                for nck in range(4):
                    ps = psB[:, 0:512] if nck % 2 == 0 else psB[:, 512:1024]
                    csl = slice(512 * nck, 512 * nck + 512)
                    for dc in range(8):
                        mm(ps, wsl[:, dc, :], xnfm[:, dc, csl],
                           start=(dc == 0), stop=(dc == 7),
                           tag=f"{tg}p{pr}_{nck}_{dc}")
                    vcopy(dst[:, csl], ps, tag=f"{tg}e{pr}_{nck}")
            vSv = vS.rearrange("p (j q) -> p j q", j=NT)
            for h01t in range(2):
                scrv = attn_t[:, 7, 1024 * h01t: 1024 * h01t + 1024].rearrange(
                    "p (j c) -> p j c", c=64)
                tk.dma("sync", scrv, vSv[64 * h01t:64 * h01t + 64, :, :],
                       tag=f"vt{h01t}_{pr}", transpose=True)
                vcopy(vT[:, :, 65 * h01t:65 * h01t + 64], scrv,
                      tag=f"vc{h01t}_{pr}")
            if pr == 0:
                tk.op("vector", lambda v: v.memset(vT[:, :, 64:65], 1.0),
                      writes=[vT[:, :, 64:65]], tag="vo0")
                tk.op("vector", lambda v: v.memset(vT[:, :, 129:130], 1.0),
                      writes=[vT[:, :, 129:130]], tag="vo1")
            pending_div = None
            for c in range(4):
                nj = 4 * c + 4

                def emit_n(j, c=c, nj=nj):
                    voff = max(0, 128 * j - 512 * c)
                    for h01 in range(2):
                        nps = psE if h01 == 0 else psF
                        pt = pt_ar[:, 512 * ((2 * j + h01) % 6):
                                   512 * ((2 * j + h01) % 6) + 512]
                        mm(nps[0:65, voff:512],
                           vT[:, j, 65 * h01:65 * h01 + 65],
                           pt[:, voff:512],
                           start=(j == 0), stop=(j == nj - 1),
                           tag=f"n{pr}_{h01}_{c}_{j}")

                for j in range(nj):
                    voff = max(0, 128 * j - 512 * c)
                    diag = j >= 4 * c
                    for h01 in range(2):
                        hb = 64 * h01
                        if j % 2 == 0:
                            sps = psC if h01 == 0 else psD
                        else:
                            sps = psB[:, 0:512] if h01 == 0 else psB[:, 512:1024]
                        mm(sps[:, voff:512],
                           kP[hb:hb + 64, 128 * j:128 * j + 128],
                           qP[hb:hb + 64, 512 * c + voff:512 * c + 512],
                           start=True, stop=not diag,
                           tag=f"s{pr}_{h01}_{c}_{j}")
                        if diag:
                            mm(sps[:, voff:voff + 128], csb["c_L128"][:, :],
                               csb["c_negid"][:, :], start=False, stop=True,
                               tag=f"m{pr}_{h01}_{c}_{j}")
                        pt = pt_ar[:, 512 * ((2 * j + h01) % 6):
                                   512 * ((2 * j + h01) % 6) + 512]
                        act(pt[:, voff:512], sps[:, voff:512], AF.Exp,
                            scale=0.125, tag=f"e{pr}_{h01}_{c}_{j}")
                    if j > 0:
                        emit_n(j - 1)
                emit_n(nj - 1)
                par = c % 2
                vcopy(attn[0:64, pr, 512 * c:512 * c + 512], psE[0:64, :],
                      tag=f"ae{pr}_{c}")
                vcopy(attn[64:128, pr, 512 * c:512 * c + 512], psF[0:64, :],
                      tag=f"ao{pr}_{c}")
                vcopy(s2cp[par][0:1, 0:512], psE[64:65, :], tag=f"se{pr}_{c}")
                vcopy(s2cp[par][0:1, 512:1024], psF[64:65, :],
                      tag=f"so{pr}_{c}")

                def mkdiv(pr=pr, c=c, par=par):
                    def emit():
                        rr = rrowp[par]
                        lsc = psA[0:1, 512:1024]
                        for hf in range(2):
                            act(lsc, s2cp[par][0:1, 512 * hf:512 * hf + 512],
                                AF.Ln, tag=f"rl{pr}_{c}_{hf}")
                            act(rr[0:1, 512 * hf:512 * hf + 512], lsc,
                                AF.Exp, scale=-1.0, tag=f"rx{pr}_{c}_{hf}")
                        mm(psA[:, 512:1024], csb["c_half0"][:, :],
                           rr[0:1, 0:512], start=True, stop=False,
                           tag=f"rb0{pr}_{c}")
                        mm(psA[:, 512:1024], csb["c_half1"][:, :],
                           rr[0:1, 512:1024], start=False, stop=True,
                           tag=f"rb1{pr}_{c}")
                        rsb = oh_bf[:, 512 * par:512 * par + 512]
                        vcopy(rsb, psA[:, 512:1024], tag=f"re{pr}_{c}")
                        vtt(attn[:, pr, 512 * c:512 * c + 512],
                            attn[:, pr, 512 * c:512 * c + 512], rsb, ALU.mult,
                            tag=f"dv{pr}_{c}")
                    return emit

                if pending_div is not None:
                    pending_div()
                pending_div = mkdiv()
                for gp in range(8 * pr + 2 * c, 8 * pr + 2 * c + 2):
                    emit_route_group(gp)
                if pr == 3 and c == 3:
                    emit_gev(0)
            pending_div()
            if pr == 7:
                emit_gev(1)
                emit_ph5()
                emit_ph6()
                emit_ph7()

    Y = []
    for e in range(NE):
        if e <= 4:
            Y.append(xnfm_ar[:, 3072 * e:3072 * e + 3072])
        elif e == 5:
            Y.append(pt_ar[:, 0:3072])
        else:
            Y.append(pairs[:, 3072 * (e - 6):3072 * (e - 6) + 3072])
    Yv = [y.rearrange("p (sc f) -> p sc f", sc=3) for y in Y]
    wstage = ar2[:, 0:8192].rearrange("p (fc h) -> p fc h", fc=8)
    wt = wt_ar[:, :].rearrange("p (hc f) -> p hc f", hc=8)

    if _on("ffn"):
        for e in range(NE):
            tk.dma("gpsimd", wstage,
                   Wf[e, :, :].rearrange("(fc p) h -> p fc h", p=128),
                   tag=f"wf{e}")
            for fc in range(8):
                tk.dma("sync" if fc % 2 == 0 else "scalar",
                       wt[:, :, 128 * fc:128 * fc + 128],
                       wstage[:, fc, :],
                       tag=f"wt{e}_{fc}", transpose=True)
            tk.dma("gpsimd", be_bf[0:1, 0:1024], bfe[e:e + 1, :], tag=f"be{e}")
            eg, eo = e // 2, e % 2
            if e == 0:
                vtt(sml[:, :], scol[:, :], csb["c_wlo"][:, :], ALU.subtract,
                    tag="sml")
            vts(padjE[:, :], sml[:, :], -float(SLOTS * e), ALU.add,
                tag=f"pa{e}")
            P0 = 3984 * eo
            pt0 = pairs[:, P0:P0 + 384]
            ptr = pairs[:, P0 + 384: P0 + 3984].rearrange(
                "p (j w) -> p j w", w=GW)
            vts(pt0, csb["c_iota"][:, 0:384], padjE[:, 0:1], ALU.is_equal,
                tag=f"p0{e}")
            iota_w = csb["c_iota"][:, 0:GW].__replace__(
                ap=[[384, 128], [0, 15], [1, GW]])
            tk.op("vector",
                  lambda v, o=ptr, i=iota_w, b=padjE:
                  v.tensor_tensor(o, i,
                                  b[:, 1:16].broadcast_to([128, 15, GW]),
                                  ALU.is_equal),
                  reads=[csb["c_iota"][:, 0:GW], padjE[:, :]],
                  writes=[pairs[:, P0 + 384: P0 + 3984]],
                  tag=f"pc{e}")
            gsp = psC if eo == 0 else psD
            gdst = pairs[:, 7968 + 3072 * eo: 7968 + 3072 * eo + 3072]
            gv = gdst.rearrange("p (hc s) -> p hc s", hc=8)
            for hcx in range(8):
                for j in range(NT):
                    if j == 0:
                        wlo, W, ptile = 0, 384, pt0
                    else:
                        wlo, W = gather_window(j)
                        ptile = ptr[:, j - 1, :]
                    mm(gsp[:, wlo:wlo + W],
                       xn_tm[:, j, 128 * hcx:128 * hcx + 128], ptile,
                       start=(j == 0), stop=(j == NT - 1),
                       tag=f"gm{e}_{hcx}_{j}")
                vcopy(gv[:, hcx, 0:SLOTS], gsp[:, 0:SLOTS], tag=f"ge{e}_{hcx}")
            for sc in range(3):
                for fh in range(2):
                    yp = (psE if fh == 0 else psF)[:, :]
                    for hcx in range(8):
                        mm(yp, gv[:, hcx, 128 * sc:128 * sc + 128],
                           wt[:, hcx, 512 * fh:512 * fh + 512],
                           start=(hcx == 0), stop=False,
                           tag=f"ym{e}_{sc}_{fh}_{hcx}")
                    mm(yp, csb["c_ones_bf"][:, :],
                       be_bf[0:1, 512 * fh:512 * fh + 512],
                       start=False, stop=True, tag=f"yb{e}_{sc}_{fh}")
                    scopy(Yv[e][:, sc, 512 * fh:512 * fh + 512], yp,
                          tag=f"ye{e}_{sc}_{fh}")

    if _on("tail"):
        tk.dma("gpsimd", wt_ar[:, :].rearrange("p (ic d) -> p ic d", ic=8),
               Wo.rearrange("(ic p) d -> p ic d", p=128), tag="wo")
        wov = wt_ar[:, :].rearrange("p (ic d) -> p ic d", ic=8)

        def emit_head(tc):
            sl = tc % 2
            xr = ar2[:, 2048 * sl:2048 * sl + 2048].bitcast(F32)
            tk.dma("sync", xr, x_own[128 * tc:128 * tc + 128, :], tag=f"xr{tc}")
            if tc == 0:
                vts(edsp[:, :], csb["c_eds"][:, :], csb["c_piota"][:, :],
                    ALU.add, tag="edsp")
            nchunk = len(scatter_chunks(tc))
            K = NE * nchunk
            rt4 = gm_ar[:, 3072 * sl: 3072 * sl + 128 * K].rearrange(
                "p (e d t) -> p e d t", d=nchunk, t=128)
            sb = slotb[:, 128 * tc:128 * tc + 128]
            slot_b = sb.__replace__(
                ap=[[2048, 128], [0, NE], [0, nchunk], [1, 128]],
                offset=sb.offset)
            ev4 = edsp[:, :].rearrange("p (e d) -> p e d", d=3)[:, :, 0:nchunk]
            tk.op("vector",
                  lambda v, o=rt4, i=slot_b, b=ev4, nch=nchunk:
                  v.tensor_tensor(o, i,
                                  b.broadcast_to([128, NE, nch, 128]),
                                  ALU.is_equal),
                  reads=[sb, edsp[:, :]],
                  writes=[gm_ar[:, 3072 * sl: 3072 * sl + 128 * K]],
                  tag=f"rc{tc}")
            wbsl = wb[:, 128 * tc:128 * tc + 128]
            wb_b = wbsl.__replace__(
                ap=[[2048, 128], [0, K], [1, 128]], offset=wbsl.offset)
            rt3 = gm_ar[:, 3072 * sl: 3072 * sl + 128 * K].rearrange(
                "p (k t) -> p k t", t=128)
            tk.op("vector",
                  lambda v, o=rt3, b=wb_b:
                  v.tensor_tensor(o, o, b, ALU.mult),
                  reads=[wbsl, gm_ar[:, 3072 * sl: 3072 * sl + 128 * K]],
                  writes=[gm_ar[:, 3072 * sl: 3072 * sl + 128 * K]],
                  tag=f"rw{tc}")

        def emit_body(tc):
            sl = tc % 2
            xr = ar2[:, 2048 * sl:2048 * sl + 2048].bitcast(F32)
            rws = []
            k = 0
            for e in range(NE):
                for sc in scatter_chunks(tc):
                    rt = gm_ar[:, 3072 * sl + 128 * k:
                               3072 * sl + 128 * k + 128]
                    k += 1
                    rws.append((e, sc, rt))
            ps = psA if sl == 0 else psB
            for dh in range(2):
                psd = ps[:, 512 * dh:512 * dh + 512]
                for ic in range(8):
                    mm(psd, attn[:, ic, 128 * tc:128 * tc + 128],
                       wov[:, ic, 512 * dh:512 * dh + 512],
                       start=(ic == 0), stop=False, tag=f"pj{tc}_{dh}_{ic}")
                mm(psd, csb["c_ones_bf"][:, :],
                   bo_bf[0:1, 512 * dh:512 * dh + 512],
                   start=False, stop=False, tag=f"pb{tc}_{dh}")
                for i, (e, sc, rt) in enumerate(rws):
                    mm(psd, rt, Yv[e][:, sc, 512 * dh:512 * dh + 512],
                       start=False, stop=(i == len(rws) - 1),
                       tag=f"sc{tc}_{dh}_{e}_{sc}")
                vstt(xr[:, 512 * dh:512 * dh + 512], psd, 1.0,
                     xr[:, 512 * dh:512 * dh + 512], ALU.mult, ALU.add,
                     tag=f"oev{tc}_{dh}")
            stores.append(tk.dma("sync", out_t[128 * tc:128 * tc + 128, :],
                                 xr, tag=f"st{tc}"))

        emit_head(0)
        for tc in range(NT):
            if tc + 1 < NT:
                emit_head(tc + 1)
            emit_body(tc)

    if DUMPS:
        def dump(nm, src, shape, dt):
            t = nc.dram_tensor(nm, shape, dt, kind="ExternalOutput").ap()
            stores.append(tk.dma("gpsimd", t, src, tag="d_" + nm))
        dump("d_xn", xn_tm_t[:, :, :], [128, NT, 1024], BF16)
        dump("d_attn", attn_t[:, :, :], [128, 8, 2048], BF16)
        dump("d_slotb", slotb[:, :], [128, 2048], F32)
        dump("d_wb", wb[:, :], [128, 2048], BF16)
        dump("d_loctok", loctok[:, :], [128, 128], F32)
        dump("d_scol", scol[:, :], [128, 16], F32)
        dump("d_wcol", wcol[:, :], [128, 16], F32)
        dump("d_xnfm", xnfm_ar[:, :], [128, 16384], BF16)
        dump("d_ptar", pt_ar[:, :], [128, 3072], BF16)
        dump("d_pairs", pairs[:, 0:13344], [128, 13344], BF16)
        dump("d_rstd", rstd[:, :], [128, NT], F32)

    tk.build(final_waits=stores, final_engine="sync")
    return nc, tk


_CACHE = {}


def _core_inputs(core, inputs):
    x = np.ascontiguousarray(inputs["L0_attn_x"])
    d = {
        "x_own": x[core],
        "x_rot": np.roll(x, -core, axis=0),
        "Wq": inputs["L0_attn_Wq"],
        "Wkv": inputs["L0_attn_Wkv"],
        "Wo": inputs["L0_attn_Wo"],
        "bo": np.ascontiguousarray(inputs["L0_attn_bo"]).reshape(1, D),
        "wg": inputs["L0_ffn_wg"],
        "Wf": inputs["L0_ffn_W"],
        "bfe": inputs["L0_ffn_b"],
    }
    d.update(host_consts(core))
    return {k: np.ascontiguousarray(v) for k, v in d.items()}


def kernel(**inputs):
    if "nc" not in _CACHE:
        _CACHE["nc"] = build_program()[0]
    nc = _CACHE["nc"]
    in_maps = [_core_inputs(c, inputs) for c in range(B)]
    res = run_bass_kernel_spmd(nc, in_maps, core_ids=list(range(B)))
    return np.stack([np.asarray(res.results[c]["out"], dtype=np.float32)
                     for c in range(B)], axis=0)


if __name__ == "__main__":
    nc, tk = build_program()
    ni = sum(len(s) for s in tk.streams.values())
    print(f"ops={ni} edges={tk.n_edges} waits={tk.n_waits}")
